# version 07
# baseline (speedup 1.0000x reference)
import numpy as np
import ml_dtypes
import concourse.bass as bass
import concourse.mybir as mybir
from concourse.bass_utils import run_bass_kernel_spmd
from contextlib import ExitStack

F32 = mybir.dt.float32
F16 = mybir.dt.float16
BF16 = mybir.dt.bfloat16
I16 = mybir.dt.int16
I32 = mybir.dt.int32
OP = mybir.AluOpType
AF = mybir.ActivationFunctionType
AX = mybir.AxisListType
bf16 = ml_dtypes.bfloat16

NCORE = 8
TCORE = 4096
MT = 512
NMT = TCORE // MT
D = 1024
FI = 2816
ICN = 22
NIG = 11
WDCH = 3
MIDS = [0.25, 0.75, 1.25, 1.75, 2.5, 3.5, 5.0]


def build_nc():
    nc = bass.Bass("TRN2", target_bir_lowering=False)
    x_in = nc.dram_tensor("x", [TCORE, D], F32, kind="ExternalInput").ap()
    wqkv_in = nc.dram_tensor("wqkv", [D, 3 * D], F32, kind="ExternalInput").ap()
    wf_in = nc.dram_tensor("wf", [64, 64], F32, kind="ExternalInput").ap()
    wout_in = nc.dram_tensor("wout", [D, D], F32, kind="ExternalInput").ap()
    bqc_in = nc.dram_tensor("bqc", [128, 8], F32, kind="ExternalInput").ap()
    bkvr_in = nc.dram_tensor("bkvr", [1, 2 * D], F32, kind="ExternalInput").ap()
    bout_in = nc.dram_tensor("boutr", [1, D], F32, kind="ExternalInput").ap()
    wg_sl_in = nc.dram_tensor("wg_sl", [FI, 128], F32, kind="ExternalInput").ap()
    wu_sl_in = nc.dram_tensor("wu_sl", [FI, 128], F32, kind="ExternalInput").ap()
    wd_sl_in = nc.dram_tensor("wd_sl", [WDCH, D, 128], F32, kind="ExternalInput").ap()
    idb_in = nc.dram_tensor("idb", [128, 128], BF16, kind="ExternalInput").ap()
    idh_in = nc.dram_tensor("idh", [128, 128], F16, kind="ExternalInput").ap()
    ones1_in = nc.dram_tensor("ones1", [1, 128], BF16, kind="ExternalInput").ap()
    onesc_in = nc.dram_tensor("onesc", [128, 8], BF16, kind="ExternalInput").ap()
    selb_in = nc.dram_tensor("selb", [16, 8 * 128], BF16, kind="ExternalInput").ap()
    out_d = nc.dram_tensor("out", [TCORE, D], F32, kind="ExternalOutput").ap()
    import os as _os
    DBG = bool(int(_os.environ.get("KDBG", "0")))
    if DBG:
        dbg_qres = nc.dram_tensor("dbg_qres", [128, 8, TCORE], F32, kind="ExternalOutput").ap()
        dbg_bdkv = nc.dram_tensor("dbg_bdkv", [128, 8, 128], F32, kind="ExternalOutput").ap()
        dbg_ksb8 = nc.dram_tensor("dbg_ksb8", [128, 8, 16], F32, kind="ExternalOutput").ap()
        dbg_xqf = nc.dram_tensor("dbg_xqf", [128, 8, MT], F32, kind="ExternalOutput").ap()
        dbg_hqT = nc.dram_tensor("dbg_hqT", [128, ICN, MT], F32, kind="ExternalOutput").ap()
        dbg_attn2 = nc.dram_tensor("dbg_attn2", [128, 8, MT], F32, kind="ExternalOutput").ap()
        dbg_htok = nc.dram_tensor("dbg_htok", [128, 2, FI], F32, kind="ExternalOutput").ap()
        dbg_nrec = nc.dram_tensor("dbg_nrec", [16, 512], F32, kind="ExternalOutput").ap()
        dbg_st = nc.dram_tensor("dbg_st", [128, 64], F32, kind="ExternalOutput").ap()
        dbg_watt = nc.dram_tensor("dbg_watt", [128, 8, 3 * D], F32, kind="ExternalOutput").ap()
        dbg_xnf = nc.dram_tensor("dbg_xnf", [128, 8, MT], F32, kind="ExternalOutput").ap()
        dbg_xnb = nc.dram_tensor("dbg_xnb", [128, 4, D], F32, kind="ExternalOutput").ap()
        dbg_kvall = nc.dram_tensor("dbg_kvall", [128, 1032], F32, kind="ExternalOutput").ap()
        dbg_xqall = nc.dram_tensor("dbg_xqall", [NMT, 128, 8, MT], F16, kind="ExternalOutput").ap()
        dbg_hqTall = nc.dram_tensor("dbg_hqTall", [NMT, 128, ICN, MT], F16, kind="ExternalOutput").ap()
        dbg_at2all = nc.dram_tensor("dbg_at2all", [NMT, 128, 8, MT], BF16, kind="ExternalOutput").ap()
        dbg_agwgu = nc.dram_tensor("dbg_agwgu", [8, 128, 2 * FI], F16, kind="ExternalOutput").ap()
        dbg_wgusb = nc.dram_tensor("dbg_wgusb", [128, 2, 8, 512], F16, kind="ExternalOutput").ap()
        dbg_gup = nc.dram_tensor("dbg_gup", [128, 512], F16, kind="ExternalOutput").ap()
        dbg_sgt = nc.dram_tensor("dbg_sgt", [128, 256], F16, kind="ExternalOutput").ap()
    ag_wgu_in = nc.dram_tensor("ag_wgu_in", [128, 2 * FI], F16)
    ag_wgu_out = nc.dram_tensor("ag_wgu_out", [8, 128, 2 * FI], F16)
    ag_wd_in = nc.dram_tensor("ag_wd_in", [WDCH, 128, D], F16)
    ag_wd_out = nc.dram_tensor("ag_wd_out", [8, WDCH, 128, D], F16)
    ar_in = nc.dram_tensor("ar_in", [128, 1032], F32)
    ar_out = nc.dram_tensor("ar_out", [128, 1032], F32)
    wout16 = nc.dram_tensor("wout16", [D, D], BF16)

    es = ExitStack()
    SB = lambda name, shape, dt: es.enter_context(nc.sbuf_tensor(name, shape, dt))
    pb = [es.enter_context(nc.psum_tensor(f"pb{i}", [128, 512], F32)) for i in range(8)]
    sem_names = ["sx", "sw", "sg1", "smm", "sv", "ss", "sgp", "scc", "sst", "swg", "swc", "sca"]
    sem = {n: es.enter_context(nc.semaphore(n)) for n in sem_names}
    cnt = {n: 0 for n in sem_names}
    block = es.enter_context(nc.Block())

    def on(eng, fn):
        getattr(block, eng)(fn)

    identb = SB("identb", [128, 128], BF16)
    identh = SB("identh", [128, 128], F16)
    ones1 = SB("ones1s", [1, 128], BF16)
    onesc = SB("onescs", [128, 8], BF16)
    selb = SB("selbs", [16, 8 * 128], BF16)
    bqc = SB("bqcs", [128, 8], F32)
    bkvr = SB("bkvrs", [1, 2 * D], BF16)
    boutr = SB("boutrs", [1, D], BF16)
    q_res = SB("q_res", [128, 8, TCORE], BF16)
    x_slab = SB("x_slab", [128, 4, D], F32)
    stA = SB("stA", [128, 16], F32)
    stB = SB("stB", [128, 16], F32)
    stC = SB("stC", [128, 16], F32)
    stD = SB("stD", [128, 16], F32)
    qv = SB("qv", [128, 1408], F32)
    ql = SB("ql", [128, 1408], F16)
    qh = SB("qh", [128, 1408], F16)
    qc7 = SB("qc7", [128, 1408], F16)
    qsm = SB("qsm", [128, 1408], F16)
    bmx = SB("bmx", [128, 88], F32)
    bsc = SB("bsc", [128, 88], F32)
    brr = SB("brr", [128, 88], F32)
    bdkv = SB("bdkv", [128, 8, 128], BF16)
    ksb8 = SB("ksb8", [128, 8, 16], BF16)

    def p0_sync(sync):
        for dst, src in [(identb, idb_in), (identh, idh_in), (ones1, ones1_in),
                         (onesc, onesc_in), (selb, selb_in), (bqc, bqc_in)]:
            sync.dma_start(out=dst[:, :], in_=src[:, :]).then_inc(sem["sw"], 16)
            cnt["sw"] += 16
    on("sync", p0_sync)
    SW_CONST = cnt["sw"]

    with nc.sbuf_tensor("W_att", [128, 8, 3 * D], BF16) as W_att, \
         nc.sbuf_tensor("ln_sq", [128, 4 * D], F32) as ln_sq:

        with nc.sbuf_tensor("wqkv_sb", [128, 8, 2 * D], BF16) as wqkv_sb, \
             nc.sbuf_tensor("wf_sb", [64, 64], BF16) as wf_sb, \
             nc.sbuf_tensor("wqt0", [64, 128], BF16) as wqt0, \
             nc.sbuf_tensor("wqt1", [64, 128], BF16) as wqt1, \
             nc.sbuf_tensor("wc16", [128, 1, D], BF16) as wc16:
            wqt = [wqt0, wqt1]

            def p1_gp(g):
                g.dma_start(out=wqkv_sb[:, :, :],
                            in_=wqkv_in[:, 0:2 * D].rearrange("(c p) f -> p c f", p=128)).then_inc(sem["sg1"], 16)
                g.dma_start(out=W_att[:, :, 2 * D:3 * D],
                            in_=wqkv_in[:, 2 * D:3 * D].rearrange("(c p) f -> p c f", p=128)).then_inc(sem["sg1"], 16)
                g.dma_start(out=wf_sb[:, :], in_=wf_in[:, :]).then_inc(sem["sg1"], 16)
                g.dma_start(out=bkvr[:, :], in_=bkvr_in[:, :]).then_inc(sem["sg1"], 16)
                g.dma_start(out=boutr[:, :], in_=bout_in[:, :]).then_inc(sem["sg1"], 16)
                cnt["sg1"] += 80
            on("gpsimd", p1_gp)
            SG1_W = cnt["sg1"]
            SV_VCOPY = cnt["sv"]

            sx_wc0 = cnt["sx"]

            def p1_wc_g(g):
                for p in range(8):
                    if p >= 1:
                        g.wait_ge(sem["sx"], sx_wc0 + 16 * p)
                    g.dma_start(out=wc16[:, 0, :],
                                in_=wout_in[p * 128:(p + 1) * 128, :]).then_inc(sem["swc"], 16)
                    cnt["swc"] += 16
            on("gpsimd", p1_wc_g)

            def p1_wc_s(sync):
                for p in range(8):
                    sync.wait_ge(sem["swc"], 16 * (p + 1))
                    sync.dma_start(out=wout16.ap()[p * 128:(p + 1) * 128, :],
                                   in_=wc16[:, 0, :]).then_inc(sem["sx"], 16)
                    cnt["sx"] += 16
            on("sync", p1_wc_s)
            WC_SX = cnt["sx"]

            ss_base = cnt["ss"]
            smm_base = cnt["smm"]

            def p1_t(t):
                t.wait_ge(sem["sg1"], SG1_W)
                t.wait_ge(sem["sw"], SW_CONST)
                it = 0
                for c in range(8):
                    for qk in range(2):
                        for h in range(16):
                            bA = pb[it % 2]
                            bB = pb[2 + it % 2]
                            if it >= 2:
                                t.wait_ge(sem["ss"], ss_base + 2 * (it - 1))
                            src = wqkv_sb[:, c, qk * D + h * 64: qk * D + (h + 1) * 64]
                            t.matmul(bA[0:64, 0:128], src, identb[:, :],
                                     start=True, stop=True).then_inc(sem["smm"], 1)
                            t.wait_ge(sem["ss"], ss_base + 2 * it + 1)
                            t.matmul(bB[0:128, 0:64], wqt[it % 2][0:64, :], wf_sb[:, :],
                                     start=True, stop=True).then_inc(sem["smm"], 1)
                            it += 1
                cnt["smm"] += 512
            on("tensor", p1_t)

            def p1_s(s):
                it = 0
                for c in range(8):
                    for qk in range(2):
                        for h in range(16):
                            bA = pb[it % 2]
                            bB = pb[2 + it % 2]
                            s.wait_ge(sem["smm"], smm_base + 2 * it + 1)
                            s.copy(wqt[it % 2][0:64, :], bA[0:64, 0:128]).then_inc(sem["ss"], 1)
                            s.wait_ge(sem["smm"], smm_base + 2 * it + 2)
                            s.copy(W_att[:, c, qk * D + h * 64: qk * D + (h + 1) * 64],
                                   bB[0:128, 0:64]).then_inc(sem["ss"], 1)
                            it += 1
                cnt["ss"] += 512
            on("scalar", p1_s)
        SS_FOLD = cnt["ss"]
        SMM_FOLD_END = cnt["smm"]

        def emit_quant_f16(v, src2d, W, out2d, sc_ap, rr_ap):
            nb = W // 16
            v3 = qv[:, 0:W].rearrange("p (b k) -> p b k", k=16)
            v.tensor_tensor(v3, src2d.rearrange("p (b k) -> p b k", k=16),
                            rr_ap.unsqueeze(2).broadcast_to([128, nb, 16]), op=OP.mult)
            v.drain()
            v.tensor_scalar(qsm[:, 0:W], qv[:, 0:W], 0.0, None, op0=OP.is_lt)
            v.drain()
            v.tensor_scalar(qv[:, 0:W].bitcast(I32), qv[:, 0:W].bitcast(I32), 0x7FFFFFFF, None,
                            op0=OP.bitwise_and)
            v.drain()
            a = qv[:, 0:W]
            v.tensor_scalar(qsm[:, 0:W], qsm[:, 0:W], -2.0, 1.0, op0=OP.mult, op1=OP.add)
            v.tensor_scalar(ql[:, 0:W], a, MIDS[0], None, op0=OP.is_gt)
            v.drain()
            v.scalar_tensor_tensor(ql[:, 0:W], a, MIDS[1], ql[:, 0:W], op0=OP.is_gt, op1=OP.add)
            v.drain()
            v.scalar_tensor_tensor(ql[:, 0:W], a, MIDS[2], ql[:, 0:W], op0=OP.is_gt, op1=OP.add)
            v.drain()
            v.scalar_tensor_tensor(ql[:, 0:W], a, MIDS[3], ql[:, 0:W], op0=OP.is_gt, op1=OP.add)
            v.tensor_scalar(qh[:, 0:W], a, MIDS[4], None, op0=OP.is_gt)
            v.drain()
            v.scalar_tensor_tensor(qh[:, 0:W], a, MIDS[5], qh[:, 0:W], op0=OP.is_gt, op1=OP.add)
            v.tensor_scalar(qc7[:, 0:W], a, MIDS[6], 2.0, op0=OP.is_gt, op1=OP.mult)
            v.drain()
            v.scalar_tensor_tensor(ql[:, 0:W], ql[:, 0:W], 0.5, qh[:, 0:W], op0=OP.mult, op1=OP.add)
            v.drain()
            v.tensor_tensor(ql[:, 0:W], ql[:, 0:W], qc7[:, 0:W], op=OP.add)
            v.drain()
            v.tensor_tensor(qh[:, 0:W].rearrange("p (b k) -> p b k", k=16),
                            ql[:, 0:W].rearrange("p (b k) -> p b k", k=16),
                            sc_ap.unsqueeze(2).broadcast_to([128, nb, 16]), op=OP.mult)
            v.drain()
            v.tensor_tensor(out2d, qh[:, 0:W], qsm[:, 0:W], op=OP.mult)

        def emit_bmax(v, src2d, W, bm_ap, sc_ap, rr_ap):
            v.tensor_reduce(bm_ap, src2d.rearrange("p (b k) -> p b k", k=16),
                            axis=AX.X, op=OP.max, apply_absolute_value=True)
            v.drain()
            v.tensor_scalar(sc_ap, bm_ap, 1.0 / 6.0, 1e-12, op0=OP.mult, op1=OP.max)
            v.drain()
            v.reciprocal(rr_ap, sc_ap)
            v.drain()

        with nc.sbuf_tensor("wg_sb", [128, ICN, 128], F32) as wg_sb, \
             nc.sbuf_tensor("wu_sb", [128, ICN, 128], F32) as wu_sb, \
             nc.sbuf_tensor("wd_sb", [128, WDCH, 8, 128], F32) as wd_sb, \
             nc.sbuf_tensor("wq16a", [128, 128], F16) as wq16a, \
             nc.sbuf_tensor("wq16b", [128, 128], F16) as wq16b, \
             nc.sbuf_tensor("wt16a", [128, 128], F16) as wt16a, \
             nc.sbuf_tensor("wt16b", [128, 128], F16) as wt16b:
            wq16 = [wq16a, wq16b]
            wt16 = [wt16a, wt16b]

            def p2_sync(sync):
                sync.wait_ge(sem["smm"], SMM_FOLD_END)
                sync.wait_ge(sem["sx"], WC_SX)
                sync.dma_start(out=wg_sb[:, :, :],
                               in_=wg_sl_in[:, :].rearrange("(a p) k -> p a k", p=128)).then_inc(sem["sw"], 16)
                sync.dma_start(out=wu_sb[:, :, :],
                               in_=wu_sl_in[:, :].rearrange("(a p) k -> p a k", p=128)).then_inc(sem["sw"], 16)
                sync.dma_start(out=wd_sb[:, :, :, :],
                               in_=wd_sl_in[:, :, :].rearrange("j (a p) k -> p j a k", p=128)).then_inc(sem["sw"], 16)
                cnt["sw"] += 48
            on("sync", p2_sync)
            SW_WSH = cnt["sw"]

            tiles = []
            for j in range(ICN):
                col = (j // 2) * 512 + (j % 2) * 128
                tiles.append((wg_sb[:, j, :], ag_wgu_in.ap()[:, col:col + 128]))
            for j in range(ICN):
                col = (j // 2) * 512 + 256 + (j % 2) * 128
                tiles.append((wu_sb[:, j, :], ag_wgu_in.ap()[:, col:col + 128]))
            for jj in range(WDCH):
                for fc in range(8):
                    tiles.append((wd_sb[:, jj, fc, :], ag_wd_in.ap()[jj, :, fc * 128:(fc + 1) * 128]))
            NT = len(tiles)
            sv0, smm0, ss0, sgp0 = cnt["sv"], cnt["smm"], cnt["ss"], cnt["sgp"]

            def p2_v(v):
                v.wait_ge(sem["sw"], SW_WSH)
                for i, (srcap, dst) in enumerate(tiles):
                    if i >= 2:
                        v.wait_ge(sem["smm"], smm0 + (i - 1))
                    emit_bmax(v, srcap, 128, bmx[:, 0:8], bsc[:, 0:8], brr[:, 0:8])
                    emit_quant_f16(v, srcap, 128, wq16[i % 2][:, :], bsc[:, 0:8], brr[:, 0:8])
                    v.drain()
                    v.engine_nop().then_inc(sem["sv"], 1)
                    cnt["sv"] += 1
            on("vector", p2_v)

            def p2_t(t):
                for i in range(NT):
                    t.wait_ge(sem["sv"], sv0 + i + 1)
                    if i >= 2:
                        t.wait_ge(sem["ss"], ss0 + (i - 1))
                    t.matmul(pb[4 + i % 2][0:128, 0:128], wq16[i % 2][:, :], identh[:, :],
                             start=True, stop=True).then_inc(sem["smm"], 1)
                    cnt["smm"] += 1
            on("tensor", p2_t)

            def p2_s(s):
                for i in range(NT):
                    s.wait_ge(sem["smm"], smm0 + i + 1)
                    if i >= 2:
                        s.wait_ge(sem["sgp"], sgp0 + 16 * (i - 1))
                    s.copy(wt16[i % 2][:, :], pb[4 + i % 2][0:128, 0:128]).then_inc(sem["ss"], 1)
                    cnt["ss"] += 1
            on("scalar", p2_s)

            def p2_dump(sync):
                for i, (src, dst) in enumerate(tiles):
                    sync.wait_ge(sem["ss"], ss0 + i + 1)
                    sync.dma_start(out=dst, in_=wt16[i % 2][:, :]).then_inc(sem["sgp"], 16)
                    cnt["sgp"] += 16
            on("sync", p2_dump)

            def p2_g(g):
                g.wait_ge(sem["sgp"], sgp0 + 16 * NT)
                g.collective_compute("AllGather", OP.bypass,
                                     replica_groups=[[0, 1, 2, 3, 4, 5, 6, 7]],
                                     ins=[ag_wgu_in.ap().opt()],
                                     outs=[ag_wgu_out.ap().opt()]).then_inc(sem["sca"], 1)
                cnt["sca"] += 1
                g.collective_compute("AllGather", OP.bypass,
                                     replica_groups=[[0, 1, 2, 3, 4, 5, 6, 7]],
                                     ins=[ag_wd_in.ap().opt()],
                                     outs=[ag_wd_out.ap().opt()]).then_inc(sem["sca"], 1)
                cnt["sca"] += 1
            on("gpsimd", p2_g)
        SCC_AG = cnt["sca"]
        P2_SMM_END = cnt["smm"]
        P2_SGP_END = cnt["sgp"]

        def emit_ln(sq_view):
            def s_sq(sc):
                sc.activation(sq_view, x_slab[:, :, :].rearrange("p c f -> p (c f)"),
                              AF.Square).then_inc(sem["ss"], 1)
                cnt["ss"] += 1
            on("scalar", s_sq)
            tok_sq = cnt["ss"]

            def v_stats(v, tok_sq=tok_sq, sq_view=sq_view):
                for c in range(4):
                    v.tensor_reduce(stA[:, c:c + 1], x_slab[:, c, :], axis=AX.X, op=OP.add)
                v.wait_ge(sem["ss"], tok_sq)
                for c in range(4):
                    v.tensor_reduce(stA[:, 4 + c:5 + c], sq_view[:, c * D:(c + 1) * D],
                                    axis=AX.X, op=OP.add)
                v.drain()
                v.tensor_scalar(stB[:, 0:4], stA[:, 0:4], 1.0 / D, None, op0=OP.mult)
                v.tensor_scalar(stB[:, 4:8], stA[:, 4:8], 1.0 / D, None, op0=OP.mult)
                v.drain()
                v.tensor_tensor(stC[:, 0:4], stB[:, 0:4], stB[:, 0:4], op=OP.mult)
                v.drain()
                v.tensor_tensor(stA[:, 8:12], stB[:, 4:8], stC[:, 0:4], op=OP.subtract)
                v.drain()
                v.tensor_scalar(stB[:, 8:12], stA[:, 8:12], 1e-5, None, op0=OP.add)
                v.drain()
                v.reciprocal(stC[:, 8:12], stB[:, 8:12])
                v.drain()
                v.engine_nop().then_inc(sem["sv"], 1)
                cnt["sv"] += 1
            on("vector", v_stats)
            tok_iv = cnt["sv"]

            def s_sqrt(sc, tok_iv=tok_iv):
                sc.wait_ge(sem["sv"], tok_iv)
                sc.activation(stA[:, 12:16], stC[:, 8:12], AF.Sqrt).then_inc(sem["ss"], 1)
                cnt["ss"] += 1
            on("scalar", s_sqrt)
            tok_r0 = cnt["ss"]

            def v_newton(v, tok_r0=tok_r0):
                v.wait_ge(sem["ss"], tok_r0)
                v.tensor_tensor(stC[:, 12:16], stA[:, 12:16], stA[:, 12:16], op=OP.mult)
                v.drain()
                v.tensor_tensor(stD[:, 0:4], stB[:, 8:12], stC[:, 12:16], op=OP.mult)
                v.drain()
                v.tensor_scalar(stC[:, 0:4], stD[:, 0:4], -0.5, 1.5, op0=OP.mult, op1=OP.add)
                v.drain()
                v.tensor_tensor(stD[:, 8:12], stA[:, 12:16], stC[:, 0:4], op=OP.mult)
                v.drain()
                v.drain()
                v.engine_nop().then_inc(sem["sv"], 1)
                cnt["sv"] += 1
            on("vector", v_newton)
            return cnt["sv"]

        with nc.sbuf_tensor("xnb", [128, 4, D], BF16) as xnb, \
             nc.sbuf_tensor("xnf", [128, 8, MT], BF16) as xnf, \
             nc.sbuf_tensor("k_t0", [128, D], BF16) as k_t0, \
             nc.sbuf_tensor("k_t1", [128, D], BF16) as k_t1, \
             nc.sbuf_tensor("v_t0", [128, D], BF16) as v_t0, \
             nc.sbuf_tensor("v_t1", [128, D], BF16) as v_t1, \
             nc.sbuf_tensor("kv_all", [128, 1032], F32) as kv_all, \
             nc.sbuf_tensor("kvs", [128, 1032], F32) as kvs:
            k_t = [k_t0, k_t1]
            v_t = [v_t0, v_t1]
            tok_x_free = 0
            tok_xnb_free = 0
            tok_xnf_cons = 0
            tok_kt_used = [0, 0]

            for m in range(NMT):
                def a_load(sync, m=m, w=tok_x_free):
                    if w:
                        sync.wait_ge(sem["sv"], w)
                    sync.dma_start(out=x_slab[:, :, :],
                                   in_=x_in[m * MT:(m + 1) * MT, :].rearrange("(c p) f -> p c f", p=128)
                                   ).then_inc(sem["sx"], 16)
                    cnt["sx"] += 16
                on("sync", a_load)
                tok_x = cnt["sx"]

                def a_wx_v(v, t=tok_x):
                    v.wait_ge(sem["sx"], t)
                on("vector", a_wx_v)

                def a_wx_s(s, t=tok_x):
                    s.wait_ge(sem["sx"], t)
                on("scalar", a_wx_s)

                tok_rstd = emit_ln(ln_sq[:, :])

                def a_xn(v, w=tok_xnb_free, tr=tok_rstd, m=m):
                    v.wait_ge(sem["sv"], tr)
                    if m == 0:
                        v.wait_ge(sem["smm"], P2_SMM_END)
                        v.wait_ge(sem["sgp"], P2_SGP_END)
                    if w:
                        v.wait_ge(sem["smm"], w)
                    for c in range(4):
                        v.tensor_scalar(xnb[:, c, :], x_slab[:, c, :],
                                        stB[:, c:c + 1], stD[:, 8 + c:9 + c],
                                        op0=OP.subtract, op1=OP.mult)
                    v.drain()
                    v.engine_nop().then_inc(sem["sv"], 1)
                    cnt["sv"] += 1
                on("vector", a_xn)
                tok_xn = cnt["sv"]
                tok_x_free = tok_xn

                smm_tp0 = cnt["smm"]
                ss_tp0 = cnt["ss"]

                def a_tp_t(t, tok_xn=tok_xn, smm_tp0=smm_tp0, ss_tp0=ss_tp0, m=m):
                    t.wait_ge(sem["sv"], tok_xn)
                    if m == 0:
                        t.wait_ge(sem["ss"], SS_FOLD)
                        t.wait_ge(sem["sv"], SV_VCOPY)
                    it = 0
                    for c in range(4):
                        for j in range(8):
                            bA = pb[it % 2]
                            if it >= 2:
                                t.wait_ge(sem["ss"], ss_tp0 + (it - 1))
                            t.matmul(bA[0:128, 0:128], xnb[:, c, j * 128:(j + 1) * 128],
                                     identb[:, :], start=True, stop=True).then_inc(sem["smm"], 1)
                            it += 1
                    cnt["smm"] += 32
                on("tensor", a_tp_t)
                tok_xnb_free = cnt["smm"]

                def a_tp_s(s, smm_tp0=smm_tp0, w=tok_xnf_cons):
                    if w:
                        s.wait_ge(sem["smm"], w)
                    it = 0
                    for c in range(4):
                        for j in range(8):
                            bA = pb[it % 2]
                            s.wait_ge(sem["smm"], smm_tp0 + it + 1)
                            s.copy(xnf[:, j, c * 128:(c + 1) * 128], bA[0:128, 0:128]).then_inc(sem["ss"], 1)
                            it += 1
                    cnt["ss"] += 32
                on("scalar", a_tp_s)
                tok_tp_evac = cnt["ss"]

                smm_q0 = cnt["smm"]
                ss_q0 = cnt["ss"]

                def a_q_t(t, tok_tp_evac=tok_tp_evac, ss_q0=ss_q0):
                    t.wait_ge(sem["ss"], tok_tp_evac)
                    for mc in range(8):
                        bA = pb[mc % 2]
                        if mc >= 2:
                            t.wait_ge(sem["ss"], ss_q0 + (mc - 1))
                        for kc in range(8):
                            ins = t.matmul(bA[:, 0:512], W_att[:, kc, mc * 128:(mc + 1) * 128],
                                           xnf[:, kc, :], start=(kc == 0), stop=(kc == 7))
                        ins.then_inc(sem["smm"], 1)
                        cnt["smm"] += 1
                on("tensor", a_q_t)

                def a_q_s(s, smm_q0=smm_q0, m=m):
                    for mc in range(8):
                        bA = pb[mc % 2]
                        s.wait_ge(sem["smm"], smm_q0 + mc + 1)
                        s.activation(q_res[:, mc, m * MT:(m + 1) * MT], bA[:, 0:512], AF.Relu,
                                     bias=bqc[:, mc:mc + 1]).then_inc(sem["ss"], 1)
                        cnt["ss"] += 1
                on("scalar", a_q_s)
                tok_q_evac = cnt["ss"]

                for tc in range(4):
                    par = tc % 2
                    smm_kv0 = cnt["smm"]
                    ss_kv0 = cnt["ss"]

                    def a_kv_t(t, tc=tc, smm_kv0=smm_kv0, ss_kv0=ss_kv0,
                               tok_q_evac=tok_q_evac):
                        if tc == 0:
                            t.wait_ge(sem["ss"], tok_q_evac)
                        else:
                            t.wait_ge(sem["ss"], ss_kv0)
                        for grp in range(4):
                            bA = pb[2 + grp % 2]
                            if grp >= 2:
                                t.wait_ge(sem["ss"], ss_kv0 + (grp - 1))
                            for kc in range(8):
                                last = (kc == 7) and (grp >= 2)
                                ins = t.matmul(bA[:, 0:512], xnf[:, kc, tc * 128:(tc + 1) * 128],
                                               W_att[:, kc, D + grp * 512: D + (grp + 1) * 512],
                                               start=(kc == 0), stop=last)
                            if grp < 2:
                                ins = t.matmul(bA[:, 0:512], ones1[0:1, :],
                                               bkvr[0:1, grp * 512:(grp + 1) * 512],
                                               start=False, stop=True)
                            ins.then_inc(sem["smm"], 1)
                            cnt["smm"] += 1
                    on("tensor", a_kv_t)
                    tok_kv_grps = cnt["smm"]

                    def a_kv_s(s, par=par, smm_kv0=smm_kv0, w=tok_kt_used[par]):
                        if w:
                            s.wait_ge(sem["smm"], w)
                        for grp in range(4):
                            bA = pb[2 + grp % 2]
                            s.wait_ge(sem["smm"], smm_kv0 + grp + 1)
                            if grp < 2:
                                s.activation(k_t[par][:, grp * 512:(grp + 1) * 512], bA[:, 0:512],
                                             AF.Relu).then_inc(sem["ss"], 1)
                            else:
                                s.copy(v_t[par][:, (grp - 2) * 512:(grp - 1) * 512],
                                       bA[:, 0:512]).then_inc(sem["ss"], 1)
                            cnt["ss"] += 1
                    on("scalar", a_kv_s)
                    tok_kv_evac = cnt["ss"]

                    def a_acc_t(t, m=m, tc=tc, par=par, tok_kv_evac=tok_kv_evac):
                        t.wait_ge(sem["ss"], tok_kv_evac)
                        first = (m == 0 and tc == 0)
                        lastt = (m == NMT - 1 and tc == 3)
                        for p in range(8):
                            t.matmul(pb[4 + p // 4][:, (p % 4) * 128:(p % 4 + 1) * 128],
                                     k_t[par][:, p * 128:(p + 1) * 128],
                                     v_t[par][:, p * 128:(p + 1) * 128],
                                     start=first, stop=lastt)
                            ins = t.matmul(pb[6][:, p:p + 1],
                                           k_t[par][:, p * 128:(p + 1) * 128],
                                           onesc[:, 0:1], start=first, stop=lastt)
                        ins.then_inc(sem["smm"], 1)
                        cnt["smm"] += 1
                    on("tensor", a_acc_t)
                    tok_kt_used[par] = cnt["smm"]
                tok_xnf_cons = cnt["smm"]
            tok_kv_done = cnt["smm"]

            def ar_v(v, tok_kv_done=tok_kv_done):
                v.wait_ge(sem["smm"], tok_kv_done)
                v.tensor_copy(kvs[:, 0:512], pb[4][:, :])
                v.tensor_copy(kvs[:, 512:1024], pb[5][:, :])
                v.tensor_copy(kvs[:, 1024:1032], pb[6][:, 0:8])
                v.drain()
                v.drain()
                v.engine_nop().then_inc(sem["sv"], 1)
                cnt["sv"] += 1
            on("vector", ar_v)
            tok_kvs = cnt["sv"]

            def ar_g(g, tok_kvs=tok_kvs):
                g.wait_ge(sem["sv"], tok_kvs)
                g.dma_start(out=ar_in.ap()[:, :], in_=kvs[:, :]).then_inc(sem["sgp"], 16)
                cnt["sgp"] += 16
                g.wait_ge(sem["sgp"], cnt["sgp"])
                g.collective_compute("AllReduce", OP.add,
                                     replica_groups=[[0, 1, 2, 3], [4, 5, 6, 7]],
                                     ins=[ar_in.ap().opt()],
                                     outs=[ar_out.ap().opt()]).then_inc(sem["scc"], 1)
                cnt["scc"] += 1
                g.wait_ge(sem["scc"], cnt["scc"])
                g.dma_start(out=kv_all[:, :], in_=ar_out.ap()[:, :]).then_inc(sem["sgp"], 16)
                cnt["sgp"] += 16
            on("gpsimd", ar_g)
            tok_kvall = cnt["sgp"]

            def ar_build(v, tok_kvall=tok_kvall):
                v.wait_ge(sem["sgp"], tok_kvall)
                v.memset(bdkv[:, :, :], 0.0)
                v.memset(ksb8[:, :, :], 0.0)
                v.drain()
                for p in range(8):
                    v.tensor_copy(bdkv[0:64, p, 0:64], kv_all[0:64, p * 128: p * 128 + 64])
                    v.tensor_copy(bdkv[64:128, p, 64:128], kv_all[64:128, p * 128 + 64: p * 128 + 128])
                    v.tensor_copy(ksb8[0:64, p, 2 * p:2 * p + 1], kv_all[0:64, 1024 + p:1025 + p])
                    v.tensor_copy(ksb8[64:128, p, 2 * p + 1:2 * p + 2], kv_all[64:128, 1024 + p:1025 + p])
                v.drain()
                v.drain()
                v.engine_nop().then_inc(sem["sv"], 1)
                cnt["sv"] += 1
            on("vector", ar_build)
            tok_attc = cnt["sv"]
            tok_x_free_ab = tok_x_free

            DUMPA_SGP = 0
            if DBG:
                def dump_a(g, tok_attc=tok_attc):
                    g.wait_ge(sem["sv"], tok_attc)
                    g.dma_start(out=dbg_watt, in_=W_att[:, :, :]).then_inc(sem["sgp"], 16)
                    g.dma_start(out=dbg_xnf, in_=xnf[:, :, :]).then_inc(sem["sgp"], 16)
                    g.dma_start(out=dbg_xnb, in_=xnb[:, :, :]).then_inc(sem["sgp"], 16)
                    g.dma_start(out=dbg_kvall, in_=kv_all[:, :]).then_inc(sem["sgp"], 16)
                    cnt["sgp"] += 64
                on("gpsimd", dump_a)
                DUMPA_SGP = cnt["sgp"]

    wgu_sb = SB("wgu_sb", [128, 2, 8, 512], F16)
    wd_ic = [SB("wd_ic0", [128, D], F16), SB("wd_ic1", [128, D], F16)]
    wout_p = [SB("wout_p0", [128, D], BF16), SB("wout_p1", [128, D], BF16)]
    xn32 = SB("xn32", [128, D], F32)
    xq_t1 = SB("xq_tok0", [128, D], F16)
    xq_tok = [xq_t1, xq_t1]
    xq_feat = SB("xq_feat", [128, 8, MT], F16)
    h_tok = SB("h_tok", [128, 2, FI], F16)
    hq_t = SB("hq0", [128, 1408], F16)
    hqx = [hq_t, hq_t]
    hqT = SB("hqT", [128, ICN, MT], F16)
    sgt = [SB("sgt0", [128, 256], F32), SB("sgt1", [128, 256], F32)]
    attn2 = SB("attn2", [128, 8, MT], BF16)
    rbx = [SB("rb0", [128, 512], BF16), SB("rb1", [128, 512], BF16)]
    nplus = SB("nplus", [16, 512], F32)
    nrec = SB("nrec", [16, 512], F32)
    nrec16 = SB("nrec16", [16, 512], BF16)
    out_sb = [SB("out_sb0", [128, D], F32), SB("out_sb1", [128, D], F32)]
    sq_view = hqT[:, :, :].rearrange("p a f -> p (a f)").bitcast(F32)[:, 0:4 * D]

    dbg_cap_gup = SB("dbg_cap_gup", [128, 512], F16) if DBG else None
    dbg_cap_sgt = SB("dbg_cap_sgt", [128, 256], F16) if DBG else None
    tok_x_free_b = tok_attc
    tok_wgu_free = [0, 0]
    tok_hqT_free = 0
    tok_outsb_free = [0, 0]
    tok_sgt_used = [0, 0]
    wd_used_tok = [0, 0]
    wo_used = [0, 0]

    for m in range(NMT):
        def b_load(sync, m=m, w=tok_x_free_b):
            if w:
                sync.wait_ge(sem["sv"], w)
            sync.dma_start(out=x_slab[:, :, :],
                           in_=x_in[m * MT:(m + 1) * MT, :].rearrange("(c p) f -> p c f", p=128)
                           ).then_inc(sem["sx"], 16)
            cnt["sx"] += 16
        on("sync", b_load)
        tok_xb = cnt["sx"]

        def b_wx_v(v, t=tok_xb, thq=tok_hqT_free, dg=(DUMPA_SGP if m == 0 else 0)):
            if dg:
                v.wait_ge(sem["sgp"], dg)
            v.wait_ge(sem["sx"], t)
            if thq:
                v.wait_ge(sem["smm"], thq)
        on("vector", b_wx_v)

        def b_wx_s(s, t=tok_xb, thq=tok_hqT_free):
            s.wait_ge(sem["sx"], t)
            if thq:
                s.wait_ge(sem["smm"], thq)
        on("scalar", b_wx_s)

        def emit_ln_b():
            def s_sq(sc):
                sc.activation(sq_view, x_slab[:, :, :].rearrange("p c f -> p (c f)"),
                              AF.Square).then_inc(sem["ss"], 1)
                cnt["ss"] += 1
            on("scalar", s_sq)
            tok_sq = cnt["ss"]

            def v_stats(v, tok_sq=tok_sq):
                for c in range(4):
                    v.tensor_reduce(stA[:, c:c + 1], x_slab[:, c, :], axis=AX.X, op=OP.add)
                v.wait_ge(sem["ss"], tok_sq)
                for c in range(4):
                    v.tensor_reduce(stA[:, 4 + c:5 + c], sq_view[:, c * D:(c + 1) * D],
                                    axis=AX.X, op=OP.add)
                v.drain()
                v.tensor_scalar(stB[:, 0:4], stA[:, 0:4], 1.0 / D, None, op0=OP.mult)
                v.tensor_scalar(stB[:, 4:8], stA[:, 4:8], 1.0 / D, None, op0=OP.mult)
                v.drain()
                v.tensor_tensor(stC[:, 0:4], stB[:, 0:4], stB[:, 0:4], op=OP.mult)
                v.drain()
                v.tensor_tensor(stA[:, 8:12], stB[:, 4:8], stC[:, 0:4], op=OP.subtract)
                v.drain()
                v.tensor_scalar(stB[:, 8:12], stA[:, 8:12], 1e-5, None, op0=OP.add)
                v.drain()
                v.reciprocal(stC[:, 8:12], stB[:, 8:12])
                v.drain()
                v.engine_nop().then_inc(sem["sv"], 1)
                cnt["sv"] += 1
            on("vector", v_stats)
            tok_iv = cnt["sv"]

            def s_sqrt(sc, tok_iv=tok_iv):
                sc.wait_ge(sem["sv"], tok_iv)
                sc.activation(stA[:, 12:16], stC[:, 8:12], AF.Sqrt).then_inc(sem["ss"], 1)
                cnt["ss"] += 1
            on("scalar", s_sqrt)
            tok_r0 = cnt["ss"]

            def v_newton(v, tok_r0=tok_r0):
                v.wait_ge(sem["ss"], tok_r0)
                v.tensor_tensor(stC[:, 12:16], stA[:, 12:16], stA[:, 12:16], op=OP.mult)
                v.drain()
                v.tensor_tensor(stD[:, 0:4], stB[:, 8:12], stC[:, 12:16], op=OP.mult)
                v.drain()
                v.tensor_scalar(stC[:, 0:4], stD[:, 0:4], -0.5, 1.5, op0=OP.mult, op1=OP.add)
                v.drain()
                v.tensor_tensor(stD[:, 8:12], stA[:, 12:16], stC[:, 0:4], op=OP.mult)
                v.drain()
                v.drain()
                v.engine_nop().then_inc(sem["sv"], 1)
                cnt["sv"] += 1
            on("vector", v_newton)
            return cnt["sv"]

        tok_rstd = emit_ln_b()

        smm_xq0 = cnt["smm"]
        ss_xq0 = cnt["ss"]
        for c in range(4):
            par = c % 2

            def b_xq_v(v, c=c, par=par, tr=tok_rstd, smm_xq0=smm_xq0):
                if c == 0:
                    v.wait_ge(sem["sv"], tr)
                if c >= 1:
                    v.wait_ge(sem["smm"], smm_xq0 + c * 8)
                v.tensor_scalar(xn32[:, :], x_slab[:, c, :], stB[:, c:c + 1], stD[:, 8 + c:9 + c],
                                op0=OP.subtract, op1=OP.mult)
                v.drain()
                emit_bmax(v, xn32[:, :], D, bmx[:, 0:64], bsc[:, 0:64], brr[:, 0:64])
                emit_quant_f16(v, xn32[:, :], D, xq_tok[par][:, :], bsc[:, 0:64], brr[:, 0:64])
                v.drain()
                v.drain()
                v.engine_nop().then_inc(sem["sv"], 1)
                cnt["sv"] += 1
            on("vector", b_xq_v)
            tok_xq = cnt["sv"]

            def b_xq_t(t, c=c, par=par, tok_xq=tok_xq, ss_xq0=ss_xq0, m=m):
                t.wait_ge(sem["sv"], tok_xq)
                if m == 0 and c == 0:
                    t.wait_ge(sem["sv"], tok_attc)
                for j in range(8):
                    it = c * 8 + j
                    bA = pb[it % 2]
                    if it >= 2:
                        t.wait_ge(sem["ss"], ss_xq0 + (it - 1))
                    t.matmul(bA[0:128, 0:128], xq_tok[par][:, j * 128:(j + 1) * 128], identh[:, :],
                             start=True, stop=True).then_inc(sem["smm"], 1)
                    cnt["smm"] += 1
            on("tensor", b_xq_t)

            def b_xq_s(s, c=c, smm_xq0=smm_xq0):
                for j in range(8):
                    it = c * 8 + j
                    bA = pb[it % 2]
                    s.wait_ge(sem["smm"], smm_xq0 + it + 1)
                    s.copy(xq_feat[:, j, c * 128:(c + 1) * 128], bA[0:128, 0:128]).then_inc(sem["ss"], 1)
                    cnt["ss"] += 1
            on("scalar", b_xq_s)
        tok_xqf = cnt["ss"]

        if DBG:
            def dump_xq(sy, m=m, t=tok_xqf):
                sy.wait_ge(sem["ss"], t)
                sy.dma_start(out=dbg_xqall[m], in_=xq_feat[:, :, :]).then_inc(sem["sgp"], 16)
                cnt["sgp"] += 16
            on("sync", dump_xq)

        for half in range(2):
            for ig in range(NIG):
                par = ig % 2

                def b_wgu_load(sync, ig=ig, par=par, w=tok_wgu_free[par],
                               first=(m == 0 and half == 0 and ig < 2)):
                    if first:
                        sync.wait_ge(sem["sca"], SCC_AG)
                    if w:
                        sync.wait_ge(sem["smm"], w)
                    sync.dma_start(out=wgu_sb[:, par, :, :],
                                   in_=ag_wgu_out.ap()[:, :, ig * 512:(ig + 1) * 512]
                                   .rearrange("c p f -> p c f")).then_inc(sem["swg"], 16)
                    cnt["swg"] += 16
                on("sync", b_wgu_load)
                tok_wgu = cnt["swg"]

                for tl in range(2):
                    tcg = half * 2 + tl
                    bank = pb[2 + tl]

                    def b_gu_t(t, par=par, tcg=tcg, bank=bank, tok_wgu=tok_wgu,
                               tok_xqf=tok_xqf, need_free=cnt["sv"],
                               first=(ig == 0 and half == 0)):
                        t.wait_ge(sem["swg"], tok_wgu)
                        if first:
                            t.wait_ge(sem["ss"], tok_xqf)
                        if need_free:
                            t.wait_ge(sem["sv"], need_free)
                        for fg in range(2):
                            for kc in range(8):
                                ins = t.matmul(bank[:, fg * 256:(fg + 1) * 256],
                                               xq_feat[:, kc, tcg * 128:(tcg + 1) * 128],
                                               wgu_sb[:, par, kc, fg * 256:(fg + 1) * 256],
                                               start=(kc == 0), stop=(kc == 7))
                        ins.then_inc(sem["smm"], 1)
                        cnt["smm"] += 1
                    on("tensor", b_gu_t)
                    tok_gu = cnt["smm"]

                    def b_silu_s(s, bank=bank, tl=tl, tok_gu=tok_gu, w=tok_sgt_used[tl]):
                        s.wait_ge(sem["smm"], tok_gu)
                        if w:
                            s.wait_ge(sem["sv"], w)
                        s.activation(sgt[tl][:, :], bank[:, 0:256], AF.Silu)
                        s.drain()
                        s.nop().then_inc(sem["ss"], 1)
                        cnt["ss"] += 1
                    on("scalar", b_silu_s)
                    tok_silu = cnt["ss"]

                    def b_h_v(v, bank=bank, tl=tl, ig=ig, tok_silu=tok_silu,
                              cap=(DBG and m == 7 and half == 1 and ig == 4 and tl == 0)):
                        v.wait_ge(sem["ss"], tok_silu)
                        if cap:
                            v.tensor_copy(dbg_cap_gup[:, :], bank[:, 0:512])
                            v.tensor_copy(dbg_cap_sgt[:, :], sgt[tl][:, :])
                        v.tensor_tensor(h_tok[:, tl, ig * 256:(ig + 1) * 256], sgt[tl][:, :],
                                        bank[:, 256:512], op=OP.mult)
                        v.drain()
                        v.engine_nop().then_inc(sem["sv"], 1)
                        cnt["sv"] += 1
                    on("vector", b_h_v)
                    tok_sgt_used[tl] = cnt["sv"]
                tok_wgu_free[par] = cnt["smm"]
            tok_h_done = cnt["sv"]

            for tl in range(2):
                tcg = half * 2 + tl
                for hf in range(2):
                    src = h_tok[:, tl, hf * 1408:(hf + 1) * 1408]
                    par = (tcg * 2 + hf) % 2

                    def b_hq_v(v, src=src, par=par, tok_h_done=tok_h_done, need=cnt["smm"]):
                        v.wait_ge(sem["sv"], tok_h_done)
                        if need:
                            v.wait_ge(sem["smm"], need)
                        emit_bmax(v, src, 1408, bmx[:, 0:88], bsc[:, 0:88], brr[:, 0:88])
                        emit_quant_f16(v, src, 1408, hqx[par][:, :], bsc[:, 0:88], brr[:, 0:88])
                        v.drain()
                        v.engine_nop().then_inc(sem["sv"], 1)
                        cnt["sv"] += 1
                    on("vector", b_hq_v)
                    tok_hq = cnt["sv"]

                    smm_ht0 = cnt["smm"]
                    ss_ht0 = cnt["ss"]

                    def b_hq_t(t, par=par, tok_hq=tok_hq, ss_ht0=ss_ht0):
                        t.wait_ge(sem["sv"], tok_hq)
                        t.wait_ge(sem["ss"], ss_ht0)
                        for j in range(11):
                            bA = pb[4 + j % 2]
                            if j >= 2:
                                t.wait_ge(sem["ss"], ss_ht0 + (j - 1))
                            t.matmul(bA[0:128, 0:128], hqx[par][:, j * 128:(j + 1) * 128],
                                     identh[:, :], start=True, stop=True).then_inc(sem["smm"], 1)
                            cnt["smm"] += 1
                    on("tensor", b_hq_t)

                    def b_hq_s(s, tcg=tcg, hf=hf, smm_ht0=smm_ht0):
                        for j in range(11):
                            bA = pb[4 + j % 2]
                            s.wait_ge(sem["smm"], smm_ht0 + j + 1)
                            s.copy(hqT[:, hf * 11 + j, tcg * 128:(tcg + 1) * 128],
                                   bA[0:128, 0:128]).then_inc(sem["ss"], 1)
                            cnt["ss"] += 1
                    on("scalar", b_hq_s)
        tok_hqT_done = cnt["ss"]


        def b_norm_t(t, m=m, prev=cnt["ss"]):
            if m == 0:
                t.wait_ge(sem["sv"], tok_attc)
            t.wait_ge(sem["ss"], prev)
            for p in range(8):
                ins = t.matmul(pb[6][0:16, 0:512], ksb8[:, p, :], q_res[:, p, m * MT:(m + 1) * MT],
                               start=(p == 0), stop=(p == 7))
            ins.then_inc(sem["smm"], 1)
            cnt["smm"] += 1
        on("tensor", b_norm_t)
        tok_norm = cnt["smm"]

        def b_nrec_v(v, tok_norm=tok_norm):
            v.wait_ge(sem["smm"], tok_norm)
            v.tensor_scalar(nplus[:, :], pb[6][0:16, 0:512], 1e-6, None, op0=OP.add)
            v.drain()
            v.reciprocal(nrec[:, :], nplus[:, :])
            v.drain()
            v.tensor_copy(nrec16[:, :], nrec[:, :])
            v.drain()
            v.engine_nop().then_inc(sem["sv"], 1)
            cnt["sv"] += 1
        on("vector", b_nrec_v)
        tok_nrec = cnt["sv"]

        ss_rb0 = cnt["ss"]
        sv_at0 = cnt["sv"]
        for p in range(8):
            par = p % 2

            def b_rb_t(t, p=p, par=par, tok_nrec=tok_nrec, ss_rb0=ss_rb0, sv_at0=sv_at0, m=m):
                if p == 0:
                    t.wait_ge(sem["sv"], tok_nrec)
                if p >= 2:
                    t.wait_ge(sem["ss"], ss_rb0 + (p - 1))
                    t.wait_ge(sem["sv"], sv_at0 + (p - 1))
                t.matmul(pb[6 + par][:, 0:512], selb[:, p * 128:(p + 1) * 128], nrec16[:, :],
                         start=True, stop=True).then_inc(sem["smm"], 1)
                cnt["smm"] += 1
                t.matmul(pb[2 + par][:, 0:512], bdkv[:, p, :], q_res[:, p, m * MT:(m + 1) * MT],
                         start=True, stop=True).then_inc(sem["smm"], 1)
                cnt["smm"] += 1
            on("tensor", b_rb_t)
            tok_rbmm = cnt["smm"]

            def b_rb_s(s, par=par, tok_rbmm=tok_rbmm):
                s.wait_ge(sem["smm"], tok_rbmm - 1)
                s.copy(rbx[par][:, :], pb[6 + par][:, 0:512])
                s.drain()
                s.nop().then_inc(sem["ss"], 1)
                cnt["ss"] += 1
            on("scalar", b_rb_s)
            tok_rbs = cnt["ss"]

            def b_at_v(v, p=p, par=par, tok_rbmm=tok_rbmm, tok_rbs=tok_rbs):
                v.wait_ge(sem["smm"], tok_rbmm)
                v.wait_ge(sem["ss"], tok_rbs)
                v.tensor_tensor(attn2[:, p, :], pb[2 + par][:, 0:512], rbx[par][:, :], op=OP.mult)
                v.drain()
                v.drain()
                v.engine_nop().then_inc(sem["sv"], 1)
                cnt["sv"] += 1
            on("vector", b_at_v)
        tok_attn2 = cnt["sv"]

        if DBG:
            def dump_at2(sy, m=m, t=tok_attn2):
                sy.wait_ge(sem["sv"], t)
                sy.dma_start(out=dbg_at2all[m], in_=attn2[:, :, :]).then_inc(sem["sgp"], 16)
                cnt["sgp"] += 16
            on("sync", dump_at2)

        for ic in range(ICN):
            par = ic % 2

            def b_wd_l(sync, ic=ic, par=par, w=wd_used_tok[par], m=m):
                if m == 0 and ic == 0:
                    sync.wait_ge(sem["sca"], SCC_AG)
                if w:
                    sync.wait_ge(sem["smm"], w)
                sync.dma_start(out=wd_ic[par][:, :],
                               in_=ag_wd_out.ap()[ic % 8, ic // 8, :, :]).then_inc(sem["swg"], 16)
                cnt["swg"] += 16
            on("sync", b_wd_l)
            wd_tok = cnt["swg"]

            def b_wd_t(t, ic=ic, par=par, wt=wd_tok, ta=tok_attn2, tok_hqT_done=tok_hqT_done):
                t.wait_ge(sem["swg"], wt)
                if ic == 0:
                    t.wait_ge(sem["sv"], ta)
                    t.wait_ge(sem["ss"], tok_hqT_done)
                for tcg in range(4):
                    for fg in range(2):
                        ins = t.matmul(pb[tcg * 2 + fg][:, 0:512],
                                       hqT[:, ic, tcg * 128:(tcg + 1) * 128],
                                       wd_ic[par][:, fg * 512:(fg + 1) * 512],
                                       start=(ic == 0), stop=False)
                ins.then_inc(sem["smm"], 1)
                cnt["smm"] += 1
            on("tensor", b_wd_t)
            wd_used_tok[par] = cnt["smm"]

        for p in range(8):
            par = p % 2

            def b_wo_l(sync, p=p, par=par, w=wo_used[par], first=(m == 0 and p < 2)):
                if first:
                    sync.wait_ge(sem["sx"], WC_SX)
                if w:
                    sync.wait_ge(sem["smm"], w)
                sync.dma_start(out=wout_p[par][:, :],
                               in_=wout16.ap()[p * 128:(p + 1) * 128, :]).then_inc(sem["swg"], 16)
                cnt["swg"] += 16
            on("sync", b_wo_l)
            wo_tok = cnt["swg"]

            def b_wo_t(t, p=p, par=par, wt=wo_tok):
                t.wait_ge(sem["swg"], wt)
                for tcg in range(4):
                    for fg in range(2):
                        ins = t.matmul(pb[tcg * 2 + fg][:, 0:512],
                                       attn2[:, p, tcg * 128:(tcg + 1) * 128],
                                       wout_p[par][:, fg * 512:(fg + 1) * 512],
                                       start=False, stop=False)
                ins.then_inc(sem["smm"], 1)
                cnt["smm"] += 1
            on("tensor", b_wo_t)
            wo_used[par] = cnt["smm"]

        out_mm_tok = []

        def b_bias_t(t):
            for tcg in range(4):
                for fg in range(2):
                    t.matmul(pb[tcg * 2 + fg][:, 0:512], ones1[0:1, :],
                             boutr[0:1, fg * 512:(fg + 1) * 512],
                             start=False, stop=True).then_inc(sem["smm"], 1)
                    cnt["smm"] += 1
                    out_mm_tok.append(cnt["smm"])
        on("tensor", b_bias_t)

        for tcg in range(4):
            par = tcg % 2

            def b_oev_v(v, tcg=tcg, par=par, w=tok_outsb_free[par],
                        tokk=out_mm_tok[tcg * 2 + 1]):
                v.wait_ge(sem["smm"], tokk)
                if w:
                    v.wait_ge(sem["sst"], w)
                for fg in range(2):
                    v.tensor_tensor(out_sb[par][:, fg * 512:(fg + 1) * 512],
                                    pb[tcg * 2 + fg][:, 0:512],
                                    x_slab[:, tcg, fg * 512:(fg + 1) * 512], op=OP.add)
                v.drain()
                v.drain()
                v.engine_nop().then_inc(sem["sv"], 1)
                cnt["sv"] += 1
            on("vector", b_oev_v)
            tok_oev = cnt["sv"]

            def b_store(sync, tcg=tcg, par=par, m=m, tok_oev=tok_oev):
                sync.wait_ge(sem["sv"], tok_oev)
                sync.dma_start(out=out_d[m * MT + tcg * 128: m * MT + (tcg + 1) * 128, :],
                               in_=out_sb[par][:, :]).then_inc(sem["sst"], 16)
                cnt["sst"] += 16
            on("sync", b_store)
            tok_outsb_free[par] = cnt["sst"]
        tok_x_free_b = cnt["sv"]
        tok_hqT_free = cnt["smm"]

    SST_TOT = cnt["sst"]

    if DBG:
        def dump(g):
            g.wait_ge(sem["sst"], SST_TOT)
            g.dma_start(out=dbg_agwgu, in_=ag_wgu_out.ap()[:, :, :]).then_inc(sem["sgp"], 16)
            cnt["sgp"] += 16
            g.dma_start(out=dbg_wgusb, in_=wgu_sb[:, :, :, :]).then_inc(sem["sgp"], 16)
            cnt["sgp"] += 16
            g.dma_start(out=dbg_gup, in_=dbg_cap_gup[:, :]).then_inc(sem["sgp"], 16)
            cnt["sgp"] += 16
            g.dma_start(out=dbg_sgt, in_=dbg_cap_sgt[:, :]).then_inc(sem["sgp"], 16)
            cnt["sgp"] += 16
            for dst, srct in [(dbg_qres, q_res), (dbg_bdkv, bdkv), (dbg_ksb8, ksb8),
                              (dbg_xqf, xq_feat), (dbg_hqT, hqT), (dbg_attn2, attn2),
                              (dbg_htok, h_tok), (dbg_nrec, nrec)]:
                g.dma_start(out=dst, in_=srct[:, :] if len(dst.shape) == 2 else srct[:, :, :]).then_inc(sem["sgp"], 16)
                cnt["sgp"] += 16
            g.dma_start(out=dbg_st[:, 0:16], in_=stA[:, :]).then_inc(sem["sgp"], 16)
            g.dma_start(out=dbg_st[:, 16:32], in_=stB[:, :]).then_inc(sem["sgp"], 16)
            g.dma_start(out=dbg_st[:, 32:48], in_=stC[:, :]).then_inc(sem["sgp"], 16)
            g.dma_start(out=dbg_st[:, 48:64], in_=stD[:, :]).then_inc(sem["sgp"], 16)
            cnt["sgp"] += 64
            g.wait_ge(sem["sgp"], cnt["sgp"])
        on("gpsimd", dump)

    def fin(sync):
        sync.wait_ge(sem["sst"], SST_TOT)
        sync.wait_ge(sem["scc"], cnt["scc"])
        sync.wait_ge(sem["sca"], cnt["sca"])
        if DBG:
            sync.wait_ge(sem["sgp"], cnt["sgp"])
    on("sync", fin)

    es.close()
    return nc


_NC_CACHE = {}


def kernel(L0_attn_x, L0_attn_W_qkv, L0_attn_W_f, L0_attn_b_f, L0_attn_W_out,
           L0_attn_b_out, L0_ffn_w_gate, L0_ffn_w_up, L0_ffn_w_down):
    x = np.ascontiguousarray(np.asarray(L0_attn_x, dtype=np.float32)).reshape(NCORE, TCORE, D)
    wqkv = np.ascontiguousarray(np.asarray(L0_attn_W_qkv, dtype=np.float32))
    wf = np.ascontiguousarray(np.asarray(L0_attn_W_f, dtype=np.float32))
    b_f = np.asarray(L0_attn_b_f, dtype=np.float32)
    wout = np.ascontiguousarray(np.asarray(L0_attn_W_out, dtype=np.float32))
    b_out = np.asarray(L0_attn_b_out, dtype=np.float32)
    wg = np.ascontiguousarray(np.asarray(L0_ffn_w_gate, dtype=np.float32))
    wu = np.ascontiguousarray(np.asarray(L0_ffn_w_up, dtype=np.float32))
    wd = np.ascontiguousarray(np.asarray(L0_ffn_w_down, dtype=np.float32))

    bft = np.tile(b_f, 16)
    bqc = np.ascontiguousarray(bft.reshape(8, 128).T)
    bkvr = np.ascontiguousarray(np.concatenate([bft, np.zeros(D, np.float32)])[None, :])
    boutr = np.ascontiguousarray(b_out[None, :])
    idb = np.eye(128, dtype=np.float32).astype(bf16)
    idh = np.eye(128, dtype=np.float32).astype(np.float16)
    ones1 = np.ones((1, 128), np.float32).astype(bf16)
    onesc = np.ones((128, 8), np.float32).astype(bf16)
    selb = np.zeros((16, 8 * 128), np.float32)
    for p in range(8):
        selb[2 * p, p * 128: p * 128 + 64] = 1.0
        selb[2 * p + 1, p * 128 + 64: p * 128 + 128] = 1.0
    selb = selb.astype(bf16)

    in_maps = []
    for c in range(NCORE):
        wd_ch = []
        for s in range(WDCH):
            ic = s * 8 + c
            if ic < ICN:
                wd_ch.append(wd[:, ic * 128:(ic + 1) * 128])
            else:
                wd_ch.append(np.zeros((D, 128), np.float32))
        in_maps.append({
            "x": np.ascontiguousarray(x[c]),
            "wqkv": wqkv, "wf": wf, "wout": wout,
            "bqc": bqc, "bkvr": bkvr, "boutr": boutr,
            "wg_sl": np.ascontiguousarray(wg[:, c * 128:(c + 1) * 128]),
            "wu_sl": np.ascontiguousarray(wu[:, c * 128:(c + 1) * 128]),
            "wd_sl": np.ascontiguousarray(np.stack(wd_ch, 0)),
            "idb": idb, "idh": idh, "ones1": ones1, "onesc": onesc, "selb": selb,
        })

    if "prog" not in _NC_CACHE:
        _NC_CACHE["prog"] = build_nc()
    nc = _NC_CACHE["prog"]
    res = run_bass_kernel_spmd(nc, in_maps, core_ids=list(range(NCORE)))
    import os as _os
    if int(_os.environ.get("KDBG", "0")):
        kernel.DBG_RES = res
    out = np.concatenate([res.results[c]["out"][None] for c in range(NCORE)], 0)
    return np.ascontiguousarray(out.reshape(2, 16384, D))


# version 08
# speedup vs baseline: 1.0003x; 1.0003x over previous
import numpy as np
import ml_dtypes
import concourse.bass as bass
import concourse.mybir as mybir
from concourse.bass_utils import run_bass_kernel_spmd
from contextlib import ExitStack

F32 = mybir.dt.float32
F16 = mybir.dt.float16
BF16 = mybir.dt.bfloat16
I16 = mybir.dt.int16
I32 = mybir.dt.int32
OP = mybir.AluOpType
AF = mybir.ActivationFunctionType
AX = mybir.AxisListType
bf16 = ml_dtypes.bfloat16

NCORE = 8
TCORE = 4096
MT = 512
NMT = TCORE // MT
D = 1024
FI = 2816
ICN = 22
NIG = 11
WDCH = 3
MIDS = [0.25, 0.75, 1.25, 1.75, 2.5, 3.5, 5.0]


def build_nc():
    nc = bass.Bass("TRN2", target_bir_lowering=False)
    x_in = nc.dram_tensor("x", [TCORE, D], F32, kind="ExternalInput").ap()
    wqkv_in = nc.dram_tensor("wqkv", [D, 3 * D], F32, kind="ExternalInput").ap()
    wf_in = nc.dram_tensor("wf", [64, 64], F32, kind="ExternalInput").ap()
    wout_in = nc.dram_tensor("wout", [D, D], F32, kind="ExternalInput").ap()
    bqc_in = nc.dram_tensor("bqc", [128, 8], F32, kind="ExternalInput").ap()
    bkvr_in = nc.dram_tensor("bkvr", [1, 2 * D], F32, kind="ExternalInput").ap()
    bout_in = nc.dram_tensor("boutr", [1, D], F32, kind="ExternalInput").ap()
    wg_sl_in = nc.dram_tensor("wg_sl", [FI, 128], F32, kind="ExternalInput").ap()
    wu_sl_in = nc.dram_tensor("wu_sl", [FI, 128], F32, kind="ExternalInput").ap()
    wd_sl_in = nc.dram_tensor("wd_sl", [WDCH, D, 128], F32, kind="ExternalInput").ap()
    idb_in = nc.dram_tensor("idb", [128, 128], BF16, kind="ExternalInput").ap()
    idh_in = nc.dram_tensor("idh", [128, 128], F16, kind="ExternalInput").ap()
    ones1_in = nc.dram_tensor("ones1", [1, 128], BF16, kind="ExternalInput").ap()
    onesc_in = nc.dram_tensor("onesc", [128, 8], BF16, kind="ExternalInput").ap()
    selb_in = nc.dram_tensor("selb", [16, 8 * 128], BF16, kind="ExternalInput").ap()
    out_d = nc.dram_tensor("out", [TCORE, D], F32, kind="ExternalOutput").ap()
    import os as _os
    DBG = bool(int(_os.environ.get("KDBG", "0")))
    if DBG:
        dbg_qres = nc.dram_tensor("dbg_qres", [128, 8, TCORE], F32, kind="ExternalOutput").ap()
        dbg_bdkv = nc.dram_tensor("dbg_bdkv", [128, 8, 128], F32, kind="ExternalOutput").ap()
        dbg_ksb8 = nc.dram_tensor("dbg_ksb8", [128, 8, 16], F32, kind="ExternalOutput").ap()
        dbg_xqf = nc.dram_tensor("dbg_xqf", [128, 8, MT], F32, kind="ExternalOutput").ap()
        dbg_hqT = nc.dram_tensor("dbg_hqT", [128, ICN, MT], F32, kind="ExternalOutput").ap()
        dbg_attn2 = nc.dram_tensor("dbg_attn2", [128, 8, MT], F32, kind="ExternalOutput").ap()
        dbg_htok = nc.dram_tensor("dbg_htok", [128, 2, FI], F32, kind="ExternalOutput").ap()
        dbg_nrec = nc.dram_tensor("dbg_nrec", [16, 512], F32, kind="ExternalOutput").ap()
        dbg_st = nc.dram_tensor("dbg_st", [128, 64], F32, kind="ExternalOutput").ap()
        dbg_watt = nc.dram_tensor("dbg_watt", [128, 8, 3 * D], F32, kind="ExternalOutput").ap()
        dbg_xnf = nc.dram_tensor("dbg_xnf", [128, 8, MT], F32, kind="ExternalOutput").ap()
        dbg_xnb = nc.dram_tensor("dbg_xnb", [128, 4, D], F32, kind="ExternalOutput").ap()
        dbg_kvall = nc.dram_tensor("dbg_kvall", [128, 1032], F32, kind="ExternalOutput").ap()
        dbg_xqall = nc.dram_tensor("dbg_xqall", [NMT, 128, 8, MT], F16, kind="ExternalOutput").ap()
        dbg_hqTall = nc.dram_tensor("dbg_hqTall", [NMT, 128, ICN, MT], F16, kind="ExternalOutput").ap()
        dbg_at2all = nc.dram_tensor("dbg_at2all", [NMT, 128, 8, MT], BF16, kind="ExternalOutput").ap()
        dbg_agwgu = nc.dram_tensor("dbg_agwgu", [8, 128, 2 * FI], F16, kind="ExternalOutput").ap()
        dbg_wgusb = nc.dram_tensor("dbg_wgusb", [128, 2, 8, 512], F16, kind="ExternalOutput").ap()
        dbg_gup = nc.dram_tensor("dbg_gup", [128, 512], F16, kind="ExternalOutput").ap()
        dbg_sgt = nc.dram_tensor("dbg_sgt", [128, 256], F16, kind="ExternalOutput").ap()
    ag_wgu_in = nc.dram_tensor("ag_wgu_in", [128, 2 * FI], F16)
    ag_wgu_out = nc.dram_tensor("ag_wgu_out", [8, 128, 2 * FI], F16)
    ag_wd_in = nc.dram_tensor("ag_wd_in", [WDCH, 128, D], F16)
    ag_wd_out = nc.dram_tensor("ag_wd_out", [8, WDCH, 128, D], F16)
    ar_in = nc.dram_tensor("ar_in", [128, 1032], F32)
    ar_out = nc.dram_tensor("ar_out", [128, 1032], F32)
    wout16 = nc.dram_tensor("wout16", [D, D], BF16)

    es = ExitStack()
    SB = lambda name, shape, dt: es.enter_context(nc.sbuf_tensor(name, shape, dt))
    pb = [es.enter_context(nc.psum_tensor(f"pb{i}", [128, 512], F32)) for i in range(8)]
    sem_names = ["sx", "sw", "sg1", "smm", "sv", "ss", "sgp", "scc", "swc", "sca",
                 "sgpa", "sgpb", "swga", "swgb", "swda", "swdb", "swoa", "swob", "ssta", "sstb"]
    sem = {n: es.enter_context(nc.semaphore(n)) for n in sem_names}
    cnt = {n: 0 for n in sem_names}
    block = es.enter_context(nc.Block())

    def on(eng, fn):
        getattr(block, eng)(fn)

    identb = SB("identb", [128, 128], BF16)
    identh = SB("identh", [128, 128], F16)
    ones1 = SB("ones1s", [1, 128], BF16)
    onesc = SB("onescs", [128, 8], BF16)
    selb = SB("selbs", [16, 8 * 128], BF16)
    bqc = SB("bqcs", [128, 8], F32)
    bkvr = SB("bkvrs", [1, 2 * D], BF16)
    boutr = SB("boutrs", [1, D], BF16)
    q_res = SB("q_res", [128, 8, TCORE], BF16)
    x_slab = SB("x_slab", [128, 4, D], F32)
    stA = SB("stA", [128, 16], F32)
    stB = SB("stB", [128, 16], F32)
    stC = SB("stC", [128, 16], F32)
    stD = SB("stD", [128, 16], F32)
    qv = SB("qv", [128, 1408], F32)
    ql = SB("ql", [128, 1408], F16)
    qh = SB("qh", [128, 1408], F16)
    qc7 = SB("qc7", [128, 1408], F16)
    qsm = SB("qsm", [128, 1408], F16)
    bmx = SB("bmx", [128, 88], F32)
    bsc = SB("bsc", [128, 88], F32)
    brr = SB("brr", [128, 88], F32)
    bdkv = SB("bdkv", [128, 8, 128], BF16)
    ksb8 = SB("ksb8", [128, 8, 16], BF16)

    def p0_sync(sync):
        for dst, src in [(identb, idb_in), (identh, idh_in), (ones1, ones1_in),
                         (onesc, onesc_in), (selb, selb_in), (bqc, bqc_in)]:
            sync.dma_start(out=dst[:, :], in_=src[:, :]).then_inc(sem["sw"], 16)
            cnt["sw"] += 16
    on("sync", p0_sync)
    SW_CONST = cnt["sw"]

    with nc.sbuf_tensor("W_att", [128, 8, 3 * D], BF16) as W_att, \
         nc.sbuf_tensor("ln_sq", [128, 4 * D], F32) as ln_sq:

        with nc.sbuf_tensor("wqkv_sb", [128, 8, 2 * D], BF16) as wqkv_sb, \
             nc.sbuf_tensor("wf_sb", [64, 64], BF16) as wf_sb, \
             nc.sbuf_tensor("wqt0", [64, 128], BF16) as wqt0, \
             nc.sbuf_tensor("wqt1", [64, 128], BF16) as wqt1, \
             nc.sbuf_tensor("wc16", [128, 1, D], BF16) as wc16:
            wqt = [wqt0, wqt1]

            def p1_gp(g):
                g.dma_start(out=wqkv_sb[:, :, :],
                            in_=wqkv_in[:, 0:2 * D].rearrange("(c p) f -> p c f", p=128)).then_inc(sem["sg1"], 16)
                g.dma_start(out=W_att[:, :, 2 * D:3 * D],
                            in_=wqkv_in[:, 2 * D:3 * D].rearrange("(c p) f -> p c f", p=128)).then_inc(sem["sg1"], 16)
                g.dma_start(out=wf_sb[:, :], in_=wf_in[:, :]).then_inc(sem["sg1"], 16)
                g.dma_start(out=bkvr[:, :], in_=bkvr_in[:, :]).then_inc(sem["sg1"], 16)
                g.dma_start(out=boutr[:, :], in_=bout_in[:, :]).then_inc(sem["sg1"], 16)
                cnt["sg1"] += 80
            on("gpsimd", p1_gp)
            SG1_W = cnt["sg1"]
            SV_VCOPY = cnt["sv"]

            sx_wc0 = cnt["sx"]

            def p1_wc_g(g):
                for p in range(8):
                    if p >= 1:
                        g.wait_ge(sem["sx"], sx_wc0 + 16 * p)
                    g.dma_start(out=wc16[:, 0, :],
                                in_=wout_in[p * 128:(p + 1) * 128, :]).then_inc(sem["swc"], 16)
                    cnt["swc"] += 16
            on("gpsimd", p1_wc_g)

            def p1_wc_s(sync):
                for p in range(8):
                    sync.wait_ge(sem["swc"], 16 * (p + 1))
                    sync.dma_start(out=wout16.ap()[p * 128:(p + 1) * 128, :],
                                   in_=wc16[:, 0, :]).then_inc(sem["sx"], 16)
                    cnt["sx"] += 16
            on("sync", p1_wc_s)
            WC_SX = cnt["sx"]

            ss_base = cnt["ss"]
            smm_base = cnt["smm"]

            def p1_t(t):
                t.wait_ge(sem["sg1"], SG1_W)
                t.wait_ge(sem["sw"], SW_CONST)
                it = 0
                for c in range(8):
                    for qk in range(2):
                        for h in range(16):
                            bA = pb[it % 2]
                            bB = pb[2 + it % 2]
                            if it >= 2:
                                t.wait_ge(sem["ss"], ss_base + 2 * (it - 1))
                            src = wqkv_sb[:, c, qk * D + h * 64: qk * D + (h + 1) * 64]
                            t.matmul(bA[0:64, 0:128], src, identb[:, :],
                                     start=True, stop=True).then_inc(sem["smm"], 1)
                            t.wait_ge(sem["ss"], ss_base + 2 * it + 1)
                            t.matmul(bB[0:128, 0:64], wqt[it % 2][0:64, :], wf_sb[:, :],
                                     start=True, stop=True).then_inc(sem["smm"], 1)
                            it += 1
                cnt["smm"] += 512
            on("tensor", p1_t)

            def p1_s(s):
                it = 0
                for c in range(8):
                    for qk in range(2):
                        for h in range(16):
                            bA = pb[it % 2]
                            bB = pb[2 + it % 2]
                            s.wait_ge(sem["smm"], smm_base + 2 * it + 1)
                            s.copy(wqt[it % 2][0:64, :], bA[0:64, 0:128]).then_inc(sem["ss"], 1)
                            s.wait_ge(sem["smm"], smm_base + 2 * it + 2)
                            s.copy(W_att[:, c, qk * D + h * 64: qk * D + (h + 1) * 64],
                                   bB[0:128, 0:64]).then_inc(sem["ss"], 1)
                            it += 1
                cnt["ss"] += 512
            on("scalar", p1_s)
        SS_FOLD = cnt["ss"]
        SMM_FOLD_END = cnt["smm"]

        def emit_quant_f16(v, src2d, W, out2d, sc_ap, rr_ap):
            nb = W // 16
            v3 = qv[:, 0:W].rearrange("p (b k) -> p b k", k=16)
            v.tensor_tensor(v3, src2d.rearrange("p (b k) -> p b k", k=16),
                            rr_ap.unsqueeze(2).broadcast_to([128, nb, 16]), op=OP.mult)
            v.drain()
            v.tensor_scalar(qsm[:, 0:W], qv[:, 0:W], 0.0, None, op0=OP.is_lt)
            v.drain()
            v.tensor_scalar(qv[:, 0:W].bitcast(I32), qv[:, 0:W].bitcast(I32), 0x7FFFFFFF, None,
                            op0=OP.bitwise_and)
            v.drain()
            a = qv[:, 0:W]
            v.tensor_scalar(qsm[:, 0:W], qsm[:, 0:W], -2.0, 1.0, op0=OP.mult, op1=OP.add)
            v.tensor_scalar(ql[:, 0:W], a, MIDS[0], None, op0=OP.is_gt)
            v.drain()
            v.scalar_tensor_tensor(ql[:, 0:W], a, MIDS[1], ql[:, 0:W], op0=OP.is_gt, op1=OP.add)
            v.drain()
            v.scalar_tensor_tensor(ql[:, 0:W], a, MIDS[2], ql[:, 0:W], op0=OP.is_gt, op1=OP.add)
            v.drain()
            v.scalar_tensor_tensor(ql[:, 0:W], a, MIDS[3], ql[:, 0:W], op0=OP.is_gt, op1=OP.add)
            v.tensor_scalar(qh[:, 0:W], a, MIDS[4], None, op0=OP.is_gt)
            v.drain()
            v.scalar_tensor_tensor(qh[:, 0:W], a, MIDS[5], qh[:, 0:W], op0=OP.is_gt, op1=OP.add)
            v.tensor_scalar(qc7[:, 0:W], a, MIDS[6], 2.0, op0=OP.is_gt, op1=OP.mult)
            v.drain()
            v.scalar_tensor_tensor(ql[:, 0:W], ql[:, 0:W], 0.5, qh[:, 0:W], op0=OP.mult, op1=OP.add)
            v.drain()
            v.tensor_tensor(ql[:, 0:W], ql[:, 0:W], qc7[:, 0:W], op=OP.add)
            v.drain()
            v.tensor_tensor(qh[:, 0:W].rearrange("p (b k) -> p b k", k=16),
                            ql[:, 0:W].rearrange("p (b k) -> p b k", k=16),
                            sc_ap.unsqueeze(2).broadcast_to([128, nb, 16]), op=OP.mult)
            v.drain()
            v.tensor_tensor(out2d, qh[:, 0:W], qsm[:, 0:W], op=OP.mult)

        def emit_bmax(v, src2d, W, bm_ap, sc_ap, rr_ap):
            v.tensor_reduce(bm_ap, src2d.rearrange("p (b k) -> p b k", k=16),
                            axis=AX.X, op=OP.max, apply_absolute_value=True)
            v.drain()
            v.tensor_scalar(sc_ap, bm_ap, 1.0 / 6.0, 1e-12, op0=OP.mult, op1=OP.max)
            v.drain()
            v.reciprocal(rr_ap, sc_ap)
            v.drain()

        with nc.sbuf_tensor("wg_sb", [128, ICN, 128], F32) as wg_sb, \
             nc.sbuf_tensor("wu_sb", [128, ICN, 128], F32) as wu_sb, \
             nc.sbuf_tensor("wd_sb", [128, WDCH, 8, 128], F32) as wd_sb, \
             nc.sbuf_tensor("wq16a", [128, 128], F16) as wq16a, \
             nc.sbuf_tensor("wq16b", [128, 128], F16) as wq16b, \
             nc.sbuf_tensor("wt16a", [128, 128], F16) as wt16a, \
             nc.sbuf_tensor("wt16b", [128, 128], F16) as wt16b:
            wq16 = [wq16a, wq16b]
            wt16 = [wt16a, wt16b]

            def p2_sync(sync):
                sync.wait_ge(sem["smm"], SMM_FOLD_END)
                sync.wait_ge(sem["sx"], WC_SX)
                sync.dma_start(out=wg_sb[:, :, :],
                               in_=wg_sl_in[:, :].rearrange("(a p) k -> p a k", p=128)).then_inc(sem["sw"], 16)
                sync.dma_start(out=wu_sb[:, :, :],
                               in_=wu_sl_in[:, :].rearrange("(a p) k -> p a k", p=128)).then_inc(sem["sw"], 16)
                sync.dma_start(out=wd_sb[:, :, :, :],
                               in_=wd_sl_in[:, :, :].rearrange("j (a p) k -> p j a k", p=128)).then_inc(sem["sw"], 16)
                cnt["sw"] += 48
            on("sync", p2_sync)
            SW_WSH = cnt["sw"]

            tiles = []
            for j in range(ICN):
                col = (j // 2) * 512 + (j % 2) * 128
                tiles.append((wg_sb[:, j, :], ag_wgu_in.ap()[:, col:col + 128]))
            for j in range(ICN):
                col = (j // 2) * 512 + 256 + (j % 2) * 128
                tiles.append((wu_sb[:, j, :], ag_wgu_in.ap()[:, col:col + 128]))
            for jj in range(WDCH):
                for fc in range(8):
                    tiles.append((wd_sb[:, jj, fc, :], ag_wd_in.ap()[jj, :, fc * 128:(fc + 1) * 128]))
            NT = len(tiles)
            sv0, smm0, ss0, sgp0 = cnt["sv"], cnt["smm"], cnt["ss"], cnt["sgp"]

            def p2_v(v):
                v.wait_ge(sem["sw"], SW_WSH)
                for i, (srcap, dst) in enumerate(tiles):
                    if i >= 2:
                        v.wait_ge(sem["smm"], smm0 + (i - 1))
                    emit_bmax(v, srcap, 128, bmx[:, 0:8], bsc[:, 0:8], brr[:, 0:8])
                    emit_quant_f16(v, srcap, 128, wq16[i % 2][:, :], bsc[:, 0:8], brr[:, 0:8])
                    v.drain()
                    v.engine_nop().then_inc(sem["sv"], 1)
                    cnt["sv"] += 1
            on("vector", p2_v)

            def p2_t(t):
                for i in range(NT):
                    t.wait_ge(sem["sv"], sv0 + i + 1)
                    if i >= 2:
                        t.wait_ge(sem["ss"], ss0 + (i - 1))
                    t.matmul(pb[4 + i % 2][0:128, 0:128], wq16[i % 2][:, :], identh[:, :],
                             start=True, stop=True).then_inc(sem["smm"], 1)
                    cnt["smm"] += 1
            on("tensor", p2_t)

            def p2_s(s):
                for i in range(NT):
                    s.wait_ge(sem["smm"], smm0 + i + 1)
                    if i >= 2:
                        sn = "sgpa" if i % 2 == 0 else "sgpb"
                        s.wait_ge(sem[sn], 16 * ((i - 2) // 2 + 1))
                    s.copy(wt16[i % 2][:, :], pb[4 + i % 2][0:128, 0:128]).then_inc(sem["ss"], 1)
                    cnt["ss"] += 1
            on("scalar", p2_s)

            def p2_dump(sync):
                for i, (src, dst) in enumerate(tiles):
                    sync.wait_ge(sem["ss"], ss0 + i + 1)
                    sn = "sgpa" if i % 2 == 0 else "sgpb"
                    sync.dma_start(out=dst, in_=wt16[i % 2][:, :]).then_inc(sem[sn], 16)
                    cnt[sn] += 16
            on("sync", p2_dump)

            def p2_g(g):
                g.wait_ge(sem["sgpa"], 16 * ((NT + 1) // 2))
                g.wait_ge(sem["sgpb"], 16 * (NT // 2))
                g.collective_compute("AllGather", OP.bypass,
                                     replica_groups=[[0, 1, 2, 3, 4, 5, 6, 7]],
                                     ins=[ag_wgu_in.ap().opt()],
                                     outs=[ag_wgu_out.ap().opt()]).then_inc(sem["sca"], 1)
                cnt["sca"] += 1
                g.collective_compute("AllGather", OP.bypass,
                                     replica_groups=[[0, 1, 2, 3, 4, 5, 6, 7]],
                                     ins=[ag_wd_in.ap().opt()],
                                     outs=[ag_wd_out.ap().opt()]).then_inc(sem["sca"], 1)
                cnt["sca"] += 1
            on("gpsimd", p2_g)
        SCC_AG = cnt["sca"]
        P2_SMM_END = cnt["smm"]
        P2_SGPA_END = cnt["sgpa"]
        P2_SGPB_END = cnt["sgpb"]

        def emit_ln(sq_view):
            def s_sq(sc):
                sc.activation(sq_view, x_slab[:, :, :].rearrange("p c f -> p (c f)"),
                              AF.Square).then_inc(sem["ss"], 1)
                cnt["ss"] += 1
            on("scalar", s_sq)
            tok_sq = cnt["ss"]

            def v_stats(v, tok_sq=tok_sq, sq_view=sq_view):
                for c in range(4):
                    v.tensor_reduce(stA[:, c:c + 1], x_slab[:, c, :], axis=AX.X, op=OP.add)
                v.wait_ge(sem["ss"], tok_sq)
                for c in range(4):
                    v.tensor_reduce(stA[:, 4 + c:5 + c], sq_view[:, c * D:(c + 1) * D],
                                    axis=AX.X, op=OP.add)
                v.drain()
                v.tensor_scalar(stB[:, 0:4], stA[:, 0:4], 1.0 / D, None, op0=OP.mult)
                v.tensor_scalar(stB[:, 4:8], stA[:, 4:8], 1.0 / D, None, op0=OP.mult)
                v.drain()
                v.tensor_tensor(stC[:, 0:4], stB[:, 0:4], stB[:, 0:4], op=OP.mult)
                v.drain()
                v.tensor_tensor(stA[:, 8:12], stB[:, 4:8], stC[:, 0:4], op=OP.subtract)
                v.drain()
                v.tensor_scalar(stB[:, 8:12], stA[:, 8:12], 1e-5, None, op0=OP.add)
                v.drain()
                v.reciprocal(stC[:, 8:12], stB[:, 8:12])
                v.drain()
                v.engine_nop().then_inc(sem["sv"], 1)
                cnt["sv"] += 1
            on("vector", v_stats)
            tok_iv = cnt["sv"]

            def s_sqrt(sc, tok_iv=tok_iv):
                sc.wait_ge(sem["sv"], tok_iv)
                sc.activation(stA[:, 12:16], stC[:, 8:12], AF.Sqrt).then_inc(sem["ss"], 1)
                cnt["ss"] += 1
            on("scalar", s_sqrt)
            tok_r0 = cnt["ss"]

            def v_newton(v, tok_r0=tok_r0):
                v.wait_ge(sem["ss"], tok_r0)
                v.tensor_tensor(stC[:, 12:16], stA[:, 12:16], stA[:, 12:16], op=OP.mult)
                v.drain()
                v.tensor_tensor(stD[:, 0:4], stB[:, 8:12], stC[:, 12:16], op=OP.mult)
                v.drain()
                v.tensor_scalar(stC[:, 0:4], stD[:, 0:4], -0.5, 1.5, op0=OP.mult, op1=OP.add)
                v.drain()
                v.tensor_tensor(stD[:, 8:12], stA[:, 12:16], stC[:, 0:4], op=OP.mult)
                v.drain()
                v.drain()
                v.engine_nop().then_inc(sem["sv"], 1)
                cnt["sv"] += 1
            on("vector", v_newton)
            return cnt["sv"]

        with nc.sbuf_tensor("xnb", [128, 4, D], BF16) as xnb, \
             nc.sbuf_tensor("xnf", [128, 8, MT], BF16) as xnf, \
             nc.sbuf_tensor("k_t0", [128, D], BF16) as k_t0, \
             nc.sbuf_tensor("k_t1", [128, D], BF16) as k_t1, \
             nc.sbuf_tensor("v_t0", [128, D], BF16) as v_t0, \
             nc.sbuf_tensor("v_t1", [128, D], BF16) as v_t1, \
             nc.sbuf_tensor("kv_all", [128, 1032], F32) as kv_all, \
             nc.sbuf_tensor("kvs", [128, 1032], F32) as kvs:
            k_t = [k_t0, k_t1]
            v_t = [v_t0, v_t1]
            tok_x_free = 0
            tok_xnb_free = 0
            tok_xnf_cons = 0
            tok_kt_used = [0, 0]

            for m in range(NMT):
                def a_load(sync, m=m, w=tok_x_free):
                    if w:
                        sync.wait_ge(sem["sv"], w)
                    sync.dma_start(out=x_slab[:, :, :],
                                   in_=x_in[m * MT:(m + 1) * MT, :].rearrange("(c p) f -> p c f", p=128)
                                   ).then_inc(sem["sx"], 16)
                    cnt["sx"] += 16
                on("sync", a_load)
                tok_x = cnt["sx"]

                def a_wx_v(v, t=tok_x):
                    v.wait_ge(sem["sx"], t)
                on("vector", a_wx_v)

                def a_wx_s(s, t=tok_x):
                    s.wait_ge(sem["sx"], t)
                on("scalar", a_wx_s)

                tok_rstd = emit_ln(ln_sq[:, :])

                def a_xn(v, w=tok_xnb_free, tr=tok_rstd, m=m):
                    v.wait_ge(sem["sv"], tr)
                    if m == 0:
                        v.wait_ge(sem["smm"], P2_SMM_END)
                        v.wait_ge(sem["sgpa"], P2_SGPA_END)
                        v.wait_ge(sem["sgpb"], P2_SGPB_END)
                    if w:
                        v.wait_ge(sem["smm"], w)
                    for c in range(4):
                        v.tensor_scalar(xnb[:, c, :], x_slab[:, c, :],
                                        stB[:, c:c + 1], stD[:, 8 + c:9 + c],
                                        op0=OP.subtract, op1=OP.mult)
                    v.drain()
                    v.engine_nop().then_inc(sem["sv"], 1)
                    cnt["sv"] += 1
                on("vector", a_xn)
                tok_xn = cnt["sv"]
                tok_x_free = tok_xn

                smm_tp0 = cnt["smm"]
                ss_tp0 = cnt["ss"]

                def a_tp_t(t, tok_xn=tok_xn, smm_tp0=smm_tp0, ss_tp0=ss_tp0, m=m):
                    t.wait_ge(sem["sv"], tok_xn)
                    if m == 0:
                        t.wait_ge(sem["ss"], SS_FOLD)
                        t.wait_ge(sem["sv"], SV_VCOPY)
                    it = 0
                    for c in range(4):
                        for j in range(8):
                            bA = pb[it % 2]
                            if it >= 2:
                                t.wait_ge(sem["ss"], ss_tp0 + (it - 1))
                            t.matmul(bA[0:128, 0:128], xnb[:, c, j * 128:(j + 1) * 128],
                                     identb[:, :], start=True, stop=True).then_inc(sem["smm"], 1)
                            it += 1
                    cnt["smm"] += 32
                on("tensor", a_tp_t)
                tok_xnb_free = cnt["smm"]

                def a_tp_s(s, smm_tp0=smm_tp0, w=tok_xnf_cons):
                    if w:
                        s.wait_ge(sem["smm"], w)
                    it = 0
                    for c in range(4):
                        for j in range(8):
                            bA = pb[it % 2]
                            s.wait_ge(sem["smm"], smm_tp0 + it + 1)
                            s.copy(xnf[:, j, c * 128:(c + 1) * 128], bA[0:128, 0:128]).then_inc(sem["ss"], 1)
                            it += 1
                    cnt["ss"] += 32
                on("scalar", a_tp_s)
                tok_tp_evac = cnt["ss"]

                smm_q0 = cnt["smm"]
                ss_q0 = cnt["ss"]

                def a_q_t(t, tok_tp_evac=tok_tp_evac, ss_q0=ss_q0):
                    t.wait_ge(sem["ss"], tok_tp_evac)
                    for mc in range(8):
                        bA = pb[mc % 2]
                        if mc >= 2:
                            t.wait_ge(sem["ss"], ss_q0 + (mc - 1))
                        for kc in range(8):
                            ins = t.matmul(bA[:, 0:512], W_att[:, kc, mc * 128:(mc + 1) * 128],
                                           xnf[:, kc, :], start=(kc == 0), stop=(kc == 7))
                        ins.then_inc(sem["smm"], 1)
                        cnt["smm"] += 1
                on("tensor", a_q_t)

                def a_q_s(s, smm_q0=smm_q0, m=m):
                    for mc in range(8):
                        bA = pb[mc % 2]
                        s.wait_ge(sem["smm"], smm_q0 + mc + 1)
                        s.activation(q_res[:, mc, m * MT:(m + 1) * MT], bA[:, 0:512], AF.Relu,
                                     bias=bqc[:, mc:mc + 1]).then_inc(sem["ss"], 1)
                        cnt["ss"] += 1
                on("scalar", a_q_s)
                tok_q_evac = cnt["ss"]

                for tc in range(4):
                    par = tc % 2
                    smm_kv0 = cnt["smm"]
                    ss_kv0 = cnt["ss"]

                    def a_kv_t(t, tc=tc, smm_kv0=smm_kv0, ss_kv0=ss_kv0,
                               tok_q_evac=tok_q_evac):
                        if tc == 0:
                            t.wait_ge(sem["ss"], tok_q_evac)
                        else:
                            t.wait_ge(sem["ss"], ss_kv0)
                        for grp in range(4):
                            bA = pb[2 + grp % 2]
                            if grp >= 2:
                                t.wait_ge(sem["ss"], ss_kv0 + (grp - 1))
                            for kc in range(8):
                                last = (kc == 7) and (grp >= 2)
                                ins = t.matmul(bA[:, 0:512], xnf[:, kc, tc * 128:(tc + 1) * 128],
                                               W_att[:, kc, D + grp * 512: D + (grp + 1) * 512],
                                               start=(kc == 0), stop=last)
                            if grp < 2:
                                ins = t.matmul(bA[:, 0:512], ones1[0:1, :],
                                               bkvr[0:1, grp * 512:(grp + 1) * 512],
                                               start=False, stop=True)
                            ins.then_inc(sem["smm"], 1)
                            cnt["smm"] += 1
                    on("tensor", a_kv_t)
                    tok_kv_grps = cnt["smm"]

                    def a_kv_s(s, par=par, smm_kv0=smm_kv0, w=tok_kt_used[par]):
                        if w:
                            s.wait_ge(sem["smm"], w)
                        for grp in range(4):
                            bA = pb[2 + grp % 2]
                            s.wait_ge(sem["smm"], smm_kv0 + grp + 1)
                            if grp < 2:
                                s.activation(k_t[par][:, grp * 512:(grp + 1) * 512], bA[:, 0:512],
                                             AF.Relu).then_inc(sem["ss"], 1)
                            else:
                                s.copy(v_t[par][:, (grp - 2) * 512:(grp - 1) * 512],
                                       bA[:, 0:512]).then_inc(sem["ss"], 1)
                            cnt["ss"] += 1
                    on("scalar", a_kv_s)
                    tok_kv_evac = cnt["ss"]

                    def a_acc_t(t, m=m, tc=tc, par=par, tok_kv_evac=tok_kv_evac):
                        t.wait_ge(sem["ss"], tok_kv_evac)
                        first = (m == 0 and tc == 0)
                        lastt = (m == NMT - 1 and tc == 3)
                        for p in range(8):
                            t.matmul(pb[4 + p // 4][:, (p % 4) * 128:(p % 4 + 1) * 128],
                                     k_t[par][:, p * 128:(p + 1) * 128],
                                     v_t[par][:, p * 128:(p + 1) * 128],
                                     start=first, stop=lastt)
                            ins = t.matmul(pb[6][:, p:p + 1],
                                           k_t[par][:, p * 128:(p + 1) * 128],
                                           onesc[:, 0:1], start=first, stop=lastt)
                        ins.then_inc(sem["smm"], 1)
                        cnt["smm"] += 1
                    on("tensor", a_acc_t)
                    tok_kt_used[par] = cnt["smm"]
                tok_xnf_cons = cnt["smm"]
            tok_kv_done = cnt["smm"]

            def ar_v(v, tok_kv_done=tok_kv_done):
                v.wait_ge(sem["smm"], tok_kv_done)
                v.tensor_copy(kvs[:, 0:512], pb[4][:, :])
                v.tensor_copy(kvs[:, 512:1024], pb[5][:, :])
                v.tensor_copy(kvs[:, 1024:1032], pb[6][:, 0:8])
                v.drain()
                v.drain()
                v.engine_nop().then_inc(sem["sv"], 1)
                cnt["sv"] += 1
            on("vector", ar_v)
            tok_kvs = cnt["sv"]

            def ar_g(g, tok_kvs=tok_kvs):
                g.wait_ge(sem["sv"], tok_kvs)
                g.dma_start(out=ar_in.ap()[:, :], in_=kvs[:, :]).then_inc(sem["sgp"], 16)
                cnt["sgp"] += 16
                g.wait_ge(sem["sgp"], cnt["sgp"])
                g.collective_compute("AllReduce", OP.add,
                                     replica_groups=[[0, 1, 2, 3], [4, 5, 6, 7]],
                                     ins=[ar_in.ap().opt()],
                                     outs=[ar_out.ap().opt()]).then_inc(sem["scc"], 1)
                cnt["scc"] += 1
                g.wait_ge(sem["scc"], cnt["scc"])
                g.dma_start(out=kv_all[:, :], in_=ar_out.ap()[:, :]).then_inc(sem["sgp"], 16)
                cnt["sgp"] += 16
            on("gpsimd", ar_g)
            tok_kvall = cnt["sgp"]

            def ar_build(v, tok_kvall=tok_kvall):
                v.wait_ge(sem["sgp"], tok_kvall)
                v.memset(bdkv[:, :, :], 0.0)
                v.memset(ksb8[:, :, :], 0.0)
                v.drain()
                for p in range(8):
                    v.tensor_copy(bdkv[0:64, p, 0:64], kv_all[0:64, p * 128: p * 128 + 64])
                    v.tensor_copy(bdkv[64:128, p, 64:128], kv_all[64:128, p * 128 + 64: p * 128 + 128])
                    v.tensor_copy(ksb8[0:64, p, 2 * p:2 * p + 1], kv_all[0:64, 1024 + p:1025 + p])
                    v.tensor_copy(ksb8[64:128, p, 2 * p + 1:2 * p + 2], kv_all[64:128, 1024 + p:1025 + p])
                v.drain()
                v.drain()
                v.engine_nop().then_inc(sem["sv"], 1)
                cnt["sv"] += 1
            on("vector", ar_build)
            tok_attc = cnt["sv"]
            tok_x_free_ab = tok_x_free

            DUMPA_SGP = 0
            if DBG:
                def dump_a(g, tok_attc=tok_attc):
                    g.wait_ge(sem["sv"], tok_attc)
                    g.dma_start(out=dbg_watt, in_=W_att[:, :, :]).then_inc(sem["sgp"], 16)
                    g.dma_start(out=dbg_xnf, in_=xnf[:, :, :]).then_inc(sem["sgp"], 16)
                    g.dma_start(out=dbg_xnb, in_=xnb[:, :, :]).then_inc(sem["sgp"], 16)
                    g.dma_start(out=dbg_kvall, in_=kv_all[:, :]).then_inc(sem["sgp"], 16)
                    cnt["sgp"] += 64
                on("gpsimd", dump_a)
                DUMPA_SGP = cnt["sgp"]

    wgu_sb = SB("wgu_sb", [128, 2, 8, 512], F16)
    wd_ic = [SB("wd_ic0", [128, D], F16), SB("wd_ic1", [128, D], F16)]
    wout_p = [SB("wout_p0", [128, D], BF16), SB("wout_p1", [128, D], BF16)]
    xn32 = SB("xn32", [128, D], F32)
    xq_t1 = SB("xq_tok0", [128, D], F16)
    xq_tok = [xq_t1, xq_t1]
    xq_feat = SB("xq_feat", [128, 8, MT], F16)
    h_tok = SB("h_tok", [128, 2, FI], F16)
    hq_t = SB("hq0", [128, 1408], F16)
    hqx = [hq_t, hq_t]
    hqT = SB("hqT", [128, ICN, MT], F16)
    sgt = [SB("sgt0", [128, 256], F32), SB("sgt1", [128, 256], F32)]
    attn2 = SB("attn2", [128, 8, MT], BF16)
    rbx = [SB("rb0", [128, 512], BF16), SB("rb1", [128, 512], BF16)]
    nplus = SB("nplus", [16, 512], F32)
    nrec = SB("nrec", [16, 512], F32)
    nrec16 = SB("nrec16", [16, 512], BF16)
    out_sb = [SB("out_sb0", [128, D], F32), SB("out_sb1", [128, D], F32)]
    sq_view = hqT[:, :, :].rearrange("p a f -> p (a f)").bitcast(F32)[:, 0:4 * D]

    dbg_cap_gup = SB("dbg_cap_gup", [128, 512], F16) if DBG else None
    dbg_cap_sgt = SB("dbg_cap_sgt", [128, 256], F16) if DBG else None
    tok_x_free_b = tok_attc
    tok_wgu_free = [0, 0]
    tok_hqT_free = 0
    tok_outsb_free = [0, 0]
    tok_sgt_used = [0, 0]
    wd_used_tok = [0, 0]
    wo_used = [0, 0]

    for m in range(NMT):
        def b_load(sync, m=m, w=tok_x_free_b):
            if w:
                sync.wait_ge(sem["sv"], w)
            sync.dma_start(out=x_slab[:, :, :],
                           in_=x_in[m * MT:(m + 1) * MT, :].rearrange("(c p) f -> p c f", p=128)
                           ).then_inc(sem["sx"], 16)
            cnt["sx"] += 16
        on("sync", b_load)
        tok_xb = cnt["sx"]

        def b_wx_v(v, t=tok_xb, thq=tok_hqT_free, dg=(DUMPA_SGP if m == 0 else 0)):
            if dg:
                v.wait_ge(sem["sgp"], dg)
            v.wait_ge(sem["sx"], t)
            if thq:
                v.wait_ge(sem["smm"], thq)
        on("vector", b_wx_v)

        def b_wx_s(s, t=tok_xb, thq=tok_hqT_free):
            s.wait_ge(sem["sx"], t)
            if thq:
                s.wait_ge(sem["smm"], thq)
        on("scalar", b_wx_s)

        def emit_ln_b():
            def s_sq(sc):
                sc.activation(sq_view, x_slab[:, :, :].rearrange("p c f -> p (c f)"),
                              AF.Square).then_inc(sem["ss"], 1)
                cnt["ss"] += 1
            on("scalar", s_sq)
            tok_sq = cnt["ss"]

            def v_stats(v, tok_sq=tok_sq):
                for c in range(4):
                    v.tensor_reduce(stA[:, c:c + 1], x_slab[:, c, :], axis=AX.X, op=OP.add)
                v.wait_ge(sem["ss"], tok_sq)
                for c in range(4):
                    v.tensor_reduce(stA[:, 4 + c:5 + c], sq_view[:, c * D:(c + 1) * D],
                                    axis=AX.X, op=OP.add)
                v.drain()
                v.tensor_scalar(stB[:, 0:4], stA[:, 0:4], 1.0 / D, None, op0=OP.mult)
                v.tensor_scalar(stB[:, 4:8], stA[:, 4:8], 1.0 / D, None, op0=OP.mult)
                v.drain()
                v.tensor_tensor(stC[:, 0:4], stB[:, 0:4], stB[:, 0:4], op=OP.mult)
                v.drain()
                v.tensor_tensor(stA[:, 8:12], stB[:, 4:8], stC[:, 0:4], op=OP.subtract)
                v.drain()
                v.tensor_scalar(stB[:, 8:12], stA[:, 8:12], 1e-5, None, op0=OP.add)
                v.drain()
                v.reciprocal(stC[:, 8:12], stB[:, 8:12])
                v.drain()
                v.engine_nop().then_inc(sem["sv"], 1)
                cnt["sv"] += 1
            on("vector", v_stats)
            tok_iv = cnt["sv"]

            def s_sqrt(sc, tok_iv=tok_iv):
                sc.wait_ge(sem["sv"], tok_iv)
                sc.activation(stA[:, 12:16], stC[:, 8:12], AF.Sqrt).then_inc(sem["ss"], 1)
                cnt["ss"] += 1
            on("scalar", s_sqrt)
            tok_r0 = cnt["ss"]

            def v_newton(v, tok_r0=tok_r0):
                v.wait_ge(sem["ss"], tok_r0)
                v.tensor_tensor(stC[:, 12:16], stA[:, 12:16], stA[:, 12:16], op=OP.mult)
                v.drain()
                v.tensor_tensor(stD[:, 0:4], stB[:, 8:12], stC[:, 12:16], op=OP.mult)
                v.drain()
                v.tensor_scalar(stC[:, 0:4], stD[:, 0:4], -0.5, 1.5, op0=OP.mult, op1=OP.add)
                v.drain()
                v.tensor_tensor(stD[:, 8:12], stA[:, 12:16], stC[:, 0:4], op=OP.mult)
                v.drain()
                v.drain()
                v.engine_nop().then_inc(sem["sv"], 1)
                cnt["sv"] += 1
            on("vector", v_newton)
            return cnt["sv"]

        tok_rstd = emit_ln_b()

        smm_xq0 = cnt["smm"]
        ss_xq0 = cnt["ss"]
        for c in range(4):
            par = c % 2

            def b_xq_v(v, c=c, par=par, tr=tok_rstd, smm_xq0=smm_xq0):
                if c == 0:
                    v.wait_ge(sem["sv"], tr)
                if c >= 1:
                    v.wait_ge(sem["smm"], smm_xq0 + c * 8)
                v.tensor_scalar(xn32[:, :], x_slab[:, c, :], stB[:, c:c + 1], stD[:, 8 + c:9 + c],
                                op0=OP.subtract, op1=OP.mult)
                v.drain()
                emit_bmax(v, xn32[:, :], D, bmx[:, 0:64], bsc[:, 0:64], brr[:, 0:64])
                emit_quant_f16(v, xn32[:, :], D, xq_tok[par][:, :], bsc[:, 0:64], brr[:, 0:64])
                v.drain()
                v.drain()
                v.engine_nop().then_inc(sem["sv"], 1)
                cnt["sv"] += 1
            on("vector", b_xq_v)
            tok_xq = cnt["sv"]

            def b_xq_t(t, c=c, par=par, tok_xq=tok_xq, ss_xq0=ss_xq0, m=m):
                t.wait_ge(sem["sv"], tok_xq)
                if m == 0 and c == 0:
                    t.wait_ge(sem["sv"], tok_attc)
                for j in range(8):
                    it = c * 8 + j
                    bA = pb[it % 2]
                    if it >= 2:
                        t.wait_ge(sem["ss"], ss_xq0 + (it - 1))
                    t.matmul(bA[0:128, 0:128], xq_tok[par][:, j * 128:(j + 1) * 128], identh[:, :],
                             start=True, stop=True).then_inc(sem["smm"], 1)
                    cnt["smm"] += 1
            on("tensor", b_xq_t)

            def b_xq_s(s, c=c, smm_xq0=smm_xq0):
                for j in range(8):
                    it = c * 8 + j
                    bA = pb[it % 2]
                    s.wait_ge(sem["smm"], smm_xq0 + it + 1)
                    s.copy(xq_feat[:, j, c * 128:(c + 1) * 128], bA[0:128, 0:128]).then_inc(sem["ss"], 1)
                    cnt["ss"] += 1
            on("scalar", b_xq_s)
        tok_xqf = cnt["ss"]

        if DBG:
            def dump_xq(sy, m=m, t=tok_xqf):
                sy.wait_ge(sem["ss"], t)
                sy.dma_start(out=dbg_xqall[m], in_=xq_feat[:, :, :]).then_inc(sem["sgp"], 16)
                cnt["sgp"] += 16
            on("sync", dump_xq)

        for half in range(2):
            for ig in range(NIG):
                par = ig % 2

                def b_wgu_load(sync, ig=ig, par=par, w=tok_wgu_free[par],
                               first=(m == 0 and half == 0 and ig < 2)):
                    if first:
                        sync.wait_ge(sem["sca"], SCC_AG)
                    if w:
                        sync.wait_ge(sem["smm"], w)
                    sn = "swga" if par == 0 else "swgb"
                    sync.dma_start(out=wgu_sb[:, par, :, :],
                                   in_=ag_wgu_out.ap()[:, :, ig * 512:(ig + 1) * 512]
                                   .rearrange("c p f -> p c f")).then_inc(sem[sn], 16)
                    cnt[sn] += 16
                on("sync", b_wgu_load)
                tok_wgu = cnt["swga" if par == 0 else "swgb"]
                sem_wgu = "swga" if par == 0 else "swgb"

                for tl in range(2):
                    tcg = half * 2 + tl
                    bank = pb[2 + tl]

                    def b_gu_t(t, par=par, tcg=tcg, bank=bank, tok_wgu=tok_wgu,
                               tok_xqf=tok_xqf, need_free=cnt["sv"], sem_wgu=sem_wgu,
                               first=(ig == 0 and half == 0)):
                        t.wait_ge(sem[sem_wgu], tok_wgu)
                        if first:
                            t.wait_ge(sem["ss"], tok_xqf)
                        if need_free:
                            t.wait_ge(sem["sv"], need_free)
                        for fg in range(2):
                            for kc in range(8):
                                ins = t.matmul(bank[:, fg * 256:(fg + 1) * 256],
                                               xq_feat[:, kc, tcg * 128:(tcg + 1) * 128],
                                               wgu_sb[:, par, kc, fg * 256:(fg + 1) * 256],
                                               start=(kc == 0), stop=(kc == 7))
                        ins.then_inc(sem["smm"], 1)
                        cnt["smm"] += 1
                    on("tensor", b_gu_t)
                    tok_gu = cnt["smm"]

                    def b_silu_s(s, bank=bank, tl=tl, tok_gu=tok_gu, w=tok_sgt_used[tl]):
                        s.wait_ge(sem["smm"], tok_gu)
                        if w:
                            s.wait_ge(sem["sv"], w)
                        s.activation(sgt[tl][:, :], bank[:, 0:256], AF.Silu)
                        s.drain()
                        s.nop().then_inc(sem["ss"], 1)
                        cnt["ss"] += 1
                    on("scalar", b_silu_s)
                    tok_silu = cnt["ss"]

                    def b_h_v(v, bank=bank, tl=tl, ig=ig, tok_silu=tok_silu,
                              cap=(DBG and m == 7 and half == 1 and ig == 4 and tl == 0)):
                        v.wait_ge(sem["ss"], tok_silu)
                        if cap:
                            v.tensor_copy(dbg_cap_gup[:, :], bank[:, 0:512])
                            v.tensor_copy(dbg_cap_sgt[:, :], sgt[tl][:, :])
                        v.tensor_tensor(h_tok[:, tl, ig * 256:(ig + 1) * 256], sgt[tl][:, :],
                                        bank[:, 256:512], op=OP.mult)
                        v.drain()
                        v.engine_nop().then_inc(sem["sv"], 1)
                        cnt["sv"] += 1
                    on("vector", b_h_v)
                    tok_sgt_used[tl] = cnt["sv"]
                tok_wgu_free[par] = cnt["smm"]
            tok_h_done = cnt["sv"]

            for tl in range(2):
                tcg = half * 2 + tl
                for hf in range(2):
                    src = h_tok[:, tl, hf * 1408:(hf + 1) * 1408]
                    par = (tcg * 2 + hf) % 2

                    def b_hq_v(v, src=src, par=par, tok_h_done=tok_h_done, need=cnt["smm"]):
                        v.wait_ge(sem["sv"], tok_h_done)
                        if need:
                            v.wait_ge(sem["smm"], need)
                        emit_bmax(v, src, 1408, bmx[:, 0:88], bsc[:, 0:88], brr[:, 0:88])
                        emit_quant_f16(v, src, 1408, hqx[par][:, :], bsc[:, 0:88], brr[:, 0:88])
                        v.drain()
                        v.engine_nop().then_inc(sem["sv"], 1)
                        cnt["sv"] += 1
                    on("vector", b_hq_v)
                    tok_hq = cnt["sv"]

                    smm_ht0 = cnt["smm"]
                    ss_ht0 = cnt["ss"]

                    def b_hq_t(t, par=par, tok_hq=tok_hq, ss_ht0=ss_ht0):
                        t.wait_ge(sem["sv"], tok_hq)
                        t.wait_ge(sem["ss"], ss_ht0)
                        for j in range(11):
                            bA = pb[4 + j % 2]
                            if j >= 2:
                                t.wait_ge(sem["ss"], ss_ht0 + (j - 1))
                            t.matmul(bA[0:128, 0:128], hqx[par][:, j * 128:(j + 1) * 128],
                                     identh[:, :], start=True, stop=True).then_inc(sem["smm"], 1)
                            cnt["smm"] += 1
                    on("tensor", b_hq_t)

                    def b_hq_s(s, tcg=tcg, hf=hf, smm_ht0=smm_ht0):
                        for j in range(11):
                            bA = pb[4 + j % 2]
                            s.wait_ge(sem["smm"], smm_ht0 + j + 1)
                            s.copy(hqT[:, hf * 11 + j, tcg * 128:(tcg + 1) * 128],
                                   bA[0:128, 0:128]).then_inc(sem["ss"], 1)
                            cnt["ss"] += 1
                    on("scalar", b_hq_s)
        tok_hqT_done = cnt["ss"]


        def b_norm_t(t, m=m, prev=cnt["ss"]):
            if m == 0:
                t.wait_ge(sem["sv"], tok_attc)
            t.wait_ge(sem["ss"], prev)
            for p in range(8):
                ins = t.matmul(pb[6][0:16, 0:512], ksb8[:, p, :], q_res[:, p, m * MT:(m + 1) * MT],
                               start=(p == 0), stop=(p == 7))
            ins.then_inc(sem["smm"], 1)
            cnt["smm"] += 1
        on("tensor", b_norm_t)
        tok_norm = cnt["smm"]

        def b_nrec_v(v, tok_norm=tok_norm):
            v.wait_ge(sem["smm"], tok_norm)
            v.tensor_scalar(nplus[:, :], pb[6][0:16, 0:512], 1e-6, None, op0=OP.add)
            v.drain()
            v.reciprocal(nrec[:, :], nplus[:, :])
            v.drain()
            v.tensor_copy(nrec16[:, :], nrec[:, :])
            v.drain()
            v.engine_nop().then_inc(sem["sv"], 1)
            cnt["sv"] += 1
        on("vector", b_nrec_v)
        tok_nrec = cnt["sv"]

        ss_rb0 = cnt["ss"]
        sv_at0 = cnt["sv"]
        for p in range(8):
            par = p % 2

            def b_rb_t(t, p=p, par=par, tok_nrec=tok_nrec, ss_rb0=ss_rb0, sv_at0=sv_at0, m=m):
                if p == 0:
                    t.wait_ge(sem["sv"], tok_nrec)
                if p >= 2:
                    t.wait_ge(sem["ss"], ss_rb0 + (p - 1))
                    t.wait_ge(sem["sv"], sv_at0 + (p - 1))
                t.matmul(pb[6 + par][:, 0:512], selb[:, p * 128:(p + 1) * 128], nrec16[:, :],
                         start=True, stop=True).then_inc(sem["smm"], 1)
                cnt["smm"] += 1
                t.matmul(pb[2 + par][:, 0:512], bdkv[:, p, :], q_res[:, p, m * MT:(m + 1) * MT],
                         start=True, stop=True).then_inc(sem["smm"], 1)
                cnt["smm"] += 1
            on("tensor", b_rb_t)
            tok_rbmm = cnt["smm"]

            def b_rb_s(s, par=par, tok_rbmm=tok_rbmm):
                s.wait_ge(sem["smm"], tok_rbmm - 1)
                s.copy(rbx[par][:, :], pb[6 + par][:, 0:512])
                s.drain()
                s.nop().then_inc(sem["ss"], 1)
                cnt["ss"] += 1
            on("scalar", b_rb_s)
            tok_rbs = cnt["ss"]

            def b_at_v(v, p=p, par=par, tok_rbmm=tok_rbmm, tok_rbs=tok_rbs):
                v.wait_ge(sem["smm"], tok_rbmm)
                v.wait_ge(sem["ss"], tok_rbs)
                v.tensor_tensor(attn2[:, p, :], pb[2 + par][:, 0:512], rbx[par][:, :], op=OP.mult)
                v.drain()
                v.drain()
                v.engine_nop().then_inc(sem["sv"], 1)
                cnt["sv"] += 1
            on("vector", b_at_v)
        tok_attn2 = cnt["sv"]

        if DBG:
            def dump_at2(sy, m=m, t=tok_attn2):
                sy.wait_ge(sem["sv"], t)
                sy.dma_start(out=dbg_at2all[m], in_=attn2[:, :, :]).then_inc(sem["sgp"], 16)
                cnt["sgp"] += 16
            on("sync", dump_at2)

        for ic in range(ICN):
            par = ic % 2

            def b_wd_l(sync, ic=ic, par=par, w=wd_used_tok[par], m=m):
                if m == 0 and ic == 0:
                    sync.wait_ge(sem["sca"], SCC_AG)
                if w:
                    sync.wait_ge(sem["smm"], w)
                sn = "swda" if par == 0 else "swdb"
                sync.dma_start(out=wd_ic[par][:, :],
                               in_=ag_wd_out.ap()[ic % 8, ic // 8, :, :]).then_inc(sem[sn], 16)
                cnt[sn] += 16
            on("sync", b_wd_l)
            wd_tok = cnt["swda" if par == 0 else "swdb"]
            sem_wd = "swda" if par == 0 else "swdb"

            def b_wd_t(t, ic=ic, par=par, wt=wd_tok, ta=tok_attn2, tok_hqT_done=tok_hqT_done,
                       sem_wd=sem_wd):
                t.wait_ge(sem[sem_wd], wt)
                if ic == 0:
                    t.wait_ge(sem["sv"], ta)
                    t.wait_ge(sem["ss"], tok_hqT_done)
                for tcg in range(4):
                    for fg in range(2):
                        ins = t.matmul(pb[tcg * 2 + fg][:, 0:512],
                                       hqT[:, ic, tcg * 128:(tcg + 1) * 128],
                                       wd_ic[par][:, fg * 512:(fg + 1) * 512],
                                       start=(ic == 0), stop=False)
                ins.then_inc(sem["smm"], 1)
                cnt["smm"] += 1
            on("tensor", b_wd_t)
            wd_used_tok[par] = cnt["smm"]

        for p in range(8):
            par = p % 2

            def b_wo_l(sync, p=p, par=par, w=wo_used[par], first=(m == 0 and p < 2)):
                if first:
                    sync.wait_ge(sem["sx"], WC_SX)
                if w:
                    sync.wait_ge(sem["smm"], w)
                sn = "swoa" if par == 0 else "swob"
                sync.dma_start(out=wout_p[par][:, :],
                               in_=wout16.ap()[p * 128:(p + 1) * 128, :]).then_inc(sem[sn], 16)
                cnt[sn] += 16
            on("sync", b_wo_l)
            wo_tok = cnt["swoa" if par == 0 else "swob"]
            sem_wo = "swoa" if par == 0 else "swob"

            def b_wo_t(t, p=p, par=par, wt=wo_tok, sem_wo=sem_wo):
                t.wait_ge(sem[sem_wo], wt)
                for tcg in range(4):
                    for fg in range(2):
                        ins = t.matmul(pb[tcg * 2 + fg][:, 0:512],
                                       attn2[:, p, tcg * 128:(tcg + 1) * 128],
                                       wout_p[par][:, fg * 512:(fg + 1) * 512],
                                       start=False, stop=False)
                ins.then_inc(sem["smm"], 1)
                cnt["smm"] += 1
            on("tensor", b_wo_t)
            wo_used[par] = cnt["smm"]

        out_mm_tok = []

        def b_bias_t(t):
            for tcg in range(4):
                for fg in range(2):
                    t.matmul(pb[tcg * 2 + fg][:, 0:512], ones1[0:1, :],
                             boutr[0:1, fg * 512:(fg + 1) * 512],
                             start=False, stop=True).then_inc(sem["smm"], 1)
                    cnt["smm"] += 1
                    out_mm_tok.append(cnt["smm"])
        on("tensor", b_bias_t)

        for tcg in range(4):
            par = tcg % 2

            def b_oev_v(v, tcg=tcg, par=par, w=tok_outsb_free[par],
                        tokk=out_mm_tok[tcg * 2 + 1]):
                v.wait_ge(sem["smm"], tokk)
                if w:
                    v.wait_ge(sem["ssta" if par == 0 else "sstb"], w)
                for fg in range(2):
                    v.tensor_tensor(out_sb[par][:, fg * 512:(fg + 1) * 512],
                                    pb[tcg * 2 + fg][:, 0:512],
                                    x_slab[:, tcg, fg * 512:(fg + 1) * 512], op=OP.add)
                v.drain()
                v.drain()
                v.engine_nop().then_inc(sem["sv"], 1)
                cnt["sv"] += 1
            on("vector", b_oev_v)
            tok_oev = cnt["sv"]

            def b_store(sync, tcg=tcg, par=par, m=m, tok_oev=tok_oev):
                sync.wait_ge(sem["sv"], tok_oev)
                sn = "ssta" if par == 0 else "sstb"
                sync.dma_start(out=out_d[m * MT + tcg * 128: m * MT + (tcg + 1) * 128, :],
                               in_=out_sb[par][:, :]).then_inc(sem[sn], 16)
                cnt[sn] += 16
            on("sync", b_store)
            tok_outsb_free[par] = cnt["ssta" if par == 0 else "sstb"]
        tok_x_free_b = cnt["sv"]
        tok_hqT_free = cnt["smm"]

    SSTA_TOT = cnt["ssta"]
    SSTB_TOT = cnt["sstb"]

    if DBG:
        def dump(g):
            g.wait_ge(sem["ssta"], SSTA_TOT)
            g.wait_ge(sem["sstb"], SSTB_TOT)
            g.dma_start(out=dbg_agwgu, in_=ag_wgu_out.ap()[:, :, :]).then_inc(sem["sgp"], 16)
            cnt["sgp"] += 16
            g.dma_start(out=dbg_wgusb, in_=wgu_sb[:, :, :, :]).then_inc(sem["sgp"], 16)
            cnt["sgp"] += 16
            g.dma_start(out=dbg_gup, in_=dbg_cap_gup[:, :]).then_inc(sem["sgp"], 16)
            cnt["sgp"] += 16
            g.dma_start(out=dbg_sgt, in_=dbg_cap_sgt[:, :]).then_inc(sem["sgp"], 16)
            cnt["sgp"] += 16
            for dst, srct in [(dbg_qres, q_res), (dbg_bdkv, bdkv), (dbg_ksb8, ksb8),
                              (dbg_xqf, xq_feat), (dbg_hqT, hqT), (dbg_attn2, attn2),
                              (dbg_htok, h_tok), (dbg_nrec, nrec)]:
                g.dma_start(out=dst, in_=srct[:, :] if len(dst.shape) == 2 else srct[:, :, :]).then_inc(sem["sgp"], 16)
                cnt["sgp"] += 16
            g.dma_start(out=dbg_st[:, 0:16], in_=stA[:, :]).then_inc(sem["sgp"], 16)
            g.dma_start(out=dbg_st[:, 16:32], in_=stB[:, :]).then_inc(sem["sgp"], 16)
            g.dma_start(out=dbg_st[:, 32:48], in_=stC[:, :]).then_inc(sem["sgp"], 16)
            g.dma_start(out=dbg_st[:, 48:64], in_=stD[:, :]).then_inc(sem["sgp"], 16)
            cnt["sgp"] += 64
            g.wait_ge(sem["sgp"], cnt["sgp"])
        on("gpsimd", dump)

    def fin(sync):
        sync.wait_ge(sem["ssta"], SSTA_TOT)
        sync.wait_ge(sem["sstb"], SSTB_TOT)
        sync.wait_ge(sem["scc"], cnt["scc"])
        sync.wait_ge(sem["sca"], cnt["sca"])
        if DBG:
            sync.wait_ge(sem["sgp"], cnt["sgp"])
    on("sync", fin)

    es.close()
    return nc


_NC_CACHE = {}


def kernel(L0_attn_x, L0_attn_W_qkv, L0_attn_W_f, L0_attn_b_f, L0_attn_W_out,
           L0_attn_b_out, L0_ffn_w_gate, L0_ffn_w_up, L0_ffn_w_down):
    x = np.ascontiguousarray(np.asarray(L0_attn_x, dtype=np.float32)).reshape(NCORE, TCORE, D)
    wqkv = np.ascontiguousarray(np.asarray(L0_attn_W_qkv, dtype=np.float32))
    wf = np.ascontiguousarray(np.asarray(L0_attn_W_f, dtype=np.float32))
    b_f = np.asarray(L0_attn_b_f, dtype=np.float32)
    wout = np.ascontiguousarray(np.asarray(L0_attn_W_out, dtype=np.float32))
    b_out = np.asarray(L0_attn_b_out, dtype=np.float32)
    wg = np.ascontiguousarray(np.asarray(L0_ffn_w_gate, dtype=np.float32))
    wu = np.ascontiguousarray(np.asarray(L0_ffn_w_up, dtype=np.float32))
    wd = np.ascontiguousarray(np.asarray(L0_ffn_w_down, dtype=np.float32))

    bft = np.tile(b_f, 16)
    bqc = np.ascontiguousarray(bft.reshape(8, 128).T)
    bkvr = np.ascontiguousarray(np.concatenate([bft, np.zeros(D, np.float32)])[None, :])
    boutr = np.ascontiguousarray(b_out[None, :])
    idb = np.eye(128, dtype=np.float32).astype(bf16)
    idh = np.eye(128, dtype=np.float32).astype(np.float16)
    ones1 = np.ones((1, 128), np.float32).astype(bf16)
    onesc = np.ones((128, 8), np.float32).astype(bf16)
    selb = np.zeros((16, 8 * 128), np.float32)
    for p in range(8):
        selb[2 * p, p * 128: p * 128 + 64] = 1.0
        selb[2 * p + 1, p * 128 + 64: p * 128 + 128] = 1.0
    selb = selb.astype(bf16)

    in_maps = []
    for c in range(NCORE):
        wd_ch = []
        for s in range(WDCH):
            ic = s * 8 + c
            if ic < ICN:
                wd_ch.append(wd[:, ic * 128:(ic + 1) * 128])
            else:
                wd_ch.append(np.zeros((D, 128), np.float32))
        in_maps.append({
            "x": np.ascontiguousarray(x[c]),
            "wqkv": wqkv, "wf": wf, "wout": wout,
            "bqc": bqc, "bkvr": bkvr, "boutr": boutr,
            "wg_sl": np.ascontiguousarray(wg[:, c * 128:(c + 1) * 128]),
            "wu_sl": np.ascontiguousarray(wu[:, c * 128:(c + 1) * 128]),
            "wd_sl": np.ascontiguousarray(np.stack(wd_ch, 0)),
            "idb": idb, "idh": idh, "ones1": ones1, "onesc": onesc, "selb": selb,
        })

    if "prog" not in _NC_CACHE:
        _NC_CACHE["prog"] = build_nc()
    nc = _NC_CACHE["prog"]
    res = run_bass_kernel_spmd(nc, in_maps, core_ids=list(range(NCORE)))
    import os as _os
    if int(_os.environ.get("KDBG", "0")):
        kernel.DBG_RES = res
    out = np.concatenate([res.results[c]["out"][None] for c in range(NCORE)], 0)
    return np.ascontiguousarray(out.reshape(2, 16384, D))


# version 09
# speedup vs baseline: 1.0085x; 1.0082x over previous
import numpy as np
import ml_dtypes
import concourse.bass as bass
import concourse.mybir as mybir
from concourse.bass_utils import run_bass_kernel_spmd
from contextlib import ExitStack

F32 = mybir.dt.float32
F16 = mybir.dt.float16
BF16 = mybir.dt.bfloat16
I16 = mybir.dt.int16
I32 = mybir.dt.int32
OP = mybir.AluOpType
AF = mybir.ActivationFunctionType
AX = mybir.AxisListType
bf16 = ml_dtypes.bfloat16

NCORE = 8
TCORE = 4096
MT = 512
NMT = TCORE // MT
D = 1024
FI = 2816
ICN = 22
NIG = 11
WDCH = 3
MIDS = [0.25, 0.75, 1.25, 1.75, 2.5, 3.5, 5.0]


def build_nc():
    nc = bass.Bass("TRN2", target_bir_lowering=False)
    x_in = nc.dram_tensor("x", [TCORE, D], F32, kind="ExternalInput").ap()
    wqkv_in = nc.dram_tensor("wqkv", [D, 3 * D], F32, kind="ExternalInput").ap()
    wf_in = nc.dram_tensor("wf", [64, 64], F32, kind="ExternalInput").ap()
    wout_in = nc.dram_tensor("wout", [D, D], F32, kind="ExternalInput").ap()
    bqc_in = nc.dram_tensor("bqc", [128, 8], F32, kind="ExternalInput").ap()
    bkvr_in = nc.dram_tensor("bkvr", [1, 2 * D], F32, kind="ExternalInput").ap()
    bout_in = nc.dram_tensor("boutr", [1, D], F32, kind="ExternalInput").ap()
    wg_sl_in = nc.dram_tensor("wg_sl", [FI, 128], F32, kind="ExternalInput").ap()
    wu_sl_in = nc.dram_tensor("wu_sl", [FI, 128], F32, kind="ExternalInput").ap()
    wd_sl_in = nc.dram_tensor("wd_sl", [WDCH, D, 128], F32, kind="ExternalInput").ap()
    idb_in = nc.dram_tensor("idb", [128, 128], BF16, kind="ExternalInput").ap()
    idh_in = nc.dram_tensor("idh", [128, 128], F16, kind="ExternalInput").ap()
    ones1_in = nc.dram_tensor("ones1", [1, 128], BF16, kind="ExternalInput").ap()
    onesc_in = nc.dram_tensor("onesc", [128, 8], BF16, kind="ExternalInput").ap()
    selb_in = nc.dram_tensor("selb", [16, 8 * 128], BF16, kind="ExternalInput").ap()
    out_d = nc.dram_tensor("out", [TCORE, D], F32, kind="ExternalOutput").ap()
    import os as _os
    DBG = bool(int(_os.environ.get("KDBG", "0")))
    if DBG:
        dbg_qres = nc.dram_tensor("dbg_qres", [128, 8, TCORE], F32, kind="ExternalOutput").ap()
        dbg_bdkv = nc.dram_tensor("dbg_bdkv", [128, 8, 128], F32, kind="ExternalOutput").ap()
        dbg_ksb8 = nc.dram_tensor("dbg_ksb8", [128, 8, 16], F32, kind="ExternalOutput").ap()
        dbg_xqf = nc.dram_tensor("dbg_xqf", [128, 8, MT], F32, kind="ExternalOutput").ap()
        dbg_hqT = nc.dram_tensor("dbg_hqT", [128, ICN, MT], F32, kind="ExternalOutput").ap()
        dbg_attn2 = nc.dram_tensor("dbg_attn2", [128, 8, MT], F32, kind="ExternalOutput").ap()
        dbg_htok = nc.dram_tensor("dbg_htok", [128, 2, FI], F32, kind="ExternalOutput").ap()
        dbg_nrec = nc.dram_tensor("dbg_nrec", [16, 512], F32, kind="ExternalOutput").ap()
        dbg_st = nc.dram_tensor("dbg_st", [128, 64], F32, kind="ExternalOutput").ap()
        dbg_watt = nc.dram_tensor("dbg_watt", [128, 8, 3 * D], F32, kind="ExternalOutput").ap()
        dbg_xnf = nc.dram_tensor("dbg_xnf", [128, 8, MT], F32, kind="ExternalOutput").ap()
        dbg_xnb = nc.dram_tensor("dbg_xnb", [128, 4, D], F32, kind="ExternalOutput").ap()
        dbg_kvall = nc.dram_tensor("dbg_kvall", [128, 1032], F32, kind="ExternalOutput").ap()
        dbg_xqall = nc.dram_tensor("dbg_xqall", [NMT, 128, 8, MT], F16, kind="ExternalOutput").ap()
        dbg_hqTall = nc.dram_tensor("dbg_hqTall", [NMT, 128, ICN, MT], F16, kind="ExternalOutput").ap()
        dbg_at2all = nc.dram_tensor("dbg_at2all", [NMT, 128, 8, MT], BF16, kind="ExternalOutput").ap()
        dbg_agwgu = nc.dram_tensor("dbg_agwgu", [8, 128, 2 * FI], F16, kind="ExternalOutput").ap()
        dbg_wgusb = nc.dram_tensor("dbg_wgusb", [128, 2, 8, 512], F16, kind="ExternalOutput").ap()
        dbg_gup = nc.dram_tensor("dbg_gup", [128, 512], F16, kind="ExternalOutput").ap()
        dbg_sgt = nc.dram_tensor("dbg_sgt", [128, 256], F16, kind="ExternalOutput").ap()
    ag_wgu_in = nc.dram_tensor("ag_wgu_in", [128, 2 * FI], F16)
    ag_wgu_out = nc.dram_tensor("ag_wgu_out", [8, 128, 2 * FI], F16)
    ag_wd_in = nc.dram_tensor("ag_wd_in", [WDCH, 128, D], F16)
    ag_wd_out = nc.dram_tensor("ag_wd_out", [8, WDCH, 128, D], F16)
    ar_in = nc.dram_tensor("ar_in", [128, 1032], F32)
    ar_out = nc.dram_tensor("ar_out", [128, 1032], F32)
    wout16 = nc.dram_tensor("wout16", [D, D], BF16)

    es = ExitStack()
    SB = lambda name, shape, dt: es.enter_context(nc.sbuf_tensor(name, shape, dt))
    pb = [es.enter_context(nc.psum_tensor(f"pb{i}", [128, 512], F32)) for i in range(8)]
    sem_names = ["sx", "sw", "sg1", "smm", "sv", "ss", "sgp", "scc", "swc", "sca",
                 "sgpa", "sgpb", "swga", "swgb", "swda", "swdb", "swoa", "swob", "ssta", "sstb"]
    sem = {n: es.enter_context(nc.semaphore(n)) for n in sem_names}
    cnt = {n: 0 for n in sem_names}
    block = es.enter_context(nc.Block())

    def on(eng, fn):
        getattr(block, eng)(fn)

    identb = SB("identb", [128, 128], BF16)
    identh = SB("identh", [128, 128], F16)
    ones1 = SB("ones1s", [1, 128], BF16)
    onesc = SB("onescs", [128, 8], BF16)
    selb = SB("selbs", [16, 8 * 128], BF16)
    bqc = SB("bqcs", [128, 8], F32)
    bkvr = SB("bkvrs", [1, 2 * D], BF16)
    boutr = SB("boutrs", [1, D], BF16)
    q_res = SB("q_res", [128, 8, TCORE], BF16)
    x_slab = SB("x_slab", [128, 4, D], F32)
    stA = SB("stA", [128, 16], F32)
    stB = SB("stB", [128, 16], F32)
    stC = SB("stC", [128, 16], F32)
    stD = SB("stD", [128, 16], F32)
    qv = SB("qv", [128, 1408], F32)
    ql = SB("ql", [128, 1408], F16)
    qh = SB("qh", [128, 1408], F16)
    qc7 = SB("qc7", [128, 1408], F16)
    qsm = SB("qsm", [128, 1408], F16)
    bmx = SB("bmx", [128, 88], F32)
    bsc = SB("bsc", [128, 88], F32)
    brr = SB("brr", [128, 88], F32)
    bdkv = SB("bdkv", [128, 8, 128], BF16)
    ksb8 = SB("ksb8", [128, 8, 16], BF16)

    def p0_sync(sync):
        for dst, src in [(identb, idb_in), (identh, idh_in), (ones1, ones1_in),
                         (onesc, onesc_in), (selb, selb_in), (bqc, bqc_in)]:
            sync.dma_start(out=dst[:, :], in_=src[:, :]).then_inc(sem["sw"], 16)
            cnt["sw"] += 16
    on("sync", p0_sync)
    SW_CONST = cnt["sw"]

    with nc.sbuf_tensor("W_att", [128, 8, 3 * D], BF16) as W_att, \
         nc.sbuf_tensor("ln_sq", [128, 4 * D], F32) as ln_sq:

        with nc.sbuf_tensor("wqkv_sb", [128, 8, 2 * D], BF16) as wqkv_sb, \
             nc.sbuf_tensor("wf_sb", [64, 64], BF16) as wf_sb, \
             nc.sbuf_tensor("wqt0", [64, 128], BF16) as wqt0, \
             nc.sbuf_tensor("wqt1", [64, 128], BF16) as wqt1, \
             nc.sbuf_tensor("wc16", [128, 1, D], BF16) as wc16:
            wqt = [wqt0, wqt1]

            def p1_gp(g):
                g.dma_start(out=wqkv_sb[:, :, :],
                            in_=wqkv_in[:, 0:2 * D].rearrange("(c p) f -> p c f", p=128)).then_inc(sem["sg1"], 16)
                g.dma_start(out=W_att[:, :, 2 * D:3 * D],
                            in_=wqkv_in[:, 2 * D:3 * D].rearrange("(c p) f -> p c f", p=128)).then_inc(sem["sg1"], 16)
                g.dma_start(out=wf_sb[:, :], in_=wf_in[:, :]).then_inc(sem["sg1"], 16)
                g.dma_start(out=bkvr[:, :], in_=bkvr_in[:, :]).then_inc(sem["sg1"], 16)
                g.dma_start(out=boutr[:, :], in_=bout_in[:, :]).then_inc(sem["sg1"], 16)
                cnt["sg1"] += 80
            on("gpsimd", p1_gp)
            SG1_W = cnt["sg1"]
            SV_VCOPY = cnt["sv"]

            sx_wc0 = cnt["sx"]

            def p1_wc_g(g):
                for p in range(8):
                    if p >= 1:
                        g.wait_ge(sem["sx"], sx_wc0 + 16 * p)
                    g.dma_start(out=wc16[:, 0, :],
                                in_=wout_in[p * 128:(p + 1) * 128, :]).then_inc(sem["swc"], 16)
                    cnt["swc"] += 16
            on("gpsimd", p1_wc_g)

            def p1_wc_s(sync):
                for p in range(8):
                    sync.wait_ge(sem["swc"], 16 * (p + 1))
                    sync.dma_start(out=wout16.ap()[p * 128:(p + 1) * 128, :],
                                   in_=wc16[:, 0, :]).then_inc(sem["sx"], 16)
                    cnt["sx"] += 16
            on("sync", p1_wc_s)
            WC_SX = cnt["sx"]

            ss_base = cnt["ss"]
            smm_base = cnt["smm"]

            def p1_t(t):
                t.wait_ge(sem["sg1"], SG1_W)
                t.wait_ge(sem["sw"], SW_CONST)
                it = 0
                for c in range(8):
                    for qk in range(2):
                        for h in range(16):
                            bA = pb[it % 2]
                            bB = pb[2 + it % 2]
                            if it >= 2:
                                t.wait_ge(sem["ss"], ss_base + 2 * (it - 1))
                            src = wqkv_sb[:, c, qk * D + h * 64: qk * D + (h + 1) * 64]
                            t.matmul(bA[0:64, 0:128], src, identb[:, :],
                                     start=True, stop=True).then_inc(sem["smm"], 1)
                            t.wait_ge(sem["ss"], ss_base + 2 * it + 1)
                            t.matmul(bB[0:128, 0:64], wqt[it % 2][0:64, :], wf_sb[:, :],
                                     start=True, stop=True).then_inc(sem["smm"], 1)
                            it += 1
                cnt["smm"] += 512
            on("tensor", p1_t)

            def p1_s(s):
                it = 0
                for c in range(8):
                    for qk in range(2):
                        for h in range(16):
                            bA = pb[it % 2]
                            bB = pb[2 + it % 2]
                            s.wait_ge(sem["smm"], smm_base + 2 * it + 1)
                            s.copy(wqt[it % 2][0:64, :], bA[0:64, 0:128]).then_inc(sem["ss"], 1)
                            s.wait_ge(sem["smm"], smm_base + 2 * it + 2)
                            s.copy(W_att[:, c, qk * D + h * 64: qk * D + (h + 1) * 64],
                                   bB[0:128, 0:64]).then_inc(sem["ss"], 1)
                            it += 1
                cnt["ss"] += 512
            on("scalar", p1_s)
        SS_FOLD = cnt["ss"]
        SMM_FOLD_END = cnt["smm"]

        def emit_quant_f16(v, src2d, W, out2d, sc_ap, rr_ap):
            nb = W // 16
            v3 = qv[:, 0:W].rearrange("p (b k) -> p b k", k=16)
            v.tensor_tensor(v3, src2d.rearrange("p (b k) -> p b k", k=16),
                            rr_ap.unsqueeze(2).broadcast_to([128, nb, 16]), op=OP.mult)
            v.drain()
            v.tensor_scalar(qsm[:, 0:W], qv[:, 0:W], 0.0, None, op0=OP.is_lt)
            v.drain()
            v.tensor_scalar(qv[:, 0:W].bitcast(I32), qv[:, 0:W].bitcast(I32), 0x7FFFFFFF, None,
                            op0=OP.bitwise_and)
            v.drain()
            a = qv[:, 0:W]
            v.tensor_scalar(qsm[:, 0:W], qsm[:, 0:W], -2.0, 1.0, op0=OP.mult, op1=OP.add)
            v.tensor_scalar(ql[:, 0:W], a, MIDS[0], None, op0=OP.is_gt)
            v.drain()
            v.scalar_tensor_tensor(ql[:, 0:W], a, MIDS[1], ql[:, 0:W], op0=OP.is_gt, op1=OP.add)
            v.drain()
            v.scalar_tensor_tensor(ql[:, 0:W], a, MIDS[2], ql[:, 0:W], op0=OP.is_gt, op1=OP.add)
            v.drain()
            v.scalar_tensor_tensor(ql[:, 0:W], a, MIDS[3], ql[:, 0:W], op0=OP.is_gt, op1=OP.add)
            v.tensor_scalar(qh[:, 0:W], a, MIDS[4], None, op0=OP.is_gt)
            v.drain()
            v.scalar_tensor_tensor(qh[:, 0:W], a, MIDS[5], qh[:, 0:W], op0=OP.is_gt, op1=OP.add)
            v.tensor_scalar(qc7[:, 0:W], a, MIDS[6], 2.0, op0=OP.is_gt, op1=OP.mult)
            v.drain()
            v.scalar_tensor_tensor(ql[:, 0:W], ql[:, 0:W], 0.5, qh[:, 0:W], op0=OP.mult, op1=OP.add)
            v.drain()
            v.tensor_tensor(ql[:, 0:W], ql[:, 0:W], qc7[:, 0:W], op=OP.add)
            v.drain()
            v.tensor_tensor(qh[:, 0:W].rearrange("p (b k) -> p b k", k=16),
                            ql[:, 0:W].rearrange("p (b k) -> p b k", k=16),
                            sc_ap.unsqueeze(2).broadcast_to([128, nb, 16]), op=OP.mult)
            v.drain()
            v.tensor_tensor(out2d, qh[:, 0:W], qsm[:, 0:W], op=OP.mult)

        def emit_bmax(v, src2d, W, bm_ap, sc_ap, rr_ap):
            v.tensor_reduce(bm_ap, src2d.rearrange("p (b k) -> p b k", k=16),
                            axis=AX.X, op=OP.max, apply_absolute_value=True)
            v.drain()
            v.tensor_scalar(sc_ap, bm_ap, 1.0 / 6.0, 1e-12, op0=OP.mult, op1=OP.max)
            v.drain()
            v.reciprocal(rr_ap, sc_ap)
            v.drain()

        with nc.sbuf_tensor("wg_sb", [128, ICN, 128], F32) as wg_sb, \
             nc.sbuf_tensor("wu_sb", [128, ICN, 128], F32) as wu_sb, \
             nc.sbuf_tensor("wd_sb", [128, WDCH, 8, 128], F32) as wd_sb, \
             nc.sbuf_tensor("wq16a", [128, 128], F16) as wq16a, \
             nc.sbuf_tensor("wq16b", [128, 128], F16) as wq16b, \
             nc.sbuf_tensor("wt16a", [128, 128], F16) as wt16a, \
             nc.sbuf_tensor("wt16b", [128, 128], F16) as wt16b:
            wq16 = [wq16a, wq16b]
            wt16 = [wt16a, wt16b]

            def p2_sync(sync):
                sync.wait_ge(sem["smm"], SMM_FOLD_END)
                sync.wait_ge(sem["sx"], WC_SX)
                sync.dma_start(out=wg_sb[:, :, :],
                               in_=wg_sl_in[:, :].rearrange("(a p) k -> p a k", p=128)).then_inc(sem["sw"], 16)
                sync.dma_start(out=wu_sb[:, :, :],
                               in_=wu_sl_in[:, :].rearrange("(a p) k -> p a k", p=128)).then_inc(sem["sw"], 16)
                sync.dma_start(out=wd_sb[:, :, :, :],
                               in_=wd_sl_in[:, :, :].rearrange("j (a p) k -> p j a k", p=128)).then_inc(sem["sw"], 16)
                cnt["sw"] += 48
            on("sync", p2_sync)
            SW_WSH = cnt["sw"]

            tiles = []
            for j in range(ICN):
                col = (j // 2) * 512 + (j % 2) * 128
                tiles.append((wg_sb[:, j, :], ag_wgu_in.ap()[:, col:col + 128]))
            for j in range(ICN):
                col = (j // 2) * 512 + 256 + (j % 2) * 128
                tiles.append((wu_sb[:, j, :], ag_wgu_in.ap()[:, col:col + 128]))
            for jj in range(WDCH):
                for fc in range(8):
                    tiles.append((wd_sb[:, jj, fc, :], ag_wd_in.ap()[jj, :, fc * 128:(fc + 1) * 128]))
            NT = len(tiles)
            sv0, smm0, ss0, sgp0 = cnt["sv"], cnt["smm"], cnt["ss"], cnt["sgp"]

            def p2_v(v):
                v.wait_ge(sem["sw"], SW_WSH)
                for i, (srcap, dst) in enumerate(tiles):
                    if i >= 2:
                        v.wait_ge(sem["smm"], smm0 + (i - 1))
                    emit_bmax(v, srcap, 128, bmx[:, 0:8], bsc[:, 0:8], brr[:, 0:8])
                    emit_quant_f16(v, srcap, 128, wq16[i % 2][:, :], bsc[:, 0:8], brr[:, 0:8])
                    v.drain()
                    v.engine_nop().then_inc(sem["sv"], 1)
                    cnt["sv"] += 1
            on("vector", p2_v)

            def p2_t(t):
                for i in range(NT):
                    t.wait_ge(sem["sv"], sv0 + i + 1)
                    if i >= 2:
                        t.wait_ge(sem["ss"], ss0 + (i - 1))
                    t.matmul(pb[4 + i % 2][0:128, 0:128], wq16[i % 2][:, :], identh[:, :],
                             start=True, stop=True).then_inc(sem["smm"], 1)
                    cnt["smm"] += 1
            on("tensor", p2_t)

            def p2_s(s):
                for i in range(NT):
                    s.wait_ge(sem["smm"], smm0 + i + 1)
                    if i >= 2:
                        sn = "sgpa" if i % 2 == 0 else "sgpb"
                        s.wait_ge(sem[sn], 16 * ((i - 2) // 2 + 1))
                    s.copy(wt16[i % 2][:, :], pb[4 + i % 2][0:128, 0:128]).then_inc(sem["ss"], 1)
                    cnt["ss"] += 1
            on("scalar", p2_s)

            def p2_dump(sync):
                for i, (src, dst) in enumerate(tiles):
                    sync.wait_ge(sem["ss"], ss0 + i + 1)
                    sn = "sgpa" if i % 2 == 0 else "sgpb"
                    sync.dma_start(out=dst, in_=wt16[i % 2][:, :]).then_inc(sem[sn], 16)
                    cnt[sn] += 16
            on("sync", p2_dump)

            def p2_g(g):
                g.wait_ge(sem["sgpa"], 16 * ((NT + 1) // 2))
                g.wait_ge(sem["sgpb"], 16 * (NT // 2))
                g.collective_compute("AllGather", OP.bypass,
                                     replica_groups=[[0, 1, 2, 3, 4, 5, 6, 7]],
                                     ins=[ag_wgu_in.ap().opt()],
                                     outs=[ag_wgu_out.ap().opt()]).then_inc(sem["sca"], 1)
                cnt["sca"] += 1
                g.collective_compute("AllGather", OP.bypass,
                                     replica_groups=[[0, 1, 2, 3, 4, 5, 6, 7]],
                                     ins=[ag_wd_in.ap().opt()],
                                     outs=[ag_wd_out.ap().opt()]).then_inc(sem["sca"], 1)
                cnt["sca"] += 1
            on("gpsimd", p2_g)
        SCC_AG = cnt["sca"]
        P2_SMM_END = cnt["smm"]
        P2_SGPA_END = cnt["sgpa"]
        P2_SGPB_END = cnt["sgpb"]

        def emit_ln(sq_view):
            def s_sq(sc):
                sc.activation(sq_view, x_slab[:, :, :].rearrange("p c f -> p (c f)"),
                              AF.Square).then_inc(sem["ss"], 1)
                cnt["ss"] += 1
            on("scalar", s_sq)
            tok_sq = cnt["ss"]

            def v_stats(v, tok_sq=tok_sq, sq_view=sq_view):
                for c in range(4):
                    v.tensor_reduce(stA[:, c:c + 1], x_slab[:, c, :], axis=AX.X, op=OP.add)
                v.wait_ge(sem["ss"], tok_sq)
                for c in range(4):
                    v.tensor_reduce(stA[:, 4 + c:5 + c], sq_view[:, c * D:(c + 1) * D],
                                    axis=AX.X, op=OP.add)
                v.drain()
                v.tensor_scalar(stB[:, 0:4], stA[:, 0:4], 1.0 / D, None, op0=OP.mult)
                v.tensor_scalar(stB[:, 4:8], stA[:, 4:8], 1.0 / D, None, op0=OP.mult)
                v.drain()
                v.tensor_tensor(stC[:, 0:4], stB[:, 0:4], stB[:, 0:4], op=OP.mult)
                v.drain()
                v.tensor_tensor(stA[:, 8:12], stB[:, 4:8], stC[:, 0:4], op=OP.subtract)
                v.drain()
                v.tensor_scalar(stB[:, 8:12], stA[:, 8:12], 1e-5, None, op0=OP.add)
                v.drain()
                v.reciprocal(stC[:, 8:12], stB[:, 8:12])
                v.drain()
                v.engine_nop().then_inc(sem["sv"], 1)
                cnt["sv"] += 1
            on("vector", v_stats)
            tok_iv = cnt["sv"]

            def s_sqrt(sc, tok_iv=tok_iv):
                sc.wait_ge(sem["sv"], tok_iv)
                sc.activation(stA[:, 12:16], stC[:, 8:12], AF.Sqrt).then_inc(sem["ss"], 1)
                cnt["ss"] += 1
            on("scalar", s_sqrt)
            tok_r0 = cnt["ss"]

            def v_newton(v, tok_r0=tok_r0):
                v.wait_ge(sem["ss"], tok_r0)
                v.tensor_tensor(stC[:, 12:16], stA[:, 12:16], stA[:, 12:16], op=OP.mult)
                v.drain()
                v.tensor_tensor(stD[:, 0:4], stB[:, 8:12], stC[:, 12:16], op=OP.mult)
                v.drain()
                v.tensor_scalar(stC[:, 0:4], stD[:, 0:4], -0.5, 1.5, op0=OP.mult, op1=OP.add)
                v.drain()
                v.tensor_tensor(stD[:, 8:12], stA[:, 12:16], stC[:, 0:4], op=OP.mult)
                v.drain()
                v.drain()
                v.engine_nop().then_inc(sem["sv"], 1)
                cnt["sv"] += 1
            on("vector", v_newton)
            return cnt["sv"]

        with nc.sbuf_tensor("xnb", [128, 4, D], BF16) as xnb, \
             nc.sbuf_tensor("xnf", [128, 8, MT], BF16) as xnf, \
             nc.sbuf_tensor("k_t0", [128, D], BF16) as k_t0, \
             nc.sbuf_tensor("k_t1", [128, D], BF16) as k_t1, \
             nc.sbuf_tensor("v_t0", [128, D], BF16) as v_t0, \
             nc.sbuf_tensor("v_t1", [128, D], BF16) as v_t1, \
             nc.sbuf_tensor("kv_all", [128, 1032], F32) as kv_all, \
             nc.sbuf_tensor("kvs", [128, 1032], F32) as kvs:
            k_t = [k_t0, k_t1]
            v_t = [v_t0, v_t1]
            tok_x_free = 0
            tok_xnb_free = 0
            tok_xnf_cons = 0
            tok_kt_used = [0, 0]

            for m in range(NMT):
                def a_load(sync, m=m, w=tok_x_free):
                    if w:
                        sync.wait_ge(sem["sv"], w)
                    sync.dma_start(out=x_slab[:, :, :],
                                   in_=x_in[m * MT:(m + 1) * MT, :].rearrange("(c p) f -> p c f", p=128)
                                   ).then_inc(sem["sx"], 16)
                    cnt["sx"] += 16
                on("sync", a_load)
                tok_x = cnt["sx"]

                def a_wx_v(v, t=tok_x):
                    v.wait_ge(sem["sx"], t)
                on("vector", a_wx_v)

                def a_wx_s(s, t=tok_x):
                    s.wait_ge(sem["sx"], t)
                on("scalar", a_wx_s)

                tok_rstd = emit_ln(ln_sq[:, :])

                def a_xn(v, w=tok_xnb_free, tr=tok_rstd, m=m):
                    v.wait_ge(sem["sv"], tr)
                    if m == 0:
                        v.wait_ge(sem["smm"], P2_SMM_END)
                        v.wait_ge(sem["sgpa"], P2_SGPA_END)
                        v.wait_ge(sem["sgpb"], P2_SGPB_END)
                    if w:
                        v.wait_ge(sem["smm"], w)
                    for c in range(4):
                        v.tensor_scalar(xnb[:, c, :], x_slab[:, c, :],
                                        stB[:, c:c + 1], stD[:, 8 + c:9 + c],
                                        op0=OP.subtract, op1=OP.mult)
                    v.drain()
                    v.engine_nop().then_inc(sem["sv"], 1)
                    cnt["sv"] += 1
                on("vector", a_xn)
                tok_xn = cnt["sv"]
                tok_x_free = tok_xn

                smm_tp0 = cnt["smm"]
                ss_tp0 = cnt["ss"]

                def a_tp_t(t, tok_xn=tok_xn, smm_tp0=smm_tp0, ss_tp0=ss_tp0, m=m):
                    t.wait_ge(sem["sv"], tok_xn)
                    if m == 0:
                        t.wait_ge(sem["ss"], SS_FOLD)
                        t.wait_ge(sem["sv"], SV_VCOPY)
                    it = 0
                    for c in range(4):
                        for j in range(8):
                            bA = pb[it % 2]
                            if it >= 2:
                                t.wait_ge(sem["ss"], ss_tp0 + (it - 1))
                            t.matmul(bA[0:128, 0:128], xnb[:, c, j * 128:(j + 1) * 128],
                                     identb[:, :], start=True, stop=True).then_inc(sem["smm"], 1)
                            it += 1
                    cnt["smm"] += 32
                on("tensor", a_tp_t)
                tok_xnb_free = cnt["smm"]

                def a_tp_s(s, smm_tp0=smm_tp0, w=tok_xnf_cons):
                    if w:
                        s.wait_ge(sem["smm"], w)
                    it = 0
                    for c in range(4):
                        for j in range(8):
                            bA = pb[it % 2]
                            s.wait_ge(sem["smm"], smm_tp0 + it + 1)
                            s.copy(xnf[:, j, c * 128:(c + 1) * 128], bA[0:128, 0:128]).then_inc(sem["ss"], 1)
                            it += 1
                    cnt["ss"] += 32
                on("scalar", a_tp_s)
                tok_tp_evac = cnt["ss"]

                smm_q0 = cnt["smm"]
                ss_q0 = cnt["ss"]

                def a_q_t(t, tok_tp_evac=tok_tp_evac, ss_q0=ss_q0):
                    t.wait_ge(sem["ss"], tok_tp_evac)
                    for mc in range(8):
                        bA = pb[mc % 2]
                        if mc >= 2:
                            t.wait_ge(sem["ss"], ss_q0 + (mc - 1))
                        for kc in range(8):
                            ins = t.matmul(bA[:, 0:512], W_att[:, kc, mc * 128:(mc + 1) * 128],
                                           xnf[:, kc, :], start=(kc == 0), stop=(kc == 7))
                        ins.then_inc(sem["smm"], 1)
                        cnt["smm"] += 1
                on("tensor", a_q_t)

                def a_q_s(s, smm_q0=smm_q0, m=m):
                    for mc in range(8):
                        bA = pb[mc % 2]
                        s.wait_ge(sem["smm"], smm_q0 + mc + 1)
                        s.activation(q_res[:, mc, m * MT:(m + 1) * MT], bA[:, 0:512], AF.Relu,
                                     bias=bqc[:, mc:mc + 1]).then_inc(sem["ss"], 1)
                        cnt["ss"] += 1
                on("scalar", a_q_s)
                tok_q_evac = cnt["ss"]

                for tc in range(4):
                    par = tc % 2
                    smm_kv0 = cnt["smm"]
                    ss_kv0 = cnt["ss"]

                    def a_kv_t(t, tc=tc, smm_kv0=smm_kv0, ss_kv0=ss_kv0,
                               tok_q_evac=tok_q_evac):
                        if tc == 0:
                            t.wait_ge(sem["ss"], tok_q_evac)
                        else:
                            t.wait_ge(sem["ss"], ss_kv0)
                        for grp in range(4):
                            bA = pb[2 + grp % 2]
                            if grp >= 2:
                                t.wait_ge(sem["ss"], ss_kv0 + (grp - 1))
                            for kc in range(8):
                                last = (kc == 7) and (grp >= 2)
                                ins = t.matmul(bA[:, 0:512], xnf[:, kc, tc * 128:(tc + 1) * 128],
                                               W_att[:, kc, D + grp * 512: D + (grp + 1) * 512],
                                               start=(kc == 0), stop=last)
                            if grp < 2:
                                ins = t.matmul(bA[:, 0:512], ones1[0:1, :],
                                               bkvr[0:1, grp * 512:(grp + 1) * 512],
                                               start=False, stop=True)
                            ins.then_inc(sem["smm"], 1)
                            cnt["smm"] += 1
                    on("tensor", a_kv_t)
                    tok_kv_grps = cnt["smm"]

                    def a_kv_s(s, par=par, smm_kv0=smm_kv0, w=tok_kt_used[par]):
                        if w:
                            s.wait_ge(sem["smm"], w)
                        for grp in range(4):
                            bA = pb[2 + grp % 2]
                            s.wait_ge(sem["smm"], smm_kv0 + grp + 1)
                            if grp < 2:
                                s.activation(k_t[par][:, grp * 512:(grp + 1) * 512], bA[:, 0:512],
                                             AF.Relu).then_inc(sem["ss"], 1)
                            else:
                                s.copy(v_t[par][:, (grp - 2) * 512:(grp - 1) * 512],
                                       bA[:, 0:512]).then_inc(sem["ss"], 1)
                            cnt["ss"] += 1
                    on("scalar", a_kv_s)
                    tok_kv_evac = cnt["ss"]

                    def a_acc_t(t, m=m, tc=tc, par=par, tok_kv_evac=tok_kv_evac):
                        t.wait_ge(sem["ss"], tok_kv_evac)
                        first = (m == 0 and tc == 0)
                        lastt = (m == NMT - 1 and tc == 3)
                        for p in range(8):
                            t.matmul(pb[4 + p // 4][:, (p % 4) * 128:(p % 4 + 1) * 128],
                                     k_t[par][:, p * 128:(p + 1) * 128],
                                     v_t[par][:, p * 128:(p + 1) * 128],
                                     start=first, stop=lastt, skip_group_check=True)
                            ins = t.matmul(pb[6][:, p:p + 1],
                                           k_t[par][:, p * 128:(p + 1) * 128],
                                           onesc[:, 0:1], start=first, stop=lastt,
                                           skip_group_check=True)
                        ins.then_inc(sem["smm"], 1)
                        cnt["smm"] += 1
                    on("tensor", a_acc_t)
                    tok_kt_used[par] = cnt["smm"]
                tok_xnf_cons = cnt["smm"]
            tok_kv_done = cnt["smm"]

            def ar_v(v, tok_kv_done=tok_kv_done):
                v.wait_ge(sem["smm"], tok_kv_done)
                v.tensor_copy(kvs[:, 0:512], pb[4][:, :])
                v.tensor_copy(kvs[:, 512:1024], pb[5][:, :])
                v.tensor_copy(kvs[:, 1024:1032], pb[6][:, 0:8])
                v.drain()
                v.drain()
                v.engine_nop().then_inc(sem["sv"], 1)
                cnt["sv"] += 1
            on("vector", ar_v)
            tok_kvs = cnt["sv"]

            def ar_g(g, tok_kvs=tok_kvs):
                g.wait_ge(sem["sv"], tok_kvs)
                g.dma_start(out=ar_in.ap()[:, :], in_=kvs[:, :]).then_inc(sem["sgp"], 16)
                cnt["sgp"] += 16
                g.wait_ge(sem["sgp"], cnt["sgp"])
                g.collective_compute("AllReduce", OP.add,
                                     replica_groups=[[0, 1, 2, 3], [4, 5, 6, 7]],
                                     ins=[ar_in.ap().opt()],
                                     outs=[ar_out.ap().opt()]).then_inc(sem["scc"], 1)
                cnt["scc"] += 1
                g.wait_ge(sem["scc"], cnt["scc"])
                g.dma_start(out=kv_all[:, :], in_=ar_out.ap()[:, :]).then_inc(sem["sgp"], 16)
                cnt["sgp"] += 16
            on("gpsimd", ar_g)
            tok_kvall = cnt["sgp"]

            def ar_build(v, tok_kvall=tok_kvall):
                v.wait_ge(sem["sgp"], tok_kvall)
                v.memset(bdkv[:, :, :], 0.0)
                v.memset(ksb8[:, :, :], 0.0)
                v.drain()
                for p in range(8):
                    v.tensor_copy(bdkv[0:64, p, 0:64], kv_all[0:64, p * 128: p * 128 + 64])
                    v.tensor_copy(bdkv[64:128, p, 64:128], kv_all[64:128, p * 128 + 64: p * 128 + 128])
                    v.tensor_copy(ksb8[0:64, p, 2 * p:2 * p + 1], kv_all[0:64, 1024 + p:1025 + p])
                    v.tensor_copy(ksb8[64:128, p, 2 * p + 1:2 * p + 2], kv_all[64:128, 1024 + p:1025 + p])
                v.drain()
                v.drain()
                v.engine_nop().then_inc(sem["sv"], 1)
                cnt["sv"] += 1
            on("vector", ar_build)
            tok_attc = cnt["sv"]
            tok_x_free_ab = tok_x_free

            DUMPA_SGP = 0
            if DBG:
                def dump_a(g, tok_attc=tok_attc):
                    g.wait_ge(sem["sv"], tok_attc)
                    g.dma_start(out=dbg_watt, in_=W_att[:, :, :]).then_inc(sem["sgp"], 16)
                    g.dma_start(out=dbg_xnf, in_=xnf[:, :, :]).then_inc(sem["sgp"], 16)
                    g.dma_start(out=dbg_xnb, in_=xnb[:, :, :]).then_inc(sem["sgp"], 16)
                    g.dma_start(out=dbg_kvall, in_=kv_all[:, :]).then_inc(sem["sgp"], 16)
                    cnt["sgp"] += 64
                on("gpsimd", dump_a)
                DUMPA_SGP = cnt["sgp"]

    wgu_sb = SB("wgu_sb", [128, 2, 8, 512], F16)
    wd_ic = [SB("wd_ic0", [128, D], F16), SB("wd_ic1", [128, D], F16)]
    wout_p = [SB("wout_p0", [128, D], BF16), SB("wout_p1", [128, D], BF16)]
    xn32 = SB("xn32", [128, D], F32)
    xq_t1 = SB("xq_tok0", [128, D], F16)
    xq_tok = [xq_t1, xq_t1]
    xq_feat = SB("xq_feat", [128, 8, MT], F16)
    h_tok = SB("h_tok", [128, 2, FI], F16)
    hq_t = SB("hq0", [128, 1408], F16)
    hqx = [hq_t, hq_t]
    hqT = SB("hqT", [128, ICN, MT], F16)
    sgt = [SB("sgt0", [128, 256], F32), SB("sgt1", [128, 256], F32)]
    attn2 = SB("attn2", [128, 8, MT], BF16)
    rbx = [SB("rb0", [128, 512], BF16), SB("rb1", [128, 512], BF16)]
    nplus = SB("nplus", [16, 512], F32)
    nrec = SB("nrec", [16, 512], F32)
    nrec16 = SB("nrec16", [16, 512], BF16)
    out_sb = [SB("out_sb0", [128, D], F32), SB("out_sb1", [128, D], F32)]
    sq_view = hqT[:, :, :].rearrange("p a f -> p (a f)").bitcast(F32)[:, 0:4 * D]

    dbg_cap_gup = SB("dbg_cap_gup", [128, 512], F16) if DBG else None
    dbg_cap_sgt = SB("dbg_cap_sgt", [128, 256], F16) if DBG else None
    tok_x_free_b = tok_attc
    tok_wgu_free = [0, 0]
    tok_hqT_free = 0
    tok_outsb_free = [0, 0]
    tok_sgt_used = [0, 0]
    wd_used_tok = [0, 0]
    wo_used = [0, 0]

    for m in range(NMT):
        def b_load(sync, m=m, w=tok_x_free_b):
            if w:
                sync.wait_ge(sem["sv"], w)
            sync.dma_start(out=x_slab[:, :, :],
                           in_=x_in[m * MT:(m + 1) * MT, :].rearrange("(c p) f -> p c f", p=128)
                           ).then_inc(sem["sx"], 16)
            cnt["sx"] += 16
        on("sync", b_load)
        tok_xb = cnt["sx"]

        def b_wx_v(v, t=tok_xb, thq=tok_hqT_free, dg=(DUMPA_SGP if m == 0 else 0)):
            if dg:
                v.wait_ge(sem["sgp"], dg)
            v.wait_ge(sem["sx"], t)
            if thq:
                v.wait_ge(sem["smm"], thq)
        on("vector", b_wx_v)

        def b_wx_s(s, t=tok_xb, thq=tok_hqT_free):
            s.wait_ge(sem["sx"], t)
            if thq:
                s.wait_ge(sem["smm"], thq)
        on("scalar", b_wx_s)

        def emit_ln_b():
            def s_sq(sc):
                sc.activation(sq_view, x_slab[:, :, :].rearrange("p c f -> p (c f)"),
                              AF.Square).then_inc(sem["ss"], 1)
                cnt["ss"] += 1
            on("scalar", s_sq)
            tok_sq = cnt["ss"]

            def v_stats(v, tok_sq=tok_sq):
                for c in range(4):
                    v.tensor_reduce(stA[:, c:c + 1], x_slab[:, c, :], axis=AX.X, op=OP.add)
                v.wait_ge(sem["ss"], tok_sq)
                for c in range(4):
                    v.tensor_reduce(stA[:, 4 + c:5 + c], sq_view[:, c * D:(c + 1) * D],
                                    axis=AX.X, op=OP.add)
                v.drain()
                v.tensor_scalar(stB[:, 0:4], stA[:, 0:4], 1.0 / D, None, op0=OP.mult)
                v.tensor_scalar(stB[:, 4:8], stA[:, 4:8], 1.0 / D, None, op0=OP.mult)
                v.drain()
                v.tensor_tensor(stC[:, 0:4], stB[:, 0:4], stB[:, 0:4], op=OP.mult)
                v.drain()
                v.tensor_tensor(stA[:, 8:12], stB[:, 4:8], stC[:, 0:4], op=OP.subtract)
                v.drain()
                v.tensor_scalar(stB[:, 8:12], stA[:, 8:12], 1e-5, None, op0=OP.add)
                v.drain()
                v.reciprocal(stC[:, 8:12], stB[:, 8:12])
                v.drain()
                v.engine_nop().then_inc(sem["sv"], 1)
                cnt["sv"] += 1
            on("vector", v_stats)
            tok_iv = cnt["sv"]

            def s_sqrt(sc, tok_iv=tok_iv):
                sc.wait_ge(sem["sv"], tok_iv)
                sc.activation(stA[:, 12:16], stC[:, 8:12], AF.Sqrt).then_inc(sem["ss"], 1)
                cnt["ss"] += 1
            on("scalar", s_sqrt)
            tok_r0 = cnt["ss"]

            def v_newton(v, tok_r0=tok_r0):
                v.wait_ge(sem["ss"], tok_r0)
                v.tensor_tensor(stC[:, 12:16], stA[:, 12:16], stA[:, 12:16], op=OP.mult)
                v.drain()
                v.tensor_tensor(stD[:, 0:4], stB[:, 8:12], stC[:, 12:16], op=OP.mult)
                v.drain()
                v.tensor_scalar(stC[:, 0:4], stD[:, 0:4], -0.5, 1.5, op0=OP.mult, op1=OP.add)
                v.drain()
                v.tensor_tensor(stD[:, 8:12], stA[:, 12:16], stC[:, 0:4], op=OP.mult)
                v.drain()
                v.drain()
                v.engine_nop().then_inc(sem["sv"], 1)
                cnt["sv"] += 1
            on("vector", v_newton)
            return cnt["sv"]

        tok_rstd = emit_ln_b()

        smm_xq0 = cnt["smm"]
        ss_xq0 = cnt["ss"]
        for c in range(4):
            par = c % 2

            def b_xq_v(v, c=c, par=par, tr=tok_rstd, smm_xq0=smm_xq0):
                if c == 0:
                    v.wait_ge(sem["sv"], tr)
                if c >= 1:
                    v.wait_ge(sem["smm"], smm_xq0 + c * 8)
                v.tensor_scalar(xn32[:, :], x_slab[:, c, :], stB[:, c:c + 1], stD[:, 8 + c:9 + c],
                                op0=OP.subtract, op1=OP.mult)
                v.drain()
                emit_bmax(v, xn32[:, :], D, bmx[:, 0:64], bsc[:, 0:64], brr[:, 0:64])
                emit_quant_f16(v, xn32[:, :], D, xq_tok[par][:, :], bsc[:, 0:64], brr[:, 0:64])
                v.drain()
                v.drain()
                v.engine_nop().then_inc(sem["sv"], 1)
                cnt["sv"] += 1
            on("vector", b_xq_v)
            tok_xq = cnt["sv"]

            def b_xq_t(t, c=c, par=par, tok_xq=tok_xq, ss_xq0=ss_xq0, m=m):
                t.wait_ge(sem["sv"], tok_xq)
                if m == 0 and c == 0:
                    t.wait_ge(sem["sv"], tok_attc)
                for j in range(8):
                    it = c * 8 + j
                    bA = pb[it % 2]
                    if it >= 2:
                        t.wait_ge(sem["ss"], ss_xq0 + (it - 1))
                    t.matmul(bA[0:128, 0:128], xq_tok[par][:, j * 128:(j + 1) * 128], identh[:, :],
                             start=True, stop=True).then_inc(sem["smm"], 1)
                    cnt["smm"] += 1
            on("tensor", b_xq_t)

            def b_xq_s(s, c=c, smm_xq0=smm_xq0):
                for j in range(8):
                    it = c * 8 + j
                    bA = pb[it % 2]
                    s.wait_ge(sem["smm"], smm_xq0 + it + 1)
                    s.copy(xq_feat[:, j, c * 128:(c + 1) * 128], bA[0:128, 0:128]).then_inc(sem["ss"], 1)
                    cnt["ss"] += 1
            on("scalar", b_xq_s)
        tok_xqf = cnt["ss"]

        if DBG:
            def dump_xq(sy, m=m, t=tok_xqf):
                sy.wait_ge(sem["ss"], t)
                sy.dma_start(out=dbg_xqall[m], in_=xq_feat[:, :, :]).then_inc(sem["sgp"], 16)
                cnt["sgp"] += 16
            on("sync", dump_xq)

        for half in range(2):
            for ig in range(NIG):
                par = ig % 2

                def b_wgu_load(sync, ig=ig, par=par, w=tok_wgu_free[par],
                               first=(m == 0 and half == 0 and ig < 2)):
                    if first:
                        sync.wait_ge(sem["sca"], SCC_AG)
                    if w:
                        sync.wait_ge(sem["smm"], w)
                    sn = "swga" if par == 0 else "swgb"
                    sync.dma_start(out=wgu_sb[:, par, :, :],
                                   in_=ag_wgu_out.ap()[:, :, ig * 512:(ig + 1) * 512]
                                   .rearrange("c p f -> p c f")).then_inc(sem[sn], 16)
                    cnt[sn] += 16
                on("sync", b_wgu_load)
                tok_wgu = cnt["swga" if par == 0 else "swgb"]
                sem_wgu = "swga" if par == 0 else "swgb"

                for tl in range(2):
                    tcg = half * 2 + tl
                    bank = pb[2 + tl]

                    def b_gu_t(t, par=par, tcg=tcg, bank=bank, tok_wgu=tok_wgu,
                               tok_xqf=tok_xqf, need_free=cnt["sv"], sem_wgu=sem_wgu,
                               first=(ig == 0 and half == 0)):
                        t.wait_ge(sem[sem_wgu], tok_wgu)
                        if first:
                            t.wait_ge(sem["ss"], tok_xqf)
                        if need_free:
                            t.wait_ge(sem["sv"], need_free)
                        for fg in range(2):
                            for kc in range(8):
                                ins = t.matmul(bank[:, fg * 256:(fg + 1) * 256],
                                               xq_feat[:, kc, tcg * 128:(tcg + 1) * 128],
                                               wgu_sb[:, par, kc, fg * 256:(fg + 1) * 256],
                                               start=(kc == 0), stop=(kc == 7))
                        ins.then_inc(sem["smm"], 1)
                        cnt["smm"] += 1
                    on("tensor", b_gu_t)
                    tok_gu = cnt["smm"]

                    def b_silu_s(s, bank=bank, tl=tl, tok_gu=tok_gu, w=tok_sgt_used[tl]):
                        s.wait_ge(sem["smm"], tok_gu)
                        if w:
                            s.wait_ge(sem["sv"], w)
                        s.activation(sgt[tl][:, :], bank[:, 0:256], AF.Silu)
                        s.drain()
                        s.nop().then_inc(sem["ss"], 1)
                        cnt["ss"] += 1
                    on("scalar", b_silu_s)
                    tok_silu = cnt["ss"]

                    def b_h_v(v, bank=bank, tl=tl, ig=ig, tok_silu=tok_silu,
                              cap=(DBG and m == 7 and half == 1 and ig == 4 and tl == 0)):
                        v.wait_ge(sem["ss"], tok_silu)
                        if cap:
                            v.tensor_copy(dbg_cap_gup[:, :], bank[:, 0:512])
                            v.tensor_copy(dbg_cap_sgt[:, :], sgt[tl][:, :])
                        v.tensor_tensor(h_tok[:, tl, ig * 256:(ig + 1) * 256], sgt[tl][:, :],
                                        bank[:, 256:512], op=OP.mult)
                        v.drain()
                        v.engine_nop().then_inc(sem["sv"], 1)
                        cnt["sv"] += 1
                    on("vector", b_h_v)
                    tok_sgt_used[tl] = cnt["sv"]
                tok_wgu_free[par] = cnt["smm"]
            tok_h_done = cnt["sv"]

            for tl in range(2):
                tcg = half * 2 + tl
                for hf in range(2):
                    src = h_tok[:, tl, hf * 1408:(hf + 1) * 1408]
                    par = (tcg * 2 + hf) % 2

                    def b_hq_v(v, src=src, par=par, tok_h_done=tok_h_done, need=cnt["smm"]):
                        v.wait_ge(sem["sv"], tok_h_done)
                        if need:
                            v.wait_ge(sem["smm"], need)
                        emit_bmax(v, src, 1408, bmx[:, 0:88], bsc[:, 0:88], brr[:, 0:88])
                        emit_quant_f16(v, src, 1408, hqx[par][:, :], bsc[:, 0:88], brr[:, 0:88])
                        v.drain()
                        v.engine_nop().then_inc(sem["sv"], 1)
                        cnt["sv"] += 1
                    on("vector", b_hq_v)
                    tok_hq = cnt["sv"]

                    smm_ht0 = cnt["smm"]
                    ss_ht0 = cnt["ss"]

                    def b_hq_t(t, par=par, tok_hq=tok_hq, ss_ht0=ss_ht0):
                        t.wait_ge(sem["sv"], tok_hq)
                        t.wait_ge(sem["ss"], ss_ht0)
                        for j in range(11):
                            bA = pb[4 + j % 2]
                            if j >= 2:
                                t.wait_ge(sem["ss"], ss_ht0 + (j - 1))
                            t.matmul(bA[0:128, 0:128], hqx[par][:, j * 128:(j + 1) * 128],
                                     identh[:, :], start=True, stop=True).then_inc(sem["smm"], 1)
                            cnt["smm"] += 1
                    on("tensor", b_hq_t)

                    def b_hq_s(s, tcg=tcg, hf=hf, smm_ht0=smm_ht0):
                        for j in range(11):
                            bA = pb[4 + j % 2]
                            s.wait_ge(sem["smm"], smm_ht0 + j + 1)
                            s.copy(hqT[:, hf * 11 + j, tcg * 128:(tcg + 1) * 128],
                                   bA[0:128, 0:128]).then_inc(sem["ss"], 1)
                            cnt["ss"] += 1
                    on("scalar", b_hq_s)
        tok_hqT_done = cnt["ss"]


        def b_norm_t(t, m=m, prev=cnt["ss"]):
            if m == 0:
                t.wait_ge(sem["sv"], tok_attc)
            t.wait_ge(sem["ss"], prev)
            for p in range(8):
                ins = t.matmul(pb[6][0:16, 0:512], ksb8[:, p, :], q_res[:, p, m * MT:(m + 1) * MT],
                               start=(p == 0), stop=(p == 7))
            ins.then_inc(sem["smm"], 1)
            cnt["smm"] += 1
        on("tensor", b_norm_t)
        tok_norm = cnt["smm"]

        def b_nrec_v(v, tok_norm=tok_norm):
            v.wait_ge(sem["smm"], tok_norm)
            v.tensor_scalar(nplus[:, :], pb[6][0:16, 0:512], 1e-6, None, op0=OP.add)
            v.drain()
            v.reciprocal(nrec[:, :], nplus[:, :])
            v.drain()
            v.tensor_copy(nrec16[:, :], nrec[:, :])
            v.drain()
            v.engine_nop().then_inc(sem["sv"], 1)
            cnt["sv"] += 1
        on("vector", b_nrec_v)
        tok_nrec = cnt["sv"]

        ss_rb0 = cnt["ss"]
        sv_at0 = cnt["sv"]
        for p in range(8):
            par = p % 2

            def b_rb_t(t, p=p, par=par, tok_nrec=tok_nrec, ss_rb0=ss_rb0, sv_at0=sv_at0, m=m):
                if p == 0:
                    t.wait_ge(sem["sv"], tok_nrec)
                if p >= 2:
                    t.wait_ge(sem["ss"], ss_rb0 + (p - 1))
                    t.wait_ge(sem["sv"], sv_at0 + (p - 1))
                t.matmul(pb[6 + par][:, 0:512], selb[:, p * 128:(p + 1) * 128], nrec16[:, :],
                         start=True, stop=True).then_inc(sem["smm"], 1)
                cnt["smm"] += 1
                t.matmul(pb[2 + par][:, 0:512], bdkv[:, p, :], q_res[:, p, m * MT:(m + 1) * MT],
                         start=True, stop=True).then_inc(sem["smm"], 1)
                cnt["smm"] += 1
            on("tensor", b_rb_t)
            tok_rbmm = cnt["smm"]

            def b_rb_s(s, par=par, tok_rbmm=tok_rbmm):
                s.wait_ge(sem["smm"], tok_rbmm - 1)
                s.copy(rbx[par][:, :], pb[6 + par][:, 0:512])
                s.drain()
                s.nop().then_inc(sem["ss"], 1)
                cnt["ss"] += 1
            on("scalar", b_rb_s)
            tok_rbs = cnt["ss"]

            def b_at_v(v, p=p, par=par, tok_rbmm=tok_rbmm, tok_rbs=tok_rbs):
                v.wait_ge(sem["smm"], tok_rbmm)
                v.wait_ge(sem["ss"], tok_rbs)
                v.tensor_tensor(attn2[:, p, :], pb[2 + par][:, 0:512], rbx[par][:, :], op=OP.mult)
                v.drain()
                v.drain()
                v.engine_nop().then_inc(sem["sv"], 1)
                cnt["sv"] += 1
            on("vector", b_at_v)
        tok_attn2 = cnt["sv"]

        if DBG:
            def dump_at2(sy, m=m, t=tok_attn2):
                sy.wait_ge(sem["sv"], t)
                sy.dma_start(out=dbg_at2all[m], in_=attn2[:, :, :]).then_inc(sem["sgp"], 16)
                cnt["sgp"] += 16
            on("sync", dump_at2)

        for ic in range(ICN):
            par = ic % 2

            def b_wd_l(sync, ic=ic, par=par, w=wd_used_tok[par], m=m):
                if m == 0 and ic == 0:
                    sync.wait_ge(sem["sca"], SCC_AG)
                if w:
                    sync.wait_ge(sem["smm"], w)
                sn = "swda" if par == 0 else "swdb"
                sync.dma_start(out=wd_ic[par][:, :],
                               in_=ag_wd_out.ap()[ic % 8, ic // 8, :, :]).then_inc(sem[sn], 16)
                cnt[sn] += 16
            on("sync", b_wd_l)
            wd_tok = cnt["swda" if par == 0 else "swdb"]
            sem_wd = "swda" if par == 0 else "swdb"

            def b_wd_t(t, ic=ic, par=par, wt=wd_tok, ta=tok_attn2, tok_hqT_done=tok_hqT_done,
                       sem_wd=sem_wd):
                t.wait_ge(sem[sem_wd], wt)
                if ic == 0:
                    t.wait_ge(sem["sv"], ta)
                    t.wait_ge(sem["ss"], tok_hqT_done)
                for tcg in range(4):
                    for fg in range(2):
                        ins = t.matmul(pb[tcg * 2 + fg][:, 0:512],
                                       hqT[:, ic, tcg * 128:(tcg + 1) * 128],
                                       wd_ic[par][:, fg * 512:(fg + 1) * 512],
                                       start=(ic == 0), stop=False)
                ins.then_inc(sem["smm"], 1)
                cnt["smm"] += 1
            on("tensor", b_wd_t)
            wd_used_tok[par] = cnt["smm"]

        for p in range(8):
            par = p % 2

            def b_wo_l(sync, p=p, par=par, w=wo_used[par], first=(m == 0 and p < 2)):
                if first:
                    sync.wait_ge(sem["sx"], WC_SX)
                if w:
                    sync.wait_ge(sem["smm"], w)
                sn = "swoa" if par == 0 else "swob"
                sync.dma_start(out=wout_p[par][:, :],
                               in_=wout16.ap()[p * 128:(p + 1) * 128, :]).then_inc(sem[sn], 16)
                cnt[sn] += 16
            on("sync", b_wo_l)
            wo_tok = cnt["swoa" if par == 0 else "swob"]
            sem_wo = "swoa" if par == 0 else "swob"

            def b_wo_t(t, p=p, par=par, wt=wo_tok, sem_wo=sem_wo):
                t.wait_ge(sem[sem_wo], wt)
                for tcg in range(4):
                    for fg in range(2):
                        ins = t.matmul(pb[tcg * 2 + fg][:, 0:512],
                                       attn2[:, p, tcg * 128:(tcg + 1) * 128],
                                       wout_p[par][:, fg * 512:(fg + 1) * 512],
                                       start=False, stop=False)
                ins.then_inc(sem["smm"], 1)
                cnt["smm"] += 1
            on("tensor", b_wo_t)
            wo_used[par] = cnt["smm"]

        out_mm_tok = []

        def b_bias_t(t):
            for tcg in range(4):
                for fg in range(2):
                    t.matmul(pb[tcg * 2 + fg][:, 0:512], ones1[0:1, :],
                             boutr[0:1, fg * 512:(fg + 1) * 512],
                             start=False, stop=True).then_inc(sem["smm"], 1)
                    cnt["smm"] += 1
                    out_mm_tok.append(cnt["smm"])
        on("tensor", b_bias_t)

        for tcg in range(4):
            par = tcg % 2

            def b_oev_v(v, tcg=tcg, par=par, w=tok_outsb_free[par],
                        tokk=out_mm_tok[tcg * 2 + 1]):
                v.wait_ge(sem["smm"], tokk)
                if w:
                    v.wait_ge(sem["ssta" if par == 0 else "sstb"], w)
                for fg in range(2):
                    v.tensor_tensor(out_sb[par][:, fg * 512:(fg + 1) * 512],
                                    pb[tcg * 2 + fg][:, 0:512],
                                    x_slab[:, tcg, fg * 512:(fg + 1) * 512], op=OP.add)
                v.drain()
                v.drain()
                v.engine_nop().then_inc(sem["sv"], 1)
                cnt["sv"] += 1
            on("vector", b_oev_v)
            tok_oev = cnt["sv"]

            def b_store(sync, tcg=tcg, par=par, m=m, tok_oev=tok_oev):
                sync.wait_ge(sem["sv"], tok_oev)
                sn = "ssta" if par == 0 else "sstb"
                sync.dma_start(out=out_d[m * MT + tcg * 128: m * MT + (tcg + 1) * 128, :],
                               in_=out_sb[par][:, :]).then_inc(sem[sn], 16)
                cnt[sn] += 16
            on("sync", b_store)
            tok_outsb_free[par] = cnt["ssta" if par == 0 else "sstb"]
        tok_x_free_b = cnt["sv"]
        tok_hqT_free = cnt["smm"]

    SSTA_TOT = cnt["ssta"]
    SSTB_TOT = cnt["sstb"]

    if DBG:
        def dump(g):
            g.wait_ge(sem["ssta"], SSTA_TOT)
            g.wait_ge(sem["sstb"], SSTB_TOT)
            g.dma_start(out=dbg_agwgu, in_=ag_wgu_out.ap()[:, :, :]).then_inc(sem["sgp"], 16)
            cnt["sgp"] += 16
            g.dma_start(out=dbg_wgusb, in_=wgu_sb[:, :, :, :]).then_inc(sem["sgp"], 16)
            cnt["sgp"] += 16
            g.dma_start(out=dbg_gup, in_=dbg_cap_gup[:, :]).then_inc(sem["sgp"], 16)
            cnt["sgp"] += 16
            g.dma_start(out=dbg_sgt, in_=dbg_cap_sgt[:, :]).then_inc(sem["sgp"], 16)
            cnt["sgp"] += 16
            for dst, srct in [(dbg_qres, q_res), (dbg_bdkv, bdkv), (dbg_ksb8, ksb8),
                              (dbg_xqf, xq_feat), (dbg_hqT, hqT), (dbg_attn2, attn2),
                              (dbg_htok, h_tok), (dbg_nrec, nrec)]:
                g.dma_start(out=dst, in_=srct[:, :] if len(dst.shape) == 2 else srct[:, :, :]).then_inc(sem["sgp"], 16)
                cnt["sgp"] += 16
            g.dma_start(out=dbg_st[:, 0:16], in_=stA[:, :]).then_inc(sem["sgp"], 16)
            g.dma_start(out=dbg_st[:, 16:32], in_=stB[:, :]).then_inc(sem["sgp"], 16)
            g.dma_start(out=dbg_st[:, 32:48], in_=stC[:, :]).then_inc(sem["sgp"], 16)
            g.dma_start(out=dbg_st[:, 48:64], in_=stD[:, :]).then_inc(sem["sgp"], 16)
            cnt["sgp"] += 64
            g.wait_ge(sem["sgp"], cnt["sgp"])
        on("gpsimd", dump)

    def fin(sync):
        sync.wait_ge(sem["ssta"], SSTA_TOT)
        sync.wait_ge(sem["sstb"], SSTB_TOT)
        sync.wait_ge(sem["scc"], cnt["scc"])
        sync.wait_ge(sem["sca"], cnt["sca"])
        if DBG:
            sync.wait_ge(sem["sgp"], cnt["sgp"])
    on("sync", fin)

    es.close()
    return nc


_NC_CACHE = {}


def kernel(L0_attn_x, L0_attn_W_qkv, L0_attn_W_f, L0_attn_b_f, L0_attn_W_out,
           L0_attn_b_out, L0_ffn_w_gate, L0_ffn_w_up, L0_ffn_w_down):
    x = np.ascontiguousarray(np.asarray(L0_attn_x, dtype=np.float32)).reshape(NCORE, TCORE, D)
    wqkv = np.ascontiguousarray(np.asarray(L0_attn_W_qkv, dtype=np.float32))
    wf = np.ascontiguousarray(np.asarray(L0_attn_W_f, dtype=np.float32))
    b_f = np.asarray(L0_attn_b_f, dtype=np.float32)
    wout = np.ascontiguousarray(np.asarray(L0_attn_W_out, dtype=np.float32))
    b_out = np.asarray(L0_attn_b_out, dtype=np.float32)
    wg = np.ascontiguousarray(np.asarray(L0_ffn_w_gate, dtype=np.float32))
    wu = np.ascontiguousarray(np.asarray(L0_ffn_w_up, dtype=np.float32))
    wd = np.ascontiguousarray(np.asarray(L0_ffn_w_down, dtype=np.float32))

    bft = np.tile(b_f, 16)
    bqc = np.ascontiguousarray(bft.reshape(8, 128).T)
    bkvr = np.ascontiguousarray(np.concatenate([bft, np.zeros(D, np.float32)])[None, :])
    boutr = np.ascontiguousarray(b_out[None, :])
    idb = np.eye(128, dtype=np.float32).astype(bf16)
    idh = np.eye(128, dtype=np.float32).astype(np.float16)
    ones1 = np.ones((1, 128), np.float32).astype(bf16)
    onesc = np.ones((128, 8), np.float32).astype(bf16)
    selb = np.zeros((16, 8 * 128), np.float32)
    for p in range(8):
        selb[2 * p, p * 128: p * 128 + 64] = 1.0
        selb[2 * p + 1, p * 128 + 64: p * 128 + 128] = 1.0
    selb = selb.astype(bf16)

    in_maps = []
    for c in range(NCORE):
        wd_ch = []
        for s in range(WDCH):
            ic = s * 8 + c
            if ic < ICN:
                wd_ch.append(wd[:, ic * 128:(ic + 1) * 128])
            else:
                wd_ch.append(np.zeros((D, 128), np.float32))
        in_maps.append({
            "x": np.ascontiguousarray(x[c]),
            "wqkv": wqkv, "wf": wf, "wout": wout,
            "bqc": bqc, "bkvr": bkvr, "boutr": boutr,
            "wg_sl": np.ascontiguousarray(wg[:, c * 128:(c + 1) * 128]),
            "wu_sl": np.ascontiguousarray(wu[:, c * 128:(c + 1) * 128]),
            "wd_sl": np.ascontiguousarray(np.stack(wd_ch, 0)),
            "idb": idb, "idh": idh, "ones1": ones1, "onesc": onesc, "selb": selb,
        })

    if "prog" not in _NC_CACHE:
        _NC_CACHE["prog"] = build_nc()
    nc = _NC_CACHE["prog"]
    res = run_bass_kernel_spmd(nc, in_maps, core_ids=list(range(NCORE)))
    import os as _os
    if int(_os.environ.get("KDBG", "0")):
        kernel.DBG_RES = res
    out = np.concatenate([res.results[c]["out"][None] for c in range(NCORE)], 0)
    return np.ascontiguousarray(out.reshape(2, 16384, D))


# version 16
# speedup vs baseline: 1.1785x; 1.1685x over previous
import numpy as np
import ml_dtypes
import concourse.bass as bass
import concourse.mybir as mybir
from concourse.bass_utils import run_bass_kernel_spmd
from contextlib import ExitStack

F32 = mybir.dt.float32
F16 = mybir.dt.float16
BF16 = mybir.dt.bfloat16
I16 = mybir.dt.int16
I32 = mybir.dt.int32
OP = mybir.AluOpType
AF = mybir.ActivationFunctionType
AX = mybir.AxisListType
bf16 = ml_dtypes.bfloat16

NCORE = 8
TCORE = 4096
MT = 512
NMT = TCORE // MT
D = 1024
FI = 2816
ICN = 22
NIG = 11
WDCH = 3
MIDS = [0.25, 0.75, 1.25, 1.75, 2.5, 3.5, 5.0]


def build_nc():
    nc = bass.Bass("TRN2", target_bir_lowering=False)
    x_in = nc.dram_tensor("x", [TCORE, D], F32, kind="ExternalInput").ap()
    wqkv_in = nc.dram_tensor("wqkv", [D, 3 * D], F32, kind="ExternalInput").ap()
    wf_in = nc.dram_tensor("wf", [64, 64], F32, kind="ExternalInput").ap()
    wout_in = nc.dram_tensor("wout", [D, D], F32, kind="ExternalInput").ap()
    bqc_in = nc.dram_tensor("bqc", [128, 8], F32, kind="ExternalInput").ap()
    bkvr_in = nc.dram_tensor("bkvr", [1, 2 * D], F32, kind="ExternalInput").ap()
    bout_in = nc.dram_tensor("boutr", [1, D], F32, kind="ExternalInput").ap()
    wg_sl_in = nc.dram_tensor("wg_sl", [FI, 128], F32, kind="ExternalInput").ap()
    wu_sl_in = nc.dram_tensor("wu_sl", [FI, 128], F32, kind="ExternalInput").ap()
    wd_sl_in = nc.dram_tensor("wd_sl", [WDCH, D, 128], F32, kind="ExternalInput").ap()
    idb_in = nc.dram_tensor("idb", [128, 128], BF16, kind="ExternalInput").ap()
    idh_in = nc.dram_tensor("idh", [128, 128], F16, kind="ExternalInput").ap()
    ones1_in = nc.dram_tensor("ones1", [1, 128], BF16, kind="ExternalInput").ap()
    onesc_in = nc.dram_tensor("onesc", [128, 8], BF16, kind="ExternalInput").ap()
    selb_in = nc.dram_tensor("selb", [16, 8 * 128], BF16, kind="ExternalInput").ap()
    out_d = nc.dram_tensor("out", [TCORE, D], F32, kind="ExternalOutput").ap()
    import os as _os
    DBG = bool(int(_os.environ.get("KDBG", "0")))
    if DBG:
        dbg_qres = nc.dram_tensor("dbg_qres", [128, 8, TCORE], F32, kind="ExternalOutput").ap()
        dbg_bdkv = nc.dram_tensor("dbg_bdkv", [128, 8, 128], F32, kind="ExternalOutput").ap()
        dbg_ksb8 = nc.dram_tensor("dbg_ksb8", [128, 8, 16], F32, kind="ExternalOutput").ap()
        dbg_xqf = nc.dram_tensor("dbg_xqf", [128, 8, MT], F32, kind="ExternalOutput").ap()
        dbg_hqT = nc.dram_tensor("dbg_hqT", [128, ICN, MT], F32, kind="ExternalOutput").ap()
        dbg_attn2 = nc.dram_tensor("dbg_attn2", [128, 8, MT], F32, kind="ExternalOutput").ap()
        dbg_htok = nc.dram_tensor("dbg_htok", [128, 2, FI], F32, kind="ExternalOutput").ap()
        dbg_nrec = nc.dram_tensor("dbg_nrec", [16, 512], F32, kind="ExternalOutput").ap()
        dbg_st = nc.dram_tensor("dbg_st", [128, 64], F32, kind="ExternalOutput").ap()
        dbg_watt = nc.dram_tensor("dbg_watt", [128, 8, 3 * D], F32, kind="ExternalOutput").ap()
        dbg_xnf = nc.dram_tensor("dbg_xnf", [128, 8, MT], F32, kind="ExternalOutput").ap()
        dbg_xnb = nc.dram_tensor("dbg_xnb", [128, 4, D], F32, kind="ExternalOutput").ap()
        dbg_kvall = nc.dram_tensor("dbg_kvall", [128, 1032], F32, kind="ExternalOutput").ap()
        dbg_xqall = nc.dram_tensor("dbg_xqall", [NMT, 128, 8, MT], F16, kind="ExternalOutput").ap()
        dbg_hqTall = nc.dram_tensor("dbg_hqTall", [NMT, 128, ICN, MT], F16, kind="ExternalOutput").ap()
        dbg_at2all = nc.dram_tensor("dbg_at2all", [NMT, 128, 8, MT], BF16, kind="ExternalOutput").ap()
        dbg_agwgu = nc.dram_tensor("dbg_agwgu", [8, 128, 2 * FI], F16, kind="ExternalOutput").ap()
        dbg_wgusb = nc.dram_tensor("dbg_wgusb", [128, 2, 8, 512], F16, kind="ExternalOutput").ap()
        dbg_gup = nc.dram_tensor("dbg_gup", [128, 512], F16, kind="ExternalOutput").ap()
        dbg_sgt = nc.dram_tensor("dbg_sgt", [128, 256], F16, kind="ExternalOutput").ap()
    ag_wgu_in = nc.dram_tensor("ag_wgu_in", [128, 2 * FI], F16)
    ag_wgu_out = nc.dram_tensor("ag_wgu_out", [8, 128, 2 * FI], F16)
    ag_wd_in = nc.dram_tensor("ag_wd_in", [WDCH, 128, D], F16)
    ag_wd_out = nc.dram_tensor("ag_wd_out", [8, WDCH, 128, D], F16)
    ar_in = nc.dram_tensor("ar_in", [128, 1032], F32)
    ar_out = nc.dram_tensor("ar_out", [128, 1032], F32)
    wout16 = nc.dram_tensor("wout16", [D, D], BF16)

    es = ExitStack()
    SB = lambda name, shape, dt: es.enter_context(nc.sbuf_tensor(name, shape, dt))
    pb = [es.enter_context(nc.psum_tensor(f"pb{i}", [128, 512], F32)) for i in range(8)]
    sem_names = ["sx", "sw", "sg1", "smm", "sv", "ss", "sgp", "scc", "swc", "sca",
                 "sgpa", "sgpb", "swga", "swgb", "swda", "swdb", "swoa", "swob", "ssta", "sstb"]
    sem = {n: es.enter_context(nc.semaphore(n)) for n in sem_names}
    cnt = {n: 0 for n in sem_names}
    block = es.enter_context(nc.Block())

    def on(eng, fn):
        getattr(block, eng)(fn)

    identb = SB("identb", [128, 128], BF16)
    identh = SB("identh", [128, 128], F16)
    ones1 = SB("ones1s", [1, 128], BF16)
    onesc = SB("onescs", [128, 8], BF16)
    selb = SB("selbs", [16, 8 * 128], BF16)
    bqc = SB("bqcs", [128, 8], F32)
    boutr = SB("boutrs", [1, D], BF16)
    q_res = SB("q_res", [128, 8, TCORE], BF16)
    x_slab = SB("x_slab", [128, 4, D], F32)
    stA = SB("stA", [128, 16], F32)
    stB = SB("stB", [128, 16], F32)
    stC = SB("stC", [128, 16], F32)
    stD = SB("stD", [128, 16], F32)
    qv = SB("qv", [128, 1408], F32)
    ql = SB("ql", [128, 1408], F16)
    qh = SB("qh", [128, 1408], F16)
    qsm = SB("qsm", [128, 1408], F16)
    bmx = SB("bmx", [128, 88], F32)
    bsc = SB("bsc", [128, 88], F32)
    brr = SB("brr", [128, 88], F32)
    bdkv = SB("bdkv", [128, 8, 128], BF16)
    ksb8 = SB("ksb8", [128, 8, 16], BF16)

    def p0_sync(sync):
        for dst, src in [(identb, idb_in), (identh, idh_in), (ones1, ones1_in),
                         (onesc, onesc_in), (selb, selb_in), (bqc, bqc_in)]:
            sync.dma_start(out=dst[:, :], in_=src[:, :]).then_inc(sem["sw"], 16)
            cnt["sw"] += 16
    on("sync", p0_sync)
    SW_CONST = cnt["sw"]

    with nc.sbuf_tensor("W_att", [128, 8, 3 * D], BF16) as W_att, \
         nc.sbuf_tensor("ln_sq", [128, 4 * D], F32) as ln_sq, \
         nc.sbuf_tensor("bkvrs", [1, 2 * D], BF16) as bkvr:

        with nc.sbuf_tensor("wqkv_sb", [128, 8, 2 * D], BF16) as wqkv_sb, \
             nc.sbuf_tensor("wf_sb", [64, 64], BF16) as wf_sb, \
             nc.sbuf_tensor("wqt0", [64, 128], BF16) as wqt0, \
             nc.sbuf_tensor("wqt1", [64, 128], BF16) as wqt1, \
             nc.sbuf_tensor("wc16", [128, 1, D], BF16) as wc16:
            wqt = [wqt0, wqt1]

            def p1_gp(g):
                g.dma_start(out=wqkv_sb[:, :, :],
                            in_=wqkv_in[:, 0:2 * D].rearrange("(c p) f -> p c f", p=128)).then_inc(sem["sg1"], 16)
                g.dma_start(out=W_att[:, :, 2 * D:3 * D],
                            in_=wqkv_in[:, 2 * D:3 * D].rearrange("(c p) f -> p c f", p=128)).then_inc(sem["sg1"], 16)
                g.dma_start(out=wf_sb[:, :], in_=wf_in[:, :]).then_inc(sem["sg1"], 16)
                g.dma_start(out=bkvr[:, :], in_=bkvr_in[:, :]).then_inc(sem["sg1"], 16)
                g.dma_start(out=boutr[:, :], in_=bout_in[:, :]).then_inc(sem["sg1"], 16)
                cnt["sg1"] += 80
            on("gpsimd", p1_gp)
            SG1_W = cnt["sg1"]
            SV_VCOPY = cnt["sv"]

            sx_wc0 = cnt["sx"]

            def p1_wc_g(g):
                for p in range(8):
                    if p >= 1:
                        g.wait_ge(sem["sx"], sx_wc0 + 16 * p)
                    g.dma_start(out=wc16[:, 0, :],
                                in_=wout_in[p * 128:(p + 1) * 128, :]).then_inc(sem["swc"], 16)
                    cnt["swc"] += 16
            on("gpsimd", p1_wc_g)

            def p1_wc_s(sync):
                for p in range(8):
                    sync.wait_ge(sem["swc"], 16 * (p + 1))
                    sync.dma_start(out=wout16.ap()[p * 128:(p + 1) * 128, :],
                                   in_=wc16[:, 0, :]).then_inc(sem["sx"], 16)
                    cnt["sx"] += 16
            on("sync", p1_wc_s)
            WC_SX = cnt["sx"]

            ss_base = cnt["ss"]
            smm_base = cnt["smm"]

            def p1_t(t):
                t.wait_ge(sem["sg1"], SG1_W)
                t.wait_ge(sem["sw"], SW_CONST)
                it = 0
                for c in range(8):
                    for qk in range(2):
                        for h in range(16):
                            bA = pb[it % 2]
                            bB = pb[2 + it % 2]
                            if it >= 2:
                                t.wait_ge(sem["ss"], ss_base + 2 * (it - 1))
                            src = wqkv_sb[:, c, qk * D + h * 64: qk * D + (h + 1) * 64]
                            t.matmul(bA[0:64, 0:128], src, identb[:, :],
                                     start=True, stop=True).then_inc(sem["smm"], 1)
                            t.wait_ge(sem["ss"], ss_base + 2 * it + 1)
                            t.matmul(bB[0:128, 0:64], wqt[it % 2][0:64, :], wf_sb[:, :],
                                     start=True, stop=True).then_inc(sem["smm"], 1)
                            it += 1
                cnt["smm"] += 512
            on("tensor", p1_t)

            def p1_s(s):
                it = 0
                for c in range(8):
                    for qk in range(2):
                        for h in range(16):
                            bA = pb[it % 2]
                            bB = pb[2 + it % 2]
                            s.wait_ge(sem["smm"], smm_base + 2 * it + 1)
                            s.copy(wqt[it % 2][0:64, :], bA[0:64, 0:128]).then_inc(sem["ss"], 1)
                            s.wait_ge(sem["smm"], smm_base + 2 * it + 2)
                            s.copy(W_att[:, c, qk * D + h * 64: qk * D + (h + 1) * 64],
                                   bB[0:128, 0:64]).then_inc(sem["ss"], 1)
                            it += 1
                cnt["ss"] += 512
            on("scalar", p1_s)
        SS_FOLD = cnt["ss"]
        SMM_FOLD_END = cnt["smm"]

        def emit_quant_f16(v, src2d, W, out2d, sc_ap, rr_ap):
            nb = W // 16
            v3 = qv[:, 0:W].rearrange("p (b k) -> p b k", k=16)
            v.tensor_tensor(v3, src2d.rearrange("p (b k) -> p b k", k=16),
                            rr_ap.unsqueeze(2).broadcast_to([128, nb, 16]), op=OP.mult)
            v.drain()
            v.tensor_scalar(qsm[:, 0:W], qv[:, 0:W], 0.0, None, op0=OP.is_lt)
            v.drain()
            v.tensor_scalar(qv[:, 0:W].bitcast(I32), qv[:, 0:W].bitcast(I32), 0x7FFFFFFF, None,
                            op0=OP.bitwise_and)
            v.drain()
            a = qv[:, 0:W]
            v.tensor_scalar(qsm[:, 0:W], qsm[:, 0:W], -2.0, 1.0, op0=OP.mult, op1=OP.add)
            v.tensor_scalar(ql[:, 0:W], a, MIDS[0], None, op0=OP.is_gt)
            v.drain()
            v.scalar_tensor_tensor(ql[:, 0:W], a, MIDS[1], ql[:, 0:W], op0=OP.is_gt, op1=OP.add)
            v.drain()
            v.scalar_tensor_tensor(ql[:, 0:W], a, MIDS[2], ql[:, 0:W], op0=OP.is_gt, op1=OP.add)
            v.drain()
            v.scalar_tensor_tensor(ql[:, 0:W], a, MIDS[3], ql[:, 0:W], op0=OP.is_gt, op1=OP.add)
            v.tensor_scalar(qh[:, 0:W], a, MIDS[4], None, op0=OP.is_gt)
            v.drain()
            v.scalar_tensor_tensor(qh[:, 0:W], a, MIDS[5], qh[:, 0:W], op0=OP.is_gt, op1=OP.add)
            v.drain()
            v.scalar_tensor_tensor(qh[:, 0:W], a, MIDS[6], qh[:, 0:W], op0=OP.is_gt, op1=OP.add)
            v.drain()
            v.scalar_tensor_tensor(qh[:, 0:W], a, MIDS[6], qh[:, 0:W], op0=OP.is_gt, op1=OP.add)
            v.drain()
            v.scalar_tensor_tensor(ql[:, 0:W], ql[:, 0:W], 0.5, qh[:, 0:W], op0=OP.mult, op1=OP.add)
            v.drain()
            v.tensor_tensor(qh[:, 0:W].rearrange("p (b k) -> p b k", k=16),
                            ql[:, 0:W].rearrange("p (b k) -> p b k", k=16),
                            sc_ap.unsqueeze(2).broadcast_to([128, nb, 16]), op=OP.mult)
            v.drain()
            v.tensor_tensor(out2d, qh[:, 0:W], qsm[:, 0:W], op=OP.mult)

        def emit_bmax(v, src2d, W, bm_ap, sc_ap, rr_ap):
            v.tensor_reduce(bm_ap, src2d.rearrange("p (b k) -> p b k", k=16),
                            axis=AX.X, op=OP.max, apply_absolute_value=True)
            v.drain()
            v.tensor_scalar(sc_ap, bm_ap, 1.0 / 6.0, 1e-12, op0=OP.mult, op1=OP.max)
            v.drain()
            v.reciprocal(rr_ap, sc_ap)
            v.drain()

        with nc.sbuf_tensor("wg_sb", [128, ICN, 128], F32) as wg_sb, \
             nc.sbuf_tensor("wu_sb", [128, ICN, 128], F32) as wu_sb, \
             nc.sbuf_tensor("wd_sb", [128, WDCH, 8, 128], F32) as wd_sb, \
             nc.sbuf_tensor("wq16a", [128, 128], F16) as wq16a, \
             nc.sbuf_tensor("wq16b", [128, 128], F16) as wq16b, \
             nc.sbuf_tensor("wt16a", [128, 128], F16) as wt16a, \
             nc.sbuf_tensor("wt16b", [128, 128], F16) as wt16b:
            wq16 = [wq16a, wq16b]
            wt16 = [wt16a, wt16b]

            def p2_sync(sync):
                sync.wait_ge(sem["smm"], SMM_FOLD_END)
                sync.wait_ge(sem["sx"], WC_SX)
                sync.dma_start(out=wg_sb[:, :, :],
                               in_=wg_sl_in[:, :].rearrange("(a p) k -> p a k", p=128)).then_inc(sem["sw"], 16)
                sync.dma_start(out=wu_sb[:, :, :],
                               in_=wu_sl_in[:, :].rearrange("(a p) k -> p a k", p=128)).then_inc(sem["sw"], 16)
                sync.dma_start(out=wd_sb[:, :, :, :],
                               in_=wd_sl_in[:, :, :].rearrange("j (a p) k -> p j a k", p=128)).then_inc(sem["sw"], 16)
                cnt["sw"] += 48
            on("sync", p2_sync)
            SW_WSH = cnt["sw"]

            tiles = []
            for j in range(ICN):
                col = (j // 2) * 512 + (j % 2) * 128
                tiles.append((wg_sb[:, j, :], ag_wgu_in.ap()[:, col:col + 128]))
            for j in range(ICN):
                col = (j // 2) * 512 + 256 + (j % 2) * 128
                tiles.append((wu_sb[:, j, :], ag_wgu_in.ap()[:, col:col + 128]))
            for jj in range(WDCH):
                for fc in range(8):
                    tiles.append((wd_sb[:, jj, fc, :], ag_wd_in.ap()[jj, :, fc * 128:(fc + 1) * 128]))
            NT = len(tiles)
            sv0, smm0, ss0, sgp0 = cnt["sv"], cnt["smm"], cnt["ss"], cnt["sgp"]

            def p2_v(v):
                v.wait_ge(sem["sw"], SW_WSH)
                for i, (srcap, dst) in enumerate(tiles):
                    if i >= 2:
                        v.wait_ge(sem["smm"], smm0 + (i - 1))
                    emit_bmax(v, srcap, 128, bmx[:, 0:8], bsc[:, 0:8], brr[:, 0:8])
                    emit_quant_f16(v, srcap, 128, wq16[i % 2][:, :], bsc[:, 0:8], brr[:, 0:8])
                    v.drain()
                    v.engine_nop().then_inc(sem["sv"], 1)
                    cnt["sv"] += 1
            on("vector", p2_v)

            def p2_t(t):
                for i in range(NT):
                    t.wait_ge(sem["sv"], sv0 + i + 1)
                    if i >= 2:
                        t.wait_ge(sem["ss"], ss0 + (i - 1))
                    t.matmul(pb[4 + i % 2][0:128, 0:128], wq16[i % 2][:, :], identh[:, :],
                             start=True, stop=True).then_inc(sem["smm"], 1)
                    cnt["smm"] += 1
            on("tensor", p2_t)

            def p2_s(s):
                for i in range(NT):
                    s.wait_ge(sem["smm"], smm0 + i + 1)
                    if i >= 2:
                        sn = "sgpa" if i % 2 == 0 else "sgpb"
                        s.wait_ge(sem[sn], 16 * ((i - 2) // 2 + 1))
                    s.copy(wt16[i % 2][:, :], pb[4 + i % 2][0:128, 0:128]).then_inc(sem["ss"], 1)
                    cnt["ss"] += 1
            on("scalar", p2_s)

            def p2_dump(sync):
                for i, (src, dst) in enumerate(tiles):
                    sync.wait_ge(sem["ss"], ss0 + i + 1)
                    sn = "sgpa" if i % 2 == 0 else "sgpb"
                    sync.dma_start(out=dst, in_=wt16[i % 2][:, :]).then_inc(sem[sn], 16)
                    cnt[sn] += 16
            on("sync", p2_dump)

            def p2_g(g):
                g.wait_ge(sem["sgpa"], 16 * ((NT + 1) // 2))
                g.wait_ge(sem["sgpb"], 16 * (NT // 2))
                g.collective_compute("AllGather", OP.bypass,
                                     replica_groups=[[0, 1, 2, 3, 4, 5, 6, 7]],
                                     ins=[ag_wgu_in.ap().opt()],
                                     outs=[ag_wgu_out.ap().opt()]).then_inc(sem["sca"], 1)
                cnt["sca"] += 1
                g.collective_compute("AllGather", OP.bypass,
                                     replica_groups=[[0, 1, 2, 3, 4, 5, 6, 7]],
                                     ins=[ag_wd_in.ap().opt()],
                                     outs=[ag_wd_out.ap().opt()]).then_inc(sem["sca"], 1)
                cnt["sca"] += 1
            on("gpsimd", p2_g)
        SCC_AG = cnt["sca"]
        P2_SMM_END = cnt["smm"]
        P2_SGPA_END = cnt["sgpa"]
        P2_SGPB_END = cnt["sgpb"]

        def emit_ln(sq_view):
            def s_sq(sc):
                sc.activation(sq_view, x_slab[:, :, :].rearrange("p c f -> p (c f)"),
                              AF.Square).then_inc(sem["ss"], 1)
                cnt["ss"] += 1
            on("scalar", s_sq)
            tok_sq = cnt["ss"]

            def v_stats(v, tok_sq=tok_sq, sq_view=sq_view):
                for c in range(4):
                    v.tensor_reduce(stA[:, c:c + 1], x_slab[:, c, :], axis=AX.X, op=OP.add)
                v.wait_ge(sem["ss"], tok_sq)
                for c in range(4):
                    v.tensor_reduce(stA[:, 4 + c:5 + c], sq_view[:, c * D:(c + 1) * D],
                                    axis=AX.X, op=OP.add)
                v.drain()
                v.tensor_scalar(stB[:, 0:4], stA[:, 0:4], 1.0 / D, None, op0=OP.mult)
                v.tensor_scalar(stB[:, 4:8], stA[:, 4:8], 1.0 / D, None, op0=OP.mult)
                v.drain()
                v.tensor_tensor(stC[:, 0:4], stB[:, 0:4], stB[:, 0:4], op=OP.mult)
                v.drain()
                v.tensor_tensor(stA[:, 8:12], stB[:, 4:8], stC[:, 0:4], op=OP.subtract)
                v.drain()
                v.tensor_scalar(stB[:, 8:12], stA[:, 8:12], 1e-5, None, op0=OP.add)
                v.drain()
                v.reciprocal(stC[:, 8:12], stB[:, 8:12])
                v.drain()
                v.engine_nop().then_inc(sem["sv"], 1)
                cnt["sv"] += 1
            on("vector", v_stats)
            tok_iv = cnt["sv"]

            def s_sqrt(sc, tok_iv=tok_iv):
                sc.wait_ge(sem["sv"], tok_iv)
                sc.activation(stA[:, 12:16], stC[:, 8:12], AF.Sqrt).then_inc(sem["ss"], 1)
                cnt["ss"] += 1
            on("scalar", s_sqrt)
            tok_r0 = cnt["ss"]

            def v_newton(v, tok_r0=tok_r0):
                v.wait_ge(sem["ss"], tok_r0)
                v.tensor_tensor(stC[:, 12:16], stA[:, 12:16], stA[:, 12:16], op=OP.mult)
                v.drain()
                v.tensor_tensor(stD[:, 0:4], stB[:, 8:12], stC[:, 12:16], op=OP.mult)
                v.drain()
                v.tensor_scalar(stC[:, 0:4], stD[:, 0:4], -0.5, 1.5, op0=OP.mult, op1=OP.add)
                v.drain()
                v.tensor_tensor(stD[:, 8:12], stA[:, 12:16], stC[:, 0:4], op=OP.mult)
                v.drain()
                v.drain()
                v.engine_nop().then_inc(sem["sv"], 1)
                cnt["sv"] += 1
            on("vector", v_newton)
            return cnt["sv"]

        with nc.sbuf_tensor("xnb", [128, 4, D], BF16) as xnb, \
             nc.sbuf_tensor("xnf", [128, 8, MT], BF16) as xnf, \
             nc.sbuf_tensor("k_t0", [128, D], BF16) as k_t0, \
             nc.sbuf_tensor("k_t1", [128, D], BF16) as k_t1, \
             nc.sbuf_tensor("v_t0", [128, D], BF16) as v_t0, \
             nc.sbuf_tensor("v_t1", [128, D], BF16) as v_t1, \
             nc.sbuf_tensor("kv_all", [128, 1032], F32) as kv_all, \
             nc.sbuf_tensor("kvs", [128, 1032], F32) as kvs:
            k_t = [k_t0, k_t1]
            v_t = [v_t0, v_t1]
            tok_x_free = 0
            tok_xnb_free = 0
            tok_xnf_cons = 0
            tok_kt_used = [0, 0]

            for m in range(NMT):
                def a_load(sync, m=m, w=tok_x_free):
                    if w:
                        sync.wait_ge(sem["sv"], w)
                    sync.dma_start(out=x_slab[:, :, :],
                                   in_=x_in[m * MT:(m + 1) * MT, :].rearrange("(c p) f -> p c f", p=128)
                                   ).then_inc(sem["sx"], 16)
                    cnt["sx"] += 16
                on("sync", a_load)
                tok_x = cnt["sx"]

                def a_wx_v(v, t=tok_x):
                    v.wait_ge(sem["sx"], t)
                on("vector", a_wx_v)

                def a_wx_s(s, t=tok_x):
                    s.wait_ge(sem["sx"], t)
                on("scalar", a_wx_s)

                tok_rstd = emit_ln(ln_sq[:, :])

                def a_xn(v, w=tok_xnb_free, tr=tok_rstd, m=m):
                    v.wait_ge(sem["sv"], tr)
                    if m == 0:
                        v.wait_ge(sem["smm"], P2_SMM_END)
                        v.wait_ge(sem["sgpa"], P2_SGPA_END)
                        v.wait_ge(sem["sgpb"], P2_SGPB_END)
                    if w:
                        v.wait_ge(sem["smm"], w)
                    for c in range(4):
                        v.tensor_scalar(xnb[:, c, :], x_slab[:, c, :],
                                        stB[:, c:c + 1], stD[:, 8 + c:9 + c],
                                        op0=OP.subtract, op1=OP.mult)
                    v.drain()
                    v.engine_nop().then_inc(sem["sv"], 1)
                    cnt["sv"] += 1
                on("vector", a_xn)
                tok_xn = cnt["sv"]
                tok_x_free = tok_xn

                smm_tp0 = cnt["smm"]
                ss_tp0 = cnt["ss"]

                def a_tp_t(t, tok_xn=tok_xn, smm_tp0=smm_tp0, ss_tp0=ss_tp0, m=m):
                    t.wait_ge(sem["sv"], tok_xn)
                    if m == 0:
                        t.wait_ge(sem["ss"], SS_FOLD)
                        t.wait_ge(sem["sv"], SV_VCOPY)
                    it = 0
                    for c in range(4):
                        for j in range(8):
                            bA = pb[it % 2]
                            if it >= 2:
                                t.wait_ge(sem["ss"], ss_tp0 + (it - 1))
                            t.matmul(bA[0:128, 0:128], xnb[:, c, j * 128:(j + 1) * 128],
                                     identb[:, :], start=True, stop=True).then_inc(sem["smm"], 1)
                            it += 1
                    cnt["smm"] += 32
                on("tensor", a_tp_t)
                tok_xnb_free = cnt["smm"]

                def a_tp_s(s, smm_tp0=smm_tp0, w=tok_xnf_cons):
                    if w:
                        s.wait_ge(sem["smm"], w)
                    it = 0
                    for c in range(4):
                        for j in range(8):
                            bA = pb[it % 2]
                            s.wait_ge(sem["smm"], smm_tp0 + it + 1)
                            s.copy(xnf[:, j, c * 128:(c + 1) * 128], bA[0:128, 0:128]).then_inc(sem["ss"], 1)
                            it += 1
                    cnt["ss"] += 32
                on("scalar", a_tp_s)
                tok_tp_evac = cnt["ss"]

                smm_q0 = cnt["smm"]
                ss_q0 = cnt["ss"]

                def a_q_t(t, tok_tp_evac=tok_tp_evac, ss_q0=ss_q0):
                    t.wait_ge(sem["ss"], tok_tp_evac)
                    for mc in range(8):
                        bA = pb[mc % 2]
                        if mc >= 2:
                            t.wait_ge(sem["ss"], ss_q0 + (mc - 1))
                        for kc in range(8):
                            ins = t.matmul(bA[:, 0:512], W_att[:, kc, mc * 128:(mc + 1) * 128],
                                           xnf[:, kc, :], start=(kc == 0), stop=(kc == 7))
                        ins.then_inc(sem["smm"], 1)
                        cnt["smm"] += 1
                on("tensor", a_q_t)

                def a_q_s(s, smm_q0=smm_q0, m=m):
                    for mc in range(8):
                        bA = pb[mc % 2]
                        s.wait_ge(sem["smm"], smm_q0 + mc + 1)
                        s.activation(q_res[:, mc, m * MT:(m + 1) * MT], bA[:, 0:512], AF.Relu,
                                     bias=bqc[:, mc:mc + 1]).then_inc(sem["ss"], 1)
                        cnt["ss"] += 1
                on("scalar", a_q_s)
                tok_q_evac = cnt["ss"]

                for tc in range(4):
                    par = tc % 2
                    smm_kv0 = cnt["smm"]
                    ss_kv0 = cnt["ss"]

                    def a_kv_t(t, tc=tc, smm_kv0=smm_kv0, ss_kv0=ss_kv0,
                               tok_q_evac=tok_q_evac):
                        if tc == 0:
                            t.wait_ge(sem["ss"], tok_q_evac)
                        else:
                            t.wait_ge(sem["ss"], ss_kv0)
                        for grp in range(4):
                            bA = pb[2 + grp % 2]
                            if grp >= 2:
                                t.wait_ge(sem["ss"], ss_kv0 + (grp - 1))
                            for kc in range(8):
                                last = (kc == 7) and (grp >= 2)
                                ins = t.matmul(bA[:, 0:512], xnf[:, kc, tc * 128:(tc + 1) * 128],
                                               W_att[:, kc, D + grp * 512: D + (grp + 1) * 512],
                                               start=(kc == 0), stop=last)
                            if grp < 2:
                                ins = t.matmul(bA[:, 0:512], ones1[0:1, :],
                                               bkvr[0:1, grp * 512:(grp + 1) * 512],
                                               start=False, stop=True)
                            ins.then_inc(sem["smm"], 1)
                            cnt["smm"] += 1
                    on("tensor", a_kv_t)
                    tok_kv_grps = cnt["smm"]

                    def a_kv_s(s, par=par, smm_kv0=smm_kv0, w=tok_kt_used[par]):
                        if w:
                            s.wait_ge(sem["smm"], w)
                        for grp in range(4):
                            bA = pb[2 + grp % 2]
                            s.wait_ge(sem["smm"], smm_kv0 + grp + 1)
                            if grp < 2:
                                s.activation(k_t[par][:, grp * 512:(grp + 1) * 512], bA[:, 0:512],
                                             AF.Relu).then_inc(sem["ss"], 1)
                            else:
                                s.copy(v_t[par][:, (grp - 2) * 512:(grp - 1) * 512],
                                       bA[:, 0:512]).then_inc(sem["ss"], 1)
                            cnt["ss"] += 1
                    on("scalar", a_kv_s)
                    tok_kv_evac = cnt["ss"]

                    def a_acc_t(t, m=m, tc=tc, par=par, tok_kv_evac=tok_kv_evac):
                        t.wait_ge(sem["ss"], tok_kv_evac)
                        first = (m == 0 and tc == 0)
                        lastt = (m == NMT - 1 and tc == 3)
                        for p in range(8):
                            t.matmul(pb[4 + p // 4][:, (p % 4) * 128:(p % 4 + 1) * 128],
                                     k_t[par][:, p * 128:(p + 1) * 128],
                                     v_t[par][:, p * 128:(p + 1) * 128],
                                     start=first, stop=lastt, skip_group_check=True)
                            ins = t.matmul(pb[6][:, p:p + 1],
                                           k_t[par][:, p * 128:(p + 1) * 128],
                                           onesc[:, 0:1], start=first, stop=lastt,
                                           skip_group_check=True)
                        ins.then_inc(sem["smm"], 1)
                        cnt["smm"] += 1
                    on("tensor", a_acc_t)
                    tok_kt_used[par] = cnt["smm"]
                tok_xnf_cons = cnt["smm"]
            tok_kv_done = cnt["smm"]

            def ar_v(v, tok_kv_done=tok_kv_done):
                v.wait_ge(sem["smm"], tok_kv_done)
                v.tensor_copy(kvs[:, 0:512], pb[4][:, :])
                v.tensor_copy(kvs[:, 512:1024], pb[5][:, :])
                v.tensor_copy(kvs[:, 1024:1032], pb[6][:, 0:8])
                v.drain()
                v.drain()
                v.engine_nop().then_inc(sem["sv"], 1)
                cnt["sv"] += 1
            on("vector", ar_v)
            tok_kvs = cnt["sv"]

            def ar_g(g, tok_kvs=tok_kvs):
                g.wait_ge(sem["sv"], tok_kvs)
                g.dma_start(out=ar_in.ap()[:, :], in_=kvs[:, :]).then_inc(sem["sgp"], 16)
                cnt["sgp"] += 16
                g.wait_ge(sem["sgp"], cnt["sgp"])
                g.collective_compute("AllReduce", OP.add,
                                     replica_groups=[[0, 1, 2, 3], [4, 5, 6, 7]],
                                     ins=[ar_in.ap().opt()],
                                     outs=[ar_out.ap().opt()]).then_inc(sem["scc"], 1)
                cnt["scc"] += 1
                g.wait_ge(sem["scc"], cnt["scc"])
                g.dma_start(out=kv_all[:, :], in_=ar_out.ap()[:, :]).then_inc(sem["sgp"], 16)
                cnt["sgp"] += 16
            on("gpsimd", ar_g)
            tok_kvall = cnt["sgp"]

            def ar_build(v, tok_kvall=tok_kvall):
                v.wait_ge(sem["sgp"], tok_kvall)
                v.memset(bdkv[:, :, :], 0.0)
                v.memset(ksb8[:, :, :], 0.0)
                v.drain()
                for p in range(8):
                    v.tensor_copy(bdkv[0:64, p, 0:64], kv_all[0:64, p * 128: p * 128 + 64])
                    v.tensor_copy(bdkv[64:128, p, 64:128], kv_all[64:128, p * 128 + 64: p * 128 + 128])
                    v.tensor_copy(ksb8[0:64, p, 2 * p:2 * p + 1], kv_all[0:64, 1024 + p:1025 + p])
                    v.tensor_copy(ksb8[64:128, p, 2 * p + 1:2 * p + 2], kv_all[64:128, 1024 + p:1025 + p])
                v.drain()
                v.drain()
                v.engine_nop().then_inc(sem["sv"], 1)
                cnt["sv"] += 1
            on("vector", ar_build)
            tok_attc = cnt["sv"]
            tok_x_free_ab = tok_x_free

            DUMPA_SGP = 0
            if DBG:
                def dump_a(g, tok_attc=tok_attc):
                    g.wait_ge(sem["sv"], tok_attc)
                    g.dma_start(out=dbg_watt, in_=W_att[:, :, :]).then_inc(sem["sgp"], 16)
                    g.dma_start(out=dbg_xnf, in_=xnf[:, :, :]).then_inc(sem["sgp"], 16)
                    g.dma_start(out=dbg_xnb, in_=xnb[:, :, :]).then_inc(sem["sgp"], 16)
                    g.dma_start(out=dbg_kvall, in_=kv_all[:, :]).then_inc(sem["sgp"], 16)
                    cnt["sgp"] += 64
                on("gpsimd", dump_a)
                DUMPA_SGP = cnt["sgp"]

    wgu_sb = SB("wgu_sb", [128, 2, 8, 512], F16)
    wd_ic = [SB("wd_ic0", [128, D], F16), SB("wd_ic1", [128, D], F16)]
    wout_p = [SB("wout_p0", [128, D], BF16), SB("wout_p1", [128, D], BF16)]
    xn32 = SB("xn32", [128, D], F32)
    xq_t1 = SB("xq_tok0", [128, D], F16)
    xq_tok = [xq_t1, xq_t1]
    xq_feat = SB("xq_feat", [128, 8, MT], F16)
    h_tok = SB("h_tok", [128, 4, FI], F16)
    out_sb = [SB("out_sb0", [128, D], F32), SB("out_sb1", [128, D], F32)]
    hqv = qv[:, :].bitcast(F16)
    hqx = [hqv, hqv]
    hqT = SB("hqT", [128, ICN, MT], F16)
    sgt = [SB("sgt0", [128, 256], F16), SB("sgt1", [128, 256], F16)]
    attn2 = SB("attn2", [128, 8, MT], BF16)
    rbx = [SB("rb0", [128, 512], BF16), SB("rb1", [128, 512], BF16)]
    nplus = SB("nplus", [16, 512], F32)
    nrec16 = SB("nrec16", [16, 512], BF16)
    sq_view = hqT[:, :, :].rearrange("p a f -> p (a f)").bitcast(F32)[:, 0:4 * D]

    dbg_cap_gup = SB("dbg_cap_gup", [128, 512], F16) if DBG else None
    dbg_cap_sgt = SB("dbg_cap_sgt", [128, 256], F16) if DBG else None
    tok_x_free_b = tok_attc
    tok_wgu_free = [0, 0]
    tok_hqT_free = 0
    tok_sgt_used = [0, 0]
    tok_bank_free = [0, 0]
    tok_outsb_free = [0, 0]
    wd_used_tok = [0, 0]
    wo_used = [0, 0]
    sta_prev = 0
    stb_prev = 0

    for m in range(NMT):
        def b_load(sync, m=m, w=tok_x_free_b):
            if w:
                sync.wait_ge(sem["sv"], w)
            sync.dma_start(out=x_slab[:, :, :],
                           in_=x_in[m * MT:(m + 1) * MT, :].rearrange("(c p) f -> p c f", p=128)
                           ).then_inc(sem["sx"], 16)
            cnt["sx"] += 16
        on("sync", b_load)
        tok_xb = cnt["sx"]

        def b_wx_v(v, t=tok_xb, thq=tok_hqT_free, dg=(DUMPA_SGP if m == 0 else 0)):
            if dg:
                v.wait_ge(sem["sgp"], dg)
            v.wait_ge(sem["sx"], t)
            if thq:
                v.wait_ge(sem["smm"], thq)
        on("vector", b_wx_v)

        def b_wx_s(s, t=tok_xb, thq=tok_hqT_free):
            s.wait_ge(sem["sx"], t)
            if thq:
                s.wait_ge(sem["smm"], thq)
        on("scalar", b_wx_s)

        def emit_ln_b():
            def s_sq(sc):
                sc.activation(sq_view, x_slab[:, :, :].rearrange("p c f -> p (c f)"),
                              AF.Square).then_inc(sem["ss"], 1)
                cnt["ss"] += 1
            on("scalar", s_sq)
            tok_sq = cnt["ss"]

            def v_stats(v, tok_sq=tok_sq):
                for c in range(4):
                    v.tensor_reduce(stA[:, c:c + 1], x_slab[:, c, :], axis=AX.X, op=OP.add)
                v.wait_ge(sem["ss"], tok_sq)
                for c in range(4):
                    v.tensor_reduce(stA[:, 4 + c:5 + c], sq_view[:, c * D:(c + 1) * D],
                                    axis=AX.X, op=OP.add)
                v.drain()
                v.tensor_scalar(stB[:, 0:4], stA[:, 0:4], 1.0 / D, None, op0=OP.mult)
                v.tensor_scalar(stB[:, 4:8], stA[:, 4:8], 1.0 / D, None, op0=OP.mult)
                v.drain()
                v.tensor_tensor(stC[:, 0:4], stB[:, 0:4], stB[:, 0:4], op=OP.mult)
                v.drain()
                v.tensor_tensor(stA[:, 8:12], stB[:, 4:8], stC[:, 0:4], op=OP.subtract)
                v.drain()
                v.tensor_scalar(stB[:, 8:12], stA[:, 8:12], 1e-5, None, op0=OP.add)
                v.drain()
                v.reciprocal(stC[:, 8:12], stB[:, 8:12])
                v.drain()
                v.engine_nop().then_inc(sem["sv"], 1)
                cnt["sv"] += 1
            on("vector", v_stats)
            tok_iv = cnt["sv"]

            def s_sqrt(sc, tok_iv=tok_iv):
                sc.wait_ge(sem["sv"], tok_iv)
                sc.activation(stA[:, 12:16], stC[:, 8:12], AF.Sqrt).then_inc(sem["ss"], 1)
                cnt["ss"] += 1
            on("scalar", s_sqrt)
            tok_r0 = cnt["ss"]

            def v_newton(v, tok_r0=tok_r0):
                v.wait_ge(sem["ss"], tok_r0)
                v.tensor_tensor(stC[:, 12:16], stA[:, 12:16], stA[:, 12:16], op=OP.mult)
                v.drain()
                v.tensor_tensor(stD[:, 0:4], stB[:, 8:12], stC[:, 12:16], op=OP.mult)
                v.drain()
                v.tensor_scalar(stC[:, 0:4], stD[:, 0:4], -0.5, 1.5, op0=OP.mult, op1=OP.add)
                v.drain()
                v.tensor_tensor(stD[:, 8:12], stA[:, 12:16], stC[:, 0:4], op=OP.mult)
                v.drain()
                v.drain()
                v.engine_nop().then_inc(sem["sv"], 1)
                cnt["sv"] += 1
            on("vector", v_newton)
            return cnt["sv"]

        tok_rstd = emit_ln_b()

        smm_xq0 = cnt["smm"]
        ss_xq0 = cnt["ss"]
        for c in range(4):
            par = c % 2

            def b_xq_v(v, c=c, par=par, tr=tok_rstd, smm_xq0=smm_xq0):
                if c == 0:
                    v.wait_ge(sem["sv"], tr)
                if c >= 1:
                    v.wait_ge(sem["smm"], smm_xq0 + c * 8)
                v.tensor_scalar(xn32[:, :], x_slab[:, c, :], stB[:, c:c + 1], stD[:, 8 + c:9 + c],
                                op0=OP.subtract, op1=OP.mult)
                v.drain()
                emit_bmax(v, xn32[:, :], D, bmx[:, 0:64], bsc[:, 0:64], brr[:, 0:64])
                emit_quant_f16(v, xn32[:, :], D, xq_tok[par][:, :], bsc[:, 0:64], brr[:, 0:64])
                v.drain()
                v.drain()
                v.engine_nop().then_inc(sem["sv"], 1)
                cnt["sv"] += 1
            on("vector", b_xq_v)
            tok_xq = cnt["sv"]

            def b_xq_t(t, c=c, par=par, tok_xq=tok_xq, ss_xq0=ss_xq0, m=m,
                       stw=((sta_prev, stb_prev) if (m >= 1 and c == 0) else None)):
                t.wait_ge(sem["sv"], tok_xq)
                if m == 0 and c == 0:
                    t.wait_ge(sem["sv"], tok_attc)
                if stw:
                    t.wait_ge(sem["ssta"], stw[0])
                    t.wait_ge(sem["sstb"], stw[1])
                for j in range(8):
                    it = c * 8 + j
                    bA = pb[it % 2]
                    if it >= 2:
                        t.wait_ge(sem["ss"], ss_xq0 + (it - 1))
                    t.matmul(bA[0:128, 0:128], xq_tok[par][:, j * 128:(j + 1) * 128], identh[:, :],
                             start=True, stop=True).then_inc(sem["smm"], 1)
                    cnt["smm"] += 1
            on("tensor", b_xq_t)

            def b_xq_s(s, c=c, smm_xq0=smm_xq0):
                for j in range(8):
                    it = c * 8 + j
                    bA = pb[it % 2]
                    s.wait_ge(sem["smm"], smm_xq0 + it + 1)
                    s.copy(xq_feat[:, j, c * 128:(c + 1) * 128], bA[0:128, 0:128]).then_inc(sem["ss"], 1)
                    cnt["ss"] += 1
            on("scalar", b_xq_s)
        tok_xqf = cnt["ss"]

        if DBG:
            def dump_xq(sy, m=m, t=tok_xqf):
                sy.wait_ge(sem["ss"], t)
                sy.dma_start(out=dbg_xqall[m], in_=xq_feat[:, :, :]).then_inc(sem["sgp"], 16)
                cnt["sgp"] += 16
            on("sync", dump_xq)

        for ig in range(NIG):
            par = ig % 2

            def b_wgu_load(sync, ig=ig, par=par, w=tok_wgu_free[par],
                           first=(m == 0 and ig < 2)):
                if first:
                    sync.wait_ge(sem["sca"], SCC_AG)
                if w:
                    sync.wait_ge(sem["smm"], w)
                sn = "swga" if par == 0 else "swgb"
                sync.dma_start(out=wgu_sb[:, par, :, :],
                               in_=ag_wgu_out.ap()[:, :, ig * 512:(ig + 1) * 512]
                               .rearrange("c p f -> p c f")).then_inc(sem[sn], 16)
                cnt[sn] += 16
            on("sync", b_wgu_load)
            tok_wgu = cnt["swga" if par == 0 else "swgb"]
            sem_wgu = "swga" if par == 0 else "swgb"

            for tcg in range(4):
                bank = pb[2 + tcg % 2]

                def b_gu_t(t, par=par, tcg=tcg, bank=bank, tok_wgu=tok_wgu,
                           tok_xqf=tok_xqf, need_free=tok_bank_free[tcg % 2], sem_wgu=sem_wgu,
                           first=(ig == 0 and tcg == 0)):
                    t.wait_ge(sem[sem_wgu], tok_wgu)
                    if first:
                        t.wait_ge(sem["ss"], tok_xqf)
                    if need_free:
                        t.wait_ge(sem["sv"], need_free)
                    for kc in range(8):
                        ins = t.matmul(bank[:, 0:512],
                                       xq_feat[:, kc, tcg * 128:(tcg + 1) * 128],
                                       wgu_sb[:, par, kc, :],
                                       start=(kc == 0), stop=(kc == 7))
                    ins.then_inc(sem["smm"], 1)
                    cnt["smm"] += 1
                on("tensor", b_gu_t)
                tok_gu = cnt["smm"]

                def b_silu_s(s, bank=bank, sl=tcg % 2, tok_gu=tok_gu, w=tok_sgt_used[tcg % 2]):
                    s.wait_ge(sem["smm"], tok_gu)
                    if w:
                        s.wait_ge(sem["sv"], w)
                    s.activation(sgt[sl][:, :], bank[:, 0:256], AF.Silu)
                    s.drain()
                    s.nop().then_inc(sem["ss"], 1)
                    cnt["ss"] += 1
                on("scalar", b_silu_s)
                tok_silu = cnt["ss"]

                def b_h_v(v, bank=bank, sl=tcg % 2, tcg=tcg, ig=ig, tok_silu=tok_silu):
                    v.wait_ge(sem["ss"], tok_silu)
                    v.tensor_tensor(h_tok[:, tcg, ig * 256:(ig + 1) * 256], sgt[sl][:, :],
                                    bank[:, 256:512], op=OP.mult)
                    v.drain()
                    v.engine_nop().then_inc(sem["sv"], 1)
                    cnt["sv"] += 1
                on("vector", b_h_v)
                tok_sgt_used[tcg % 2] = cnt["sv"]
                tok_bank_free[tcg % 2] = cnt["sv"]
            tok_wgu_free[par] = cnt["smm"]
        tok_h_done = cnt["sv"]

        for tcg in range(4):
            for hf in range(2):
                src = h_tok[:, tcg, hf * 1408:(hf + 1) * 1408]
                par = (tcg * 2 + hf) % 2

                def b_hq_v(v, src=src, par=par, tok_h_done=tok_h_done, need=cnt["smm"]):
                    v.wait_ge(sem["sv"], tok_h_done)
                    if need:
                        v.wait_ge(sem["smm"], need)
                    emit_bmax(v, src, 1408, bmx[:, 0:88], bsc[:, 0:88], brr[:, 0:88])
                    emit_quant_f16(v, src, 1408, hqx[par][:, 0:1408], bsc[:, 0:88], brr[:, 0:88])
                    v.drain()
                    v.engine_nop().then_inc(sem["sv"], 1)
                    cnt["sv"] += 1
                on("vector", b_hq_v)
                tok_hq = cnt["sv"]

                smm_ht0 = cnt["smm"]
                ss_ht0 = cnt["ss"]

                def b_hq_t(t, par=par, tok_hq=tok_hq, ss_ht0=ss_ht0):
                    t.wait_ge(sem["sv"], tok_hq)
                    t.wait_ge(sem["ss"], ss_ht0)
                    for j in range(11):
                        bA = pb[4 + j % 2]
                        if j >= 2:
                            t.wait_ge(sem["ss"], ss_ht0 + (j - 1))
                        t.matmul(bA[0:128, 0:128], hqx[par][:, j * 128:(j + 1) * 128],
                                 identh[:, :], start=True, stop=True).then_inc(sem["smm"], 1)
                        cnt["smm"] += 1
                on("tensor", b_hq_t)

                def b_hq_s(s, tcg=tcg, hf=hf, smm_ht0=smm_ht0):
                    for j in range(11):
                        bA = pb[4 + j % 2]
                        s.wait_ge(sem["smm"], smm_ht0 + j + 1)
                        s.copy(hqT[:, hf * 11 + j, tcg * 128:(tcg + 1) * 128],
                               bA[0:128, 0:128]).then_inc(sem["ss"], 1)
                        cnt["ss"] += 1
                on("scalar", b_hq_s)
        tok_hqT_done = cnt["ss"]


        def b_norm_t(t, m=m, prev=cnt["ss"]):
            if m == 0:
                t.wait_ge(sem["sv"], tok_attc)
            t.wait_ge(sem["ss"], prev)
            for p in range(8):
                ins = t.matmul(pb[6][0:16, 0:512], ksb8[:, p, :], q_res[:, p, m * MT:(m + 1) * MT],
                               start=(p == 0), stop=(p == 7))
            ins.then_inc(sem["smm"], 1)
            cnt["smm"] += 1
        on("tensor", b_norm_t)
        tok_norm = cnt["smm"]

        def b_nrec_v(v, tok_norm=tok_norm):
            v.wait_ge(sem["smm"], tok_norm)
            v.tensor_scalar(nplus[:, :], pb[6][0:16, 0:512], 1e-6, None, op0=OP.add)
            v.drain()
            v.reciprocal(nplus[:, :], nplus[:, :])
            v.drain()
            v.tensor_copy(nrec16[:, :], nplus[:, :])
            v.drain()
            v.engine_nop().then_inc(sem["sv"], 1)
            cnt["sv"] += 1
        on("vector", b_nrec_v)
        tok_nrec = cnt["sv"]

        ss_rb0 = cnt["ss"]
        sv_at0 = cnt["sv"]
        for p in range(8):
            par = p % 2

            def b_rb_t(t, p=p, par=par, tok_nrec=tok_nrec, ss_rb0=ss_rb0, sv_at0=sv_at0, m=m):
                if p == 0:
                    t.wait_ge(sem["sv"], tok_nrec)
                if p >= 2:
                    t.wait_ge(sem["ss"], ss_rb0 + (p - 1))
                    t.wait_ge(sem["sv"], sv_at0 + (p - 1))
                t.matmul(pb[6 + par][:, 0:512], selb[:, p * 128:(p + 1) * 128], nrec16[:, :],
                         start=True, stop=True).then_inc(sem["smm"], 1)
                cnt["smm"] += 1
                t.matmul(pb[2 + par][:, 0:512], bdkv[:, p, :], q_res[:, p, m * MT:(m + 1) * MT],
                         start=True, stop=True).then_inc(sem["smm"], 1)
                cnt["smm"] += 1
            on("tensor", b_rb_t)
            tok_rbmm = cnt["smm"]

            def b_rb_s(s, par=par, tok_rbmm=tok_rbmm):
                s.wait_ge(sem["smm"], tok_rbmm - 1)
                s.copy(rbx[par][:, :], pb[6 + par][:, 0:512])
                s.drain()
                s.nop().then_inc(sem["ss"], 1)
                cnt["ss"] += 1
            on("scalar", b_rb_s)
            tok_rbs = cnt["ss"]

            def b_at_v(v, p=p, par=par, tok_rbmm=tok_rbmm, tok_rbs=tok_rbs):
                v.wait_ge(sem["smm"], tok_rbmm)
                v.wait_ge(sem["ss"], tok_rbs)
                v.tensor_tensor(attn2[:, p, :], pb[2 + par][:, 0:512], rbx[par][:, :], op=OP.mult)
                v.drain()
                v.drain()
                v.engine_nop().then_inc(sem["sv"], 1)
                cnt["sv"] += 1
            on("vector", b_at_v)
        tok_attn2 = cnt["sv"]
        tok_bank_free[0] = tok_attn2
        tok_bank_free[1] = tok_attn2

        if DBG:
            def dump_at2(sy, m=m, t=tok_attn2):
                sy.wait_ge(sem["sv"], t)
                sy.dma_start(out=dbg_at2all[m], in_=attn2[:, :, :]).then_inc(sem["sgp"], 16)
                cnt["sgp"] += 16
            on("sync", dump_at2)

        for ic in range(ICN):
            par = ic % 2

            def b_wd_l(sync, ic=ic, par=par, w=wd_used_tok[par], m=m):
                if m == 0 and ic == 0:
                    sync.wait_ge(sem["sca"], SCC_AG)
                if w:
                    sync.wait_ge(sem["smm"], w)
                sn = "swda" if par == 0 else "swdb"
                sync.dma_start(out=wd_ic[par][:, :],
                               in_=ag_wd_out.ap()[ic % 8, ic // 8, :, :]).then_inc(sem[sn], 16)
                cnt[sn] += 16
            on("sync", b_wd_l)
            wd_tok = cnt["swda" if par == 0 else "swdb"]
            sem_wd = "swda" if par == 0 else "swdb"

            def b_wd_t(t, ic=ic, par=par, wt=wd_tok, ta=tok_attn2, tok_hqT_done=tok_hqT_done,
                       sem_wd=sem_wd):
                t.wait_ge(sem[sem_wd], wt)
                if ic == 0:
                    t.wait_ge(sem["sv"], ta)
                    t.wait_ge(sem["ss"], tok_hqT_done)
                for tcg in range(4):
                    for fg in range(2):
                        ins = t.matmul(pb[tcg * 2 + fg][:, 0:512],
                                       hqT[:, ic, tcg * 128:(tcg + 1) * 128],
                                       wd_ic[par][:, fg * 512:(fg + 1) * 512],
                                       start=(ic == 0), stop=False)
                ins.then_inc(sem["smm"], 1)
                cnt["smm"] += 1
            on("tensor", b_wd_t)
            wd_used_tok[par] = cnt["smm"]

        for p in range(8):
            par = p % 2

            def b_wo_l(sync, p=p, par=par, w=wo_used[par], first=(m == 0 and p < 2)):
                if first:
                    sync.wait_ge(sem["sx"], WC_SX)
                if w:
                    sync.wait_ge(sem["smm"], w)
                sn = "swoa" if par == 0 else "swob"
                sync.dma_start(out=wout_p[par][:, :],
                               in_=wout16.ap()[p * 128:(p + 1) * 128, :]).then_inc(sem[sn], 16)
                cnt[sn] += 16
            on("sync", b_wo_l)
            wo_tok = cnt["swoa" if par == 0 else "swob"]
            sem_wo = "swoa" if par == 0 else "swob"

            def b_wo_t(t, p=p, par=par, wt=wo_tok, sem_wo=sem_wo):
                t.wait_ge(sem[sem_wo], wt)
                for tcg in range(4):
                    for fg in range(2):
                        ins = t.matmul(pb[tcg * 2 + fg][:, 0:512],
                                       attn2[:, p, tcg * 128:(tcg + 1) * 128],
                                       wout_p[par][:, fg * 512:(fg + 1) * 512],
                                       start=False, stop=False)
                ins.then_inc(sem["smm"], 1)
                cnt["smm"] += 1
            on("tensor", b_wo_t)
            wo_used[par] = cnt["smm"]

        out_mm_tok = []

        def b_bias_t(t):
            for tcg in range(4):
                for fg in range(2):
                    t.matmul(pb[tcg * 2 + fg][:, 0:512], ones1[0:1, :],
                             boutr[0:1, fg * 512:(fg + 1) * 512],
                             start=False, stop=True).then_inc(sem["smm"], 1)
                    cnt["smm"] += 1
                    out_mm_tok.append(cnt["smm"])
        on("tensor", b_bias_t)

        for tcg in range(4):
            par = tcg % 2

            def b_oev_v(v, tcg=tcg, par=par, w=tok_outsb_free[par],
                        tokk=out_mm_tok[tcg * 2 + 1]):
                v.wait_ge(sem["smm"], tokk)
                if w:
                    v.wait_ge(sem["ssta" if par == 0 else "sstb"], w)
                for fg in range(2):
                    v.tensor_tensor(out_sb[par][:, fg * 512:(fg + 1) * 512],
                                    pb[tcg * 2 + fg][:, 0:512],
                                    x_slab[:, tcg, fg * 512:(fg + 1) * 512], op=OP.add)
                v.drain()
                v.drain()
                v.engine_nop().then_inc(sem["sv"], 1)
                cnt["sv"] += 1
            on("vector", b_oev_v)
            tok_oev = cnt["sv"]

            def b_store(sync, tcg=tcg, par=par, m=m, tok_oev=tok_oev):
                sync.wait_ge(sem["sv"], tok_oev)
                sn = "ssta" if par == 0 else "sstb"
                sync.dma_start(out=out_d[m * MT + tcg * 128: m * MT + (tcg + 1) * 128, :],
                               in_=out_sb[par][:, :]).then_inc(sem[sn], 16)
                cnt[sn] += 16
            on("sync", b_store)
            tok_outsb_free[par] = cnt["ssta" if par == 0 else "sstb"]
        sta_prev = cnt["ssta"]
        stb_prev = cnt["sstb"]
        tok_x_free_b = cnt["sv"]
        tok_bank_free[0] = cnt["sv"]
        tok_bank_free[1] = cnt["sv"]
        tok_hqT_free = cnt["smm"]

    SSTA_TOT = cnt["ssta"]
    SSTB_TOT = cnt["sstb"]

    if DBG:
        def dump(g):
            g.wait_ge(sem["ssta"], SSTA_TOT)
            g.wait_ge(sem["sstb"], SSTB_TOT)
            g.dma_start(out=dbg_agwgu, in_=ag_wgu_out.ap()[:, :, :]).then_inc(sem["sgp"], 16)
            cnt["sgp"] += 16
            g.dma_start(out=dbg_wgusb, in_=wgu_sb[:, :, :, :]).then_inc(sem["sgp"], 16)
            cnt["sgp"] += 16
            g.dma_start(out=dbg_gup, in_=dbg_cap_gup[:, :]).then_inc(sem["sgp"], 16)
            cnt["sgp"] += 16
            g.dma_start(out=dbg_sgt, in_=dbg_cap_sgt[:, :]).then_inc(sem["sgp"], 16)
            cnt["sgp"] += 16
            for dst, srct in [(dbg_qres, q_res), (dbg_bdkv, bdkv), (dbg_ksb8, ksb8),
                              (dbg_xqf, xq_feat), (dbg_hqT, hqT), (dbg_attn2, attn2),
                              (dbg_htok, h_tok), (dbg_nrec, nplus)]:
                g.dma_start(out=dst, in_=srct[:, :] if len(dst.shape) == 2 else srct[:, :, :]).then_inc(sem["sgp"], 16)
                cnt["sgp"] += 16
            g.dma_start(out=dbg_st[:, 0:16], in_=stA[:, :]).then_inc(sem["sgp"], 16)
            g.dma_start(out=dbg_st[:, 16:32], in_=stB[:, :]).then_inc(sem["sgp"], 16)
            g.dma_start(out=dbg_st[:, 32:48], in_=stC[:, :]).then_inc(sem["sgp"], 16)
            g.dma_start(out=dbg_st[:, 48:64], in_=stD[:, :]).then_inc(sem["sgp"], 16)
            cnt["sgp"] += 64
            g.wait_ge(sem["sgp"], cnt["sgp"])
        on("gpsimd", dump)

    def fin(sync):
        sync.wait_ge(sem["ssta"], SSTA_TOT)
        sync.wait_ge(sem["sstb"], SSTB_TOT)
        sync.wait_ge(sem["scc"], cnt["scc"])
        sync.wait_ge(sem["sca"], cnt["sca"])
        if DBG:
            sync.wait_ge(sem["sgp"], cnt["sgp"])
    on("sync", fin)

    es.close()
    return nc


_NC_CACHE = {}


def kernel(L0_attn_x, L0_attn_W_qkv, L0_attn_W_f, L0_attn_b_f, L0_attn_W_out,
           L0_attn_b_out, L0_ffn_w_gate, L0_ffn_w_up, L0_ffn_w_down):
    x = np.ascontiguousarray(np.asarray(L0_attn_x, dtype=np.float32)).reshape(NCORE, TCORE, D)
    wqkv = np.ascontiguousarray(np.asarray(L0_attn_W_qkv, dtype=np.float32))
    wf = np.ascontiguousarray(np.asarray(L0_attn_W_f, dtype=np.float32))
    b_f = np.asarray(L0_attn_b_f, dtype=np.float32)
    wout = np.ascontiguousarray(np.asarray(L0_attn_W_out, dtype=np.float32))
    b_out = np.asarray(L0_attn_b_out, dtype=np.float32)
    wg = np.ascontiguousarray(np.asarray(L0_ffn_w_gate, dtype=np.float32))
    wu = np.ascontiguousarray(np.asarray(L0_ffn_w_up, dtype=np.float32))
    wd = np.ascontiguousarray(np.asarray(L0_ffn_w_down, dtype=np.float32))

    bft = np.tile(b_f, 16)
    bqc = np.ascontiguousarray(bft.reshape(8, 128).T)
    bkvr = np.ascontiguousarray(np.concatenate([bft, np.zeros(D, np.float32)])[None, :])
    boutr = np.ascontiguousarray(b_out[None, :])
    idb = np.eye(128, dtype=np.float32).astype(bf16)
    idh = np.eye(128, dtype=np.float32).astype(np.float16)
    ones1 = np.ones((1, 128), np.float32).astype(bf16)
    onesc = np.ones((128, 8), np.float32).astype(bf16)
    selb = np.zeros((16, 8 * 128), np.float32)
    for p in range(8):
        selb[2 * p, p * 128: p * 128 + 64] = 1.0
        selb[2 * p + 1, p * 128 + 64: p * 128 + 128] = 1.0
    selb = selb.astype(bf16)

    in_maps = []
    for c in range(NCORE):
        wd_ch = []
        for s in range(WDCH):
            ic = s * 8 + c
            if ic < ICN:
                wd_ch.append(wd[:, ic * 128:(ic + 1) * 128])
            else:
                wd_ch.append(np.zeros((D, 128), np.float32))
        in_maps.append({
            "x": np.ascontiguousarray(x[c]),
            "wqkv": wqkv, "wf": wf, "wout": wout,
            "bqc": bqc, "bkvr": bkvr, "boutr": boutr,
            "wg_sl": np.ascontiguousarray(wg[:, c * 128:(c + 1) * 128]),
            "wu_sl": np.ascontiguousarray(wu[:, c * 128:(c + 1) * 128]),
            "wd_sl": np.ascontiguousarray(np.stack(wd_ch, 0)),
            "idb": idb, "idh": idh, "ones1": ones1, "onesc": onesc, "selb": selb,
        })

    if "prog" not in _NC_CACHE:
        _NC_CACHE["prog"] = build_nc()
    nc = _NC_CACHE["prog"]
    res = run_bass_kernel_spmd(nc, in_maps, core_ids=list(range(NCORE)))
    import os as _os
    if int(_os.environ.get("KDBG", "0")):
        kernel.DBG_RES = res
    out = np.concatenate([res.results[c]["out"][None] for c in range(NCORE)], 0)
    return np.ascontiguousarray(out.reshape(2, 16384, D))


# version 19
# speedup vs baseline: 1.1910x; 1.0107x over previous
import numpy as np
import ml_dtypes
import concourse.bass as bass
import concourse.mybir as mybir
from concourse.bass_utils import run_bass_kernel_spmd
from contextlib import ExitStack

F32 = mybir.dt.float32
F16 = mybir.dt.float16
BF16 = mybir.dt.bfloat16
I16 = mybir.dt.int16
I32 = mybir.dt.int32
OP = mybir.AluOpType
AF = mybir.ActivationFunctionType
AX = mybir.AxisListType
bf16 = ml_dtypes.bfloat16

NCORE = 8
TCORE = 4096
MT = 512
NMT = TCORE // MT
D = 1024
FI = 2816
ICN = 22
NIG = 11
WDCH = 3
MIDS = [0.25, 0.75, 1.25, 1.75, 2.5, 3.5, 5.0]


def build_nc():
    nc = bass.Bass("TRN2", target_bir_lowering=False)
    x_in = nc.dram_tensor("x", [TCORE, D], F32, kind="ExternalInput").ap()
    wqkv_in = nc.dram_tensor("wqkv", [D, 3 * D], F32, kind="ExternalInput").ap()
    wf_in = nc.dram_tensor("wf", [64, 64], F32, kind="ExternalInput").ap()
    wout_in = nc.dram_tensor("wout", [D, D], F32, kind="ExternalInput").ap()
    bqc_in = nc.dram_tensor("bqc", [128, 8], F32, kind="ExternalInput").ap()
    bkvr_in = nc.dram_tensor("bkvr", [1, 2 * D], F32, kind="ExternalInput").ap()
    bout_in = nc.dram_tensor("boutr", [1, D], F32, kind="ExternalInput").ap()
    wg_sl_in = nc.dram_tensor("wg_sl", [FI, 128], F32, kind="ExternalInput").ap()
    wu_sl_in = nc.dram_tensor("wu_sl", [FI, 128], F32, kind="ExternalInput").ap()
    wd_sl_in = nc.dram_tensor("wd_sl", [WDCH, D, 128], F32, kind="ExternalInput").ap()
    idb_in = nc.dram_tensor("idb", [128, 128], BF16, kind="ExternalInput").ap()
    idh_in = nc.dram_tensor("idh", [128, 128], F16, kind="ExternalInput").ap()
    ones1_in = nc.dram_tensor("ones1", [1, 128], BF16, kind="ExternalInput").ap()
    onesc_in = nc.dram_tensor("onesc", [128, 8], BF16, kind="ExternalInput").ap()
    selb_in = nc.dram_tensor("selb", [16, 8 * 128], BF16, kind="ExternalInput").ap()
    out_d = nc.dram_tensor("out", [TCORE, D], F32, kind="ExternalOutput").ap()
    import os as _os
    DBG = bool(int(_os.environ.get("KDBG", "0")))
    if DBG:
        dbg_qres = nc.dram_tensor("dbg_qres", [128, 8, TCORE], F32, kind="ExternalOutput").ap()
        dbg_bdkv = nc.dram_tensor("dbg_bdkv", [128, 8, 128], F32, kind="ExternalOutput").ap()
        dbg_ksb8 = nc.dram_tensor("dbg_ksb8", [128, 8, 16], F32, kind="ExternalOutput").ap()
        dbg_xqf = nc.dram_tensor("dbg_xqf", [128, 8, MT], F32, kind="ExternalOutput").ap()
        dbg_hqT = nc.dram_tensor("dbg_hqT", [128, ICN, MT], F32, kind="ExternalOutput").ap()
        dbg_attn2 = nc.dram_tensor("dbg_attn2", [128, 8, MT], F32, kind="ExternalOutput").ap()
        dbg_htok = nc.dram_tensor("dbg_htok", [128, 2, FI], F32, kind="ExternalOutput").ap()
        dbg_nrec = nc.dram_tensor("dbg_nrec", [16, 512], F32, kind="ExternalOutput").ap()
        dbg_st = nc.dram_tensor("dbg_st", [128, 64], F32, kind="ExternalOutput").ap()
        dbg_watt = nc.dram_tensor("dbg_watt", [128, 8, 3 * D], F32, kind="ExternalOutput").ap()
        dbg_xnf = nc.dram_tensor("dbg_xnf", [128, 8, MT], F32, kind="ExternalOutput").ap()
        dbg_xnb = nc.dram_tensor("dbg_xnb", [128, 4, D], F32, kind="ExternalOutput").ap()
        dbg_kvall = nc.dram_tensor("dbg_kvall", [128, 1032], F32, kind="ExternalOutput").ap()
        dbg_xqall = nc.dram_tensor("dbg_xqall", [NMT, 128, 8, MT], F16, kind="ExternalOutput").ap()
        dbg_hqTall = nc.dram_tensor("dbg_hqTall", [NMT, 128, ICN, MT], F16, kind="ExternalOutput").ap()
        dbg_at2all = nc.dram_tensor("dbg_at2all", [NMT, 128, 8, MT], BF16, kind="ExternalOutput").ap()
        dbg_agwgu = nc.dram_tensor("dbg_agwgu", [8, 128, 2 * FI], F16, kind="ExternalOutput").ap()
        dbg_wgusb = nc.dram_tensor("dbg_wgusb", [128, 2, 8, 512], F16, kind="ExternalOutput").ap()
        dbg_gup = nc.dram_tensor("dbg_gup", [128, 512], F16, kind="ExternalOutput").ap()
        dbg_sgt = nc.dram_tensor("dbg_sgt", [128, 256], F16, kind="ExternalOutput").ap()
    ag_wgu_in = nc.dram_tensor("ag_wgu_in", [128, 2 * FI], F16)
    ag_wgu_out = nc.dram_tensor("ag_wgu_out", [8, 128, 2 * FI], F16)
    ag_wd_in = nc.dram_tensor("ag_wd_in", [WDCH, 128, D], F16)
    ag_wd_out = nc.dram_tensor("ag_wd_out", [8, WDCH, 128, D], F16)
    ar_in = nc.dram_tensor("ar_in", [128, 1032], F32)
    ar_out = nc.dram_tensor("ar_out", [128, 1032], F32)
    wout16 = nc.dram_tensor("wout16", [D, D], BF16)

    es = ExitStack()
    SB = lambda name, shape, dt: es.enter_context(nc.sbuf_tensor(name, shape, dt))
    pb = [es.enter_context(nc.psum_tensor(f"pb{i}", [128, 512], F32)) for i in range(8)]
    sem_names = ["sx", "sw", "sg1", "smm", "sv", "ss", "sgp", "scc", "swc", "sca",
                 "sgpa", "sgpb", "swga", "swgb", "swda", "swdb", "swoa", "swob", "ssta", "sstb"]
    sem = {n: es.enter_context(nc.semaphore(n)) for n in sem_names}
    cnt = {n: 0 for n in sem_names}
    block = es.enter_context(nc.Block())

    def on(eng, fn):
        getattr(block, eng)(fn)

    identb = SB("identb", [128, 128], BF16)
    identh = SB("identh", [128, 128], F16)
    ones1 = SB("ones1s", [1, 128], BF16)
    onesc = SB("onescs", [128, 8], BF16)
    selb = SB("selbs", [16, 8 * 128], BF16)
    bqc = SB("bqcs", [128, 8], F32)
    boutr = SB("boutrs", [1, D], BF16)
    q_res = SB("q_res", [128, 8, TCORE], BF16)
    x_slab = SB("x_slab", [128, 4, D], F32)
    stA = SB("stA", [128, 16], F32)
    stB = SB("stB", [128, 16], F32)
    stC = SB("stC", [128, 16], F32)
    stD = SB("stD", [128, 16], F32)
    qv = SB("qv", [128, 1408], F32)
    ql = SB("ql", [128, 1408], F16)
    qh = SB("qh", [128, 1408], F16)
    qsm = SB("qsm", [128, 1408], F16)
    bmx = SB("bmx", [128, 88], F32)
    bsc = SB("bsc", [128, 88], F32)
    brr = SB("brr", [128, 88], F32)
    bdkv = SB("bdkv", [128, 8, 128], BF16)
    ksb8 = SB("ksb8", [128, 8, 16], BF16)

    def p0_sync(sync):
        for dst, src in [(identb, idb_in), (identh, idh_in), (ones1, ones1_in),
                         (onesc, onesc_in), (selb, selb_in), (bqc, bqc_in)]:
            sync.dma_start(out=dst[:, :], in_=src[:, :]).then_inc(sem["sw"], 16)
            cnt["sw"] += 16
    on("sync", p0_sync)
    SW_CONST = cnt["sw"]


    with nc.sbuf_tensor("W_att", [128, 8, 3 * D], BF16) as W_att, \
         nc.sbuf_tensor("ln_sq", [128, 4 * D], F32) as ln_sq, \
         nc.sbuf_tensor("bkvrs", [1, 2 * D], BF16) as bkvr:

        with nc.sbuf_tensor("wqkv_sb", [128, 8, 2 * D], BF16) as wqkv_sb, \
             nc.sbuf_tensor("wf_sb", [64, 64], BF16) as wf_sb, \
             nc.sbuf_tensor("wqt0", [64, 128], BF16) as wqt0, \
             nc.sbuf_tensor("wqt1", [64, 128], BF16) as wqt1, \
             nc.sbuf_tensor("wc16", [128, 1, D], BF16) as wc16:
            wqt = [wqt0, wqt1]

            def p1_gp(g):
                g.dma_start(out=wqkv_sb[:, :, :],
                            in_=wqkv_in[:, 0:2 * D].rearrange("(c p) f -> p c f", p=128)).then_inc(sem["sg1"], 16)
                g.dma_start(out=W_att[:, :, 2 * D:3 * D],
                            in_=wqkv_in[:, 2 * D:3 * D].rearrange("(c p) f -> p c f", p=128)).then_inc(sem["sg1"], 16)
                g.dma_start(out=wf_sb[:, :], in_=wf_in[:, :]).then_inc(sem["sg1"], 16)
                g.dma_start(out=bkvr[:, :], in_=bkvr_in[:, :]).then_inc(sem["sg1"], 16)
                g.dma_start(out=boutr[:, :], in_=bout_in[:, :]).then_inc(sem["sg1"], 16)
                cnt["sg1"] += 80
            on("gpsimd", p1_gp)
            SG1_W = cnt["sg1"]
            SV_VCOPY = cnt["sv"]

            sx_wc0 = cnt["sx"]

            def p1_wc_g(g):
                for p in range(8):
                    if p >= 1:
                        g.wait_ge(sem["sx"], sx_wc0 + 16 * p)
                    g.dma_start(out=wc16[:, 0, :],
                                in_=wout_in[p * 128:(p + 1) * 128, :]).then_inc(sem["swc"], 16)
                    cnt["swc"] += 16
            on("gpsimd", p1_wc_g)

            def p1_wc_s(sync):
                for p in range(8):
                    sync.wait_ge(sem["swc"], 16 * (p + 1))
                    sync.dma_start(out=wout16.ap()[p * 128:(p + 1) * 128, :],
                                   in_=wc16[:, 0, :]).then_inc(sem["sx"], 16)
                    cnt["sx"] += 16
            on("sync", p1_wc_s)
            WC_SX = cnt["sx"]

            ss_base = cnt["ss"]
            smm_base = cnt["smm"]

            def p1_t(t):
                t.wait_ge(sem["sg1"], SG1_W)
                t.wait_ge(sem["sw"], SW_CONST)
                it = 0
                for c in range(8):
                    for qk in range(2):
                        for h in range(16):
                            bA = pb[it % 2]
                            bB = pb[2 + it % 2]
                            if it >= 2:
                                t.wait_ge(sem["ss"], ss_base + 2 * (it - 1))
                            src = wqkv_sb[:, c, qk * D + h * 64: qk * D + (h + 1) * 64]
                            t.matmul(bA[0:64, 0:128], src, identb[:, :],
                                     start=True, stop=True).then_inc(sem["smm"], 1)
                            t.wait_ge(sem["ss"], ss_base + 2 * it + 1)
                            t.matmul(bB[0:128, 0:64], wqt[it % 2][0:64, :], wf_sb[:, :],
                                     start=True, stop=True).then_inc(sem["smm"], 1)
                            it += 1
                cnt["smm"] += 512
            on("tensor", p1_t)

            def p1_s(s):
                it = 0
                for c in range(8):
                    for qk in range(2):
                        for h in range(16):
                            bA = pb[it % 2]
                            bB = pb[2 + it % 2]
                            s.wait_ge(sem["smm"], smm_base + 2 * it + 1)
                            s.copy(wqt[it % 2][0:64, :], bA[0:64, 0:128]).then_inc(sem["ss"], 1)
                            s.wait_ge(sem["smm"], smm_base + 2 * it + 2)
                            s.copy(W_att[:, c, qk * D + h * 64: qk * D + (h + 1) * 64],
                                   bB[0:128, 0:64]).then_inc(sem["ss"], 1)
                            it += 1
                cnt["ss"] += 512
            on("scalar", p1_s)
        SS_FOLD = cnt["ss"]
        SMM_FOLD_END = cnt["smm"]

        def emit_quant_f16(v, src2d, W, out2d, sc_ap, rr_ap):
            nb = W // 16
            v3 = qv[:, 0:W].rearrange("p (b k) -> p b k", k=16)
            v.tensor_tensor(v3, src2d.rearrange("p (b k) -> p b k", k=16),
                            rr_ap.unsqueeze(2).broadcast_to([128, nb, 16]), op=OP.mult)
            v.drain()
            v.tensor_scalar(qsm[:, 0:W], qv[:, 0:W], 0.0, None, op0=OP.is_lt)
            v.tensor_scalar(qv[:, 0:W].bitcast(I32), qv[:, 0:W].bitcast(I32), 0x7FFFFFFF, None,
                            op0=OP.bitwise_and)
            v.drain()
            a = qv[:, 0:W]
            v.tensor_scalar(qsm[:, 0:W], qsm[:, 0:W], -2.0, 1.0, op0=OP.mult, op1=OP.add)
            v.tensor_scalar(ql[:, 0:W], a, MIDS[0], None, op0=OP.is_gt)
            v.tensor_scalar(qh[:, 0:W], a, MIDS[4], None, op0=OP.is_gt)
            v.drain()
            v.scalar_tensor_tensor(ql[:, 0:W], a, MIDS[1], ql[:, 0:W], op0=OP.is_gt, op1=OP.add)
            v.scalar_tensor_tensor(qh[:, 0:W], a, MIDS[5], qh[:, 0:W], op0=OP.is_gt, op1=OP.add)
            v.drain()
            v.scalar_tensor_tensor(ql[:, 0:W], a, MIDS[2], ql[:, 0:W], op0=OP.is_gt, op1=OP.add)
            v.scalar_tensor_tensor(qh[:, 0:W], a, MIDS[6], qh[:, 0:W], op0=OP.is_gt, op1=OP.add)
            v.drain()
            v.scalar_tensor_tensor(ql[:, 0:W], a, MIDS[3], ql[:, 0:W], op0=OP.is_gt, op1=OP.add)
            v.scalar_tensor_tensor(qh[:, 0:W], a, MIDS[6], qh[:, 0:W], op0=OP.is_gt, op1=OP.add)
            v.drain()
            v.scalar_tensor_tensor(ql[:, 0:W], ql[:, 0:W], 0.5, qh[:, 0:W], op0=OP.mult, op1=OP.add)
            v.drain()
            v.tensor_tensor(qh[:, 0:W].rearrange("p (b k) -> p b k", k=16),
                            ql[:, 0:W].rearrange("p (b k) -> p b k", k=16),
                            sc_ap.unsqueeze(2).broadcast_to([128, nb, 16]), op=OP.mult)
            v.drain()
            v.tensor_tensor(out2d, qh[:, 0:W], qsm[:, 0:W], op=OP.mult)

        def emit_bmax(v, src2d, W, bm_ap, sc_ap, rr_ap):
            v.tensor_reduce(bm_ap, src2d.rearrange("p (b k) -> p b k", k=16),
                            axis=AX.X, op=OP.max, apply_absolute_value=True)
            v.drain()
            v.tensor_scalar(sc_ap, bm_ap, 1.0 / 6.0, 1e-12, op0=OP.mult, op1=OP.max)
            v.drain()
            v.reciprocal(rr_ap, sc_ap)
            v.drain()

        with nc.sbuf_tensor("wg_sb", [128, ICN, 128], F32) as wg_sb, \
             nc.sbuf_tensor("wu_sb", [128, ICN, 128], F32) as wu_sb, \
             nc.sbuf_tensor("wd_sb", [128, WDCH, 8, 128], F32) as wd_sb, \
             nc.sbuf_tensor("wq16a", [128, 128], F16) as wq16a, \
             nc.sbuf_tensor("wq16b", [128, 128], F16) as wq16b, \
             nc.sbuf_tensor("wt16a", [128, 128], F16) as wt16a, \
             nc.sbuf_tensor("wt16b", [128, 128], F16) as wt16b:
            wq16 = [wq16a, wq16b]
            wt16 = [wt16a, wt16b]

            def p2_sync(sync):
                sync.wait_ge(sem["smm"], SMM_FOLD_END)
                sync.wait_ge(sem["sx"], WC_SX)
                sync.dma_start(out=wg_sb[:, :, :],
                               in_=wg_sl_in[:, :].rearrange("(a p) k -> p a k", p=128)).then_inc(sem["sw"], 16)
                sync.dma_start(out=wu_sb[:, :, :],
                               in_=wu_sl_in[:, :].rearrange("(a p) k -> p a k", p=128)).then_inc(sem["sw"], 16)
                sync.dma_start(out=wd_sb[:, :, :, :],
                               in_=wd_sl_in[:, :, :].rearrange("j (a p) k -> p j a k", p=128)).then_inc(sem["sw"], 16)
                cnt["sw"] += 48
            on("sync", p2_sync)
            SW_WSH = cnt["sw"]

            tiles = []
            for j in range(ICN):
                col = (j // 2) * 512 + (j % 2) * 128
                tiles.append((wg_sb[:, j, :], ag_wgu_in.ap()[:, col:col + 128]))
            for j in range(ICN):
                col = (j // 2) * 512 + 256 + (j % 2) * 128
                tiles.append((wu_sb[:, j, :], ag_wgu_in.ap()[:, col:col + 128]))
            for jj in range(WDCH):
                for fc in range(8):
                    tiles.append((wd_sb[:, jj, fc, :], ag_wd_in.ap()[jj, :, fc * 128:(fc + 1) * 128]))
            NT = len(tiles)
            sv0, smm0, ss0, sgp0 = cnt["sv"], cnt["smm"], cnt["ss"], cnt["sgp"]

            def p2_v(v):
                v.wait_ge(sem["sw"], SW_WSH)
                for i, (srcap, dst) in enumerate(tiles):
                    if i >= 2:
                        v.wait_ge(sem["smm"], smm0 + (i - 1))
                    emit_bmax(v, srcap, 128, bmx[:, 0:8], bsc[:, 0:8], brr[:, 0:8])
                    emit_quant_f16(v, srcap, 128, wq16[i % 2][:, :], bsc[:, 0:8], brr[:, 0:8])
                    v.drain()
                    v.engine_nop().then_inc(sem["sv"], 1)
                    cnt["sv"] += 1
            on("vector", p2_v)

            def p2_t(t):
                for i in range(NT):
                    t.wait_ge(sem["sv"], sv0 + i + 1)
                    if i >= 2:
                        t.wait_ge(sem["ss"], ss0 + (i - 1))
                    t.matmul(pb[4 + i % 2][0:128, 0:128], wq16[i % 2][:, :], identh[:, :],
                             start=True, stop=True).then_inc(sem["smm"], 1)
                    cnt["smm"] += 1
            on("tensor", p2_t)

            def p2_s(s):
                for i in range(NT):
                    s.wait_ge(sem["smm"], smm0 + i + 1)
                    if i >= 2:
                        sn = "sgpa" if i % 2 == 0 else "sgpb"
                        s.wait_ge(sem[sn], 16 * ((i - 2) // 2 + 1))
                    s.copy(wt16[i % 2][:, :], pb[4 + i % 2][0:128, 0:128]).then_inc(sem["ss"], 1)
                    cnt["ss"] += 1
            on("scalar", p2_s)

            def p2_dump(sync):
                for i, (src, dst) in enumerate(tiles):
                    sync.wait_ge(sem["ss"], ss0 + i + 1)
                    sn = "sgpa" if i % 2 == 0 else "sgpb"
                    sync.dma_start(out=dst, in_=wt16[i % 2][:, :]).then_inc(sem[sn], 16)
                    cnt[sn] += 16
            on("sync", p2_dump)

            def p2_g(g):
                g.wait_ge(sem["sgpa"], 16 * ((NT + 1) // 2))
                g.wait_ge(sem["sgpb"], 16 * (NT // 2))
                g.collective_compute("AllGather", OP.bypass,
                                     replica_groups=[[0, 1, 2, 3, 4, 5, 6, 7]],
                                     ins=[ag_wgu_in.ap().opt()],
                                     outs=[ag_wgu_out.ap().opt()]).then_inc(sem["sca"], 1)
                cnt["sca"] += 1
                g.collective_compute("AllGather", OP.bypass,
                                     replica_groups=[[0, 1, 2, 3, 4, 5, 6, 7]],
                                     ins=[ag_wd_in.ap().opt()],
                                     outs=[ag_wd_out.ap().opt()]).then_inc(sem["sca"], 1)
                cnt["sca"] += 1
            on("gpsimd", p2_g)
        SCC_AG = cnt["sca"]
        P2_SMM_END = cnt["smm"]
        P2_SGPA_END = cnt["sgpa"]
        P2_SGPB_END = cnt["sgpb"]

        def emit_ln(sq_view):
            def s_sq(sc):
                sc.activation(sq_view, x_slab[:, :, :].rearrange("p c f -> p (c f)"),
                              AF.Square).then_inc(sem["ss"], 1)
                cnt["ss"] += 1
            on("scalar", s_sq)
            tok_sq = cnt["ss"]

            def v_stats(v, tok_sq=tok_sq, sq_view=sq_view):
                for c in range(4):
                    v.tensor_reduce(stA[:, c:c + 1], x_slab[:, c, :], axis=AX.X, op=OP.add)
                v.wait_ge(sem["ss"], tok_sq)
                for c in range(4):
                    v.tensor_reduce(stA[:, 4 + c:5 + c], sq_view[:, c * D:(c + 1) * D],
                                    axis=AX.X, op=OP.add)
                v.drain()
                v.tensor_scalar(stB[:, 0:4], stA[:, 0:4], 1.0 / D, None, op0=OP.mult)
                v.tensor_scalar(stB[:, 4:8], stA[:, 4:8], 1.0 / D, None, op0=OP.mult)
                v.drain()
                v.tensor_tensor(stC[:, 0:4], stB[:, 0:4], stB[:, 0:4], op=OP.mult)
                v.drain()
                v.tensor_tensor(stA[:, 8:12], stB[:, 4:8], stC[:, 0:4], op=OP.subtract)
                v.drain()
                v.tensor_scalar(stB[:, 8:12], stA[:, 8:12], 1e-5, None, op0=OP.add)
                v.drain()
                v.reciprocal(stC[:, 8:12], stB[:, 8:12])
                v.drain()
                v.engine_nop().then_inc(sem["sv"], 1)
                cnt["sv"] += 1
            on("vector", v_stats)
            tok_iv = cnt["sv"]

            def s_sqrt(sc, tok_iv=tok_iv):
                sc.wait_ge(sem["sv"], tok_iv)
                sc.activation(stA[:, 12:16], stC[:, 8:12], AF.Sqrt).then_inc(sem["ss"], 1)
                cnt["ss"] += 1
            on("scalar", s_sqrt)
            tok_r0 = cnt["ss"]

            def v_newton(v, tok_r0=tok_r0):
                v.wait_ge(sem["ss"], tok_r0)
                v.tensor_tensor(stC[:, 12:16], stA[:, 12:16], stA[:, 12:16], op=OP.mult)
                v.drain()
                v.tensor_tensor(stD[:, 0:4], stB[:, 8:12], stC[:, 12:16], op=OP.mult)
                v.drain()
                v.tensor_scalar(stC[:, 0:4], stD[:, 0:4], -0.5, 1.5, op0=OP.mult, op1=OP.add)
                v.drain()
                v.tensor_tensor(stD[:, 8:12], stA[:, 12:16], stC[:, 0:4], op=OP.mult)
                v.drain()
                v.drain()
                v.engine_nop().then_inc(sem["sv"], 1)
                cnt["sv"] += 1
            on("vector", v_newton)
            return cnt["sv"]

        with nc.sbuf_tensor("xnb", [128, 4, D], BF16) as xnb, \
             nc.sbuf_tensor("xnf", [128, 8, MT], BF16) as xnf, \
             nc.sbuf_tensor("k_t0", [128, D], BF16) as k_t0, \
             nc.sbuf_tensor("k_t1", [128, D], BF16) as k_t1, \
             nc.sbuf_tensor("v_t0", [128, D], BF16) as v_t0, \
             nc.sbuf_tensor("v_t1", [128, D], BF16) as v_t1, \
             nc.sbuf_tensor("kv_all", [128, 1032], F32) as kv_all, \
             nc.sbuf_tensor("kvs", [128, 1032], F32) as kvs:
            k_t = [k_t0, k_t1]
            v_t = [v_t0, v_t1]
            tok_x_free = 0
            tok_xnb_free = 0
            tok_xnf_cons = 0
            tok_kt_used = [0, 0]

            for m in range(NMT):
                def a_load(sync, m=m, w=tok_x_free):
                    if w:
                        sync.wait_ge(sem["sv"], w)
                    sync.dma_start(out=x_slab[:, :, :],
                                   in_=x_in[m * MT:(m + 1) * MT, :].rearrange("(c p) f -> p c f", p=128)
                                   ).then_inc(sem["sx"], 16)
                    cnt["sx"] += 16
                on("sync", a_load)
                tok_x = cnt["sx"]

                def a_wx_v(v, t=tok_x):
                    v.wait_ge(sem["sx"], t)
                on("vector", a_wx_v)

                def a_wx_s(s, t=tok_x):
                    s.wait_ge(sem["sx"], t)
                on("scalar", a_wx_s)

                tok_rstd = emit_ln(ln_sq[:, :])

                def a_xn(v, w=tok_xnb_free, tr=tok_rstd, m=m):
                    v.wait_ge(sem["sv"], tr)
                    if m == 0:
                        v.wait_ge(sem["smm"], P2_SMM_END)
                        v.wait_ge(sem["sgpa"], P2_SGPA_END)
                        v.wait_ge(sem["sgpb"], P2_SGPB_END)
                    if w:
                        v.wait_ge(sem["smm"], w)
                    for c in range(4):
                        v.tensor_scalar(xnb[:, c, :], x_slab[:, c, :],
                                        stB[:, c:c + 1], stD[:, 8 + c:9 + c],
                                        op0=OP.subtract, op1=OP.mult)
                    v.drain()
                    v.engine_nop().then_inc(sem["sv"], 1)
                    cnt["sv"] += 1
                on("vector", a_xn)
                tok_xn = cnt["sv"]
                tok_x_free = tok_xn

                smm_tp0 = cnt["smm"]
                ss_tp0 = cnt["ss"]

                def a_tp_t(t, tok_xn=tok_xn, smm_tp0=smm_tp0, ss_tp0=ss_tp0, m=m):
                    t.wait_ge(sem["sv"], tok_xn)
                    if m == 0:
                        t.wait_ge(sem["ss"], SS_FOLD)
                        t.wait_ge(sem["sv"], SV_VCOPY)
                    it = 0
                    for c in range(4):
                        for j in range(8):
                            bA = pb[it % 2]
                            if it >= 2:
                                t.wait_ge(sem["ss"], ss_tp0 + (it - 1))
                            t.matmul(bA[0:128, 0:128], xnb[:, c, j * 128:(j + 1) * 128],
                                     identb[:, :], start=True, stop=True).then_inc(sem["smm"], 1)
                            it += 1
                    cnt["smm"] += 32
                on("tensor", a_tp_t)
                tok_xnb_free = cnt["smm"]

                def a_tp_s(s, smm_tp0=smm_tp0, w=tok_xnf_cons):
                    if w:
                        s.wait_ge(sem["smm"], w)
                    it = 0
                    for c in range(4):
                        for j in range(8):
                            bA = pb[it % 2]
                            s.wait_ge(sem["smm"], smm_tp0 + it + 1)
                            s.copy(xnf[:, j, c * 128:(c + 1) * 128], bA[0:128, 0:128]).then_inc(sem["ss"], 1)
                            it += 1
                    cnt["ss"] += 32
                on("scalar", a_tp_s)
                tok_tp_evac = cnt["ss"]

                smm_q0 = cnt["smm"]
                ss_q0 = cnt["ss"]

                def a_q_t(t, tok_tp_evac=tok_tp_evac, ss_q0=ss_q0):
                    t.wait_ge(sem["ss"], tok_tp_evac)
                    for mc in range(8):
                        bA = pb[mc % 2]
                        if mc >= 2:
                            t.wait_ge(sem["ss"], ss_q0 + (mc - 1))
                        for kc in range(8):
                            ins = t.matmul(bA[:, 0:512], W_att[:, kc, mc * 128:(mc + 1) * 128],
                                           xnf[:, kc, :], start=(kc == 0), stop=(kc == 7))
                        ins.then_inc(sem["smm"], 1)
                        cnt["smm"] += 1
                on("tensor", a_q_t)

                def a_q_s(s, smm_q0=smm_q0, m=m):
                    for mc in range(8):
                        bA = pb[mc % 2]
                        s.wait_ge(sem["smm"], smm_q0 + mc + 1)
                        s.activation(q_res[:, mc, m * MT:(m + 1) * MT], bA[:, 0:512], AF.Relu,
                                     bias=bqc[:, mc:mc + 1]).then_inc(sem["ss"], 1)
                        cnt["ss"] += 1
                on("scalar", a_q_s)
                tok_q_evac = cnt["ss"]

                for tc in range(4):
                    par = tc % 2
                    smm_kv0 = cnt["smm"]
                    ss_kv0 = cnt["ss"]

                    def a_kv_t(t, tc=tc, smm_kv0=smm_kv0, ss_kv0=ss_kv0,
                               tok_q_evac=tok_q_evac):
                        if tc == 0:
                            t.wait_ge(sem["ss"], tok_q_evac)
                        else:
                            t.wait_ge(sem["ss"], ss_kv0)
                        for grp in range(4):
                            bA = pb[2 + grp % 2]
                            if grp >= 2:
                                t.wait_ge(sem["ss"], ss_kv0 + (grp - 1))
                            for kc in range(8):
                                last = (kc == 7) and (grp >= 2)
                                ins = t.matmul(bA[:, 0:512], xnf[:, kc, tc * 128:(tc + 1) * 128],
                                               W_att[:, kc, D + grp * 512: D + (grp + 1) * 512],
                                               start=(kc == 0), stop=last)
                            if grp < 2:
                                ins = t.matmul(bA[:, 0:512], ones1[0:1, :],
                                               bkvr[0:1, grp * 512:(grp + 1) * 512],
                                               start=False, stop=True)
                            ins.then_inc(sem["smm"], 1)
                            cnt["smm"] += 1
                    on("tensor", a_kv_t)
                    tok_kv_grps = cnt["smm"]

                    def a_kv_s(s, par=par, smm_kv0=smm_kv0, w=tok_kt_used[par]):
                        if w:
                            s.wait_ge(sem["smm"], w)
                        for grp in range(4):
                            bA = pb[2 + grp % 2]
                            s.wait_ge(sem["smm"], smm_kv0 + grp + 1)
                            if grp < 2:
                                s.activation(k_t[par][:, grp * 512:(grp + 1) * 512], bA[:, 0:512],
                                             AF.Relu).then_inc(sem["ss"], 1)
                            else:
                                s.copy(v_t[par][:, (grp - 2) * 512:(grp - 1) * 512],
                                       bA[:, 0:512]).then_inc(sem["ss"], 1)
                            cnt["ss"] += 1
                    on("scalar", a_kv_s)
                    tok_kv_evac = cnt["ss"]

                    def a_acc_t(t, m=m, tc=tc, par=par, tok_kv_evac=tok_kv_evac):
                        t.wait_ge(sem["ss"], tok_kv_evac)
                        first = (m == 0 and tc == 0)
                        lastt = (m == NMT - 1 and tc == 3)
                        for p in range(8):
                            t.matmul(pb[4 + p // 4][:, (p % 4) * 128:(p % 4 + 1) * 128],
                                     k_t[par][:, p * 128:(p + 1) * 128],
                                     v_t[par][:, p * 128:(p + 1) * 128],
                                     start=first, stop=lastt, skip_group_check=True)
                            ins = t.matmul(pb[6][:, p:p + 1],
                                           k_t[par][:, p * 128:(p + 1) * 128],
                                           onesc[:, 0:1], start=first, stop=lastt,
                                           skip_group_check=True)
                        ins.then_inc(sem["smm"], 1)
                        cnt["smm"] += 1
                    on("tensor", a_acc_t)
                    tok_kt_used[par] = cnt["smm"]
                tok_xnf_cons = cnt["smm"]
            tok_kv_done = cnt["smm"]

            def ar_v(v, tok_kv_done=tok_kv_done):
                v.wait_ge(sem["smm"], tok_kv_done)
                v.tensor_copy(kvs[:, 0:512], pb[4][:, :])
                v.tensor_copy(kvs[:, 512:1024], pb[5][:, :])
                v.tensor_copy(kvs[:, 1024:1032], pb[6][:, 0:8])
                v.drain()
                v.drain()
                v.engine_nop().then_inc(sem["sv"], 1)
                cnt["sv"] += 1
            on("vector", ar_v)
            tok_kvs = cnt["sv"]

            def ar_g(g, tok_kvs=tok_kvs):
                g.wait_ge(sem["sv"], tok_kvs)
                g.dma_start(out=ar_in.ap()[:, :], in_=kvs[:, :]).then_inc(sem["sgp"], 16)
                cnt["sgp"] += 16
                g.wait_ge(sem["sgp"], cnt["sgp"])
                g.collective_compute("AllReduce", OP.add,
                                     replica_groups=[[0, 1, 2, 3], [4, 5, 6, 7]],
                                     ins=[ar_in.ap().opt()],
                                     outs=[ar_out.ap().opt()]).then_inc(sem["scc"], 1)
                cnt["scc"] += 1
                g.wait_ge(sem["scc"], cnt["scc"])
                g.dma_start(out=kv_all[:, :], in_=ar_out.ap()[:, :]).then_inc(sem["sgp"], 16)
                cnt["sgp"] += 16
            on("gpsimd", ar_g)
            tok_kvall = cnt["sgp"]

            def ar_build(v, tok_kvall=tok_kvall):
                v.wait_ge(sem["sgp"], tok_kvall)
                v.memset(bdkv[:, :, :], 0.0)
                v.memset(ksb8[:, :, :], 0.0)
                v.drain()
                for p in range(8):
                    v.tensor_copy(bdkv[0:64, p, 0:64], kv_all[0:64, p * 128: p * 128 + 64])
                    v.tensor_copy(bdkv[64:128, p, 64:128], kv_all[64:128, p * 128 + 64: p * 128 + 128])
                    v.tensor_copy(ksb8[0:64, p, 2 * p:2 * p + 1], kv_all[0:64, 1024 + p:1025 + p])
                    v.tensor_copy(ksb8[64:128, p, 2 * p + 1:2 * p + 2], kv_all[64:128, 1024 + p:1025 + p])
                v.drain()
                v.drain()
                v.engine_nop().then_inc(sem["sv"], 1)
                cnt["sv"] += 1
            on("vector", ar_build)
            tok_attc = cnt["sv"]
            tok_x_free_ab = tok_x_free

            DUMPA_SGP = 0
            if DBG:
                def dump_a(g, tok_attc=tok_attc):
                    g.wait_ge(sem["sv"], tok_attc)
                    g.dma_start(out=dbg_watt, in_=W_att[:, :, :]).then_inc(sem["sgp"], 16)
                    g.dma_start(out=dbg_xnf, in_=xnf[:, :, :]).then_inc(sem["sgp"], 16)
                    g.dma_start(out=dbg_xnb, in_=xnb[:, :, :]).then_inc(sem["sgp"], 16)
                    g.dma_start(out=dbg_kvall, in_=kv_all[:, :]).then_inc(sem["sgp"], 16)
                    cnt["sgp"] += 64
                on("gpsimd", dump_a)
                DUMPA_SGP = cnt["sgp"]

    wgu_sb = SB("wgu_sb", [128, 2, 8, 512], F16)
    wd_ic = [SB("wd_ic0", [128, D], F16), SB("wd_ic1", [128, D], F16)]
    wout_p = [SB("wout_p0", [128, D], BF16), SB("wout_p1", [128, D], BF16)]
    xn32 = SB("xn32", [128, D], F32)
    xq_t1 = SB("xq_tok0", [128, D], F16)
    xq_tok = [xq_t1, xq_t1]
    xq_feat = SB("xq_feat", [128, 8, MT], F16)
    h_tok = SB("h_tok", [128, 4, FI], F16)
    out_sb = [SB("out_sb0", [128, D], F32), SB("out_sb1", [128, D], F32)]
    hqv = qv[:, :].bitcast(F16)
    hqx = [hqv, hqv]
    hqT = SB("hqT", [128, ICN, MT], F16)
    sgt = [SB("sgt0", [128, 256], F16), SB("sgt1", [128, 256], F16)]
    attn2 = SB("attn2", [128, 8, MT], BF16)
    rbx = [SB("rb0", [128, 512], BF16), SB("rb1", [128, 512], BF16)]
    nplus = SB("nplus", [16, 512], F32)
    nrec16 = SB("nrec16", [16, 512], BF16)
    sq_view = hqT[:, :, :].rearrange("p a f -> p (a f)").bitcast(F32)[:, 0:4 * D]

    dbg_cap_gup = SB("dbg_cap_gup", [128, 512], F16) if DBG else None
    dbg_cap_sgt = SB("dbg_cap_sgt", [128, 256], F16) if DBG else None
    tok_x_free_b = tok_attc
    tok_wgu_free = [0, 0]
    tok_hqT_free = 0
    tok_sgt_used = [0, 0]
    tok_bank_free = [0, 0]
    tok_outsb_free = [0, 0]
    wd_used_tok = [0, 0]
    wo_used = [0, 0]
    sta_prev = 0
    stb_prev = 0

    for m in range(NMT):
        def b_load(sync, m=m, w=tok_x_free_b):
            if w:
                sync.wait_ge(sem["sv"], w)
            sync.dma_start(out=x_slab[:, :, :],
                           in_=x_in[m * MT:(m + 1) * MT, :].rearrange("(c p) f -> p c f", p=128)
                           ).then_inc(sem["sx"], 16)
            cnt["sx"] += 16
        on("sync", b_load)
        tok_xb = cnt["sx"]

        def b_wx_v(v, t=tok_xb, thq=tok_hqT_free, dg=(DUMPA_SGP if m == 0 else 0)):
            if dg:
                v.wait_ge(sem["sgp"], dg)
            v.wait_ge(sem["sx"], t)
            if thq:
                v.wait_ge(sem["smm"], thq)
        on("vector", b_wx_v)

        def b_wx_s(s, t=tok_xb, thq=tok_hqT_free):
            s.wait_ge(sem["sx"], t)
            if thq:
                s.wait_ge(sem["smm"], thq)
        on("scalar", b_wx_s)

        def emit_ln_b():
            def s_sq(sc):
                sc.activation(sq_view, x_slab[:, :, :].rearrange("p c f -> p (c f)"),
                              AF.Square).then_inc(sem["ss"], 1)
                cnt["ss"] += 1
            on("scalar", s_sq)
            tok_sq = cnt["ss"]

            def v_stats(v, tok_sq=tok_sq):
                for c in range(4):
                    v.tensor_reduce(stA[:, c:c + 1], x_slab[:, c, :], axis=AX.X, op=OP.add)
                v.wait_ge(sem["ss"], tok_sq)
                for c in range(4):
                    v.tensor_reduce(stA[:, 4 + c:5 + c], sq_view[:, c * D:(c + 1) * D],
                                    axis=AX.X, op=OP.add)
                v.drain()
                v.tensor_scalar(stB[:, 0:4], stA[:, 0:4], 1.0 / D, None, op0=OP.mult)
                v.tensor_scalar(stB[:, 4:8], stA[:, 4:8], 1.0 / D, None, op0=OP.mult)
                v.drain()
                v.tensor_tensor(stC[:, 0:4], stB[:, 0:4], stB[:, 0:4], op=OP.mult)
                v.drain()
                v.tensor_tensor(stA[:, 8:12], stB[:, 4:8], stC[:, 0:4], op=OP.subtract)
                v.drain()
                v.tensor_scalar(stB[:, 8:12], stA[:, 8:12], 1e-5, None, op0=OP.add)
                v.drain()
                v.reciprocal(stC[:, 8:12], stB[:, 8:12])
                v.drain()
                v.engine_nop().then_inc(sem["sv"], 1)
                cnt["sv"] += 1
            on("vector", v_stats)
            tok_iv = cnt["sv"]

            def s_sqrt(sc, tok_iv=tok_iv):
                sc.wait_ge(sem["sv"], tok_iv)
                sc.activation(stA[:, 12:16], stC[:, 8:12], AF.Sqrt).then_inc(sem["ss"], 1)
                cnt["ss"] += 1
            on("scalar", s_sqrt)
            tok_r0 = cnt["ss"]

            def v_newton(v, tok_r0=tok_r0):
                v.wait_ge(sem["ss"], tok_r0)
                v.tensor_tensor(stC[:, 12:16], stA[:, 12:16], stA[:, 12:16], op=OP.mult)
                v.drain()
                v.tensor_tensor(stD[:, 0:4], stB[:, 8:12], stC[:, 12:16], op=OP.mult)
                v.drain()
                v.tensor_scalar(stC[:, 0:4], stD[:, 0:4], -0.5, 1.5, op0=OP.mult, op1=OP.add)
                v.drain()
                v.tensor_tensor(stD[:, 8:12], stA[:, 12:16], stC[:, 0:4], op=OP.mult)
                v.drain()
                v.drain()
                v.engine_nop().then_inc(sem["sv"], 1)
                cnt["sv"] += 1
            on("vector", v_newton)
            return cnt["sv"]

        tok_rstd = emit_ln_b()

        smm_xq0 = cnt["smm"]
        ss_xq0 = cnt["ss"]
        for c in range(4):
            par = c % 2

            def b_xq_v(v, c=c, par=par, tr=tok_rstd, smm_xq0=smm_xq0):
                if c == 0:
                    v.wait_ge(sem["sv"], tr)
                if c >= 1:
                    v.wait_ge(sem["smm"], smm_xq0 + c * 8)
                v.tensor_scalar(xn32[:, :], x_slab[:, c, :], stB[:, c:c + 1], stD[:, 8 + c:9 + c],
                                op0=OP.subtract, op1=OP.mult)
                v.drain()
                emit_bmax(v, xn32[:, :], D, bmx[:, 0:64], bsc[:, 0:64], brr[:, 0:64])
                emit_quant_f16(v, xn32[:, :], D, xq_tok[par][:, :], bsc[:, 0:64], brr[:, 0:64])
                v.drain()
                v.drain()
                v.engine_nop().then_inc(sem["sv"], 1)
                cnt["sv"] += 1
            on("vector", b_xq_v)
            tok_xq = cnt["sv"]

            def b_xq_t(t, c=c, par=par, tok_xq=tok_xq, ss_xq0=ss_xq0, m=m,
                       stw=((sta_prev, stb_prev) if (m >= 1 and c == 0) else None)):
                t.wait_ge(sem["sv"], tok_xq)
                if m == 0 and c == 0:
                    t.wait_ge(sem["sv"], tok_attc)
                if stw:
                    t.wait_ge(sem["ssta"], stw[0])
                    t.wait_ge(sem["sstb"], stw[1])
                for j in range(8):
                    it = c * 8 + j
                    bA = pb[it % 2]
                    if it >= 2:
                        t.wait_ge(sem["ss"], ss_xq0 + (it - 1))
                    t.matmul(bA[0:128, 0:128], xq_tok[par][:, j * 128:(j + 1) * 128], identh[:, :],
                             start=True, stop=True).then_inc(sem["smm"], 1)
                    cnt["smm"] += 1
            on("tensor", b_xq_t)

            def b_xq_s(s, c=c, smm_xq0=smm_xq0):
                for j in range(8):
                    it = c * 8 + j
                    bA = pb[it % 2]
                    s.wait_ge(sem["smm"], smm_xq0 + it + 1)
                    s.copy(xq_feat[:, j, c * 128:(c + 1) * 128], bA[0:128, 0:128]).then_inc(sem["ss"], 1)
                    cnt["ss"] += 1
            on("scalar", b_xq_s)
        tok_xqf = cnt["ss"]

        if DBG:
            def dump_xq(sy, m=m, t=tok_xqf):
                sy.wait_ge(sem["ss"], t)
                sy.dma_start(out=dbg_xqall[m], in_=xq_feat[:, :, :]).then_inc(sem["sgp"], 16)
                cnt["sgp"] += 16
            on("sync", dump_xq)

        for ig in range(NIG):
            par = ig % 2

            def b_wgu_load(sync, ig=ig, par=par, w=tok_wgu_free[par],
                           first=(m == 0 and ig < 2)):
                if first:
                    sync.wait_ge(sem["sca"], SCC_AG)
                if w:
                    sync.wait_ge(sem["smm"], w)
                sn = "swga" if par == 0 else "swgb"
                sync.dma_start(out=wgu_sb[:, par, :, :],
                               in_=ag_wgu_out.ap()[:, :, ig * 512:(ig + 1) * 512]
                               .rearrange("c p f -> p c f")).then_inc(sem[sn], 16)
                cnt[sn] += 16
            on("sync", b_wgu_load)
            tok_wgu = cnt["swga" if par == 0 else "swgb"]
            sem_wgu = "swga" if par == 0 else "swgb"

            for tcg in range(4):
                bank = pb[2 + tcg % 2]

                def b_gu_t(t, par=par, tcg=tcg, bank=bank, tok_wgu=tok_wgu,
                           tok_xqf=tok_xqf, need_free=tok_bank_free[tcg % 2], sem_wgu=sem_wgu,
                           first=(ig == 0 and tcg == 0)):
                    t.wait_ge(sem[sem_wgu], tok_wgu)
                    if first:
                        t.wait_ge(sem["ss"], tok_xqf)
                    if need_free:
                        t.wait_ge(sem["sv"], need_free)
                    for kc in range(8):
                        ins = t.matmul(bank[:, 0:512],
                                       xq_feat[:, kc, tcg * 128:(tcg + 1) * 128],
                                       wgu_sb[:, par, kc, :],
                                       start=(kc == 0), stop=(kc == 7))
                    ins.then_inc(sem["smm"], 1)
                    cnt["smm"] += 1
                on("tensor", b_gu_t)
                tok_gu = cnt["smm"]

                def b_silu_s(s, bank=bank, sl=tcg % 2, tok_gu=tok_gu, w=tok_sgt_used[tcg % 2]):
                    s.wait_ge(sem["smm"], tok_gu)
                    if w:
                        s.wait_ge(sem["sv"], w)
                    s.activation(sgt[sl][:, :], bank[:, 0:256], AF.Silu)
                    s.drain()
                    s.nop().then_inc(sem["ss"], 1)
                    cnt["ss"] += 1
                on("scalar", b_silu_s)
                tok_silu = cnt["ss"]

                def b_h_v(v, bank=bank, sl=tcg % 2, tcg=tcg, ig=ig, tok_silu=tok_silu):
                    v.wait_ge(sem["ss"], tok_silu)
                    v.tensor_tensor(h_tok[:, tcg, ig * 256:(ig + 1) * 256], sgt[sl][:, :],
                                    bank[:, 256:512], op=OP.mult)
                    v.drain()
                    v.engine_nop().then_inc(sem["sv"], 1)
                    cnt["sv"] += 1
                on("vector", b_h_v)
                tok_sgt_used[tcg % 2] = cnt["sv"]
                tok_bank_free[tcg % 2] = cnt["sv"]
            tok_wgu_free[par] = cnt["smm"]
        tok_h_done = cnt["sv"]

        for tcg in range(4):
            for hf in range(2):
                src = h_tok[:, tcg, hf * 1408:(hf + 1) * 1408]
                par = (tcg * 2 + hf) % 2

                def b_hq_v(v, src=src, par=par, tok_h_done=tok_h_done, need=cnt["smm"]):
                    v.wait_ge(sem["sv"], tok_h_done)
                    if need:
                        v.wait_ge(sem["smm"], need)
                    emit_bmax(v, src, 1408, bmx[:, 0:88], bsc[:, 0:88], brr[:, 0:88])
                    emit_quant_f16(v, src, 1408, hqx[par][:, 0:1408], bsc[:, 0:88], brr[:, 0:88])
                    v.drain()
                    v.engine_nop().then_inc(sem["sv"], 1)
                    cnt["sv"] += 1
                on("vector", b_hq_v)
                tok_hq = cnt["sv"]

                smm_ht0 = cnt["smm"]
                ss_ht0 = cnt["ss"]

                def b_hq_t(t, par=par, tok_hq=tok_hq, ss_ht0=ss_ht0):
                    t.wait_ge(sem["sv"], tok_hq)
                    t.wait_ge(sem["ss"], ss_ht0)
                    for j in range(11):
                        bA = pb[4 + j % 2]
                        if j >= 2:
                            t.wait_ge(sem["ss"], ss_ht0 + (j - 1))
                        t.matmul(bA[0:128, 0:128], hqx[par][:, j * 128:(j + 1) * 128],
                                 identh[:, :], start=True, stop=True).then_inc(sem["smm"], 1)
                        cnt["smm"] += 1
                on("tensor", b_hq_t)

                def b_hq_s(s, tcg=tcg, hf=hf, smm_ht0=smm_ht0):
                    for j in range(11):
                        bA = pb[4 + j % 2]
                        s.wait_ge(sem["smm"], smm_ht0 + j + 1)
                        s.copy(hqT[:, hf * 11 + j, tcg * 128:(tcg + 1) * 128],
                               bA[0:128, 0:128]).then_inc(sem["ss"], 1)
                        cnt["ss"] += 1
                on("scalar", b_hq_s)
        tok_hqT_done = cnt["ss"]


        def b_norm_t(t, m=m, prev=cnt["ss"]):
            if m == 0:
                t.wait_ge(sem["sv"], tok_attc)
            t.wait_ge(sem["ss"], prev)
            for p in range(8):
                ins = t.matmul(pb[6][0:16, 0:512], ksb8[:, p, :], q_res[:, p, m * MT:(m + 1) * MT],
                               start=(p == 0), stop=(p == 7))
            ins.then_inc(sem["smm"], 1)
            cnt["smm"] += 1
        on("tensor", b_norm_t)
        tok_norm = cnt["smm"]

        def b_nrec_v(v, tok_norm=tok_norm):
            v.wait_ge(sem["smm"], tok_norm)
            v.tensor_scalar(nplus[:, :], pb[6][0:16, 0:512], 1e-6, None, op0=OP.add)
            v.drain()
            v.reciprocal(nplus[:, :], nplus[:, :])
            v.drain()
            v.tensor_copy(nrec16[:, :], nplus[:, :])
            v.drain()
            v.engine_nop().then_inc(sem["sv"], 1)
            cnt["sv"] += 1
        on("vector", b_nrec_v)
        tok_nrec = cnt["sv"]

        ss_rb0 = cnt["ss"]
        sv_at0 = cnt["sv"]
        for p in range(8):
            par = p % 2

            def b_rb_t(t, p=p, par=par, tok_nrec=tok_nrec, ss_rb0=ss_rb0, sv_at0=sv_at0, m=m):
                if p == 0:
                    t.wait_ge(sem["sv"], tok_nrec)
                if p >= 2:
                    t.wait_ge(sem["ss"], ss_rb0 + (p - 1))
                    t.wait_ge(sem["sv"], sv_at0 + (p - 1))
                t.matmul(pb[6 + par][:, 0:512], selb[:, p * 128:(p + 1) * 128], nrec16[:, :],
                         start=True, stop=True).then_inc(sem["smm"], 1)
                cnt["smm"] += 1
                t.matmul(pb[2 + par][:, 0:512], bdkv[:, p, :], q_res[:, p, m * MT:(m + 1) * MT],
                         start=True, stop=True).then_inc(sem["smm"], 1)
                cnt["smm"] += 1
            on("tensor", b_rb_t)
            tok_rbmm = cnt["smm"]

            def b_rb_s(s, par=par, tok_rbmm=tok_rbmm):
                s.wait_ge(sem["smm"], tok_rbmm - 1)
                s.copy(rbx[par][:, :], pb[6 + par][:, 0:512])
                s.drain()
                s.nop().then_inc(sem["ss"], 1)
                cnt["ss"] += 1
            on("scalar", b_rb_s)
            tok_rbs = cnt["ss"]

            def b_at_v(v, p=p, par=par, tok_rbmm=tok_rbmm, tok_rbs=tok_rbs):
                v.wait_ge(sem["smm"], tok_rbmm)
                v.wait_ge(sem["ss"], tok_rbs)
                v.tensor_tensor(attn2[:, p, :], pb[2 + par][:, 0:512], rbx[par][:, :], op=OP.mult)
                v.drain()
                v.drain()
                v.engine_nop().then_inc(sem["sv"], 1)
                cnt["sv"] += 1
            on("vector", b_at_v)
        tok_attn2 = cnt["sv"]
        tok_bank_free[0] = tok_attn2
        tok_bank_free[1] = tok_attn2

        if DBG:
            def dump_at2(sy, m=m, t=tok_attn2):
                sy.wait_ge(sem["sv"], t)
                sy.dma_start(out=dbg_at2all[m], in_=attn2[:, :, :]).then_inc(sem["sgp"], 16)
                cnt["sgp"] += 16
            on("sync", dump_at2)

        for ic in range(ICN):
            par = ic % 2

            def b_wd_l(sync, ic=ic, par=par, w=wd_used_tok[par], m=m):
                if m == 0 and ic == 0:
                    sync.wait_ge(sem["sca"], SCC_AG)
                if w:
                    sync.wait_ge(sem["smm"], w)
                sn = "swda" if par == 0 else "swdb"
                sync.dma_start(out=wd_ic[par][:, :],
                               in_=ag_wd_out.ap()[ic % 8, ic // 8, :, :]).then_inc(sem[sn], 16)
                cnt[sn] += 16
            on("sync", b_wd_l)
            wd_tok = cnt["swda" if par == 0 else "swdb"]
            sem_wd = "swda" if par == 0 else "swdb"

            def b_wd_t(t, ic=ic, par=par, wt=wd_tok, ta=tok_attn2, tok_hqT_done=tok_hqT_done,
                       sem_wd=sem_wd):
                t.wait_ge(sem[sem_wd], wt)
                if ic == 0:
                    t.wait_ge(sem["sv"], ta)
                    t.wait_ge(sem["ss"], tok_hqT_done)
                for tcg in range(4):
                    for fg in range(2):
                        ins = t.matmul(pb[tcg * 2 + fg][:, 0:512],
                                       hqT[:, ic, tcg * 128:(tcg + 1) * 128],
                                       wd_ic[par][:, fg * 512:(fg + 1) * 512],
                                       start=(ic == 0), stop=False)
                ins.then_inc(sem["smm"], 1)
                cnt["smm"] += 1
            on("tensor", b_wd_t)
            wd_used_tok[par] = cnt["smm"]

        for p in range(8):
            par = p % 2

            def b_wo_l(sync, p=p, par=par, w=wo_used[par], first=(m == 0 and p < 2)):
                if first:
                    sync.wait_ge(sem["sx"], WC_SX)
                if w:
                    sync.wait_ge(sem["smm"], w)
                sn = "swoa" if par == 0 else "swob"
                sync.dma_start(out=wout_p[par][:, :],
                               in_=wout16.ap()[p * 128:(p + 1) * 128, :]).then_inc(sem[sn], 16)
                cnt[sn] += 16
            on("sync", b_wo_l)
            wo_tok = cnt["swoa" if par == 0 else "swob"]
            sem_wo = "swoa" if par == 0 else "swob"

            def b_wo_t(t, p=p, par=par, wt=wo_tok, sem_wo=sem_wo):
                t.wait_ge(sem[sem_wo], wt)
                for tcg in range(4):
                    for fg in range(2):
                        ins = t.matmul(pb[tcg * 2 + fg][:, 0:512],
                                       attn2[:, p, tcg * 128:(tcg + 1) * 128],
                                       wout_p[par][:, fg * 512:(fg + 1) * 512],
                                       start=False, stop=False)
                ins.then_inc(sem["smm"], 1)
                cnt["smm"] += 1
            on("tensor", b_wo_t)
            wo_used[par] = cnt["smm"]

        out_mm_tok = []

        def b_bias_t(t):
            for tcg in range(4):
                for fg in range(2):
                    t.matmul(pb[tcg * 2 + fg][:, 0:512], ones1[0:1, :],
                             boutr[0:1, fg * 512:(fg + 1) * 512],
                             start=False, stop=True).then_inc(sem["smm"], 1)
                    cnt["smm"] += 1
                    out_mm_tok.append(cnt["smm"])
        on("tensor", b_bias_t)

        for tcg in range(4):
            par = tcg % 2

            def b_oev_v(v, tcg=tcg, par=par, w=tok_outsb_free[par],
                        tokk=out_mm_tok[tcg * 2 + 1]):
                v.wait_ge(sem["smm"], tokk)
                if w:
                    v.wait_ge(sem["ssta" if par == 0 else "sstb"], w)
                for fg in range(2):
                    v.tensor_tensor(out_sb[par][:, fg * 512:(fg + 1) * 512],
                                    pb[tcg * 2 + fg][:, 0:512],
                                    x_slab[:, tcg, fg * 512:(fg + 1) * 512], op=OP.add)
                v.drain()
                v.drain()
                v.engine_nop().then_inc(sem["sv"], 1)
                cnt["sv"] += 1
            on("vector", b_oev_v)
            tok_oev = cnt["sv"]

            def b_store(sync, tcg=tcg, par=par, m=m, tok_oev=tok_oev):
                sync.wait_ge(sem["sv"], tok_oev)
                sn = "ssta" if par == 0 else "sstb"
                sync.dma_start(out=out_d[m * MT + tcg * 128: m * MT + (tcg + 1) * 128, :],
                               in_=out_sb[par][:, :]).then_inc(sem[sn], 16)
                cnt[sn] += 16
            on("sync", b_store)
            tok_outsb_free[par] = cnt["ssta" if par == 0 else "sstb"]
        sta_prev = cnt["ssta"]
        stb_prev = cnt["sstb"]
        tok_x_free_b = cnt["sv"]
        tok_bank_free[0] = cnt["sv"]
        tok_bank_free[1] = cnt["sv"]
        tok_hqT_free = cnt["smm"]

    SSTA_TOT = cnt["ssta"]
    SSTB_TOT = cnt["sstb"]

    if DBG:
        def dump(g):
            g.wait_ge(sem["ssta"], SSTA_TOT)
            g.wait_ge(sem["sstb"], SSTB_TOT)
            g.dma_start(out=dbg_agwgu, in_=ag_wgu_out.ap()[:, :, :]).then_inc(sem["sgp"], 16)
            cnt["sgp"] += 16
            g.dma_start(out=dbg_wgusb, in_=wgu_sb[:, :, :, :]).then_inc(sem["sgp"], 16)
            cnt["sgp"] += 16
            g.dma_start(out=dbg_gup, in_=dbg_cap_gup[:, :]).then_inc(sem["sgp"], 16)
            cnt["sgp"] += 16
            g.dma_start(out=dbg_sgt, in_=dbg_cap_sgt[:, :]).then_inc(sem["sgp"], 16)
            cnt["sgp"] += 16
            for dst, srct in [(dbg_qres, q_res), (dbg_bdkv, bdkv), (dbg_ksb8, ksb8),
                              (dbg_xqf, xq_feat), (dbg_hqT, hqT), (dbg_attn2, attn2),
                              (dbg_htok, h_tok), (dbg_nrec, nplus)]:
                g.dma_start(out=dst, in_=srct[:, :] if len(dst.shape) == 2 else srct[:, :, :]).then_inc(sem["sgp"], 16)
                cnt["sgp"] += 16
            g.dma_start(out=dbg_st[:, 0:16], in_=stA[:, :]).then_inc(sem["sgp"], 16)
            g.dma_start(out=dbg_st[:, 16:32], in_=stB[:, :]).then_inc(sem["sgp"], 16)
            g.dma_start(out=dbg_st[:, 32:48], in_=stC[:, :]).then_inc(sem["sgp"], 16)
            g.dma_start(out=dbg_st[:, 48:64], in_=stD[:, :]).then_inc(sem["sgp"], 16)
            cnt["sgp"] += 64
            g.wait_ge(sem["sgp"], cnt["sgp"])
        on("gpsimd", dump)

    def fin(sync):
        sync.wait_ge(sem["ssta"], SSTA_TOT)
        sync.wait_ge(sem["sstb"], SSTB_TOT)
        sync.wait_ge(sem["scc"], cnt["scc"])
        sync.wait_ge(sem["sca"], cnt["sca"])
        if DBG:
            sync.wait_ge(sem["sgp"], cnt["sgp"])
    on("sync", fin)

    es.close()
    return nc


_NC_CACHE = {}


def kernel(L0_attn_x, L0_attn_W_qkv, L0_attn_W_f, L0_attn_b_f, L0_attn_W_out,
           L0_attn_b_out, L0_ffn_w_gate, L0_ffn_w_up, L0_ffn_w_down):
    x = np.ascontiguousarray(np.asarray(L0_attn_x, dtype=np.float32)).reshape(NCORE, TCORE, D)
    wqkv = np.ascontiguousarray(np.asarray(L0_attn_W_qkv, dtype=np.float32))
    wf = np.ascontiguousarray(np.asarray(L0_attn_W_f, dtype=np.float32))
    b_f = np.asarray(L0_attn_b_f, dtype=np.float32)
    wout = np.ascontiguousarray(np.asarray(L0_attn_W_out, dtype=np.float32))
    b_out = np.asarray(L0_attn_b_out, dtype=np.float32)
    wg = np.ascontiguousarray(np.asarray(L0_ffn_w_gate, dtype=np.float32))
    wu = np.ascontiguousarray(np.asarray(L0_ffn_w_up, dtype=np.float32))
    wd = np.ascontiguousarray(np.asarray(L0_ffn_w_down, dtype=np.float32))

    bft = np.tile(b_f, 16)
    bqc = np.ascontiguousarray(bft.reshape(8, 128).T)
    bkvr = np.ascontiguousarray(np.concatenate([bft, np.zeros(D, np.float32)])[None, :])
    boutr = np.ascontiguousarray(b_out[None, :])
    idb = np.eye(128, dtype=np.float32).astype(bf16)
    idh = np.eye(128, dtype=np.float32).astype(np.float16)
    ones1 = np.ones((1, 128), np.float32).astype(bf16)
    onesc = np.ones((128, 8), np.float32).astype(bf16)
    selb = np.zeros((16, 8 * 128), np.float32)
    for p in range(8):
        selb[2 * p, p * 128: p * 128 + 64] = 1.0
        selb[2 * p + 1, p * 128 + 64: p * 128 + 128] = 1.0
    selb = selb.astype(bf16)

    in_maps = []
    for c in range(NCORE):
        wd_ch = []
        for s in range(WDCH):
            ic = s * 8 + c
            if ic < ICN:
                wd_ch.append(wd[:, ic * 128:(ic + 1) * 128])
            else:
                wd_ch.append(np.zeros((D, 128), np.float32))
        in_maps.append({
            "x": np.ascontiguousarray(x[c]),
            "wqkv": wqkv, "wf": wf, "wout": wout,
            "bqc": bqc, "bkvr": bkvr, "boutr": boutr,
            "wg_sl": np.ascontiguousarray(wg[:, c * 128:(c + 1) * 128]),
            "wu_sl": np.ascontiguousarray(wu[:, c * 128:(c + 1) * 128]),
            "wd_sl": np.ascontiguousarray(np.stack(wd_ch, 0)),
            "idb": idb, "idh": idh, "ones1": ones1, "onesc": onesc, "selb": selb,
        })

    if "prog" not in _NC_CACHE:
        _NC_CACHE["prog"] = build_nc()
    nc = _NC_CACHE["prog"]
    res = run_bass_kernel_spmd(nc, in_maps, core_ids=list(range(NCORE)))
    import os as _os
    if int(_os.environ.get("KDBG", "0")):
        kernel.DBG_RES = res
    out = np.concatenate([res.results[c]["out"][None] for c in range(NCORE)], 0)
    return np.ascontiguousarray(out.reshape(2, 16384, D))


# version 20
# speedup vs baseline: 1.2068x; 1.0132x over previous
import numpy as np
import ml_dtypes
import concourse.bass as bass
import concourse.mybir as mybir
from concourse.bass_utils import run_bass_kernel_spmd
from contextlib import ExitStack

F32 = mybir.dt.float32
F16 = mybir.dt.float16
BF16 = mybir.dt.bfloat16
I16 = mybir.dt.int16
I32 = mybir.dt.int32
OP = mybir.AluOpType
AF = mybir.ActivationFunctionType
AX = mybir.AxisListType
bf16 = ml_dtypes.bfloat16

NCORE = 8
TCORE = 4096
MT = 512
NMT = TCORE // MT
D = 1024
FI = 2816
ICN = 22
NIG = 11
WDCH = 3
MIDS = [0.25, 0.75, 1.25, 1.75, 2.5, 3.5, 5.0]


def build_nc():
    nc = bass.Bass("TRN2", target_bir_lowering=False)
    x_in = nc.dram_tensor("x", [TCORE, D], F32, kind="ExternalInput").ap()
    wqkv_in = nc.dram_tensor("wqkv", [D, 3 * D], F32, kind="ExternalInput").ap()
    wf_in = nc.dram_tensor("wf", [64, 64], F32, kind="ExternalInput").ap()
    wout_in = nc.dram_tensor("wout", [D, D], F32, kind="ExternalInput").ap()
    bqc_in = nc.dram_tensor("bqc", [128, 8], F32, kind="ExternalInput").ap()
    bkvr_in = nc.dram_tensor("bkvr", [1, 2 * D], F32, kind="ExternalInput").ap()
    bout_in = nc.dram_tensor("boutr", [1, D], F32, kind="ExternalInput").ap()
    wg_sl_in = nc.dram_tensor("wg_sl", [FI, 128], F32, kind="ExternalInput").ap()
    wu_sl_in = nc.dram_tensor("wu_sl", [FI, 128], F32, kind="ExternalInput").ap()
    wd_sl_in = nc.dram_tensor("wd_sl", [WDCH, D, 128], F32, kind="ExternalInput").ap()
    idb_in = nc.dram_tensor("idb", [128, 128], BF16, kind="ExternalInput").ap()
    idh_in = nc.dram_tensor("idh", [128, 128], F16, kind="ExternalInput").ap()
    ones1_in = nc.dram_tensor("ones1", [1, 128], BF16, kind="ExternalInput").ap()
    onesc_in = nc.dram_tensor("onesc", [128, 8], BF16, kind="ExternalInput").ap()
    selb_in = nc.dram_tensor("selb", [16, 8 * 128], BF16, kind="ExternalInput").ap()
    out_d = nc.dram_tensor("out", [TCORE, D], F32, kind="ExternalOutput").ap()
    import os as _os
    DBG = bool(int(_os.environ.get("KDBG", "0")))
    if DBG:
        dbg_qres = nc.dram_tensor("dbg_qres", [128, 8, TCORE], F32, kind="ExternalOutput").ap()
        dbg_bdkv = nc.dram_tensor("dbg_bdkv", [128, 8, 128], F32, kind="ExternalOutput").ap()
        dbg_ksb8 = nc.dram_tensor("dbg_ksb8", [128, 8, 16], F32, kind="ExternalOutput").ap()
        dbg_xqf = nc.dram_tensor("dbg_xqf", [128, 8, MT], F32, kind="ExternalOutput").ap()
        dbg_hqT = nc.dram_tensor("dbg_hqT", [128, ICN, MT], F32, kind="ExternalOutput").ap()
        dbg_attn2 = nc.dram_tensor("dbg_attn2", [128, 8, MT], F32, kind="ExternalOutput").ap()
        dbg_htok = nc.dram_tensor("dbg_htok", [128, 2, FI], F32, kind="ExternalOutput").ap()
        dbg_nrec = nc.dram_tensor("dbg_nrec", [16, 512], F32, kind="ExternalOutput").ap()
        dbg_st = nc.dram_tensor("dbg_st", [128, 64], F32, kind="ExternalOutput").ap()
        dbg_watt = nc.dram_tensor("dbg_watt", [128, 8, 3 * D], F32, kind="ExternalOutput").ap()
        dbg_xnf = nc.dram_tensor("dbg_xnf", [128, 8, MT], F32, kind="ExternalOutput").ap()
        dbg_xnb = nc.dram_tensor("dbg_xnb", [128, 4, D], F32, kind="ExternalOutput").ap()
        dbg_kvall = nc.dram_tensor("dbg_kvall", [128, 1032], F32, kind="ExternalOutput").ap()
        dbg_xqall = nc.dram_tensor("dbg_xqall", [NMT, 128, 8, MT], F16, kind="ExternalOutput").ap()
        dbg_hqTall = nc.dram_tensor("dbg_hqTall", [NMT, 128, ICN, MT], F16, kind="ExternalOutput").ap()
        dbg_at2all = nc.dram_tensor("dbg_at2all", [NMT, 128, 8, MT], BF16, kind="ExternalOutput").ap()
        dbg_agwgu = nc.dram_tensor("dbg_agwgu", [8, 128, 2 * FI], F16, kind="ExternalOutput").ap()
        dbg_wgusb = nc.dram_tensor("dbg_wgusb", [128, 2, 8, 512], F16, kind="ExternalOutput").ap()
        dbg_gup = nc.dram_tensor("dbg_gup", [128, 512], F16, kind="ExternalOutput").ap()
        dbg_sgt = nc.dram_tensor("dbg_sgt", [128, 256], F16, kind="ExternalOutput").ap()
    ag_wgu_in = nc.dram_tensor("ag_wgu_in", [128, 2 * FI], F16)
    ag_wgu_out = nc.dram_tensor("ag_wgu_out", [8, 128, 2 * FI], F16)
    ag_wd_in = nc.dram_tensor("ag_wd_in", [WDCH, 128, D], F16)
    ag_wd_out = nc.dram_tensor("ag_wd_out", [8, WDCH, 128, D], F16)
    ar_in = nc.dram_tensor("ar_in", [128, 1032], F32)
    ar_out = nc.dram_tensor("ar_out", [128, 1032], F32)
    wout16 = nc.dram_tensor("wout16", [D, D], BF16)

    es = ExitStack()
    SB = lambda name, shape, dt: es.enter_context(nc.sbuf_tensor(name, shape, dt))
    pb = [es.enter_context(nc.psum_tensor(f"pb{i}", [128, 512], F32)) for i in range(8)]
    sem_names = ["sx", "sw", "sg1", "smm", "sv", "ss", "sgp", "scc", "swc", "sca",
                 "sgpa", "sgpb", "swga", "swgb", "swda", "swdb", "swoa", "swob", "ssta", "sstb"]
    sem = {n: es.enter_context(nc.semaphore(n)) for n in sem_names}
    cnt = {n: 0 for n in sem_names}
    block = es.enter_context(nc.Block())

    def on(eng, fn):
        getattr(block, eng)(fn)

    identb = SB("identb", [128, 128], BF16)
    identh = SB("identh", [128, 128], F16)
    ones1 = SB("ones1s", [1, 128], BF16)
    onesc = SB("onescs", [128, 8], BF16)
    selb = SB("selbs", [16, 8 * 128], BF16)
    bqc = SB("bqcs", [128, 8], F32)
    boutr = SB("boutrs", [1, D], BF16)
    q_res = SB("q_res", [128, 8, TCORE], BF16)
    x_slab = SB("x_slab", [128, 4, D], F32)
    stA = SB("stA", [128, 16], F32)
    stB = SB("stB", [128, 16], F32)
    stC = SB("stC", [128, 16], F32)
    stD = SB("stD", [128, 16], F32)
    qv = SB("qv", [128, 1408], F32)
    ql = SB("ql", [128, 1408], F16)
    qh = SB("qh", [128, 1408], F16)
    qsm = SB("qsm", [128, 1408], F16)
    bmx = SB("bmx", [128, 88], F32)
    bsc = SB("bsc", [128, 88], F32)
    brr = SB("brr", [128, 88], F32)
    bdkv = SB("bdkv", [128, 8, 128], BF16)
    ksb8 = SB("ksb8", [128, 8, 16], BF16)

    def p0_sync(sync):
        for dst, src in [(identb, idb_in), (identh, idh_in), (ones1, ones1_in),
                         (onesc, onesc_in), (selb, selb_in), (bqc, bqc_in)]:
            sync.dma_start(out=dst[:, :], in_=src[:, :]).then_inc(sem["sw"], 16)
            cnt["sw"] += 16
    on("sync", p0_sync)
    SW_CONST = cnt["sw"]


    with nc.sbuf_tensor("W_att", [128, 8, 3 * D], BF16) as W_att, \
         nc.sbuf_tensor("ln_sq", [128, 4 * D], F32) as ln_sq, \
         nc.sbuf_tensor("bkvrs", [1, 2 * D], BF16) as bkvr:

        with nc.sbuf_tensor("wqkv_sb", [128, 8, 2 * D], BF16) as wqkv_sb, \
             nc.sbuf_tensor("wf_sb", [64, 64], BF16) as wf_sb, \
             nc.sbuf_tensor("wqt0", [64, 128], BF16) as wqt0, \
             nc.sbuf_tensor("wqt1", [64, 128], BF16) as wqt1, \
             nc.sbuf_tensor("wc16", [128, 1, D], BF16) as wc16:
            wqt = [wqt0, wqt1]

            def p1_gp(g):
                g.dma_start(out=wqkv_sb[:, :, :],
                            in_=wqkv_in[:, 0:2 * D].rearrange("(c p) f -> p c f", p=128)).then_inc(sem["sg1"], 16)
                g.dma_start(out=W_att[:, :, 2 * D:3 * D],
                            in_=wqkv_in[:, 2 * D:3 * D].rearrange("(c p) f -> p c f", p=128)).then_inc(sem["sg1"], 16)
                g.dma_start(out=wf_sb[:, :], in_=wf_in[:, :]).then_inc(sem["sg1"], 16)
                g.dma_start(out=bkvr[:, :], in_=bkvr_in[:, :]).then_inc(sem["sg1"], 16)
                g.dma_start(out=boutr[:, :], in_=bout_in[:, :]).then_inc(sem["sg1"], 16)
                cnt["sg1"] += 80
            on("gpsimd", p1_gp)
            SG1_W = cnt["sg1"]
            SV_VCOPY = cnt["sv"]

            sx_wc0 = cnt["sx"]

            def p1_wc_g(g):
                for p in range(8):
                    if p >= 1:
                        g.wait_ge(sem["sx"], sx_wc0 + 16 * p)
                    g.dma_start(out=wc16[:, 0, :],
                                in_=wout_in[p * 128:(p + 1) * 128, :]).then_inc(sem["swc"], 16)
                    cnt["swc"] += 16
            on("gpsimd", p1_wc_g)

            def p1_wc_s(sync):
                for p in range(8):
                    sync.wait_ge(sem["swc"], 16 * (p + 1))
                    sync.dma_start(out=wout16.ap()[p * 128:(p + 1) * 128, :],
                                   in_=wc16[:, 0, :]).then_inc(sem["sx"], 16)
                    cnt["sx"] += 16
            on("sync", p1_wc_s)
            WC_SX = cnt["sx"]

            ss_base = cnt["ss"]
            smm_base = cnt["smm"]

            def p1_t(t):
                t.wait_ge(sem["sg1"], SG1_W)
                t.wait_ge(sem["sw"], SW_CONST)
                it = 0
                for c in range(8):
                    for qk in range(2):
                        for h in range(16):
                            bA = pb[it % 2]
                            bB = pb[2 + it % 2]
                            if it >= 2:
                                t.wait_ge(sem["ss"], ss_base + 2 * (it - 1))
                            src = wqkv_sb[:, c, qk * D + h * 64: qk * D + (h + 1) * 64]
                            t.matmul(bA[0:64, 0:128], src, identb[:, :],
                                     start=True, stop=True).then_inc(sem["smm"], 1)
                            t.wait_ge(sem["ss"], ss_base + 2 * it + 1)
                            t.matmul(bB[0:128, 0:64], wqt[it % 2][0:64, :], wf_sb[:, :],
                                     start=True, stop=True).then_inc(sem["smm"], 1)
                            it += 1
                cnt["smm"] += 512
            on("tensor", p1_t)

            def p1_s(s):
                it = 0
                for c in range(8):
                    for qk in range(2):
                        for h in range(16):
                            bA = pb[it % 2]
                            bB = pb[2 + it % 2]
                            s.wait_ge(sem["smm"], smm_base + 2 * it + 1)
                            s.copy(wqt[it % 2][0:64, :], bA[0:64, 0:128]).then_inc(sem["ss"], 1)
                            s.wait_ge(sem["smm"], smm_base + 2 * it + 2)
                            s.copy(W_att[:, c, qk * D + h * 64: qk * D + (h + 1) * 64],
                                   bB[0:128, 0:64]).then_inc(sem["ss"], 1)
                            it += 1
                cnt["ss"] += 512
            on("scalar", p1_s)
        SS_FOLD = cnt["ss"]
        SMM_FOLD_END = cnt["smm"]

        def emit_quant_f16(v, src2d, W, out2d, sc_ap, rr_ap):
            nb = W // 16
            v3 = qv[:, 0:W].rearrange("p (b k) -> p b k", k=16)
            v.tensor_tensor(v3, src2d.rearrange("p (b k) -> p b k", k=16),
                            rr_ap.unsqueeze(2).broadcast_to([128, nb, 16]), op=OP.mult)
            v.drain()
            v.tensor_scalar(qsm[:, 0:W], qv[:, 0:W], 0.0, None, op0=OP.is_lt)
            v.tensor_scalar(qv[:, 0:W].bitcast(I32), qv[:, 0:W].bitcast(I32), 0x7FFFFFFF, None,
                            op0=OP.bitwise_and)
            v.drain()
            a = qv[:, 0:W]
            v.tensor_scalar(qsm[:, 0:W], qsm[:, 0:W], -2.0, 1.0, op0=OP.mult, op1=OP.add)
            v.tensor_scalar(ql[:, 0:W], a, MIDS[0], None, op0=OP.is_gt)
            v.tensor_scalar(qh[:, 0:W], a, MIDS[4], None, op0=OP.is_gt)
            v.drain()
            v.scalar_tensor_tensor(ql[:, 0:W], a, MIDS[1], ql[:, 0:W], op0=OP.is_gt, op1=OP.add)
            v.scalar_tensor_tensor(qh[:, 0:W], a, MIDS[5], qh[:, 0:W], op0=OP.is_gt, op1=OP.add)
            v.drain()
            v.scalar_tensor_tensor(ql[:, 0:W], a, MIDS[2], ql[:, 0:W], op0=OP.is_gt, op1=OP.add)
            v.scalar_tensor_tensor(qh[:, 0:W], a, MIDS[6], qh[:, 0:W], op0=OP.is_gt, op1=OP.add)
            v.drain()
            v.scalar_tensor_tensor(ql[:, 0:W], a, MIDS[3], ql[:, 0:W], op0=OP.is_gt, op1=OP.add)
            v.scalar_tensor_tensor(qh[:, 0:W], a, MIDS[6], qh[:, 0:W], op0=OP.is_gt, op1=OP.add)
            v.drain()
            v.scalar_tensor_tensor(ql[:, 0:W], ql[:, 0:W], 0.5, qh[:, 0:W], op0=OP.mult, op1=OP.add)
            v.drain()
            v.tensor_tensor(qh[:, 0:W].rearrange("p (b k) -> p b k", k=16),
                            ql[:, 0:W].rearrange("p (b k) -> p b k", k=16),
                            sc_ap.unsqueeze(2).broadcast_to([128, nb, 16]), op=OP.mult)
            v.drain()
            v.tensor_tensor(out2d, qh[:, 0:W], qsm[:, 0:W], op=OP.mult)

        def emit_bmax(v, src2d, W, bm_ap, sc_ap, rr_ap):
            v.tensor_reduce(bm_ap, src2d.rearrange("p (b k) -> p b k", k=16),
                            axis=AX.X, op=OP.max, apply_absolute_value=True)
            v.drain()
            v.tensor_scalar(sc_ap, bm_ap, 1.0 / 6.0, 1e-12, op0=OP.mult, op1=OP.max)
            v.drain()
            v.reciprocal(rr_ap, sc_ap)
            v.drain()

        with nc.sbuf_tensor("wg_sb", [128, ICN, 128], F32) as wg_sb, \
             nc.sbuf_tensor("wu_sb", [128, ICN, 128], F32) as wu_sb, \
             nc.sbuf_tensor("wd_sb", [128, WDCH, 8, 128], F32) as wd_sb, \
             nc.sbuf_tensor("wq16a", [128, 128], F16) as wq16a, \
             nc.sbuf_tensor("wq16b", [128, 128], F16) as wq16b, \
             nc.sbuf_tensor("wt16a", [128, 128], F16) as wt16a, \
             nc.sbuf_tensor("wt16b", [128, 128], F16) as wt16b:
            wq16 = [wq16a, wq16b]
            wt16 = [wt16a, wt16b]

            def p2_sync(sync):
                sync.wait_ge(sem["smm"], SMM_FOLD_END)
                sync.wait_ge(sem["sx"], WC_SX)
                sync.dma_start(out=wg_sb[:, :, :],
                               in_=wg_sl_in[:, :].rearrange("(a p) k -> p a k", p=128)).then_inc(sem["sw"], 16)
                sync.dma_start(out=wu_sb[:, :, :],
                               in_=wu_sl_in[:, :].rearrange("(a p) k -> p a k", p=128)).then_inc(sem["sw"], 16)
                sync.dma_start(out=wd_sb[:, :, :, :],
                               in_=wd_sl_in[:, :, :].rearrange("j (a p) k -> p j a k", p=128)).then_inc(sem["sw"], 16)
                cnt["sw"] += 48
            on("sync", p2_sync)
            SW_WSH = cnt["sw"]

            tiles = []
            for j in range(ICN):
                col = (j // 2) * 512 + (j % 2) * 128
                tiles.append((wg_sb[:, j, :], ag_wgu_in.ap()[:, col:col + 128]))
            for j in range(ICN):
                col = (j // 2) * 512 + 256 + (j % 2) * 128
                tiles.append((wu_sb[:, j, :], ag_wgu_in.ap()[:, col:col + 128]))
            for jj in range(WDCH):
                for fc in range(8):
                    tiles.append((wd_sb[:, jj, fc, :], ag_wd_in.ap()[jj, :, fc * 128:(fc + 1) * 128]))
            NT = len(tiles)
            sv0, smm0, ss0, sgp0 = cnt["sv"], cnt["smm"], cnt["ss"], cnt["sgp"]

            def p2_v(v):
                v.wait_ge(sem["sw"], SW_WSH)
                for i, (srcap, dst) in enumerate(tiles):
                    if i >= 2:
                        v.wait_ge(sem["smm"], smm0 + (i - 1))
                    emit_bmax(v, srcap, 128, bmx[:, 0:8], bsc[:, 0:8], brr[:, 0:8])
                    emit_quant_f16(v, srcap, 128, wq16[i % 2][:, :], bsc[:, 0:8], brr[:, 0:8])
                    v.drain()
                    v.engine_nop().then_inc(sem["sv"], 1)
                    cnt["sv"] += 1
            on("vector", p2_v)

            def p2_t(t):
                for i in range(NT):
                    t.wait_ge(sem["sv"], sv0 + i + 1)
                    if i >= 2:
                        t.wait_ge(sem["ss"], ss0 + (i - 1))
                    t.matmul(pb[4 + i % 2][0:128, 0:128], wq16[i % 2][:, :], identh[:, :],
                             start=True, stop=True).then_inc(sem["smm"], 1)
                    cnt["smm"] += 1
            on("tensor", p2_t)

            def p2_s(s):
                for i in range(NT):
                    s.wait_ge(sem["smm"], smm0 + i + 1)
                    if i >= 2:
                        sn = "sgpa" if i % 2 == 0 else "sgpb"
                        s.wait_ge(sem[sn], 16 * ((i - 2) // 2 + 1))
                    s.copy(wt16[i % 2][:, :], pb[4 + i % 2][0:128, 0:128]).then_inc(sem["ss"], 1)
                    cnt["ss"] += 1
            on("scalar", p2_s)

            def p2_dump(sync):
                for i, (src, dst) in enumerate(tiles):
                    sync.wait_ge(sem["ss"], ss0 + i + 1)
                    sn = "sgpa" if i % 2 == 0 else "sgpb"
                    sync.dma_start(out=dst, in_=wt16[i % 2][:, :]).then_inc(sem[sn], 16)
                    cnt[sn] += 16
            on("sync", p2_dump)

            def p2_g(g):
                g.wait_ge(sem["sgpa"], 16 * ((NT + 1) // 2))
                g.wait_ge(sem["sgpb"], 16 * (NT // 2))
                g.collective_compute("AllGather", OP.bypass,
                                     replica_groups=[[0, 1, 2, 3, 4, 5, 6, 7]],
                                     ins=[ag_wgu_in.ap().opt()],
                                     outs=[ag_wgu_out.ap().opt()]).then_inc(sem["sca"], 1)
                cnt["sca"] += 1
                g.collective_compute("AllGather", OP.bypass,
                                     replica_groups=[[0, 1, 2, 3, 4, 5, 6, 7]],
                                     ins=[ag_wd_in.ap().opt()],
                                     outs=[ag_wd_out.ap().opt()]).then_inc(sem["sca"], 1)
                cnt["sca"] += 1
            on("gpsimd", p2_g)
        SCC_AG = cnt["sca"]
        P2_SMM_END = cnt["smm"]
        P2_SGPA_END = cnt["sgpa"]
        P2_SGPB_END = cnt["sgpb"]

        def emit_ln(sq_view):
            def s_sq(sc):
                sc.activation(sq_view, x_slab[:, :, :].rearrange("p c f -> p (c f)"),
                              AF.Square).then_inc(sem["ss"], 1)
                cnt["ss"] += 1
            on("scalar", s_sq)
            tok_sq = cnt["ss"]

            def v_stats(v, tok_sq=tok_sq, sq_view=sq_view):
                for c in range(4):
                    v.tensor_reduce(stA[:, c:c + 1], x_slab[:, c, :], axis=AX.X, op=OP.add)
                v.wait_ge(sem["ss"], tok_sq)
                for c in range(4):
                    v.tensor_reduce(stA[:, 4 + c:5 + c], sq_view[:, c * D:(c + 1) * D],
                                    axis=AX.X, op=OP.add)
                v.drain()
                v.tensor_scalar(stB[:, 0:4], stA[:, 0:4], 1.0 / D, None, op0=OP.mult)
                v.tensor_scalar(stB[:, 4:8], stA[:, 4:8], 1.0 / D, None, op0=OP.mult)
                v.drain()
                v.tensor_tensor(stC[:, 0:4], stB[:, 0:4], stB[:, 0:4], op=OP.mult)
                v.drain()
                v.tensor_tensor(stA[:, 8:12], stB[:, 4:8], stC[:, 0:4], op=OP.subtract)
                v.drain()
                v.tensor_scalar(stB[:, 8:12], stA[:, 8:12], 1e-5, None, op0=OP.add)
                v.drain()
                v.reciprocal(stC[:, 8:12], stB[:, 8:12])
                v.drain()
                v.engine_nop().then_inc(sem["sv"], 1)
                cnt["sv"] += 1
            on("vector", v_stats)
            tok_iv = cnt["sv"]

            def s_sqrt(sc, tok_iv=tok_iv):
                sc.wait_ge(sem["sv"], tok_iv)
                sc.activation(stA[:, 12:16], stC[:, 8:12], AF.Sqrt).then_inc(sem["ss"], 1)
                cnt["ss"] += 1
            on("scalar", s_sqrt)
            tok_r0 = cnt["ss"]

            def v_newton(v, tok_r0=tok_r0):
                v.wait_ge(sem["ss"], tok_r0)
                v.tensor_tensor(stC[:, 12:16], stA[:, 12:16], stA[:, 12:16], op=OP.mult)
                v.drain()
                v.tensor_tensor(stD[:, 0:4], stB[:, 8:12], stC[:, 12:16], op=OP.mult)
                v.drain()
                v.tensor_scalar(stC[:, 0:4], stD[:, 0:4], -0.5, 1.5, op0=OP.mult, op1=OP.add)
                v.drain()
                v.tensor_tensor(stD[:, 8:12], stA[:, 12:16], stC[:, 0:4], op=OP.mult)
                v.drain()
                v.drain()
                v.engine_nop().then_inc(sem["sv"], 1)
                cnt["sv"] += 1
            on("vector", v_newton)
            return cnt["sv"]

        with nc.sbuf_tensor("xnb", [128, 4, D], BF16) as xnb, \
             nc.sbuf_tensor("xnf", [128, 8, MT], BF16) as xnf, \
             nc.sbuf_tensor("k_t0", [128, D], BF16) as k_t0, \
             nc.sbuf_tensor("k_t1", [128, D], BF16) as k_t1, \
             nc.sbuf_tensor("v_t0", [128, D], BF16) as v_t0, \
             nc.sbuf_tensor("v_t1", [128, D], BF16) as v_t1, \
             nc.sbuf_tensor("kv_all", [128, 1032], F32) as kv_all, \
             nc.sbuf_tensor("kvs", [128, 1032], F32) as kvs:
            k_t = [k_t0, k_t1]
            v_t = [v_t0, v_t1]
            tok_x_free = 0
            tok_xnb_free = 0
            tok_xnf_cons = 0
            tok_kt_used = [0, 0]

            for m in range(NMT):
                def a_load(sync, m=m, w=tok_x_free):
                    if w:
                        sync.wait_ge(sem["sv"], w)
                    sync.dma_start(out=x_slab[:, :, :],
                                   in_=x_in[m * MT:(m + 1) * MT, :].rearrange("(c p) f -> p c f", p=128)
                                   ).then_inc(sem["sx"], 16)
                    cnt["sx"] += 16
                on("sync", a_load)
                tok_x = cnt["sx"]

                def a_wx_v(v, t=tok_x):
                    v.wait_ge(sem["sx"], t)
                on("vector", a_wx_v)

                def a_wx_s(s, t=tok_x):
                    s.wait_ge(sem["sx"], t)
                on("scalar", a_wx_s)

                tok_rstd = emit_ln(ln_sq[:, :])

                def a_xn(v, w=tok_xnb_free, tr=tok_rstd, m=m):
                    v.wait_ge(sem["sv"], tr)
                    if m == 0:
                        v.wait_ge(sem["smm"], P2_SMM_END)
                        v.wait_ge(sem["sgpa"], P2_SGPA_END)
                        v.wait_ge(sem["sgpb"], P2_SGPB_END)
                    if w:
                        v.wait_ge(sem["smm"], w)
                    for c in range(4):
                        v.tensor_scalar(xnb[:, c, :], x_slab[:, c, :],
                                        stB[:, c:c + 1], stD[:, 8 + c:9 + c],
                                        op0=OP.subtract, op1=OP.mult)
                    v.drain()
                    v.engine_nop().then_inc(sem["sv"], 1)
                    cnt["sv"] += 1
                on("vector", a_xn)
                tok_xn = cnt["sv"]
                tok_x_free = tok_xn

                smm_tp0 = cnt["smm"]
                ss_tp0 = cnt["ss"]

                def a_tp_t(t, tok_xn=tok_xn, smm_tp0=smm_tp0, ss_tp0=ss_tp0, m=m):
                    t.wait_ge(sem["sv"], tok_xn)
                    if m == 0:
                        t.wait_ge(sem["ss"], SS_FOLD)
                        t.wait_ge(sem["sv"], SV_VCOPY)
                    it = 0
                    for c in range(4):
                        for j in range(8):
                            bA = pb[it % 2]
                            if it >= 2:
                                t.wait_ge(sem["ss"], ss_tp0 + (it - 1))
                            t.matmul(bA[0:128, 0:128], xnb[:, c, j * 128:(j + 1) * 128],
                                     identb[:, :], start=True, stop=True).then_inc(sem["smm"], 1)
                            it += 1
                    cnt["smm"] += 32
                on("tensor", a_tp_t)
                tok_xnb_free = cnt["smm"]

                def a_tp_s(s, smm_tp0=smm_tp0, w=tok_xnf_cons):
                    if w:
                        s.wait_ge(sem["smm"], w)
                    it = 0
                    for c in range(4):
                        for j in range(8):
                            bA = pb[it % 2]
                            s.wait_ge(sem["smm"], smm_tp0 + it + 1)
                            s.copy(xnf[:, j, c * 128:(c + 1) * 128], bA[0:128, 0:128]).then_inc(sem["ss"], 1)
                            it += 1
                    cnt["ss"] += 32
                on("scalar", a_tp_s)
                tok_tp_evac = cnt["ss"]

                smm_q0 = cnt["smm"]
                ss_q0 = cnt["ss"]

                def a_q_t(t, tok_tp_evac=tok_tp_evac, ss_q0=ss_q0):
                    t.wait_ge(sem["ss"], tok_tp_evac)
                    for mc in range(8):
                        bA = pb[mc % 2]
                        if mc >= 2:
                            t.wait_ge(sem["ss"], ss_q0 + (mc - 1))
                        for kc in range(8):
                            ins = t.matmul(bA[:, 0:512], W_att[:, kc, mc * 128:(mc + 1) * 128],
                                           xnf[:, kc, :], start=(kc == 0), stop=(kc == 7))
                        ins.then_inc(sem["smm"], 1)
                        cnt["smm"] += 1
                on("tensor", a_q_t)

                def a_q_s(s, smm_q0=smm_q0, m=m):
                    for mc in range(8):
                        bA = pb[mc % 2]
                        s.wait_ge(sem["smm"], smm_q0 + mc + 1)
                        s.activation(q_res[:, mc, m * MT:(m + 1) * MT], bA[:, 0:512], AF.Relu,
                                     bias=bqc[:, mc:mc + 1]).then_inc(sem["ss"], 1)
                        cnt["ss"] += 1
                on("scalar", a_q_s)
                tok_q_evac = cnt["ss"]

                for tc in range(4):
                    par = tc % 2
                    smm_kv0 = cnt["smm"]
                    ss_kv0 = cnt["ss"]

                    def a_kv_t(t, tc=tc, smm_kv0=smm_kv0, ss_kv0=ss_kv0,
                               tok_q_evac=tok_q_evac):
                        if tc == 0:
                            t.wait_ge(sem["ss"], tok_q_evac)
                        else:
                            t.wait_ge(sem["ss"], ss_kv0)
                        for grp in range(4):
                            bA = pb[2 + grp % 2]
                            if grp >= 2:
                                t.wait_ge(sem["ss"], ss_kv0 + (grp - 1))
                            for kc in range(8):
                                last = (kc == 7) and (grp >= 2)
                                ins = t.matmul(bA[:, 0:512], xnf[:, kc, tc * 128:(tc + 1) * 128],
                                               W_att[:, kc, D + grp * 512: D + (grp + 1) * 512],
                                               start=(kc == 0), stop=last)
                            if grp < 2:
                                ins = t.matmul(bA[:, 0:512], ones1[0:1, :],
                                               bkvr[0:1, grp * 512:(grp + 1) * 512],
                                               start=False, stop=True)
                            ins.then_inc(sem["smm"], 1)
                            cnt["smm"] += 1
                    on("tensor", a_kv_t)
                    tok_kv_grps = cnt["smm"]

                    def a_kv_s(s, par=par, smm_kv0=smm_kv0, w=tok_kt_used[par]):
                        if w:
                            s.wait_ge(sem["smm"], w)
                        for grp in range(4):
                            bA = pb[2 + grp % 2]
                            s.wait_ge(sem["smm"], smm_kv0 + grp + 1)
                            if grp < 2:
                                s.activation(k_t[par][:, grp * 512:(grp + 1) * 512], bA[:, 0:512],
                                             AF.Relu).then_inc(sem["ss"], 1)
                            else:
                                s.copy(v_t[par][:, (grp - 2) * 512:(grp - 1) * 512],
                                       bA[:, 0:512]).then_inc(sem["ss"], 1)
                            cnt["ss"] += 1
                    on("scalar", a_kv_s)
                    tok_kv_evac = cnt["ss"]

                    def a_acc_t(t, m=m, tc=tc, par=par, tok_kv_evac=tok_kv_evac):
                        t.wait_ge(sem["ss"], tok_kv_evac)
                        first = (m == 0 and tc == 0)
                        lastt = (m == NMT - 1 and tc == 3)
                        for p in range(8):
                            t.matmul(pb[4 + p // 4][:, (p % 4) * 128:(p % 4 + 1) * 128],
                                     k_t[par][:, p * 128:(p + 1) * 128],
                                     v_t[par][:, p * 128:(p + 1) * 128],
                                     start=first, stop=lastt, skip_group_check=True)
                            ins = t.matmul(pb[6][:, p:p + 1],
                                           k_t[par][:, p * 128:(p + 1) * 128],
                                           onesc[:, 0:1], start=first, stop=lastt,
                                           skip_group_check=True)
                        ins.then_inc(sem["smm"], 1)
                        cnt["smm"] += 1
                    on("tensor", a_acc_t)
                    tok_kt_used[par] = cnt["smm"]
                tok_xnf_cons = cnt["smm"]
            tok_kv_done = cnt["smm"]

            def ar_v(v, tok_kv_done=tok_kv_done):
                v.wait_ge(sem["smm"], tok_kv_done)
                v.tensor_copy(kvs[:, 0:512], pb[4][:, :])
                v.tensor_copy(kvs[:, 512:1024], pb[5][:, :])
                v.tensor_copy(kvs[:, 1024:1032], pb[6][:, 0:8])
                v.drain()
                v.drain()
                v.engine_nop().then_inc(sem["sv"], 1)
                cnt["sv"] += 1
            on("vector", ar_v)
            tok_kvs = cnt["sv"]

            def ar_g(g, tok_kvs=tok_kvs):
                g.wait_ge(sem["sv"], tok_kvs)
                g.dma_start(out=ar_in.ap()[:, :], in_=kvs[:, :]).then_inc(sem["sgp"], 16)
                cnt["sgp"] += 16
                g.wait_ge(sem["sgp"], cnt["sgp"])
                g.collective_compute("AllReduce", OP.add,
                                     replica_groups=[[0, 1, 2, 3], [4, 5, 6, 7]],
                                     ins=[ar_in.ap().opt()],
                                     outs=[ar_out.ap().opt()]).then_inc(sem["scc"], 1)
                cnt["scc"] += 1
                g.wait_ge(sem["scc"], cnt["scc"])
                g.dma_start(out=kv_all[:, :], in_=ar_out.ap()[:, :]).then_inc(sem["sgp"], 16)
                cnt["sgp"] += 16
            on("gpsimd", ar_g)
            tok_kvall = cnt["sgp"]

            def ar_build(v, tok_kvall=tok_kvall):
                v.wait_ge(sem["sgp"], tok_kvall)
                v.memset(bdkv[:, :, :], 0.0)
                v.memset(ksb8[:, :, :], 0.0)
                v.drain()
                for p in range(8):
                    v.tensor_copy(bdkv[0:64, p, 0:64], kv_all[0:64, p * 128: p * 128 + 64])
                    v.tensor_copy(bdkv[64:128, p, 64:128], kv_all[64:128, p * 128 + 64: p * 128 + 128])
                    v.tensor_copy(ksb8[0:64, p, 2 * p:2 * p + 1], kv_all[0:64, 1024 + p:1025 + p])
                    v.tensor_copy(ksb8[64:128, p, 2 * p + 1:2 * p + 2], kv_all[64:128, 1024 + p:1025 + p])
                v.drain()
                v.drain()
                v.engine_nop().then_inc(sem["sv"], 1)
                cnt["sv"] += 1
            on("vector", ar_build)
            tok_attc = cnt["sv"]
            tok_x_free_ab = tok_x_free

            DUMPA_SGP = 0
            if DBG:
                def dump_a(g, tok_attc=tok_attc):
                    g.wait_ge(sem["sv"], tok_attc)
                    g.dma_start(out=dbg_watt, in_=W_att[:, :, :]).then_inc(sem["sgp"], 16)
                    g.dma_start(out=dbg_xnf, in_=xnf[:, :, :]).then_inc(sem["sgp"], 16)
                    g.dma_start(out=dbg_xnb, in_=xnb[:, :, :]).then_inc(sem["sgp"], 16)
                    g.dma_start(out=dbg_kvall, in_=kv_all[:, :]).then_inc(sem["sgp"], 16)
                    cnt["sgp"] += 64
                on("gpsimd", dump_a)
                DUMPA_SGP = cnt["sgp"]

    wgu_sb = SB("wgu_sb", [128, 2, 8, 512], F16)
    wd_ic = [SB("wd_ic0", [128, D], F16), SB("wd_ic1", [128, D], F16)]
    wout_p = [SB("wout_p0", [128, D], BF16), SB("wout_p1", [128, D], BF16)]
    xn32 = SB("xn32", [128, D], F32)
    xq_t1 = SB("xq_tok0", [128, D], F16)
    xq_tok = [xq_t1, xq_t1]
    xq_feat = SB("xq_feat", [128, 8, MT], F16)
    h_tok = SB("h_tok", [128, 4, FI], F16)
    out_sb = [SB("out_sb0", [128, D], F32), SB("out_sb1", [128, D], F32)]
    hqv = qv[:, :].bitcast(F16)
    hqx = [hqv, hqv]
    hqT = SB("hqT", [128, ICN, MT], F16)
    sgt = [SB("sgt0", [128, 256], F16), SB("sgt1", [128, 256], F16)]
    attn2 = SB("attn2", [128, 8, MT], BF16)
    rbx = [SB("rb0", [128, 512], BF16), SB("rb1", [128, 512], BF16)]
    nplus = SB("nplus", [16, 512], F32)
    nrec16 = SB("nrec16", [16, 512], BF16)
    sq_view = hqT[:, :, :].rearrange("p a f -> p (a f)").bitcast(F32)[:, 0:4 * D]

    dbg_cap_gup = SB("dbg_cap_gup", [128, 512], F16) if DBG else None
    dbg_cap_sgt = SB("dbg_cap_sgt", [128, 256], F16) if DBG else None
    tok_x_free_b = tok_attc
    tok_wgu_free = [0, 0]
    tok_hqT_free = 0
    tok_sgt_used = [0, 0]
    tok_bank_free = [0, 0]
    tok_outsb_free = [0, 0]
    wd_used_tok = [0, 0]
    wo_used = [0, 0]
    sta_prev = 0
    stb_prev = 0

    for m in range(NMT):
        def b_load(sync, m=m, w=tok_x_free_b):
            if w:
                sync.wait_ge(sem["sv"], w)
            sync.dma_start(out=x_slab[:, :, :],
                           in_=x_in[m * MT:(m + 1) * MT, :].rearrange("(c p) f -> p c f", p=128)
                           ).then_inc(sem["sx"], 16)
            cnt["sx"] += 16
        on("sync", b_load)
        tok_xb = cnt["sx"]

        def b_wx_v(v, t=tok_xb, thq=tok_hqT_free, dg=(DUMPA_SGP if m == 0 else 0)):
            if dg:
                v.wait_ge(sem["sgp"], dg)
            v.wait_ge(sem["sx"], t)
            if thq:
                v.wait_ge(sem["smm"], thq)
        on("vector", b_wx_v)

        def b_wx_s(s, t=tok_xb, thq=tok_hqT_free):
            s.wait_ge(sem["sx"], t)
            if thq:
                s.wait_ge(sem["smm"], thq)
        on("scalar", b_wx_s)

        def emit_ln_b():
            def s_sq(sc):
                sc.activation(sq_view, x_slab[:, :, :].rearrange("p c f -> p (c f)"),
                              AF.Square).then_inc(sem["ss"], 1)
                cnt["ss"] += 1
            on("scalar", s_sq)
            tok_sq = cnt["ss"]

            def v_stats(v, tok_sq=tok_sq):
                for c in range(4):
                    v.tensor_reduce(stA[:, c:c + 1], x_slab[:, c, :], axis=AX.X, op=OP.add)
                v.wait_ge(sem["ss"], tok_sq)
                for c in range(4):
                    v.tensor_reduce(stA[:, 4 + c:5 + c], sq_view[:, c * D:(c + 1) * D],
                                    axis=AX.X, op=OP.add)
                v.drain()
                v.tensor_scalar(stB[:, 0:4], stA[:, 0:4], 1.0 / D, None, op0=OP.mult)
                v.tensor_scalar(stB[:, 4:8], stA[:, 4:8], 1.0 / D, None, op0=OP.mult)
                v.drain()
                v.tensor_tensor(stC[:, 0:4], stB[:, 0:4], stB[:, 0:4], op=OP.mult)
                v.drain()
                v.tensor_tensor(stA[:, 8:12], stB[:, 4:8], stC[:, 0:4], op=OP.subtract)
                v.drain()
                v.tensor_scalar(stB[:, 8:12], stA[:, 8:12], 1e-5, None, op0=OP.add)
                v.drain()
                v.reciprocal(stC[:, 8:12], stB[:, 8:12])
                v.drain()
                v.engine_nop().then_inc(sem["sv"], 1)
                cnt["sv"] += 1
            on("vector", v_stats)
            tok_iv = cnt["sv"]

            def s_sqrt(sc, tok_iv=tok_iv):
                sc.wait_ge(sem["sv"], tok_iv)
                sc.activation(stA[:, 12:16], stC[:, 8:12], AF.Sqrt).then_inc(sem["ss"], 1)
                cnt["ss"] += 1
            on("scalar", s_sqrt)
            tok_r0 = cnt["ss"]

            def v_newton(v, tok_r0=tok_r0):
                v.wait_ge(sem["ss"], tok_r0)
                v.tensor_tensor(stC[:, 12:16], stA[:, 12:16], stA[:, 12:16], op=OP.mult)
                v.drain()
                v.tensor_tensor(stD[:, 0:4], stB[:, 8:12], stC[:, 12:16], op=OP.mult)
                v.drain()
                v.tensor_scalar(stC[:, 0:4], stD[:, 0:4], -0.5, 1.5, op0=OP.mult, op1=OP.add)
                v.drain()
                v.tensor_tensor(stD[:, 8:12], stA[:, 12:16], stC[:, 0:4], op=OP.mult)
                v.drain()
                v.drain()
                v.engine_nop().then_inc(sem["sv"], 1)
                cnt["sv"] += 1
            on("vector", v_newton)
            return cnt["sv"]

        tok_rstd = emit_ln_b()

        smm_xq0 = cnt["smm"]
        ss_xq0 = cnt["ss"]
        for c in range(4):
            par = c % 2

            def b_xq_v(v, c=c, par=par, tr=tok_rstd, smm_xq0=smm_xq0):
                if c == 0:
                    v.wait_ge(sem["sv"], tr)
                if c >= 1:
                    v.wait_ge(sem["smm"], smm_xq0 + c * 8)
                v.tensor_scalar(xn32[:, :], x_slab[:, c, :], stB[:, c:c + 1], stD[:, 8 + c:9 + c],
                                op0=OP.subtract, op1=OP.mult)
                v.drain()
                emit_bmax(v, xn32[:, :], D, bmx[:, 0:64], bsc[:, 0:64], brr[:, 0:64])
                emit_quant_f16(v, xn32[:, :], D, xq_tok[par][:, :], bsc[:, 0:64], brr[:, 0:64])
                v.drain()
                v.drain()
                v.engine_nop().then_inc(sem["sv"], 1)
                cnt["sv"] += 1
            on("vector", b_xq_v)
            tok_xq = cnt["sv"]

            def b_xq_t(t, c=c, par=par, tok_xq=tok_xq, ss_xq0=ss_xq0, m=m,
                       stw=((sta_prev, stb_prev) if (m >= 1 and c == 0) else None)):
                t.wait_ge(sem["sv"], tok_xq)
                if m == 0 and c == 0:
                    t.wait_ge(sem["sv"], tok_attc)
                if stw:
                    t.wait_ge(sem["ssta"], stw[0])
                    t.wait_ge(sem["sstb"], stw[1])
                for j in range(8):
                    it = c * 8 + j
                    bA = pb[it % 2]
                    if it >= 2:
                        t.wait_ge(sem["ss"], ss_xq0 + (it - 1))
                    t.matmul(bA[0:128, 0:128], xq_tok[par][:, j * 128:(j + 1) * 128], identh[:, :],
                             start=True, stop=True).then_inc(sem["smm"], 1)
                    cnt["smm"] += 1
            on("tensor", b_xq_t)

            def b_xq_s(s, c=c, smm_xq0=smm_xq0):
                for j in range(8):
                    it = c * 8 + j
                    bA = pb[it % 2]
                    s.wait_ge(sem["smm"], smm_xq0 + it + 1)
                    s.copy(xq_feat[:, j, c * 128:(c + 1) * 128], bA[0:128, 0:128]).then_inc(sem["ss"], 1)
                    cnt["ss"] += 1
            on("scalar", b_xq_s)
        tok_xqf = cnt["ss"]

        if DBG:
            def dump_xq(sy, m=m, t=tok_xqf):
                sy.wait_ge(sem["ss"], t)
                sy.dma_start(out=dbg_xqall[m], in_=xq_feat[:, :, :]).then_inc(sem["sgp"], 16)
                cnt["sgp"] += 16
            on("sync", dump_xq)

        for ig in range(NIG):
            par = ig % 2

            def b_wgu_load(sync, ig=ig, par=par, w=tok_wgu_free[par],
                           first=(m == 0 and ig < 2)):
                if first:
                    sync.wait_ge(sem["sca"], SCC_AG)
                if w:
                    sync.wait_ge(sem["smm"], w)
                sn = "swga" if par == 0 else "swgb"
                sync.dma_start(out=wgu_sb[:, par, :, :],
                               in_=ag_wgu_out.ap()[:, :, ig * 512:(ig + 1) * 512]
                               .rearrange("c p f -> p c f")).then_inc(sem[sn], 16)
                cnt[sn] += 16
            on("sync", b_wgu_load)
            tok_wgu = cnt["swga" if par == 0 else "swgb"]
            sem_wgu = "swga" if par == 0 else "swgb"

            for tcg in range(4):
                bank = pb[2 + tcg % 2]

                def b_gu_t(t, par=par, tcg=tcg, bank=bank, tok_wgu=tok_wgu,
                           tok_xqf=tok_xqf, need_free=tok_bank_free[tcg % 2], sem_wgu=sem_wgu,
                           first=(ig == 0 and tcg == 0)):
                    t.wait_ge(sem[sem_wgu], tok_wgu)
                    if first:
                        t.wait_ge(sem["ss"], tok_xqf)
                    if need_free:
                        t.wait_ge(sem["sv"], need_free)
                    for kc in range(8):
                        ins = t.matmul(bank[:, 0:512],
                                       xq_feat[:, kc, tcg * 128:(tcg + 1) * 128],
                                       wgu_sb[:, par, kc, :],
                                       start=(kc == 0), stop=(kc == 7))
                    ins.then_inc(sem["smm"], 1)
                    cnt["smm"] += 1
                on("tensor", b_gu_t)
                tok_gu = cnt["smm"]

                def b_silu_s(s, bank=bank, sl=tcg % 2, tok_gu=tok_gu, w=tok_sgt_used[tcg % 2]):
                    s.wait_ge(sem["smm"], tok_gu)
                    if w:
                        s.wait_ge(sem["sv"], w)
                    s.activation(sgt[sl][:, :], bank[:, 0:256], AF.Silu).then_inc(sem["ss"], 1)
                    cnt["ss"] += 1
                on("scalar", b_silu_s)
                tok_silu = cnt["ss"]

                def b_h_v(v, bank=bank, sl=tcg % 2, tcg=tcg, ig=ig, tok_silu=tok_silu):
                    v.wait_ge(sem["ss"], tok_silu)
                    v.tensor_tensor(h_tok[:, tcg, ig * 256:(ig + 1) * 256], sgt[sl][:, :],
                                    bank[:, 256:512], op=OP.mult)
                    v.drain()
                    v.engine_nop().then_inc(sem["sv"], 1)
                    cnt["sv"] += 1
                on("vector", b_h_v)
                tok_sgt_used[tcg % 2] = cnt["sv"]
                tok_bank_free[tcg % 2] = cnt["sv"]
            tok_wgu_free[par] = cnt["smm"]
        tok_h_done = cnt["sv"]

        for tcg in range(4):
            for hf in range(2):
                src = h_tok[:, tcg, hf * 1408:(hf + 1) * 1408]
                par = (tcg * 2 + hf) % 2

                def b_hq_v(v, src=src, par=par, tok_h_done=tok_h_done, need=cnt["smm"]):
                    v.wait_ge(sem["sv"], tok_h_done)
                    if need:
                        v.wait_ge(sem["smm"], need)
                    emit_bmax(v, src, 1408, bmx[:, 0:88], bsc[:, 0:88], brr[:, 0:88])
                    emit_quant_f16(v, src, 1408, hqx[par][:, 0:1408], bsc[:, 0:88], brr[:, 0:88])
                    v.drain()
                    v.engine_nop().then_inc(sem["sv"], 1)
                    cnt["sv"] += 1
                on("vector", b_hq_v)
                tok_hq = cnt["sv"]

                smm_ht0 = cnt["smm"]
                ss_ht0 = cnt["ss"]

                def b_hq_t(t, par=par, tok_hq=tok_hq, ss_ht0=ss_ht0):
                    t.wait_ge(sem["sv"], tok_hq)
                    t.wait_ge(sem["ss"], ss_ht0)
                    for j in range(11):
                        bA = pb[4 + j % 2]
                        if j >= 2:
                            t.wait_ge(sem["ss"], ss_ht0 + (j - 1))
                        t.matmul(bA[0:128, 0:128], hqx[par][:, j * 128:(j + 1) * 128],
                                 identh[:, :], start=True, stop=True).then_inc(sem["smm"], 1)
                        cnt["smm"] += 1
                on("tensor", b_hq_t)

                def b_hq_s(s, tcg=tcg, hf=hf, smm_ht0=smm_ht0):
                    for j in range(11):
                        bA = pb[4 + j % 2]
                        s.wait_ge(sem["smm"], smm_ht0 + j + 1)
                        s.copy(hqT[:, hf * 11 + j, tcg * 128:(tcg + 1) * 128],
                               bA[0:128, 0:128]).then_inc(sem["ss"], 1)
                        cnt["ss"] += 1
                on("scalar", b_hq_s)
        tok_hqT_done = cnt["ss"]


        def b_norm_t(t, m=m, prev=cnt["ss"]):
            if m == 0:
                t.wait_ge(sem["sv"], tok_attc)
            t.wait_ge(sem["ss"], prev)
            for p in range(8):
                ins = t.matmul(pb[6][0:16, 0:512], ksb8[:, p, :], q_res[:, p, m * MT:(m + 1) * MT],
                               start=(p == 0), stop=(p == 7))
            ins.then_inc(sem["smm"], 1)
            cnt["smm"] += 1
        on("tensor", b_norm_t)
        tok_norm = cnt["smm"]

        def b_nrec_v(v, tok_norm=tok_norm):
            v.wait_ge(sem["smm"], tok_norm)
            v.tensor_scalar(nplus[:, :], pb[6][0:16, 0:512], 1e-6, None, op0=OP.add)
            v.drain()
            v.reciprocal(nplus[:, :], nplus[:, :])
            v.drain()
            v.tensor_copy(nrec16[:, :], nplus[:, :])
            v.drain()
            v.engine_nop().then_inc(sem["sv"], 1)
            cnt["sv"] += 1
        on("vector", b_nrec_v)
        tok_nrec = cnt["sv"]

        ss_rb0 = cnt["ss"]
        sv_at0 = cnt["sv"]
        for p in range(8):
            par = p % 2

            def b_rb_t(t, p=p, par=par, tok_nrec=tok_nrec, ss_rb0=ss_rb0, sv_at0=sv_at0, m=m):
                if p == 0:
                    t.wait_ge(sem["sv"], tok_nrec)
                if p >= 2:
                    t.wait_ge(sem["ss"], ss_rb0 + (p - 1))
                    t.wait_ge(sem["sv"], sv_at0 + (p - 1))
                t.matmul(pb[6 + par][:, 0:512], selb[:, p * 128:(p + 1) * 128], nrec16[:, :],
                         start=True, stop=True).then_inc(sem["smm"], 1)
                cnt["smm"] += 1
                t.matmul(pb[2 + par][:, 0:512], bdkv[:, p, :], q_res[:, p, m * MT:(m + 1) * MT],
                         start=True, stop=True).then_inc(sem["smm"], 1)
                cnt["smm"] += 1
            on("tensor", b_rb_t)
            tok_rbmm = cnt["smm"]

            def b_rb_s(s, par=par, tok_rbmm=tok_rbmm):
                s.wait_ge(sem["smm"], tok_rbmm - 1)
                s.copy(rbx[par][:, :], pb[6 + par][:, 0:512]).then_inc(sem["ss"], 1)
                cnt["ss"] += 1
            on("scalar", b_rb_s)
            tok_rbs = cnt["ss"]

            def b_at_v(v, p=p, par=par, tok_rbmm=tok_rbmm, tok_rbs=tok_rbs):
                v.wait_ge(sem["smm"], tok_rbmm)
                v.wait_ge(sem["ss"], tok_rbs)
                v.tensor_tensor(attn2[:, p, :], pb[2 + par][:, 0:512], rbx[par][:, :], op=OP.mult)
                v.drain()
                v.drain()
                v.engine_nop().then_inc(sem["sv"], 1)
                cnt["sv"] += 1
            on("vector", b_at_v)
        tok_attn2 = cnt["sv"]
        tok_bank_free[0] = tok_attn2
        tok_bank_free[1] = tok_attn2

        if DBG:
            def dump_at2(sy, m=m, t=tok_attn2):
                sy.wait_ge(sem["sv"], t)
                sy.dma_start(out=dbg_at2all[m], in_=attn2[:, :, :]).then_inc(sem["sgp"], 16)
                cnt["sgp"] += 16
            on("sync", dump_at2)

        for ic in range(ICN):
            par = ic % 2

            def b_wd_l(sync, ic=ic, par=par, w=wd_used_tok[par], m=m):
                if m == 0 and ic == 0:
                    sync.wait_ge(sem["sca"], SCC_AG)
                if w:
                    sync.wait_ge(sem["smm"], w)
                sn = "swda" if par == 0 else "swdb"
                sync.dma_start(out=wd_ic[par][:, :],
                               in_=ag_wd_out.ap()[ic % 8, ic // 8, :, :]).then_inc(sem[sn], 16)
                cnt[sn] += 16
            on("sync", b_wd_l)
            wd_tok = cnt["swda" if par == 0 else "swdb"]
            sem_wd = "swda" if par == 0 else "swdb"

            def b_wd_t(t, ic=ic, par=par, wt=wd_tok, ta=tok_attn2, tok_hqT_done=tok_hqT_done,
                       sem_wd=sem_wd):
                t.wait_ge(sem[sem_wd], wt)
                if ic == 0:
                    t.wait_ge(sem["sv"], ta)
                    t.wait_ge(sem["ss"], tok_hqT_done)
                for tcg in range(4):
                    for fg in range(2):
                        ins = t.matmul(pb[tcg * 2 + fg][:, 0:512],
                                       hqT[:, ic, tcg * 128:(tcg + 1) * 128],
                                       wd_ic[par][:, fg * 512:(fg + 1) * 512],
                                       start=(ic == 0), stop=False)
                ins.then_inc(sem["smm"], 1)
                cnt["smm"] += 1
            on("tensor", b_wd_t)
            wd_used_tok[par] = cnt["smm"]

        for p in range(8):
            par = p % 2

            def b_wo_l(sync, p=p, par=par, w=wo_used[par], first=(m == 0 and p < 2)):
                if first:
                    sync.wait_ge(sem["sx"], WC_SX)
                if w:
                    sync.wait_ge(sem["smm"], w)
                sn = "swoa" if par == 0 else "swob"
                sync.dma_start(out=wout_p[par][:, :],
                               in_=wout16.ap()[p * 128:(p + 1) * 128, :]).then_inc(sem[sn], 16)
                cnt[sn] += 16
            on("sync", b_wo_l)
            wo_tok = cnt["swoa" if par == 0 else "swob"]
            sem_wo = "swoa" if par == 0 else "swob"

            def b_wo_t(t, p=p, par=par, wt=wo_tok, sem_wo=sem_wo):
                t.wait_ge(sem[sem_wo], wt)
                for tcg in range(4):
                    for fg in range(2):
                        ins = t.matmul(pb[tcg * 2 + fg][:, 0:512],
                                       attn2[:, p, tcg * 128:(tcg + 1) * 128],
                                       wout_p[par][:, fg * 512:(fg + 1) * 512],
                                       start=False, stop=False)
                ins.then_inc(sem["smm"], 1)
                cnt["smm"] += 1
            on("tensor", b_wo_t)
            wo_used[par] = cnt["smm"]

        out_mm_tok = []

        def b_bias_t(t):
            for tcg in range(4):
                for fg in range(2):
                    t.matmul(pb[tcg * 2 + fg][:, 0:512], ones1[0:1, :],
                             boutr[0:1, fg * 512:(fg + 1) * 512],
                             start=False, stop=True).then_inc(sem["smm"], 1)
                    cnt["smm"] += 1
                    out_mm_tok.append(cnt["smm"])
        on("tensor", b_bias_t)

        for tcg in range(4):
            par = tcg % 2

            def b_oev_v(v, tcg=tcg, par=par, w=tok_outsb_free[par],
                        tokk=out_mm_tok[tcg * 2 + 1]):
                v.wait_ge(sem["smm"], tokk)
                if w:
                    v.wait_ge(sem["ssta" if par == 0 else "sstb"], w)
                for fg in range(2):
                    v.tensor_tensor(out_sb[par][:, fg * 512:(fg + 1) * 512],
                                    pb[tcg * 2 + fg][:, 0:512],
                                    x_slab[:, tcg, fg * 512:(fg + 1) * 512], op=OP.add)
                v.drain()
                v.drain()
                v.engine_nop().then_inc(sem["sv"], 1)
                cnt["sv"] += 1
            on("vector", b_oev_v)
            tok_oev = cnt["sv"]

            def b_store(sync, tcg=tcg, par=par, m=m, tok_oev=tok_oev):
                sync.wait_ge(sem["sv"], tok_oev)
                sn = "ssta" if par == 0 else "sstb"
                sync.dma_start(out=out_d[m * MT + tcg * 128: m * MT + (tcg + 1) * 128, :],
                               in_=out_sb[par][:, :]).then_inc(sem[sn], 16)
                cnt[sn] += 16
            on("sync", b_store)
            tok_outsb_free[par] = cnt["ssta" if par == 0 else "sstb"]
        sta_prev = cnt["ssta"]
        stb_prev = cnt["sstb"]
        tok_x_free_b = cnt["sv"]
        tok_bank_free[0] = cnt["sv"]
        tok_bank_free[1] = cnt["sv"]
        tok_hqT_free = cnt["smm"]

    SSTA_TOT = cnt["ssta"]
    SSTB_TOT = cnt["sstb"]

    if DBG:
        def dump(g):
            g.wait_ge(sem["ssta"], SSTA_TOT)
            g.wait_ge(sem["sstb"], SSTB_TOT)
            g.dma_start(out=dbg_agwgu, in_=ag_wgu_out.ap()[:, :, :]).then_inc(sem["sgp"], 16)
            cnt["sgp"] += 16
            g.dma_start(out=dbg_wgusb, in_=wgu_sb[:, :, :, :]).then_inc(sem["sgp"], 16)
            cnt["sgp"] += 16
            g.dma_start(out=dbg_gup, in_=dbg_cap_gup[:, :]).then_inc(sem["sgp"], 16)
            cnt["sgp"] += 16
            g.dma_start(out=dbg_sgt, in_=dbg_cap_sgt[:, :]).then_inc(sem["sgp"], 16)
            cnt["sgp"] += 16
            for dst, srct in [(dbg_qres, q_res), (dbg_bdkv, bdkv), (dbg_ksb8, ksb8),
                              (dbg_xqf, xq_feat), (dbg_hqT, hqT), (dbg_attn2, attn2),
                              (dbg_htok, h_tok), (dbg_nrec, nplus)]:
                g.dma_start(out=dst, in_=srct[:, :] if len(dst.shape) == 2 else srct[:, :, :]).then_inc(sem["sgp"], 16)
                cnt["sgp"] += 16
            g.dma_start(out=dbg_st[:, 0:16], in_=stA[:, :]).then_inc(sem["sgp"], 16)
            g.dma_start(out=dbg_st[:, 16:32], in_=stB[:, :]).then_inc(sem["sgp"], 16)
            g.dma_start(out=dbg_st[:, 32:48], in_=stC[:, :]).then_inc(sem["sgp"], 16)
            g.dma_start(out=dbg_st[:, 48:64], in_=stD[:, :]).then_inc(sem["sgp"], 16)
            cnt["sgp"] += 64
            g.wait_ge(sem["sgp"], cnt["sgp"])
        on("gpsimd", dump)

    def fin(sync):
        sync.wait_ge(sem["ssta"], SSTA_TOT)
        sync.wait_ge(sem["sstb"], SSTB_TOT)
        sync.wait_ge(sem["scc"], cnt["scc"])
        sync.wait_ge(sem["sca"], cnt["sca"])
        if DBG:
            sync.wait_ge(sem["sgp"], cnt["sgp"])
    on("sync", fin)

    es.close()
    return nc


_NC_CACHE = {}


def kernel(L0_attn_x, L0_attn_W_qkv, L0_attn_W_f, L0_attn_b_f, L0_attn_W_out,
           L0_attn_b_out, L0_ffn_w_gate, L0_ffn_w_up, L0_ffn_w_down):
    x = np.ascontiguousarray(np.asarray(L0_attn_x, dtype=np.float32)).reshape(NCORE, TCORE, D)
    wqkv = np.ascontiguousarray(np.asarray(L0_attn_W_qkv, dtype=np.float32))
    wf = np.ascontiguousarray(np.asarray(L0_attn_W_f, dtype=np.float32))
    b_f = np.asarray(L0_attn_b_f, dtype=np.float32)
    wout = np.ascontiguousarray(np.asarray(L0_attn_W_out, dtype=np.float32))
    b_out = np.asarray(L0_attn_b_out, dtype=np.float32)
    wg = np.ascontiguousarray(np.asarray(L0_ffn_w_gate, dtype=np.float32))
    wu = np.ascontiguousarray(np.asarray(L0_ffn_w_up, dtype=np.float32))
    wd = np.ascontiguousarray(np.asarray(L0_ffn_w_down, dtype=np.float32))

    bft = np.tile(b_f, 16)
    bqc = np.ascontiguousarray(bft.reshape(8, 128).T)
    bkvr = np.ascontiguousarray(np.concatenate([bft, np.zeros(D, np.float32)])[None, :])
    boutr = np.ascontiguousarray(b_out[None, :])
    idb = np.eye(128, dtype=np.float32).astype(bf16)
    idh = np.eye(128, dtype=np.float32).astype(np.float16)
    ones1 = np.ones((1, 128), np.float32).astype(bf16)
    onesc = np.ones((128, 8), np.float32).astype(bf16)
    selb = np.zeros((16, 8 * 128), np.float32)
    for p in range(8):
        selb[2 * p, p * 128: p * 128 + 64] = 1.0
        selb[2 * p + 1, p * 128 + 64: p * 128 + 128] = 1.0
    selb = selb.astype(bf16)

    in_maps = []
    for c in range(NCORE):
        wd_ch = []
        for s in range(WDCH):
            ic = s * 8 + c
            if ic < ICN:
                wd_ch.append(wd[:, ic * 128:(ic + 1) * 128])
            else:
                wd_ch.append(np.zeros((D, 128), np.float32))
        in_maps.append({
            "x": np.ascontiguousarray(x[c]),
            "wqkv": wqkv, "wf": wf, "wout": wout,
            "bqc": bqc, "bkvr": bkvr, "boutr": boutr,
            "wg_sl": np.ascontiguousarray(wg[:, c * 128:(c + 1) * 128]),
            "wu_sl": np.ascontiguousarray(wu[:, c * 128:(c + 1) * 128]),
            "wd_sl": np.ascontiguousarray(np.stack(wd_ch, 0)),
            "idb": idb, "idh": idh, "ones1": ones1, "onesc": onesc, "selb": selb,
        })

    if "prog" not in _NC_CACHE:
        _NC_CACHE["prog"] = build_nc()
    nc = _NC_CACHE["prog"]
    res = run_bass_kernel_spmd(nc, in_maps, core_ids=list(range(NCORE)))
    import os as _os
    if int(_os.environ.get("KDBG", "0")):
        kernel.DBG_RES = res
    out = np.concatenate([res.results[c]["out"][None] for c in range(NCORE)], 0)
    return np.ascontiguousarray(out.reshape(2, 16384, D))


# version 21
# speedup vs baseline: 1.2384x; 1.0262x over previous
import numpy as np
import ml_dtypes
import concourse.bass as bass
import concourse.mybir as mybir
from concourse.bass_utils import run_bass_kernel_spmd
from contextlib import ExitStack

F32 = mybir.dt.float32
F16 = mybir.dt.float16
BF16 = mybir.dt.bfloat16
I16 = mybir.dt.int16
I32 = mybir.dt.int32
OP = mybir.AluOpType
AF = mybir.ActivationFunctionType
AX = mybir.AxisListType
bf16 = ml_dtypes.bfloat16

NCORE = 8
TCORE = 4096
MT = 512
NMT = TCORE // MT
D = 1024
FI = 2816
ICN = 22
NIG = 11
WDCH = 3
MIDS = [0.25, 0.75, 1.25, 1.75, 2.5, 3.5, 5.0]


def build_nc():
    nc = bass.Bass("TRN2", target_bir_lowering=False)
    x_in = nc.dram_tensor("x", [TCORE, D], F32, kind="ExternalInput").ap()
    wqkv_in = nc.dram_tensor("wqkv", [D, 3 * D], F32, kind="ExternalInput").ap()
    wf_in = nc.dram_tensor("wf", [64, 64], F32, kind="ExternalInput").ap()
    wout_in = nc.dram_tensor("wout", [D, D], F32, kind="ExternalInput").ap()
    bqc_in = nc.dram_tensor("bqc", [128, 8], F32, kind="ExternalInput").ap()
    bkvr_in = nc.dram_tensor("bkvr", [1, 2 * D], F32, kind="ExternalInput").ap()
    bout_in = nc.dram_tensor("boutr", [1, D], F32, kind="ExternalInput").ap()
    wg_sl_in = nc.dram_tensor("wg_sl", [FI, 128], F32, kind="ExternalInput").ap()
    wu_sl_in = nc.dram_tensor("wu_sl", [FI, 128], F32, kind="ExternalInput").ap()
    wd_sl_in = nc.dram_tensor("wd_sl", [WDCH, D, 128], F32, kind="ExternalInput").ap()
    idb_in = nc.dram_tensor("idb", [128, 128], BF16, kind="ExternalInput").ap()
    idh_in = nc.dram_tensor("idh", [128, 128], F16, kind="ExternalInput").ap()
    ones1_in = nc.dram_tensor("ones1", [1, 128], BF16, kind="ExternalInput").ap()
    onesc_in = nc.dram_tensor("onesc", [128, 8], BF16, kind="ExternalInput").ap()
    selb_in = nc.dram_tensor("selb", [16, 8 * 128], BF16, kind="ExternalInput").ap()
    out_d = nc.dram_tensor("out", [TCORE, D], F32, kind="ExternalOutput").ap()
    import os as _os
    DBG = bool(int(_os.environ.get("KDBG", "0")))
    if DBG:
        dbg_qres = nc.dram_tensor("dbg_qres", [128, 8, TCORE], F32, kind="ExternalOutput").ap()
        dbg_bdkv = nc.dram_tensor("dbg_bdkv", [128, 8, 128], F32, kind="ExternalOutput").ap()
        dbg_ksb8 = nc.dram_tensor("dbg_ksb8", [128, 8, 16], F32, kind="ExternalOutput").ap()
        dbg_xqf = nc.dram_tensor("dbg_xqf", [128, 8, MT], F32, kind="ExternalOutput").ap()
        dbg_hqT = nc.dram_tensor("dbg_hqT", [128, ICN, MT], F32, kind="ExternalOutput").ap()
        dbg_attn2 = nc.dram_tensor("dbg_attn2", [128, 8, MT], F32, kind="ExternalOutput").ap()
        dbg_htok = nc.dram_tensor("dbg_htok", [128, 2, FI], F32, kind="ExternalOutput").ap()
        dbg_nrec = nc.dram_tensor("dbg_nrec", [16, 512], F32, kind="ExternalOutput").ap()
        dbg_st = nc.dram_tensor("dbg_st", [128, 64], F32, kind="ExternalOutput").ap()
        dbg_watt = nc.dram_tensor("dbg_watt", [128, 8, 3 * D], F32, kind="ExternalOutput").ap()
        dbg_xnf = nc.dram_tensor("dbg_xnf", [128, 8, MT], F32, kind="ExternalOutput").ap()
        dbg_xnb = nc.dram_tensor("dbg_xnb", [128, 4, D], F32, kind="ExternalOutput").ap()
        dbg_kvall = nc.dram_tensor("dbg_kvall", [128, 1032], F32, kind="ExternalOutput").ap()
        dbg_xqall = nc.dram_tensor("dbg_xqall", [NMT, 128, 8, MT], F16, kind="ExternalOutput").ap()
        dbg_hqTall = nc.dram_tensor("dbg_hqTall", [NMT, 128, ICN, MT], F16, kind="ExternalOutput").ap()
        dbg_at2all = nc.dram_tensor("dbg_at2all", [NMT, 128, 8, MT], BF16, kind="ExternalOutput").ap()
        dbg_agwgu = nc.dram_tensor("dbg_agwgu", [8, 128, 2 * FI], F16, kind="ExternalOutput").ap()
        dbg_wgusb = nc.dram_tensor("dbg_wgusb", [128, 2, 8, 512], F16, kind="ExternalOutput").ap()
        dbg_gup = nc.dram_tensor("dbg_gup", [128, 512], F16, kind="ExternalOutput").ap()
        dbg_sgt = nc.dram_tensor("dbg_sgt", [128, 256], F16, kind="ExternalOutput").ap()
    ag_wgu_in = nc.dram_tensor("ag_wgu_in", [128, 2 * FI], F16)
    ag_wgu_out = nc.dram_tensor("ag_wgu_out", [8, 128, 2 * FI], F16)
    ag_wd_in = nc.dram_tensor("ag_wd_in", [WDCH, 128, D], F16)
    ag_wd_out = nc.dram_tensor("ag_wd_out", [8, WDCH, 128, D], F16)
    ar_in = nc.dram_tensor("ar_in", [128, 1032], F32)
    ar_out = nc.dram_tensor("ar_out", [128, 1032], F32)
    wout16 = nc.dram_tensor("wout16", [D, D], BF16)

    es = ExitStack()
    SB = lambda name, shape, dt: es.enter_context(nc.sbuf_tensor(name, shape, dt))
    pb = [es.enter_context(nc.psum_tensor(f"pb{i}", [128, 512], F32)) for i in range(8)]
    sem_names = ["sx", "sw", "sg1", "smm", "sv", "ss", "sgp", "scc", "swc", "sca",
                 "sgpa", "sgpb", "swga", "swgb", "swda", "swdb", "swoa", "swob", "ssta", "sstb"]
    sem = {n: es.enter_context(nc.semaphore(n)) for n in sem_names}
    cnt = {n: 0 for n in sem_names}
    block = es.enter_context(nc.Block())

    def on(eng, fn):
        getattr(block, eng)(fn)

    identb = SB("identb", [128, 128], BF16)
    identh = SB("identh", [128, 128], F16)
    ones1 = SB("ones1s", [1, 128], BF16)
    onesc = SB("onescs", [128, 8], BF16)
    selb = SB("selbs", [16, 8 * 128], BF16)
    bqc = SB("bqcs", [128, 8], F32)
    boutr = SB("boutrs", [1, D], BF16)
    q_res = SB("q_res", [128, 8, TCORE], BF16)
    x_slab = SB("x_slab", [128, 4, D], F32)
    stA = SB("stA", [128, 16], F32)
    stB = SB("stB", [128, 16], F32)
    stC = SB("stC", [128, 16], F32)
    stD = SB("stD", [128, 16], F32)
    qv = SB("qv", [128, 1408], F32)
    ql = SB("ql", [128, 1408], F16)
    qh = SB("qh", [128, 1408], F16)
    qsm = SB("qsm", [128, 1408], F16)
    bmx = SB("bmx", [128, 88], F32)
    bsc = SB("bsc", [128, 88], F32)
    brr = SB("brr", [128, 88], F32)
    bdkv = SB("bdkv", [128, 8, 128], BF16)
    ksb8 = SB("ksb8", [128, 8, 16], BF16)

    def p0_sync(sync):
        for dst, src in [(identb, idb_in), (identh, idh_in), (ones1, ones1_in),
                         (onesc, onesc_in), (selb, selb_in), (bqc, bqc_in)]:
            sync.dma_start(out=dst[:, :], in_=src[:, :]).then_inc(sem["sw"], 16)
            cnt["sw"] += 16
    on("sync", p0_sync)
    SW_CONST = cnt["sw"]


    with nc.sbuf_tensor("W_att", [128, 8, 3 * D], BF16) as W_att, \
         nc.sbuf_tensor("ln_sq", [128, 4 * D], F32) as ln_sq, \
         nc.sbuf_tensor("bkvrs", [1, 2 * D], BF16) as bkvr:

        with nc.sbuf_tensor("wqkv_sb", [128, 8, 2 * D], BF16) as wqkv_sb, \
             nc.sbuf_tensor("wf_sb", [64, 64], BF16) as wf_sb, \
             nc.sbuf_tensor("wqt0", [64, 128], BF16) as wqt0, \
             nc.sbuf_tensor("wqt1", [64, 128], BF16) as wqt1, \
             nc.sbuf_tensor("wc16", [128, 1, D], BF16) as wc16:
            wqt = [wqt0, wqt1]

            def p1_gp(g):
                g.dma_start(out=wqkv_sb[:, :, :],
                            in_=wqkv_in[:, 0:2 * D].rearrange("(c p) f -> p c f", p=128)).then_inc(sem["sg1"], 16)
                g.dma_start(out=W_att[:, :, 2 * D:3 * D],
                            in_=wqkv_in[:, 2 * D:3 * D].rearrange("(c p) f -> p c f", p=128)).then_inc(sem["sg1"], 16)
                g.dma_start(out=wf_sb[:, :], in_=wf_in[:, :]).then_inc(sem["sg1"], 16)
                g.dma_start(out=bkvr[:, :], in_=bkvr_in[:, :]).then_inc(sem["sg1"], 16)
                g.dma_start(out=boutr[:, :], in_=bout_in[:, :]).then_inc(sem["sg1"], 16)
                cnt["sg1"] += 80
            on("gpsimd", p1_gp)
            SG1_W = cnt["sg1"]
            SV_VCOPY = cnt["sv"]

            sx_wc0 = cnt["sx"]

            def p1_wc_g(g):
                for p in range(8):
                    if p >= 1:
                        g.wait_ge(sem["sx"], sx_wc0 + 16 * p)
                    g.dma_start(out=wc16[:, 0, :],
                                in_=wout_in[p * 128:(p + 1) * 128, :]).then_inc(sem["swc"], 16)
                    cnt["swc"] += 16
            on("gpsimd", p1_wc_g)

            def p1_wc_s(sync):
                for p in range(8):
                    sync.wait_ge(sem["swc"], 16 * (p + 1))
                    sync.dma_start(out=wout16.ap()[p * 128:(p + 1) * 128, :],
                                   in_=wc16[:, 0, :]).then_inc(sem["sx"], 16)
                    cnt["sx"] += 16
            on("sync", p1_wc_s)
            WC_SX = cnt["sx"]

            ss_base = cnt["ss"]
            smm_base = cnt["smm"]

            def p1_t(t):
                t.wait_ge(sem["sg1"], SG1_W)
                t.wait_ge(sem["sw"], SW_CONST)
                it = 0
                for c in range(8):
                    for qk in range(2):
                        for h in range(16):
                            bA = pb[it % 2]
                            bB = pb[2 + it % 2]
                            if it >= 2:
                                t.wait_ge(sem["ss"], ss_base + 2 * (it - 1))
                            src = wqkv_sb[:, c, qk * D + h * 64: qk * D + (h + 1) * 64]
                            t.matmul(bA[0:64, 0:128], src, identb[:, :],
                                     start=True, stop=True).then_inc(sem["smm"], 1)
                            t.wait_ge(sem["ss"], ss_base + 2 * it + 1)
                            t.matmul(bB[0:128, 0:64], wqt[it % 2][0:64, :], wf_sb[:, :],
                                     start=True, stop=True).then_inc(sem["smm"], 1)
                            it += 1
                cnt["smm"] += 512
            on("tensor", p1_t)

            def p1_s(s):
                it = 0
                for c in range(8):
                    for qk in range(2):
                        for h in range(16):
                            bA = pb[it % 2]
                            bB = pb[2 + it % 2]
                            s.wait_ge(sem["smm"], smm_base + 2 * it + 1)
                            s.copy(wqt[it % 2][0:64, :], bA[0:64, 0:128]).then_inc(sem["ss"], 1)
                            s.wait_ge(sem["smm"], smm_base + 2 * it + 2)
                            s.copy(W_att[:, c, qk * D + h * 64: qk * D + (h + 1) * 64],
                                   bB[0:128, 0:64]).then_inc(sem["ss"], 1)
                            it += 1
                cnt["ss"] += 512
            on("scalar", p1_s)
        SS_FOLD = cnt["ss"]
        SMM_FOLD_END = cnt["smm"]

        def emit_quant_f16(v, src2d, W, out2d, sc_ap, rr_ap):
            nb = W // 16
            v3 = qv[:, 0:W].rearrange("p (b k) -> p b k", k=16)
            v.tensor_tensor(v3, src2d.rearrange("p (b k) -> p b k", k=16),
                            rr_ap.unsqueeze(2).broadcast_to([128, nb, 16]), op=OP.mult)
            v.drain()
            v.tensor_scalar(qsm[:, 0:W], qv[:, 0:W], 0.0, None, op0=OP.is_lt)
            v.tensor_scalar(qv[:, 0:W].bitcast(I32), qv[:, 0:W].bitcast(I32), 0x7FFFFFFF, None,
                            op0=OP.bitwise_and)
            v.drain()
            a = qv[:, 0:W]
            v.tensor_scalar(qsm[:, 0:W], qsm[:, 0:W], -2.0, 1.0, op0=OP.mult, op1=OP.add)
            v.tensor_scalar(ql[:, 0:W], a, MIDS[0], None, op0=OP.is_gt)
            v.tensor_scalar(qh[:, 0:W], a, MIDS[4], None, op0=OP.is_gt)
            v.drain()
            v.scalar_tensor_tensor(ql[:, 0:W], a, MIDS[1], ql[:, 0:W], op0=OP.is_gt, op1=OP.add)
            v.scalar_tensor_tensor(qh[:, 0:W], a, MIDS[5], qh[:, 0:W], op0=OP.is_gt, op1=OP.add)
            v.drain()
            v.scalar_tensor_tensor(ql[:, 0:W], a, MIDS[2], ql[:, 0:W], op0=OP.is_gt, op1=OP.add)
            v.scalar_tensor_tensor(qh[:, 0:W], a, MIDS[6], qh[:, 0:W], op0=OP.is_gt, op1=OP.add)
            v.drain()
            v.scalar_tensor_tensor(ql[:, 0:W], a, MIDS[3], ql[:, 0:W], op0=OP.is_gt, op1=OP.add)
            v.scalar_tensor_tensor(qh[:, 0:W], a, MIDS[6], qh[:, 0:W], op0=OP.is_gt, op1=OP.add)
            v.drain()
            v.scalar_tensor_tensor(ql[:, 0:W], ql[:, 0:W], 0.5, qh[:, 0:W], op0=OP.mult, op1=OP.add)
            v.drain()
            v.tensor_tensor(qh[:, 0:W].rearrange("p (b k) -> p b k", k=16),
                            ql[:, 0:W].rearrange("p (b k) -> p b k", k=16),
                            sc_ap.unsqueeze(2).broadcast_to([128, nb, 16]), op=OP.mult)
            v.drain()
            v.tensor_tensor(out2d, qh[:, 0:W], qsm[:, 0:W], op=OP.mult)

        def emit_bmax(v, src2d, W, bm_ap, sc_ap, rr_ap):
            v.tensor_reduce(bm_ap, src2d.rearrange("p (b k) -> p b k", k=16),
                            axis=AX.X, op=OP.max, apply_absolute_value=True)
            v.drain()
            v.tensor_scalar(sc_ap, bm_ap, 1.0 / 6.0, 1e-12, op0=OP.mult, op1=OP.max)
            v.drain()
            v.reciprocal(rr_ap, sc_ap)
            v.drain()

        with nc.sbuf_tensor("wg_sb", [128, ICN, 128], F32) as wg_sb, \
             nc.sbuf_tensor("wu_sb", [128, ICN, 128], F32) as wu_sb, \
             nc.sbuf_tensor("wd_sb", [128, WDCH, 8, 128], F32) as wd_sb, \
             nc.sbuf_tensor("wq16a", [128, 512], F16) as wq16a, \
             nc.sbuf_tensor("wq16b", [128, 512], F16) as wq16b, \
             nc.sbuf_tensor("wt16a", [128, 128], F16) as wt16a, \
             nc.sbuf_tensor("wt16b", [128, 128], F16) as wt16b:
            wq16 = [wq16a, wq16b]
            wt16 = [wt16a, wt16b]

            def p2_sync(sync):
                sync.wait_ge(sem["smm"], SMM_FOLD_END)
                sync.wait_ge(sem["sx"], WC_SX)
                sync.dma_start(out=wg_sb[:, :, :],
                               in_=wg_sl_in[:, :].rearrange("(a p) k -> p a k", p=128)).then_inc(sem["sw"], 16)
                sync.dma_start(out=wu_sb[:, :, :],
                               in_=wu_sl_in[:, :].rearrange("(a p) k -> p a k", p=128)).then_inc(sem["sw"], 16)
                sync.dma_start(out=wd_sb[:, :, :, :],
                               in_=wd_sl_in[:, :, :].rearrange("j (a p) k -> p j a k", p=128)).then_inc(sem["sw"], 16)
                cnt["sw"] += 48
            on("sync", p2_sync)
            SW_WSH = cnt["sw"]

            tiles = []
            for j in range(ICN):
                col = (j // 2) * 512 + (j % 2) * 128
                tiles.append((wg_sb[:, j, :], ag_wgu_in.ap()[:, col:col + 128]))
            for j in range(ICN):
                col = (j // 2) * 512 + 256 + (j % 2) * 128
                tiles.append((wu_sb[:, j, :], ag_wgu_in.ap()[:, col:col + 128]))
            for jj in range(WDCH):
                for fc in range(8):
                    tiles.append((wd_sb[:, jj, fc, :], ag_wd_in.ap()[jj, :, fc * 128:(fc + 1) * 128]))
            NT = len(tiles)
            sv0, smm0, ss0, sgp0 = cnt["sv"], cnt["smm"], cnt["ss"], cnt["sgp"]

            groups = []
            gstart = 0
            for base, count in ((0, ICN), (ICN, ICN), (2 * ICN, 24)):
                j = 0
                while j < count:
                    g = min(4, count - j)
                    groups.append(list(range(base + j, base + j + g)))
                    j += g
            wide_src = {}
            for idxs in groups:
                i0 = idxs[0]
                g = len(idxs)
                if i0 < ICN:
                    wide_src[i0] = wg_sb[:, i0:i0 + g, :].rearrange("p a k -> p (a k)")
                elif i0 < 2 * ICN:
                    j0 = i0 - ICN
                    wide_src[i0] = wu_sb[:, j0:j0 + g, :].rearrange("p a k -> p (a k)")
                else:
                    j0 = i0 - 2 * ICN
                    wide_src[i0] = wd_sb[:, j0 // 8, j0 % 8:j0 % 8 + g, :].rearrange("p a k -> p (a k)")

            def p2_v(v):
                v.wait_ge(sem["sw"], SW_WSH)
                for gi, idxs in enumerate(groups):
                    if gi >= 2:
                        v.wait_ge(sem["smm"], smm0 + groups[gi - 2][-1] + 1)
                    W = 128 * len(idxs)
                    emit_bmax(v, wide_src[idxs[0]], W, bmx[:, 0:W // 16],
                              bsc[:, 0:W // 16], brr[:, 0:W // 16])
                    emit_quant_f16(v, wide_src[idxs[0]], W, wq16[gi % 2][:, 0:W],
                                   bsc[:, 0:W // 16], brr[:, 0:W // 16])
                    v.drain()
                    v.engine_nop().then_inc(sem["sv"], 1)
                    cnt["sv"] += 1
            on("vector", p2_v)

            def p2_t(t):
                for gi, idxs in enumerate(groups):
                    t.wait_ge(sem["sv"], sv0 + gi + 1)
                    for tl, i in enumerate(idxs):
                        if i >= 2:
                            t.wait_ge(sem["ss"], ss0 + (i - 1))
                        t.matmul(pb[4 + i % 2][0:128, 0:128],
                                 wq16[gi % 2][:, tl * 128:(tl + 1) * 128], identh[:, :],
                                 start=True, stop=True).then_inc(sem["smm"], 1)
                        cnt["smm"] += 1
            on("tensor", p2_t)

            def p2_s(s):
                for i in range(NT):
                    s.wait_ge(sem["smm"], smm0 + i + 1)
                    if i >= 2:
                        sn = "sgpa" if i % 2 == 0 else "sgpb"
                        s.wait_ge(sem[sn], 16 * ((i - 2) // 2 + 1))
                    s.copy(wt16[i % 2][:, :], pb[4 + i % 2][0:128, 0:128]).then_inc(sem["ss"], 1)
                    cnt["ss"] += 1
            on("scalar", p2_s)

            def p2_dump(sync):
                for i, (src, dst) in enumerate(tiles):
                    sync.wait_ge(sem["ss"], ss0 + i + 1)
                    sn = "sgpa" if i % 2 == 0 else "sgpb"
                    sync.dma_start(out=dst, in_=wt16[i % 2][:, :]).then_inc(sem[sn], 16)
                    cnt[sn] += 16
            on("sync", p2_dump)

            def p2_g(g):
                g.wait_ge(sem["sgpa"], 16 * ((NT + 1) // 2))
                g.wait_ge(sem["sgpb"], 16 * (NT // 2))
                g.collective_compute("AllGather", OP.bypass,
                                     replica_groups=[[0, 1, 2, 3, 4, 5, 6, 7]],
                                     ins=[ag_wgu_in.ap().opt()],
                                     outs=[ag_wgu_out.ap().opt()]).then_inc(sem["sca"], 1)
                cnt["sca"] += 1
                g.collective_compute("AllGather", OP.bypass,
                                     replica_groups=[[0, 1, 2, 3, 4, 5, 6, 7]],
                                     ins=[ag_wd_in.ap().opt()],
                                     outs=[ag_wd_out.ap().opt()]).then_inc(sem["sca"], 1)
                cnt["sca"] += 1
            on("gpsimd", p2_g)
        SCC_AG = cnt["sca"]
        P2_SMM_END = cnt["smm"]
        P2_SGPA_END = cnt["sgpa"]
        P2_SGPB_END = cnt["sgpb"]

        def emit_ln(sq_view):
            def s_sq(sc):
                sc.activation(sq_view, x_slab[:, :, :].rearrange("p c f -> p (c f)"),
                              AF.Square).then_inc(sem["ss"], 1)
                cnt["ss"] += 1
            on("scalar", s_sq)
            tok_sq = cnt["ss"]

            def v_stats(v, tok_sq=tok_sq, sq_view=sq_view):
                for c in range(4):
                    v.tensor_reduce(stA[:, c:c + 1], x_slab[:, c, :], axis=AX.X, op=OP.add)
                v.wait_ge(sem["ss"], tok_sq)
                for c in range(4):
                    v.tensor_reduce(stA[:, 4 + c:5 + c], sq_view[:, c * D:(c + 1) * D],
                                    axis=AX.X, op=OP.add)
                v.drain()
                v.tensor_scalar(stB[:, 0:4], stA[:, 0:4], 1.0 / D, None, op0=OP.mult)
                v.tensor_scalar(stB[:, 4:8], stA[:, 4:8], 1.0 / D, None, op0=OP.mult)
                v.drain()
                v.tensor_tensor(stC[:, 0:4], stB[:, 0:4], stB[:, 0:4], op=OP.mult)
                v.drain()
                v.tensor_tensor(stA[:, 8:12], stB[:, 4:8], stC[:, 0:4], op=OP.subtract)
                v.drain()
                v.tensor_scalar(stB[:, 8:12], stA[:, 8:12], 1e-5, None, op0=OP.add)
                v.drain()
                v.reciprocal(stC[:, 8:12], stB[:, 8:12])
                v.drain()
                v.engine_nop().then_inc(sem["sv"], 1)
                cnt["sv"] += 1
            on("vector", v_stats)
            tok_iv = cnt["sv"]

            def s_sqrt(sc, tok_iv=tok_iv):
                sc.wait_ge(sem["sv"], tok_iv)
                sc.activation(stA[:, 12:16], stC[:, 8:12], AF.Sqrt).then_inc(sem["ss"], 1)
                cnt["ss"] += 1
            on("scalar", s_sqrt)
            tok_r0 = cnt["ss"]

            def v_newton(v, tok_r0=tok_r0):
                v.wait_ge(sem["ss"], tok_r0)
                v.tensor_tensor(stC[:, 12:16], stA[:, 12:16], stA[:, 12:16], op=OP.mult)
                v.drain()
                v.tensor_tensor(stD[:, 0:4], stB[:, 8:12], stC[:, 12:16], op=OP.mult)
                v.drain()
                v.tensor_scalar(stC[:, 0:4], stD[:, 0:4], -0.5, 1.5, op0=OP.mult, op1=OP.add)
                v.drain()
                v.tensor_tensor(stD[:, 8:12], stA[:, 12:16], stC[:, 0:4], op=OP.mult)
                v.drain()
                v.drain()
                v.engine_nop().then_inc(sem["sv"], 1)
                cnt["sv"] += 1
            on("vector", v_newton)
            return cnt["sv"]

        with nc.sbuf_tensor("xnb", [128, 4, D], BF16) as xnb, \
             nc.sbuf_tensor("xnf", [128, 8, MT], BF16) as xnf, \
             nc.sbuf_tensor("k_t0", [128, D], BF16) as k_t0, \
             nc.sbuf_tensor("k_t1", [128, D], BF16) as k_t1, \
             nc.sbuf_tensor("v_t0", [128, D], BF16) as v_t0, \
             nc.sbuf_tensor("v_t1", [128, D], BF16) as v_t1, \
             nc.sbuf_tensor("kv_all", [128, 1032], F32) as kv_all, \
             nc.sbuf_tensor("kvs", [128, 1032], F32) as kvs:
            k_t = [k_t0, k_t1]
            v_t = [v_t0, v_t1]
            tok_x_free = 0
            tok_xnb_free = 0
            tok_xnf_cons = 0
            tok_kt_used = [0, 0]

            for m in range(NMT):
                def a_load(sync, m=m, w=tok_x_free):
                    if w:
                        sync.wait_ge(sem["sv"], w)
                    sync.dma_start(out=x_slab[:, :, :],
                                   in_=x_in[m * MT:(m + 1) * MT, :].rearrange("(c p) f -> p c f", p=128)
                                   ).then_inc(sem["sx"], 16)
                    cnt["sx"] += 16
                on("sync", a_load)
                tok_x = cnt["sx"]

                def a_wx_v(v, t=tok_x):
                    v.wait_ge(sem["sx"], t)
                on("vector", a_wx_v)

                def a_wx_s(s, t=tok_x):
                    s.wait_ge(sem["sx"], t)
                on("scalar", a_wx_s)

                tok_rstd = emit_ln(ln_sq[:, :])

                def a_xn(v, w=tok_xnb_free, tr=tok_rstd, m=m):
                    v.wait_ge(sem["sv"], tr)
                    if m == 0:
                        v.wait_ge(sem["smm"], P2_SMM_END)
                        v.wait_ge(sem["sgpa"], P2_SGPA_END)
                        v.wait_ge(sem["sgpb"], P2_SGPB_END)
                    if w:
                        v.wait_ge(sem["smm"], w)
                    for c in range(4):
                        v.tensor_scalar(xnb[:, c, :], x_slab[:, c, :],
                                        stB[:, c:c + 1], stD[:, 8 + c:9 + c],
                                        op0=OP.subtract, op1=OP.mult)
                    v.drain()
                    v.engine_nop().then_inc(sem["sv"], 1)
                    cnt["sv"] += 1
                on("vector", a_xn)
                tok_xn = cnt["sv"]
                tok_x_free = tok_xn

                smm_tp0 = cnt["smm"]
                ss_tp0 = cnt["ss"]

                def a_tp_t(t, tok_xn=tok_xn, smm_tp0=smm_tp0, ss_tp0=ss_tp0, m=m):
                    t.wait_ge(sem["sv"], tok_xn)
                    if m == 0:
                        t.wait_ge(sem["ss"], SS_FOLD)
                        t.wait_ge(sem["sv"], SV_VCOPY)
                    it = 0
                    for c in range(4):
                        for j in range(8):
                            bA = pb[it % 2]
                            if it >= 2:
                                t.wait_ge(sem["ss"], ss_tp0 + (it - 1))
                            t.matmul(bA[0:128, 0:128], xnb[:, c, j * 128:(j + 1) * 128],
                                     identb[:, :], start=True, stop=True).then_inc(sem["smm"], 1)
                            it += 1
                    cnt["smm"] += 32
                on("tensor", a_tp_t)
                tok_xnb_free = cnt["smm"]

                def a_tp_s(s, smm_tp0=smm_tp0, w=tok_xnf_cons):
                    if w:
                        s.wait_ge(sem["smm"], w)
                    it = 0
                    for c in range(4):
                        for j in range(8):
                            bA = pb[it % 2]
                            s.wait_ge(sem["smm"], smm_tp0 + it + 1)
                            s.copy(xnf[:, j, c * 128:(c + 1) * 128], bA[0:128, 0:128]).then_inc(sem["ss"], 1)
                            it += 1
                    cnt["ss"] += 32
                on("scalar", a_tp_s)
                tok_tp_evac = cnt["ss"]

                smm_q0 = cnt["smm"]
                ss_q0 = cnt["ss"]

                def a_q_t(t, tok_tp_evac=tok_tp_evac, ss_q0=ss_q0):
                    t.wait_ge(sem["ss"], tok_tp_evac)
                    for mc in range(8):
                        bA = pb[mc % 2]
                        if mc >= 2:
                            t.wait_ge(sem["ss"], ss_q0 + (mc - 1))
                        for kc in range(8):
                            ins = t.matmul(bA[:, 0:512], W_att[:, kc, mc * 128:(mc + 1) * 128],
                                           xnf[:, kc, :], start=(kc == 0), stop=(kc == 7))
                        ins.then_inc(sem["smm"], 1)
                        cnt["smm"] += 1
                on("tensor", a_q_t)

                def a_q_s(s, smm_q0=smm_q0, m=m):
                    for mc in range(8):
                        bA = pb[mc % 2]
                        s.wait_ge(sem["smm"], smm_q0 + mc + 1)
                        s.activation(q_res[:, mc, m * MT:(m + 1) * MT], bA[:, 0:512], AF.Relu,
                                     bias=bqc[:, mc:mc + 1]).then_inc(sem["ss"], 1)
                        cnt["ss"] += 1
                on("scalar", a_q_s)
                tok_q_evac = cnt["ss"]

                for tc in range(4):
                    par = tc % 2
                    smm_kv0 = cnt["smm"]
                    ss_kv0 = cnt["ss"]

                    def a_kv_t(t, tc=tc, smm_kv0=smm_kv0, ss_kv0=ss_kv0,
                               tok_q_evac=tok_q_evac):
                        if tc == 0:
                            t.wait_ge(sem["ss"], tok_q_evac)
                        else:
                            t.wait_ge(sem["ss"], ss_kv0)
                        for grp in range(4):
                            bA = pb[2 + grp % 2]
                            if grp >= 2:
                                t.wait_ge(sem["ss"], ss_kv0 + (grp - 1))
                            for kc in range(8):
                                last = (kc == 7) and (grp >= 2)
                                ins = t.matmul(bA[:, 0:512], xnf[:, kc, tc * 128:(tc + 1) * 128],
                                               W_att[:, kc, D + grp * 512: D + (grp + 1) * 512],
                                               start=(kc == 0), stop=last)
                            if grp < 2:
                                ins = t.matmul(bA[:, 0:512], ones1[0:1, :],
                                               bkvr[0:1, grp * 512:(grp + 1) * 512],
                                               start=False, stop=True)
                            ins.then_inc(sem["smm"], 1)
                            cnt["smm"] += 1
                    on("tensor", a_kv_t)
                    tok_kv_grps = cnt["smm"]

                    def a_kv_s(s, par=par, smm_kv0=smm_kv0, w=tok_kt_used[par]):
                        if w:
                            s.wait_ge(sem["smm"], w)
                        for grp in range(4):
                            bA = pb[2 + grp % 2]
                            s.wait_ge(sem["smm"], smm_kv0 + grp + 1)
                            if grp < 2:
                                s.activation(k_t[par][:, grp * 512:(grp + 1) * 512], bA[:, 0:512],
                                             AF.Relu).then_inc(sem["ss"], 1)
                            else:
                                s.copy(v_t[par][:, (grp - 2) * 512:(grp - 1) * 512],
                                       bA[:, 0:512]).then_inc(sem["ss"], 1)
                            cnt["ss"] += 1
                    on("scalar", a_kv_s)
                    tok_kv_evac = cnt["ss"]

                    def a_acc_t(t, m=m, tc=tc, par=par, tok_kv_evac=tok_kv_evac):
                        t.wait_ge(sem["ss"], tok_kv_evac)
                        first = (m == 0 and tc == 0)
                        lastt = (m == NMT - 1 and tc == 3)
                        for p in range(8):
                            t.matmul(pb[4 + p // 4][:, (p % 4) * 128:(p % 4 + 1) * 128],
                                     k_t[par][:, p * 128:(p + 1) * 128],
                                     v_t[par][:, p * 128:(p + 1) * 128],
                                     start=first, stop=lastt, skip_group_check=True)
                            ins = t.matmul(pb[6][:, p:p + 1],
                                           k_t[par][:, p * 128:(p + 1) * 128],
                                           onesc[:, 0:1], start=first, stop=lastt,
                                           skip_group_check=True)
                        ins.then_inc(sem["smm"], 1)
                        cnt["smm"] += 1
                    on("tensor", a_acc_t)
                    tok_kt_used[par] = cnt["smm"]
                tok_xnf_cons = cnt["smm"]
            tok_kv_done = cnt["smm"]

            def ar_v(v, tok_kv_done=tok_kv_done):
                v.wait_ge(sem["smm"], tok_kv_done)
                v.tensor_copy(kvs[:, 0:512], pb[4][:, :])
                v.tensor_copy(kvs[:, 512:1024], pb[5][:, :])
                v.tensor_copy(kvs[:, 1024:1032], pb[6][:, 0:8])
                v.drain()
                v.drain()
                v.engine_nop().then_inc(sem["sv"], 1)
                cnt["sv"] += 1
            on("vector", ar_v)
            tok_kvs = cnt["sv"]

            def ar_g(g, tok_kvs=tok_kvs):
                g.wait_ge(sem["sv"], tok_kvs)
                g.dma_start(out=ar_in.ap()[:, :], in_=kvs[:, :]).then_inc(sem["sgp"], 16)
                cnt["sgp"] += 16
                g.wait_ge(sem["sgp"], cnt["sgp"])
                g.collective_compute("AllReduce", OP.add,
                                     replica_groups=[[0, 1, 2, 3], [4, 5, 6, 7]],
                                     ins=[ar_in.ap().opt()],
                                     outs=[ar_out.ap().opt()]).then_inc(sem["scc"], 1)
                cnt["scc"] += 1
                g.wait_ge(sem["scc"], cnt["scc"])
                g.dma_start(out=kv_all[:, :], in_=ar_out.ap()[:, :]).then_inc(sem["sgp"], 16)
                cnt["sgp"] += 16
            on("gpsimd", ar_g)
            tok_kvall = cnt["sgp"]

            def ar_build(v, tok_kvall=tok_kvall):
                v.wait_ge(sem["sgp"], tok_kvall)
                v.memset(bdkv[:, :, :], 0.0)
                v.memset(ksb8[:, :, :], 0.0)
                v.drain()
                for p in range(8):
                    v.tensor_copy(bdkv[0:64, p, 0:64], kv_all[0:64, p * 128: p * 128 + 64])
                    v.tensor_copy(bdkv[64:128, p, 64:128], kv_all[64:128, p * 128 + 64: p * 128 + 128])
                    v.tensor_copy(ksb8[0:64, p, 2 * p:2 * p + 1], kv_all[0:64, 1024 + p:1025 + p])
                    v.tensor_copy(ksb8[64:128, p, 2 * p + 1:2 * p + 2], kv_all[64:128, 1024 + p:1025 + p])
                v.drain()
                v.drain()
                v.engine_nop().then_inc(sem["sv"], 1)
                cnt["sv"] += 1
            on("vector", ar_build)
            tok_attc = cnt["sv"]
            tok_x_free_ab = tok_x_free

            DUMPA_SGP = 0
            if DBG:
                def dump_a(g, tok_attc=tok_attc):
                    g.wait_ge(sem["sv"], tok_attc)
                    g.dma_start(out=dbg_watt, in_=W_att[:, :, :]).then_inc(sem["sgp"], 16)
                    g.dma_start(out=dbg_xnf, in_=xnf[:, :, :]).then_inc(sem["sgp"], 16)
                    g.dma_start(out=dbg_xnb, in_=xnb[:, :, :]).then_inc(sem["sgp"], 16)
                    g.dma_start(out=dbg_kvall, in_=kv_all[:, :]).then_inc(sem["sgp"], 16)
                    cnt["sgp"] += 64
                on("gpsimd", dump_a)
                DUMPA_SGP = cnt["sgp"]

    wgu_sb = SB("wgu_sb", [128, 2, 8, 512], F16)
    wd_ic = [SB("wd_ic0", [128, D], F16), SB("wd_ic1", [128, D], F16)]
    wout_p = [SB("wout_p0", [128, D], BF16), SB("wout_p1", [128, D], BF16)]
    xn32 = SB("xn32", [128, D], F32)
    xq_t1 = SB("xq_tok0", [128, D], F16)
    xq_tok = [xq_t1, xq_t1]
    xq_feat = SB("xq_feat", [128, 8, MT], F16)
    h_tok = SB("h_tok", [128, 4, FI], F16)
    out_sb = [SB("out_sb0", [128, D], F32), SB("out_sb1", [128, D], F32)]
    hqv = qv[:, :].bitcast(F16)
    hqx = [hqv, hqv]
    hqT = SB("hqT", [128, ICN, MT], F16)
    sgt = [SB("sgt0", [128, 256], F16), SB("sgt1", [128, 256], F16)]
    attn2 = SB("attn2", [128, 8, MT], BF16)
    rbx = [SB("rb0", [128, 512], BF16), SB("rb1", [128, 512], BF16)]
    nplus = SB("nplus", [16, 512], F32)
    nrec16 = SB("nrec16", [16, 512], BF16)
    sq_view = hqT[:, :, :].rearrange("p a f -> p (a f)").bitcast(F32)[:, 0:4 * D]

    dbg_cap_gup = SB("dbg_cap_gup", [128, 512], F16) if DBG else None
    dbg_cap_sgt = SB("dbg_cap_sgt", [128, 256], F16) if DBG else None
    tok_x_free_b = tok_attc
    tok_wgu_free = [0, 0]
    tok_hqT_free = 0
    tok_sgt_used = [0, 0]
    tok_bank_free = [0, 0]
    tok_outsb_free = [0, 0]
    wd_used_tok = [0, 0]
    wo_used = [0, 0]
    sta_prev = 0
    stb_prev = 0

    for m in range(NMT):
        def b_load(sync, m=m, w=tok_x_free_b):
            if w:
                sync.wait_ge(sem["sv"], w)
            sync.dma_start(out=x_slab[:, :, :],
                           in_=x_in[m * MT:(m + 1) * MT, :].rearrange("(c p) f -> p c f", p=128)
                           ).then_inc(sem["sx"], 16)
            cnt["sx"] += 16
        on("sync", b_load)
        tok_xb = cnt["sx"]

        def b_wx_v(v, t=tok_xb, thq=tok_hqT_free, dg=(DUMPA_SGP if m == 0 else 0)):
            if dg:
                v.wait_ge(sem["sgp"], dg)
            v.wait_ge(sem["sx"], t)
            if thq:
                v.wait_ge(sem["smm"], thq)
        on("vector", b_wx_v)

        def b_wx_s(s, t=tok_xb, thq=tok_hqT_free):
            s.wait_ge(sem["sx"], t)
            if thq:
                s.wait_ge(sem["smm"], thq)
        on("scalar", b_wx_s)

        def emit_ln_b():
            def s_sq(sc):
                sc.activation(sq_view, x_slab[:, :, :].rearrange("p c f -> p (c f)"),
                              AF.Square).then_inc(sem["ss"], 1)
                cnt["ss"] += 1
            on("scalar", s_sq)
            tok_sq = cnt["ss"]

            def v_stats(v, tok_sq=tok_sq):
                for c in range(4):
                    v.tensor_reduce(stA[:, c:c + 1], x_slab[:, c, :], axis=AX.X, op=OP.add)
                v.wait_ge(sem["ss"], tok_sq)
                for c in range(4):
                    v.tensor_reduce(stA[:, 4 + c:5 + c], sq_view[:, c * D:(c + 1) * D],
                                    axis=AX.X, op=OP.add)
                v.drain()
                v.tensor_scalar(stB[:, 0:4], stA[:, 0:4], 1.0 / D, None, op0=OP.mult)
                v.tensor_scalar(stB[:, 4:8], stA[:, 4:8], 1.0 / D, None, op0=OP.mult)
                v.drain()
                v.tensor_tensor(stC[:, 0:4], stB[:, 0:4], stB[:, 0:4], op=OP.mult)
                v.drain()
                v.tensor_tensor(stA[:, 8:12], stB[:, 4:8], stC[:, 0:4], op=OP.subtract)
                v.drain()
                v.tensor_scalar(stB[:, 8:12], stA[:, 8:12], 1e-5, None, op0=OP.add)
                v.drain()
                v.reciprocal(stC[:, 8:12], stB[:, 8:12])
                v.drain()
                v.engine_nop().then_inc(sem["sv"], 1)
                cnt["sv"] += 1
            on("vector", v_stats)
            tok_iv = cnt["sv"]

            def s_sqrt(sc, tok_iv=tok_iv):
                sc.wait_ge(sem["sv"], tok_iv)
                sc.activation(stA[:, 12:16], stC[:, 8:12], AF.Sqrt).then_inc(sem["ss"], 1)
                cnt["ss"] += 1
            on("scalar", s_sqrt)
            tok_r0 = cnt["ss"]

            def v_newton(v, tok_r0=tok_r0):
                v.wait_ge(sem["ss"], tok_r0)
                v.tensor_tensor(stC[:, 12:16], stA[:, 12:16], stA[:, 12:16], op=OP.mult)
                v.drain()
                v.tensor_tensor(stD[:, 0:4], stB[:, 8:12], stC[:, 12:16], op=OP.mult)
                v.drain()
                v.tensor_scalar(stC[:, 0:4], stD[:, 0:4], -0.5, 1.5, op0=OP.mult, op1=OP.add)
                v.drain()
                v.tensor_tensor(stD[:, 8:12], stA[:, 12:16], stC[:, 0:4], op=OP.mult)
                v.drain()
                v.drain()
                v.engine_nop().then_inc(sem["sv"], 1)
                cnt["sv"] += 1
            on("vector", v_newton)
            return cnt["sv"]

        tok_rstd = emit_ln_b()

        smm_xq0 = cnt["smm"]
        ss_xq0 = cnt["ss"]
        for c in range(4):
            par = c % 2

            def b_xq_v(v, c=c, par=par, tr=tok_rstd, smm_xq0=smm_xq0):
                if c == 0:
                    v.wait_ge(sem["sv"], tr)
                if c >= 1:
                    v.wait_ge(sem["smm"], smm_xq0 + c * 8)
                v.tensor_scalar(xn32[:, :], x_slab[:, c, :], stB[:, c:c + 1], stD[:, 8 + c:9 + c],
                                op0=OP.subtract, op1=OP.mult)
                v.drain()
                emit_bmax(v, xn32[:, :], D, bmx[:, 0:64], bsc[:, 0:64], brr[:, 0:64])
                emit_quant_f16(v, xn32[:, :], D, xq_tok[par][:, :], bsc[:, 0:64], brr[:, 0:64])
                v.drain()
                v.drain()
                v.engine_nop().then_inc(sem["sv"], 1)
                cnt["sv"] += 1
            on("vector", b_xq_v)
            tok_xq = cnt["sv"]

            def b_xq_t(t, c=c, par=par, tok_xq=tok_xq, ss_xq0=ss_xq0, m=m,
                       stw=((sta_prev, stb_prev) if (m >= 1 and c == 0) else None)):
                t.wait_ge(sem["sv"], tok_xq)
                if m == 0 and c == 0:
                    t.wait_ge(sem["sv"], tok_attc)
                if stw:
                    t.wait_ge(sem["ssta"], stw[0])
                    t.wait_ge(sem["sstb"], stw[1])
                for j in range(8):
                    it = c * 8 + j
                    bA = pb[it % 2]
                    if it >= 2:
                        t.wait_ge(sem["ss"], ss_xq0 + (it - 1))
                    t.matmul(bA[0:128, 0:128], xq_tok[par][:, j * 128:(j + 1) * 128], identh[:, :],
                             start=True, stop=True).then_inc(sem["smm"], 1)
                    cnt["smm"] += 1
            on("tensor", b_xq_t)

            def b_xq_s(s, c=c, smm_xq0=smm_xq0):
                for j in range(8):
                    it = c * 8 + j
                    bA = pb[it % 2]
                    s.wait_ge(sem["smm"], smm_xq0 + it + 1)
                    s.copy(xq_feat[:, j, c * 128:(c + 1) * 128], bA[0:128, 0:128]).then_inc(sem["ss"], 1)
                    cnt["ss"] += 1
            on("scalar", b_xq_s)
        tok_xqf = cnt["ss"]

        if DBG:
            def dump_xq(sy, m=m, t=tok_xqf):
                sy.wait_ge(sem["ss"], t)
                sy.dma_start(out=dbg_xqall[m], in_=xq_feat[:, :, :]).then_inc(sem["sgp"], 16)
                cnt["sgp"] += 16
            on("sync", dump_xq)

        for ig in range(NIG):
            par = ig % 2

            def b_wgu_load(sync, ig=ig, par=par, w=tok_wgu_free[par],
                           first=(m == 0 and ig < 2)):
                if first:
                    sync.wait_ge(sem["sca"], SCC_AG)
                if w:
                    sync.wait_ge(sem["smm"], w)
                sn = "swga" if par == 0 else "swgb"
                sync.dma_start(out=wgu_sb[:, par, :, :],
                               in_=ag_wgu_out.ap()[:, :, ig * 512:(ig + 1) * 512]
                               .rearrange("c p f -> p c f")).then_inc(sem[sn], 16)
                cnt[sn] += 16
            on("sync", b_wgu_load)
            tok_wgu = cnt["swga" if par == 0 else "swgb"]
            sem_wgu = "swga" if par == 0 else "swgb"

            for tcg in range(4):
                bank = pb[2 + tcg % 2]

                def b_gu_t(t, par=par, tcg=tcg, bank=bank, tok_wgu=tok_wgu,
                           tok_xqf=tok_xqf, need_free=tok_bank_free[tcg % 2], sem_wgu=sem_wgu,
                           first=(ig == 0 and tcg == 0)):
                    t.wait_ge(sem[sem_wgu], tok_wgu)
                    if first:
                        t.wait_ge(sem["ss"], tok_xqf)
                    if need_free:
                        t.wait_ge(sem["sv"], need_free)
                    for kc in range(8):
                        ins = t.matmul(bank[:, 0:512],
                                       xq_feat[:, kc, tcg * 128:(tcg + 1) * 128],
                                       wgu_sb[:, par, kc, :],
                                       start=(kc == 0), stop=(kc == 7))
                    ins.then_inc(sem["smm"], 1)
                    cnt["smm"] += 1
                on("tensor", b_gu_t)
                tok_gu = cnt["smm"]

                def b_silu_s(s, bank=bank, sl=tcg % 2, tok_gu=tok_gu, w=tok_sgt_used[tcg % 2]):
                    s.wait_ge(sem["smm"], tok_gu)
                    if w:
                        s.wait_ge(sem["sv"], w)
                    s.activation(sgt[sl][:, :], bank[:, 0:256], AF.Silu).then_inc(sem["ss"], 1)
                    cnt["ss"] += 1
                on("scalar", b_silu_s)
                tok_silu = cnt["ss"]

                def b_h_v(v, bank=bank, sl=tcg % 2, tcg=tcg, ig=ig, tok_silu=tok_silu):
                    v.wait_ge(sem["ss"], tok_silu)
                    v.tensor_tensor(h_tok[:, tcg, ig * 256:(ig + 1) * 256], sgt[sl][:, :],
                                    bank[:, 256:512], op=OP.mult)
                    v.drain()
                    v.engine_nop().then_inc(sem["sv"], 1)
                    cnt["sv"] += 1
                on("vector", b_h_v)
                tok_sgt_used[tcg % 2] = cnt["sv"]
                tok_bank_free[tcg % 2] = cnt["sv"]
            tok_wgu_free[par] = cnt["smm"]
        tok_h_done = cnt["sv"]

        for tcg in range(4):
            for hf in range(2):
                src = h_tok[:, tcg, hf * 1408:(hf + 1) * 1408]
                par = (tcg * 2 + hf) % 2

                def b_hq_v(v, src=src, par=par, tok_h_done=tok_h_done, need=cnt["smm"]):
                    v.wait_ge(sem["sv"], tok_h_done)
                    if need:
                        v.wait_ge(sem["smm"], need)
                    emit_bmax(v, src, 1408, bmx[:, 0:88], bsc[:, 0:88], brr[:, 0:88])
                    emit_quant_f16(v, src, 1408, hqx[par][:, 0:1408], bsc[:, 0:88], brr[:, 0:88])
                    v.drain()
                    v.engine_nop().then_inc(sem["sv"], 1)
                    cnt["sv"] += 1
                on("vector", b_hq_v)
                tok_hq = cnt["sv"]

                smm_ht0 = cnt["smm"]
                ss_ht0 = cnt["ss"]

                def b_hq_t(t, par=par, tok_hq=tok_hq, ss_ht0=ss_ht0):
                    t.wait_ge(sem["sv"], tok_hq)
                    t.wait_ge(sem["ss"], ss_ht0)
                    for j in range(11):
                        bA = pb[4 + j % 2]
                        if j >= 2:
                            t.wait_ge(sem["ss"], ss_ht0 + (j - 1))
                        t.matmul(bA[0:128, 0:128], hqx[par][:, j * 128:(j + 1) * 128],
                                 identh[:, :], start=True, stop=True).then_inc(sem["smm"], 1)
                        cnt["smm"] += 1
                on("tensor", b_hq_t)

                def b_hq_s(s, tcg=tcg, hf=hf, smm_ht0=smm_ht0):
                    for j in range(11):
                        bA = pb[4 + j % 2]
                        s.wait_ge(sem["smm"], smm_ht0 + j + 1)
                        s.copy(hqT[:, hf * 11 + j, tcg * 128:(tcg + 1) * 128],
                               bA[0:128, 0:128]).then_inc(sem["ss"], 1)
                        cnt["ss"] += 1
                on("scalar", b_hq_s)
        tok_hqT_done = cnt["ss"]


        def b_norm_t(t, m=m, prev=cnt["ss"]):
            if m == 0:
                t.wait_ge(sem["sv"], tok_attc)
            t.wait_ge(sem["ss"], prev)
            for p in range(8):
                ins = t.matmul(pb[6][0:16, 0:512], ksb8[:, p, :], q_res[:, p, m * MT:(m + 1) * MT],
                               start=(p == 0), stop=(p == 7))
            ins.then_inc(sem["smm"], 1)
            cnt["smm"] += 1
        on("tensor", b_norm_t)
        tok_norm = cnt["smm"]

        def b_nrec_v(v, tok_norm=tok_norm):
            v.wait_ge(sem["smm"], tok_norm)
            v.tensor_scalar(nplus[:, :], pb[6][0:16, 0:512], 1e-6, None, op0=OP.add)
            v.drain()
            v.reciprocal(nplus[:, :], nplus[:, :])
            v.drain()
            v.tensor_copy(nrec16[:, :], nplus[:, :])
            v.drain()
            v.engine_nop().then_inc(sem["sv"], 1)
            cnt["sv"] += 1
        on("vector", b_nrec_v)
        tok_nrec = cnt["sv"]

        ss_rb0 = cnt["ss"]
        sv_at0 = cnt["sv"]
        for p in range(8):
            par = p % 2

            def b_rb_t(t, p=p, par=par, tok_nrec=tok_nrec, ss_rb0=ss_rb0, sv_at0=sv_at0, m=m):
                if p == 0:
                    t.wait_ge(sem["sv"], tok_nrec)
                if p >= 2:
                    t.wait_ge(sem["ss"], ss_rb0 + (p - 1))
                    t.wait_ge(sem["sv"], sv_at0 + (p - 1))
                t.matmul(pb[6 + par][:, 0:512], selb[:, p * 128:(p + 1) * 128], nrec16[:, :],
                         start=True, stop=True).then_inc(sem["smm"], 1)
                cnt["smm"] += 1
                t.matmul(pb[2 + par][:, 0:512], bdkv[:, p, :], q_res[:, p, m * MT:(m + 1) * MT],
                         start=True, stop=True).then_inc(sem["smm"], 1)
                cnt["smm"] += 1
            on("tensor", b_rb_t)
            tok_rbmm = cnt["smm"]

            def b_rb_s(s, par=par, tok_rbmm=tok_rbmm):
                s.wait_ge(sem["smm"], tok_rbmm - 1)
                s.copy(rbx[par][:, :], pb[6 + par][:, 0:512]).then_inc(sem["ss"], 1)
                cnt["ss"] += 1
            on("scalar", b_rb_s)
            tok_rbs = cnt["ss"]

            def b_at_v(v, p=p, par=par, tok_rbmm=tok_rbmm, tok_rbs=tok_rbs):
                v.wait_ge(sem["smm"], tok_rbmm)
                v.wait_ge(sem["ss"], tok_rbs)
                v.tensor_tensor(attn2[:, p, :], pb[2 + par][:, 0:512], rbx[par][:, :], op=OP.mult)
                v.drain()
                v.drain()
                v.engine_nop().then_inc(sem["sv"], 1)
                cnt["sv"] += 1
            on("vector", b_at_v)
        tok_attn2 = cnt["sv"]
        tok_bank_free[0] = tok_attn2
        tok_bank_free[1] = tok_attn2

        if DBG:
            def dump_at2(sy, m=m, t=tok_attn2):
                sy.wait_ge(sem["sv"], t)
                sy.dma_start(out=dbg_at2all[m], in_=attn2[:, :, :]).then_inc(sem["sgp"], 16)
                cnt["sgp"] += 16
            on("sync", dump_at2)

        for ic in range(ICN):
            par = ic % 2

            def b_wd_l(sync, ic=ic, par=par, w=wd_used_tok[par], m=m):
                if m == 0 and ic == 0:
                    sync.wait_ge(sem["sca"], SCC_AG)
                if w:
                    sync.wait_ge(sem["smm"], w)
                sn = "swda" if par == 0 else "swdb"
                sync.dma_start(out=wd_ic[par][:, :],
                               in_=ag_wd_out.ap()[ic % 8, ic // 8, :, :]).then_inc(sem[sn], 16)
                cnt[sn] += 16
            on("sync", b_wd_l)
            wd_tok = cnt["swda" if par == 0 else "swdb"]
            sem_wd = "swda" if par == 0 else "swdb"

            def b_wd_t(t, ic=ic, par=par, wt=wd_tok, ta=tok_attn2, tok_hqT_done=tok_hqT_done,
                       sem_wd=sem_wd):
                t.wait_ge(sem[sem_wd], wt)
                if ic == 0:
                    t.wait_ge(sem["sv"], ta)
                    t.wait_ge(sem["ss"], tok_hqT_done)
                for tcg in range(4):
                    for fg in range(2):
                        ins = t.matmul(pb[tcg * 2 + fg][:, 0:512],
                                       hqT[:, ic, tcg * 128:(tcg + 1) * 128],
                                       wd_ic[par][:, fg * 512:(fg + 1) * 512],
                                       start=(ic == 0), stop=False)
                ins.then_inc(sem["smm"], 1)
                cnt["smm"] += 1
            on("tensor", b_wd_t)
            wd_used_tok[par] = cnt["smm"]

        for p in range(8):
            par = p % 2

            def b_wo_l(sync, p=p, par=par, w=wo_used[par], first=(m == 0 and p < 2)):
                if first:
                    sync.wait_ge(sem["sx"], WC_SX)
                if w:
                    sync.wait_ge(sem["smm"], w)
                sn = "swoa" if par == 0 else "swob"
                sync.dma_start(out=wout_p[par][:, :],
                               in_=wout16.ap()[p * 128:(p + 1) * 128, :]).then_inc(sem[sn], 16)
                cnt[sn] += 16
            on("sync", b_wo_l)
            wo_tok = cnt["swoa" if par == 0 else "swob"]
            sem_wo = "swoa" if par == 0 else "swob"

            def b_wo_t(t, p=p, par=par, wt=wo_tok, sem_wo=sem_wo):
                t.wait_ge(sem[sem_wo], wt)
                for tcg in range(4):
                    for fg in range(2):
                        ins = t.matmul(pb[tcg * 2 + fg][:, 0:512],
                                       attn2[:, p, tcg * 128:(tcg + 1) * 128],
                                       wout_p[par][:, fg * 512:(fg + 1) * 512],
                                       start=False, stop=False)
                ins.then_inc(sem["smm"], 1)
                cnt["smm"] += 1
            on("tensor", b_wo_t)
            wo_used[par] = cnt["smm"]

        out_mm_tok = []

        def b_bias_t(t):
            for tcg in range(4):
                for fg in range(2):
                    t.matmul(pb[tcg * 2 + fg][:, 0:512], ones1[0:1, :],
                             boutr[0:1, fg * 512:(fg + 1) * 512],
                             start=False, stop=True).then_inc(sem["smm"], 1)
                    cnt["smm"] += 1
                    out_mm_tok.append(cnt["smm"])
        on("tensor", b_bias_t)

        for tcg in range(4):
            par = tcg % 2

            def b_oev_v(v, tcg=tcg, par=par, w=tok_outsb_free[par],
                        tokk=out_mm_tok[tcg * 2 + 1]):
                v.wait_ge(sem["smm"], tokk)
                if w:
                    v.wait_ge(sem["ssta" if par == 0 else "sstb"], w)
                for fg in range(2):
                    v.tensor_tensor(out_sb[par][:, fg * 512:(fg + 1) * 512],
                                    pb[tcg * 2 + fg][:, 0:512],
                                    x_slab[:, tcg, fg * 512:(fg + 1) * 512], op=OP.add)
                v.drain()
                v.drain()
                v.engine_nop().then_inc(sem["sv"], 1)
                cnt["sv"] += 1
            on("vector", b_oev_v)
            tok_oev = cnt["sv"]

            def b_store(sync, tcg=tcg, par=par, m=m, tok_oev=tok_oev):
                sync.wait_ge(sem["sv"], tok_oev)
                sn = "ssta" if par == 0 else "sstb"
                sync.dma_start(out=out_d[m * MT + tcg * 128: m * MT + (tcg + 1) * 128, :],
                               in_=out_sb[par][:, :]).then_inc(sem[sn], 16)
                cnt[sn] += 16
            on("sync", b_store)
            tok_outsb_free[par] = cnt["ssta" if par == 0 else "sstb"]
        sta_prev = cnt["ssta"]
        stb_prev = cnt["sstb"]
        tok_x_free_b = cnt["sv"]
        tok_bank_free[0] = cnt["sv"]
        tok_bank_free[1] = cnt["sv"]
        tok_hqT_free = cnt["smm"]

    SSTA_TOT = cnt["ssta"]
    SSTB_TOT = cnt["sstb"]

    if DBG:
        def dump(g):
            g.wait_ge(sem["ssta"], SSTA_TOT)
            g.wait_ge(sem["sstb"], SSTB_TOT)
            g.dma_start(out=dbg_agwgu, in_=ag_wgu_out.ap()[:, :, :]).then_inc(sem["sgp"], 16)
            cnt["sgp"] += 16
            g.dma_start(out=dbg_wgusb, in_=wgu_sb[:, :, :, :]).then_inc(sem["sgp"], 16)
            cnt["sgp"] += 16
            g.dma_start(out=dbg_gup, in_=dbg_cap_gup[:, :]).then_inc(sem["sgp"], 16)
            cnt["sgp"] += 16
            g.dma_start(out=dbg_sgt, in_=dbg_cap_sgt[:, :]).then_inc(sem["sgp"], 16)
            cnt["sgp"] += 16
            for dst, srct in [(dbg_qres, q_res), (dbg_bdkv, bdkv), (dbg_ksb8, ksb8),
                              (dbg_xqf, xq_feat), (dbg_hqT, hqT), (dbg_attn2, attn2),
                              (dbg_htok, h_tok), (dbg_nrec, nplus)]:
                g.dma_start(out=dst, in_=srct[:, :] if len(dst.shape) == 2 else srct[:, :, :]).then_inc(sem["sgp"], 16)
                cnt["sgp"] += 16
            g.dma_start(out=dbg_st[:, 0:16], in_=stA[:, :]).then_inc(sem["sgp"], 16)
            g.dma_start(out=dbg_st[:, 16:32], in_=stB[:, :]).then_inc(sem["sgp"], 16)
            g.dma_start(out=dbg_st[:, 32:48], in_=stC[:, :]).then_inc(sem["sgp"], 16)
            g.dma_start(out=dbg_st[:, 48:64], in_=stD[:, :]).then_inc(sem["sgp"], 16)
            cnt["sgp"] += 64
            g.wait_ge(sem["sgp"], cnt["sgp"])
        on("gpsimd", dump)

    def fin(sync):
        sync.wait_ge(sem["ssta"], SSTA_TOT)
        sync.wait_ge(sem["sstb"], SSTB_TOT)
        sync.wait_ge(sem["scc"], cnt["scc"])
        sync.wait_ge(sem["sca"], cnt["sca"])
        if DBG:
            sync.wait_ge(sem["sgp"], cnt["sgp"])
    on("sync", fin)

    es.close()
    return nc


_NC_CACHE = {}


def kernel(L0_attn_x, L0_attn_W_qkv, L0_attn_W_f, L0_attn_b_f, L0_attn_W_out,
           L0_attn_b_out, L0_ffn_w_gate, L0_ffn_w_up, L0_ffn_w_down):
    x = np.ascontiguousarray(np.asarray(L0_attn_x, dtype=np.float32)).reshape(NCORE, TCORE, D)
    wqkv = np.ascontiguousarray(np.asarray(L0_attn_W_qkv, dtype=np.float32))
    wf = np.ascontiguousarray(np.asarray(L0_attn_W_f, dtype=np.float32))
    b_f = np.asarray(L0_attn_b_f, dtype=np.float32)
    wout = np.ascontiguousarray(np.asarray(L0_attn_W_out, dtype=np.float32))
    b_out = np.asarray(L0_attn_b_out, dtype=np.float32)
    wg = np.ascontiguousarray(np.asarray(L0_ffn_w_gate, dtype=np.float32))
    wu = np.ascontiguousarray(np.asarray(L0_ffn_w_up, dtype=np.float32))
    wd = np.ascontiguousarray(np.asarray(L0_ffn_w_down, dtype=np.float32))

    bft = np.tile(b_f, 16)
    bqc = np.ascontiguousarray(bft.reshape(8, 128).T)
    bkvr = np.ascontiguousarray(np.concatenate([bft, np.zeros(D, np.float32)])[None, :])
    boutr = np.ascontiguousarray(b_out[None, :])
    idb = np.eye(128, dtype=np.float32).astype(bf16)
    idh = np.eye(128, dtype=np.float32).astype(np.float16)
    ones1 = np.ones((1, 128), np.float32).astype(bf16)
    onesc = np.ones((128, 8), np.float32).astype(bf16)
    selb = np.zeros((16, 8 * 128), np.float32)
    for p in range(8):
        selb[2 * p, p * 128: p * 128 + 64] = 1.0
        selb[2 * p + 1, p * 128 + 64: p * 128 + 128] = 1.0
    selb = selb.astype(bf16)

    in_maps = []
    for c in range(NCORE):
        wd_ch = []
        for s in range(WDCH):
            ic = s * 8 + c
            if ic < ICN:
                wd_ch.append(wd[:, ic * 128:(ic + 1) * 128])
            else:
                wd_ch.append(np.zeros((D, 128), np.float32))
        in_maps.append({
            "x": np.ascontiguousarray(x[c]),
            "wqkv": wqkv, "wf": wf, "wout": wout,
            "bqc": bqc, "bkvr": bkvr, "boutr": boutr,
            "wg_sl": np.ascontiguousarray(wg[:, c * 128:(c + 1) * 128]),
            "wu_sl": np.ascontiguousarray(wu[:, c * 128:(c + 1) * 128]),
            "wd_sl": np.ascontiguousarray(np.stack(wd_ch, 0)),
            "idb": idb, "idh": idh, "ones1": ones1, "onesc": onesc, "selb": selb,
        })

    if "prog" not in _NC_CACHE:
        _NC_CACHE["prog"] = build_nc()
    nc = _NC_CACHE["prog"]
    res = run_bass_kernel_spmd(nc, in_maps, core_ids=list(range(NCORE)))
    import os as _os
    if int(_os.environ.get("KDBG", "0")):
        kernel.DBG_RES = res
    out = np.concatenate([res.results[c]["out"][None] for c in range(NCORE)], 0)
    return np.ascontiguousarray(out.reshape(2, 16384, D))
